# version 07
# baseline (speedup 1.0000x reference)
import numpy as np
import ml_dtypes
from contextlib import ExitStack

import concourse.bass as bass
import concourse.mybir as mybir
from concourse.bass_utils import run_bass_kernel_spmd
from concourse.bass_core_rust import IndirectOffsetOnAxis

F32 = mybir.dt.float32
F32R = mybir.dt.float32r
BF16 = mybir.dt.bfloat16
I32 = mybir.dt.int32
AF = mybir.ActivationFunctionType
ALU = mybir.AluOpType
AX = mybir.AxisListType
BF = ml_dtypes.bfloat16
SQD = float(np.sqrt(1024.0))


class Cfg:
    def __init__(self, To=4096, Tf=8192, groups=None, parts=(1, 2, 3, 4),
                 dumps=(), dumps_q=(), dbg_serial=False):
        self.dumps = dumps
        self.dumps_q = dumps_q
        self.dbg_serial = dbg_serial
        self.To = To; self.Tf = Tf
        self.D = 1024; self.DT = 8; self.H = 16
        self.KSUP = min(1024, Tf); self.NSUP = Tf // self.KSUP
        self.NKT = self.KSUP // 128
        self.QH = min(1024, To); self.NQH = To // self.QH
        self.QB = 256
        self.NQB = self.QH // self.QB
        self.QBA = 512
        self.NQBA = self.QH // self.QBA
        self.NNODE = 255; self.NLEAF = 256
        self.SLOTS = ((To + 15 * 256 + 1023) // 1024) * 1024
        self.NTILE = self.SLOTS // 128
        self.NCHUNK = self.SLOTS // 16
        self.NBLK = self.SLOTS // 1024
        self.ST = To // 512
        self.NTT = To // 128
        self.groups = groups if groups is not None else [[0, 1], [2, 3], [4, 5], [6, 7]]
        self.parts = parts


def ds(i, n):
    return slice(i * n, (i + 1) * n)



def prep_core_inputs(inputs, core, cfg):
    c = cfg
    b, h = core // 2, core % 2
    x = np.asarray(inputs['L0_attn_a_x'])
    d = {'xT': np.ascontiguousarray(x[b, h * c.To:(h + 1) * c.To, :].T)}
    d['aa_wqkv'] = np.asarray(inputs['L0_attn_a_w_qkv'])
    d['aa_wout'] = np.asarray(inputs['L0_attn_a_w_out'])
    d['aa_gcol'] = np.ascontiguousarray(
        np.asarray(inputs['L0_attn_a_gamma']).reshape(8, 128).T)
    for nm in 'qkvo':
        W = np.asarray(inputs['L0_attn_b_W' + nm])
        d['ab_w' + nm] = np.ascontiguousarray(W.T)
        bb = np.asarray(inputs['L0_attn_b_b' + nm])
        d['ab_bc' + nm] = np.ascontiguousarray(bb.reshape(8, 128).T)
        d['ab_br' + nm] = np.ascontiguousarray(bb.reshape(1, 1024))
    for t in 'qkvf':
        g = lambda s: np.asarray(inputs['L0_ffn_%s_%s' % (t, s)])
        d['f%s_nwT' % t] = np.ascontiguousarray(g('nw').T)
        nb = g('nb')
        nbp = np.zeros((128, 2), np.float32)
        nbp[:, 0] = nb[0:128]
        nbp[0:127, 1] = nb[128:255]
        d['f%s_nb' % t] = nbp
        d['f%s_w1g' % t] = np.ascontiguousarray(
            g('w1').reshape(256, 8, 128, 16).transpose(0, 2, 1, 3).reshape(256, 128, 128))
        d['f%s_b1' % t] = np.ascontiguousarray(g('b1').reshape(4096, 1))
        d['f%s_w2g' % t] = np.ascontiguousarray(g('w2').reshape(4096, 1024))
        d['f%s_b2' % t] = np.asarray(g('b2'))
    d['c_iota'] = np.arange(128, dtype=np.float32).reshape(128, 1)
    d['c_pmod'] = (np.arange(128, dtype=np.float32) % 16).reshape(128, 1)
    d['c_tokid'] = np.ascontiguousarray(
        np.arange(c.To, dtype=np.float32).reshape(c.To // 128, 128).T)
    d['c_oob'] = np.full((c.SLOTS,), 1 << 20, np.int32)
    jj, ss = np.meshgrid(np.arange(128), np.arange(128), indexing='ij')
    d['c_maskdiag'] = ((jj // 16) == (ss // 16)).astype(BF)
    ind = np.zeros((8, 128), np.float32)
    for cc in range(8):
        ind[cc, cc * 16:(cc + 1) * 16] = 1.0
    d['c_indic8'] = ind.astype(BF)
    nkt, nqb = c.Tf // 128, c.To // c.QB
    kd = np.zeros((128, nkt * nqb), np.float32)
    for kt in range(nkt):
        for qb in range(nqb):
            kd[:, kt * nqb + qb] = np.arange(128) + 128 * kt - (qb * c.QB + h * c.To) + 16
    d['c_kd'] = kd
    d['c_qrow'] = np.tile(np.arange(c.QB, dtype=np.float32), (128, 1))
    d['c_srow'] = (16.0 * np.arange(c.NCHUNK, dtype=np.float32)).reshape(1, c.NCHUNK)
    return d



class NS:
    pass


def build(cfg):
    c = cfg
    nc = bass.Bass("TRN2", target_bir_lowering=False)
    k = NS(); k.c = c; k.nc = nc; k.E = {}

    def dram(name, shape, dt, kind="Internal"):
        k.E[name] = nc.dram_tensor(name, list(shape), dt, kind=kind)

    dram('xT', (c.D, c.To), F32, "ExternalInput")
    dram('aa_wqkv', (c.D, 3072), F32, "ExternalInput")
    dram('aa_wout', (c.D, c.D), F32, "ExternalInput")
    dram('aa_gcol', (128, 8), F32, "ExternalInput")
    for nm in 'qkvo':
        dram('ab_w' + nm, (c.D, c.D), F32, "ExternalInput")
        dram('ab_bc' + nm, (128, 8), F32, "ExternalInput")
        dram('ab_br' + nm, (1, c.D), F32, "ExternalInput")
    for t in 'qkvf':
        dram('f%s_nwT' % t, (c.D, c.NNODE), F32, "ExternalInput")
        dram('f%s_nb' % t, (128, 2), F32, "ExternalInput")
        dram('f%s_w1g' % t, (c.NLEAF, 128, 128), F32, "ExternalInput")
        dram('f%s_b1' % t, (4096, 1), F32, "ExternalInput")
        dram('f%s_w2g' % t, (4096, c.D), F32, "ExternalInput")
        dram('f%s_b2' % t, (c.NLEAF, c.D), F32, "ExternalInput")
    dram('c_iota', (128, 1), F32, "ExternalInput")
    dram('c_pmod', (128, 1), F32, "ExternalInput")
    dram('c_tokid', (128, c.To // 128), F32, "ExternalInput")
    dram('c_oob', (c.SLOTS,), I32, "ExternalInput")
    dram('c_maskdiag', (128, 128), BF16, "ExternalInput")
    dram('c_indic8', (8, 128), BF16, "ExternalInput")
    dram('c_kd', (128, (c.Tf // 128) * (c.To // c.QB)), F32, "ExternalInput")
    dram('c_qrow', (128, c.QB), F32, "ExternalInput")
    dram('c_srow', (1, c.NCHUNK), F32, "ExternalInput")
    dram('outT', (c.D, c.To), F32, "ExternalOutput")

    dram('resid', (c.D, c.To), F32)
    dram('xn232', (c.D, c.To), F32)
    dram('yT32', (c.D, c.To), F32)
    dram('xtok', (c.To, c.D), BF16)
    dram('ytokM', (c.To, c.D), BF16)
    dram('xsorted', (c.SLOTS, c.D), BF16)
    dram('qTd', (c.D, c.To), BF16)
    dram('qtok', (c.To, c.D), BF16)
    dram('ktok', (c.To, 1056), BF16)
    dram('vtok', (c.To, 1056), BF16)
    dram('kfull', (c.Tf, 1056), BF16)
    dram('vfull', (c.Tf, 1056), BF16)
    dram('ktokF', (c.To, 1024), BF16)
    dram('vtokF', (c.To, 1024), BF16)
    dram('kfullF', (c.Tf, 1024), BF16)
    dram('vfullF', (c.Tf, 1024), BF16)
    dram('ftok', (c.To, c.D), BF16)
    dram('slotTab', (c.SLOTS,), I32)
    dram('rsc', (8192,), F32)
    for t in 'qkvf':
        dram('w1gB_' + t, (c.NLEAF * 128, 128), BF16)
        dram('w2gB_' + t, (4096, c.D), BF16)
        dram('b2B_' + t, (c.NLEAF, c.D), BF16)
    for nm in c.dumps:
        h = k.E[nm]
        dram('dmp_' + nm, list(h.shape), h.dtype, "ExternalOutput")
    for nm in c.dumps_q:
        if nm.startswith('S.'):
            continue
        h = k.E[nm]
        dram('dmpq_' + nm, list(h.shape), h.dtype, "ExternalOutput")

    es = ExitStack(); k.es = es

    def sbuf(name, shape, dt):
        return es.enter_context(nc.sbuf_tensor(name, list(shape), dt))

    S = NS(); k.S = S
    QH = c.QH
    S.ktT = sbuf('ktT', (128, c.DT, c.KSUP), BF16)
    S.vsup = sbuf('vsup', (128, c.NKT, 1056), BF16)
    S.qT = sbuf('qT', (128, c.DT, QH), BF16)
    S.arena = sbuf('arena', (128, c.DT * QH), F32)
    S.pT = sbuf('pT', (128, 4, 512), BF16)
    S.x32 = sbuf('x32', (128, c.DT, 512), F32)
    S.xw = sbuf('xw', (128, c.DT, 512), BF16)
    S.ttk = sbuf('ttk', (128, 2, c.D), BF16)
    S.rows2 = sbuf('rows2', (128, max(c.To, 4 * c.QH)), F32)
    S.bc = sbuf('bc', (128, 1024), F32)
    S.cols = sbuf('cols', (128, 16), F32)
    S.w0 = sbuf('w0', (128, 2, 2048), BF16)
    S.onesb = sbuf('onesb', (128, 2), BF16)
    S.ones32c = sbuf('ones32c', (128, 1), F32)
    S.ones1rb = sbuf('ones1rb', (1, 128), BF16)
    S.ones1r32 = sbuf('ones1r32', (1, 128), F32)
    S.iota = sbuf('iota', (128, 1), F32)
    S.pmod = sbuf('pmod', (128, 1), F32)
    S.iden = sbuf('iden', (128, 128), BF16)
    S.iden32 = sbuf('iden32', (128, 128), F32)
    S.rowsc = sbuf('rowsc', (128, 2048), F32)
    S.nwT = sbuf('nwT', (128, c.DT, 256), F32R)
    S.xr = sbuf('xr', (128, c.DT, 256), F32R)
    S.nb = sbuf('nb', (128, 2), F32)
    S.tokid = sbuf('tokid', (128, c.To // 128), F32)
    S.tki = sbuf('tki', (128, c.To // 128), I32)
    S.dstc = sbuf('dstc', (128, c.To // 128), I32)
    S.srt = sbuf('srt', (128, c.NTILE), I32)
    S.idxw1 = sbuf('idxw1', (128, c.NCHUNK), I32)
    S.idxw2 = sbuf('idxw2', (128, c.NTILE), I32)
    S.clri = sbuf('clri', (128, c.NTILE), I32)
    S.clrf = sbuf('clrf', (8, c.NTILE), F32)
    S.gth = sbuf('gth', (128, 2, c.D), BF16)
    S.w1sb = sbuf('w1sb', (128, 2, c.DT, 128), BF16)
    S.w2sb = sbuf('w2sb', (128, 2, c.D), BF16)
    S.b2sb = sbuf('b2sb', (128, 2, c.D), BF16)
    S.b1sb = sbuf('b1sb', (128, 2, 1), F32)
    S.hT = sbuf('hT', (128, 2, 128), BF16)
    S.hA = sbuf('hA', (128, 2, 128), BF16)
    S.ysl = sbuf('ysl', (128, 2, c.D), BF16)
    S.mdiag = sbuf('mdiag', (128, 128), BF16)
    S.indic = sbuf('indic', (8, 128), BF16)
    S.kd = sbuf('kd', (128, (c.Tf // 128) * (c.To // c.QB)), F32)
    S.qrow = sbuf('qrow', (128, c.QB), F32)
    S.srow = sbuf('srow', (1, c.NCHUNK), F32)
    S.brow = sbuf('brow', (128, 2 * c.D), BF16)
    S.vev = sbuf('vev', (128, 544), BF16)
    S.constv = sbuf('constv', (128, 1), F32)
    nc.const_aps.register(F32, 1e-5, S.constv[:, 0:1])

    for nm in c.dumps_q:
        if nm.startswith('S.'):
            t = getattr(S, nm[2:])
            dram('dmpq_' + nm[2:], list(t[:].tensor.shape), t[:].dtype,
                 "ExternalOutput")
    k.PA = es.enter_context(nc.psum_tensor('PA', [128, 2048], F32))
    k.PB = es.enter_context(nc.psum_tensor('PB', [128, 2048], F32))

    k.sems = {}
    for s in ['dq', 'dg', 'cc', 'dw0', 'dw1', 'sp0', 'sp1', 'sv', 'sb',
              'gg0', 'gg1', 'gx0', 'gx1', 'gt0', 'gt1', 'gy0', 'gy1', 'dk',
              'pe', 'ac', 'dv', 'po', 'sq']:
        k.sems[s] = es.enter_context(nc.semaphore(s))
    k.DMASEMS = ('dq', 'dg', 'cc', 'dw0', 'dw1', 'sp0', 'sp1', 'sv', 'sb',
                 'gg0', 'gg1', 'gx0', 'gx1', 'gt0', 'gt1', 'gy0', 'gy1', 'dk')
    return k



class EngProxy:
    def __init__(self, k, name):
        self._k = k
        self._name = name

    _NOWAIT = {'wait_ge', 'alloc_register', 'reg_mov', 'free_register', 'nop'}
    _OWN = {'ve': 'dv', 'sc': 'ac'}

    def __getattr__(self, attr):
        k = self._k
        name = self._name

        def call(*args, **kw):
            if k.live == name:
                own = EngProxy._OWN.get(name)
                if own and attr not in EngProxy._NOWAIT and k.cnt[own] > 0:
                    key = (name, own)
                    if k.waited.get(key, -1) < k.cnt[own]:
                        k.engs[name].wait_ge(k.sems[own], k.cnt[own])
                        k.waited[key] = k.cnt[own]
                try:
                    return getattr(k.engs[name], attr)(*args, **kw)
                except Exception as e:
                    raise RuntimeError('engine op failed: %s.%s: %r' % (name, attr, e)) from e
            elif EngProxy._OWN.get(name) and attr not in EngProxy._NOWAIT:
                own = EngProxy._OWN[name]
                k.waited[(name, own)] = max(k.waited.get((name, own), -1), k.cnt[own])
            return None
        return call

    def wait(self, sem, val=None):
        k = self._k
        if val is None:
            val = k.cnt[sem]
        val = min(val, k.cnt[sem])
        if val <= 0:
            return
        if k.live == self._name:
            key = (self._name, sem)
            if k.waited.get(key, -1) < val:
                k.engs[self._name].wait_ge(k.sems[sem], val)
                k.waited[key] = val

    def inc(self, instr, sem, step=1):
        k = self._k
        k.cnt[sem] += step
        if instr is not None:
            instr.then_inc(k.sems[sem], step)
        return k.cnt[sem]


def emit_program(k, scriptfn):
    blk = k.es.enter_context(k.nc.Block())
    k.engs = {}
    k.T = EngProxy(k, 'te'); k.V = EngProxy(k, 've'); k.C = EngProxy(k, 'sc')
    k.P = EngProxy(k, 'po'); k.Y = EngProxy(k, 'sy')

    def replay(name, eng):
        k.engs[name] = eng
        k.live = name
        k.cnt = {s: 0 for s in k.sems}
        k.waited = {}
        scriptfn(k)

    @blk.sync
    def _(e):
        replay('sy', e)

    @blk.scalar
    def _(e):
        replay('sc', e)

    @blk.vector
    def _(e):
        replay('ve', e)

    @blk.tensor
    def _(e):
        replay('te', e)

    @blk.gpsimd
    def _(e):
        replay('po', e)


def barrier(k, engines=None):
    snap = dict(k.cnt)
    own = {'te': 'pe', 've': 'dv', 'sc': 'ac', 'po': 'po', 'sy': 'sq'}
    prox = {'te': k.T, 've': k.V, 'sc': k.C, 'po': k.P, 'sy': k.Y}
    arr = {}
    for e in ('te', 've', 'sc', 'po', 'sy'):
        p = prox[e]
        for s, v in snap.items():
            if v > 0:
                p.wait(s, v)
        arr[e] = p.inc(p.nop(), own[e])
    for e in ('te', 've', 'sc', 'po', 'sy'):
        p = prox[e]
        for e2 in ('te', 've', 'sc', 'po', 'sy'):
            if e2 != e:
                p.wait(own[e2], arr[e2])



def script(k):
    c = k.c
    k.ac_exp = [0, 0]
    k.dv_msk = [0, 0]
    k.pe_qw = [0, 0]
    k.pe_kw = [0, 0]
    k.wtok_fff = [0, 0]
    k.ysl_tok = [0, 0]
    k.w1cp = [0, 0]
    k.qtd_st = [0, 0]
    k.vev_st = 0
    k.bc_st = 0
    k.qp_ev = [0, 0]
    k.qp_evv = [0, 0]
    k.tkm_tok = [0, 0]
    k.tkm_st = 0
    k.rt_pe = [0, 0]
    if k.live == 'po':
        k.breg = k.engs['po'].alloc_register('bnd')
        k.engs['po'].reg_mov(k.breg, c.To - 1)
    else:
        k.breg = None
    emit_init(k)
    src = 'xT'
    for i, p in enumerate(c.parts):
        dst = 'outT' if i == len(c.parts) - 1 else 'resid'
        if p == 1:
            emit_attn_sublayer(k, src, dst, 'aa')
        elif p == 3:
            emit_attn_sublayer(k, src, dst, 'ab')
        else:
            emit_ffn_sublayer(k, src, dst)
        src = 'resid'
    for nm in c.dumps:
        k.Y.inc(k.Y.dma_start(k.E['dmp_' + nm].ap(), k.E[nm].ap()), 'dq', 16)
        k.Y.wait('dq', k.cnt['dq'])
    for s in ('dq', 'dg', 'cc', 'pe', 'ac', 'dv', 'po'):
        if k.cnt[s] > 0:
            k.Y.wait(s, k.cnt[s])


def emit_init(k):
    c, S, E = k.c, k.S, k.E
    Y, P, V = k.Y, k.P, k.V
    for nm, dst in [('c_iota', S.iota), ('c_pmod', S.pmod), ('c_tokid', S.tokid),
                    ('c_maskdiag', S.mdiag), ('c_indic8', S.indic), ('c_kd', S.kd),
                    ('c_qrow', S.qrow), ('c_srow', S.srow)]:
        Y.inc(Y.dma_start(dst[:], E[nm].ap()), 'dq', 16)
    P.inc(P.memset(S.iden[:], 0.0), 'po')
    P.wait('po', k.cnt['po'])
    P.inc(P.affine_select(out=S.iden[:], in_=S.iden[:], compare_op=ALU.not_equal,
                          fill=1.0, base=0, pattern=[[-1, 128]],
                          channel_multiplier=1), 'po')
    P.inc(P.memset(S.iden32[:], 0.0), 'po')
    P.wait('po', k.cnt['po'])
    P.inc(P.affine_select(out=S.iden32[:], in_=S.iden32[:], compare_op=ALU.not_equal,
                          fill=1.0, base=0, pattern=[[-1, 128]],
                          channel_multiplier=1), 'po')
    P.inc(P.memset(S.constv[:], 1e-5), 'po')
    P.inc(P.memset(S.onesb[:], 1.0), 'po')
    P.inc(P.memset(S.ones32c[:], 1.0), 'po')
    P.inc(P.memset(S.ones1rb[:], 1.0), 'po')
    P.inc(P.memset(S.ones1r32[:], 1.0), 'po')
    P.inc(P.memset(S.vev[:], 0.0), 'po')
    P.wait('po', k.cnt['po'])
    for g in range(8):
        P.inc(P.memset(S.vev[:, g * 66 + 64: g * 66 + 65], 1.0), 'po')
    V.wait('dq', k.cnt['dq'])
    V.inc(V.tensor_copy(S.tki[:], S.tokid[:]), 'dv')
    P.inc(P.memset(S.gth[:, 0, 0:((c.To // 128) * 32)], 0.0), 'po')
    P.wait('po', k.cnt['po'])
    for nm in ('ktok', 'vtok'):
        P.inc(P.dma_start(E[nm].ap()[:, 1024:1056]
                          .rearrange('(a p) v -> p a v', p=128),
                          S.gth[:, 0, 0:((c.To // 128) * 32)]
                          .rearrange('p (a v) -> p a v', v=32)), 'dg', 16)
    vflat = S.vsup[:].rearrange('p a b -> p (a b)')
    piece = 4096
    if 2 in c.parts:
        for t in 'qkvf':
            for (src, dstn, n) in [('f%s_w1g' % t, 'w1gB_' + t, c.NLEAF * 128 * 128),
                                   ('f%s_w2g' % t, 'w2gB_' + t, 4096 * c.D),
                                   ('f%s_b2' % t, 'b2B_' + t, c.NLEAF * c.D)]:
                npc = max(n // (128 * piece), 1)
                per = n // 128 // npc
                sap = E[src].reshape([128, n // 128]).ap()
                dap = E[dstn].reshape([128, n // 128]).ap()
                for i in range(npc):
                    buf = vflat[:, (i % 2) * piece:(i % 2) * piece + per]
                    P.wait('dk', k.cnt['dk'])
                    P.inc(P.dma_start(buf, sap[:, ds(i, per)]), 'dk', 16)
                    P.wait('dk', k.cnt['dk'])
                    P.inc(P.dma_start(dap[:, ds(i, per)], buf), 'dk', 16)
    barrier(k)



def emit_prologue(k, src, kind):
    c, S, E = k.c, k.S, k.E
    Y, P, V, C, T = k.Y, k.P, k.V, k.C, k.T
    PA, PB = k.PA, k.PB
    xs = E[src].ap()

    if kind == 'aa':
        Y.inc(Y.dma_start(S.cols[:, 0:8], E['aa_gcol'].ap()), 'dq', 16)
        barrier(k, ('ve',))
        V.inc(V.tensor_scalar_add(S.cols[:, 0:8], S.cols[:, 0:8], 1.0), 'dv')
    if kind == 'ab':
        Y.inc(Y.dma_start(S.cols[:, 0:8], E['ab_bcq'].ap()), 'dq', 16)
        P.inc(P.dma_start(S.brow[0:1, 0:c.D], E['ab_brk'].ap()), 'dg', 16)
        P.inc(P.dma_start(S.brow[0:1, c.D:2 * c.D], E['ab_brv'].ap()), 'dg', 16)

    for s in range(c.ST):
        tsl = ds(s, 512)
        barrier(k)
        Y.inc(Y.dma_start(S.x32[:], xs[:, tsl].rearrange('(a p) t -> p a t', p=128)),
              'dq', 16)
        ldq = k.cnt['dq']
        C.wait('dq', ldq)
        for dt in range(c.DT):
            C.inc(C.activation(S.xw[:, dt, :], S.x32[:, dt, :], AF.Square), 'ac')
        T.wait('ac', k.cnt['ac'])
        T.wait('dq', ldq)
        for dt in range(c.DT):
            T.matmul(PA[0:1, 1024:1536], S.ones32c[:, 0:1], S.x32[:, dt, :],
                     start=(dt == 0), stop=(dt == c.DT - 1))
        for dt in range(c.DT):
            i = T.matmul(PA[0:1, 1536:2048], S.onesb[:, 0:1], S.xw[:, dt, :],
                         start=(dt == 0), stop=(dt == c.DT - 1))
        T.inc(i, 'pe')
        R = S.rowsc
        R2 = S.rows2
        m_ = R[0:1, 0:512]; m2_ = R[0:1, 512:1024]
        fs_ = R[0:1, 1024:1536]; f_ = R[0:1, 1536:2048]
        v_ = R2[0:1, 0:512]; sq_ = R2[0:1, 512:1024]
        r_ = R2[0:1, 1024:1536]; t_ = R2[0:1, 1536:2048]
        V.wait('pe', k.cnt['pe'])
        V.inc(V.tensor_scalar_mul(m_, PA[0:1, 1024:1536], 1.0 / 1024.0), 'dv')
        V.inc(V.tensor_mul(m2_, m_, m_), 'dv')
        V.inc(V.scalar_tensor_tensor(v_, PA[0:1, 1536:2048], 1.0 / 1024.0, m2_,
                                     ALU.mult, ALU.subtract), 'dv')
        C.wait('dv', k.cnt['dv'])
        C.inc(C.activation(sq_, v_, AF.Sqrt, bias=1e-5), 'ac')
        V.wait('ac', k.cnt['ac'])
        V.inc(V.reciprocal(r_, sq_), 'dv')
        if kind == 'aa':
            C.wait('dv', k.cnt['dv'])
            C.inc(C.activation(t_, v_, AF.Sqrt, scale=1024.0), 'ac')
            V.wait('ac', k.cnt['ac'])
            V.inc(V.tensor_mul(t_, t_, r_), 'dv')
            V.inc(V.tensor_scalar_max(t_, t_, 1e-12), 'dv')
            V.inc(V.reciprocal(t_, t_), 'dv')
            V.inc(V.scalar_tensor_tensor(t_, t_, SQD, r_, ALU.mult, ALU.mult), 'dv')
            V.inc(V.tensor_copy(f_, t_), 'dv')
        elif kind == 'ffn':
            C.wait('dv', k.cnt['dv'])
            C.inc(C.activation(t_, v_, AF.Sqrt, scale=1024.0), 'ac')
            V.wait('ac', k.cnt['ac'])
            V.inc(V.scalar_tensor_tensor(t_, t_, 1.0 / float(np.sqrt(1023.0)), r_,
                                         ALU.mult, ALU.mult), 'dv')
            C.wait('dv', k.cnt['dv'])
            C.inc(C.activation(t_, t_, AF.Sqrt, bias=1e-5), 'ac')
            V.wait('ac', k.cnt['ac'])
            V.inc(V.reciprocal(t_, t_), 'dv')
            V.inc(V.tensor_mul(t_, t_, r_), 'dv')
            V.inc(V.tensor_copy(f_, t_), 'dv')
        else:
            V.inc(V.tensor_copy(f_, r_), 'dv')
        T.wait('dv', k.cnt['dv'])
        T.matmul(PB[:, 0:512], S.ones1r32[:, :], S.rowsc[0:1, 0:512], start=True, stop=True)
        i = T.matmul(PB[:, 512:1024], S.ones1r32[:, :], S.rowsc[0:1, 1536:2048], start=True, stop=True)
        T.inc(i, 'pe')
        V.wait('pe', k.cnt['pe'])
        for dt in range(c.DT):
            V.inc(V.scalar_tensor_tensor(S.x32[:, dt, :], S.x32[:, dt, :], 1.0,
                                         PB[:, 0:512], ALU.mult, ALU.subtract), 'dv')
            if kind == 'aa':
                V.inc(V.scalar_tensor_tensor(S.x32[:, dt, :], S.x32[:, dt, :],
                                             S.cols[:, dt:dt + 1], PB[:, 512:1024],
                                             ALU.mult, ALU.mult), 'dv')
            else:
                V.inc(V.scalar_tensor_tensor(S.x32[:, dt, :], S.x32[:, dt, :], 1.0,
                                             PB[:, 512:1024], ALU.mult, ALU.mult), 'dv')
            V.inc(V.tensor_copy(S.xw[:, dt, :], S.x32[:, dt, :]), 'dv')
        if kind == 'ffn':
            Y.wait('dv', k.cnt['dv'])
            Y.inc(Y.dma_start(E['xn232'].ap()[:, tsl].rearrange('(a p) t -> p a t', p=128),
                              S.x32[:]), 'dq', 16)
            emit_tokmajor(k, 'xtok', s)
        else:
            emit_stripe_projections(k, s, kind)
    barrier(k)


def emit_tokmajor(k, dsttok, s):
    c, S, E = k.c, k.S, k.E
    Y, V, T = k.Y, k.V, k.T
    T.wait('dv', k.cnt['dv'])
    for tt in range(4):
        for dt in range(c.DT):
            bank = dt % 2
            T.wait('dv', k.tkm_tok[bank])
            i = T.matmul(PAq(k)[:, bank * 512: bank * 512 + 128],
                         S.xw[:, dt, ds(tt, 128)], S.iden[:], start=True, stop=True)
            T.inc(i, 'pe')
            V.wait('pe', k.cnt['pe'])
            V.wait('dq', k.tkm_st)
            i = V.tensor_copy(S.ttk[:, tt % 2, ds(dt, 128)],
                              PAq(k)[:, bank * 512: bank * 512 + 128])
            k.tkm_tok[bank] = V.inc(i, 'dv')
        if tt % 2 == 1:
            Y.wait('dv', k.cnt['dv'])
            k.tkm_st = Y.inc(
                Y.dma_start(E[dsttok].ap()[s * 512 + (tt - 1) * 128:
                                           s * 512 + (tt + 1) * 128, :]
                            .rearrange('(a p) d -> p a d', p=128), S.ttk[:]),
                'dq', 16)


def PAq(k):
    return k.PA


def emit_stripe_projections(k, s, kind):
    c, S, E = k.c, k.S, k.E
    Y, P, V, C, T = k.Y, k.P, k.V, k.C, k.T
    PA, PB = k.PA, k.PB
    tsl = ds(s, 512)
    w0f = S.w0[:].rearrange('p a b -> p (a b)')

    if kind == 'aa':
        qsrc = 'aa_wqkv'
        kvspec = [('aa_wqkv', 1024, 'ktok', False), ('aa_wqkv', 2048, 'vtok', True)]
    else:
        qsrc = 'ab_wq'
        kvspec = [('ab_wk', 0, 'ktokF', False), ('ab_wv', 0, 'vtokF', False)]

    k.pe_qw[0] = max(k.pe_qw[0], k.pe_kw[0])
    k.pe_qw[1] = max(k.pe_qw[1], k.pe_kw[1])
    for g in range(8):
        buf = g % 2
        if g >= 2:
            P.wait('pe', k.pe_qw[buf])
        else:
            P.wait('pe', k.pe_qw[buf])
        P.inc(P.dma_start(S.w0[:, buf, 0:1024].rearrange('p (a q) -> p a q', q=128),
                          E[qsrc].ap()[:, ds(g, 128)]
                          .rearrange('(a p) q -> p a q', p=128)), 'dw%d' % buf, 16)
        T.wait('dw%d' % buf, k.cnt['dw%d' % buf])
        T.wait('dv', k.cnt['dv'])
        T.wait('ac', k.qp_ev[g % 2])
        T.wait('dv', k.qp_evv[g % 2])
        for dt in range(c.DT):
            i = T.matmul(PB[:, 1024 + (g % 2) * 512: 1024 + (g % 2) * 512 + 512],
                         S.w0[:, buf, ds(dt, 128)], S.xw[:, dt, :],
                         start=(dt == 0), stop=(dt == c.DT - 1))
        k.pe_qw[buf] = T.inc(i, 'pe')
        if kind == 'ab':
            V.wait('pe', k.cnt['pe'])
            V.wait('sp%d' % (g % 2), k.qtd_st[g % 2])
            i = V.tensor_scalar(S.pT[:, 2 * (g % 2), :],
                                PB[:, 1024 + (g % 2) * 512:1024 + (g % 2) * 512 + 512],
                                S.cols[:, g:g + 1], None, ALU.add)
            k.qp_evv[g % 2] = V.inc(i, 'dv')
            Y.wait('dv', k.cnt['dv'])
        else:
            C.wait('pe', k.cnt['pe'])
            C.wait('sp%d' % (g % 2), k.qtd_st[g % 2])
            i = C.activation(S.pT[:, 2 * (g % 2), :],
                             PB[:, 1024 + (g % 2) * 512:1024 + (g % 2) * 512 + 512],
                             AF.Copy)
            k.qp_ev[g % 2] = C.inc(i, 'ac')
            Y.wait('ac', k.cnt['ac'])
        Y.wait('dv', k.cnt['dv'])
        Y.wait('ac', k.cnt['ac'])
        k.qtd_st[g % 2] = Y.inc(
            Y.dma_start(E['qTd'].ap()[ds(g, 128), tsl], S.pT[:, 2 * (g % 2), :]),
            'sp%d' % (g % 2), 16)

    k.pe_kw[0] = max(k.pe_kw[0], k.pe_qw[0])
    k.pe_kw[1] = max(k.pe_kw[1], k.pe_qw[1])
    for (wsrc, coff, dstname, aug) in kvspec:
        for kvg in range(4):
            buf = kvg % 2
            P.wait('pe', k.pe_kw[buf])
            P.inc(P.dma_start(S.w0[:, buf, :].rearrange('p (a q) -> p a q', q=256),
                              E[wsrc].ap()[:, coff + kvg * 256: coff + kvg * 256 + 256]
                              .rearrange('(a p) q -> p a q', p=128)), 'dw%d' % buf, 16)
            wv = S.w0[:, buf, :].rearrange('p (a q) -> p a q', q=256)
            for tt in range(4):
                T.wait('dw%d' % buf, k.cnt['dw%d' % buf])
                T.wait('dv', k.cnt['dv'])
                T.wait('dv', k.qp_evv[tt % 2])
                for dt in range(c.DT):
                    i = T.matmul(PB[:, (tt % 2) * 512:(tt % 2) * 512 + 256],
                                 S.xw[:, dt, ds(tt, 128)], wv[:, dt, :],
                                 start=(dt == 0),
                                 stop=(dt == c.DT - 1 and kind != 'ab'))
                if kind == 'ab':
                    brow = S.brow[0:1, 0:c.D] if dstname == 'ktokF' else S.brow[0:1, c.D:2 * c.D]
                    T.wait('dg', k.cnt['dg'])
                    i = T.matmul(PB[:, (tt % 2) * 512:(tt % 2) * 512 + 256],
                                 S.ones1rb[:, :],
                                 brow[:, kvg * 256:kvg * 256 + 256],
                                 start=False, stop=True)
                T.inc(i, 'pe')
                V.wait('pe', k.cnt['pe'])
                if aug:
                    V.wait('sv', k.vev_st)
                    V.inc(V.tensor_copy(
                        S.vev[:, 0:264].rearrange('p (g e) -> p g e', e=66)[:, :, 0:64],
                        PB[:, (tt % 2) * 512:(tt % 2) * 512 + 256]
                        .rearrange('p (g e) -> p g e', e=64)), 'dv')
                    k.qp_evv[tt % 2] = k.cnt['dv']
                    Y.wait('dv', k.cnt['dv'])
                    k.vev_st = Y.inc(Y.dma_start(
                        E[dstname].ap()[s * 512 + tt * 128: s * 512 + tt * 128 + 128,
                                        kvg * 264: kvg * 264 + 264],
                        S.vev[:, 0:264]), 'sv', 16)
                else:
                    V.wait('sp%d' % (tt % 2), k.qtd_st[tt % 2])
                    V.inc(V.tensor_copy(S.pT[:, 2 * (tt % 2), 0:256],
                                        PB[:, (tt % 2) * 512:(tt % 2) * 512 + 256]), 'dv')
                    k.qp_evv[tt % 2] = k.cnt['dv']
                    Y.wait('dv', k.cnt['dv'])
                    k.qtd_st[tt % 2] = Y.inc(Y.dma_start(
                        E[dstname].ap()[s * 512 + tt * 128: s * 512 + tt * 128 + 128,
                                        kvg * 256: kvg * 256 + 256],
                        S.pT[:, 2 * (tt % 2), 0:256]), 'sp%d' % (tt % 2), 16)
            k.pe_kw[buf] = k.cnt['pe']


def emit_attn_sublayer(k, src, dst, kind):
    c = k.c
    emit_prologue(k, src, kind)
    emit_exchange(k, kind == 'aa')
    for qh in range(c.NQH):
        emit_attention_quarter(k, qh, kind)
        if kind == 'aa':
            emit_epilogue_proj(k, qh, 'aa_wout', None, src, dst, heads=True)
        else:
            emit_epilogue_proj(k, qh, 'ab_wo', 'ab_bco', src, dst, heads=False)
    barrier(k)


def emit_exchange(k, aug):
    c, S, E = k.c, k.S, k.E
    P = k.P
    kk, vv = ('ktok', 'vtok') if aug else ('ktokF', 'vtokF')
    kf, vf = ('kfull', 'vfull') if aug else ('kfullF', 'vfullF')
    barrier(k)
    i = P.collective_compute("AllGather", ALU.bypass, replica_groups=c.groups,
                             ins=[E[kk].ap().opt()], outs=[E[kf].ap().opt()])
    P.inc(i, 'cc')
    i = P.collective_compute("AllGather", ALU.bypass, replica_groups=c.groups,
                             ins=[E[vv].ap().opt()], outs=[E[vf].ap().opt()])
    P.inc(i, 'cc')


def emit_attention_quarter(k, qh, kind):
    c, S, E = k.c, k.S, k.E
    Y, P, V, C, T = k.Y, k.P, k.V, k.C, k.T
    PA, PB = k.PA, k.PB
    QH = c.QH
    qsl = ds(qh, QH)
    scale = 0.125 if kind == 'aa' else 1.0 / 32.0
    arena = S.arena[:].rearrange('p (a q) -> p a q', q=QH)

    barrier(k)
    if kind == 'ffn':
        for dt in range(c.DT):
            Y.inc(Y.dma_start(S.qT[:, dt, :], E['qtok'].ap()[qsl, ds(dt, 128)],
                              transpose=True), 'dq', 16)
    else:
        Y.inc(Y.dma_start(S.qT[:], E['qTd'].ap()[:, qsl]
                          .rearrange('(a p) t -> p a t', p=128)), 'dq', 16)
    P.inc(P.memset(S.arena[:], 0.0), 'po')
    P.inc(P.memset(S.rows2[:], 0.0), 'po')
    P.inc(P.memset(S.rowsc[32:33, :], 0.0), 'po')
    P.inc(P.memset(S.rowsc[64:65, :], 0.0), 'po')
    barrier(k)

    kf, vf = ('kfull', 'vfull') if kind == 'aa' else ('kfullF', 'vfullF')
    for sup in range(c.NSUP):
        ksl = slice(sup * c.KSUP, (sup + 1) * c.KSUP)
        barrier(k)
        for dt in range(c.DT):
            Y.inc(Y.dma_start(S.ktT[:, dt, :], E[kf].ap()[ksl, ds(dt, 128)],
                              transpose=True), 'dq', 16)
        if kind == 'aa':
            Y.inc(Y.dma_start(S.vsup[:], E[vf].ap()[ksl, :]
                              .rearrange('(a p) v -> p a v', p=128)), 'dq', 16)
        else:
            Y.inc(Y.dma_start(S.vsup[:, :, 0:1024], E[vf].ap()[ksl, :]
                              .rearrange('(a p) v -> p a v', p=128)), 'dq', 16)
        barrier(k)
        if kind == 'aa':
            emit_super_aa(k, scale)
        else:
            emit_super_single(k, scale, kind, qh, sup)
        barrier(k)


def aa_sum_slot(k, hh, off, width):
    S, c = k.S, k.c
    QH = c.QH
    if hh < 12:
        return S.rows2[32 * (hh // 4):32 * (hh // 4) + 1,
                       (hh % 4) * QH + off:(hh % 4) * QH + off + width]
    h2 = hh - 12
    return S.rowsc[32 * (h2 // 2 + 1):32 * (h2 // 2 + 1) + 1,
                   (h2 % 2) * 1024 + off:(h2 % 2) * 1024 + off + width]


def emit_super_aa(k, scale):
    c, S = k.c, k.S
    V, C, T = k.V, k.C, k.T
    PA, PB = k.PA, k.PB
    QH = c.QH
    arena = S.arena[:].rearrange('p (a q) -> p a q', q=QH)
    pTf = S.pT[:].rearrange('p a b -> p (a b)')
    for qb in range(c.NQBA):
        qbs = ds(qb, c.QBA)
        for pr in range(8):
            hA, hB = 2 * pr, 2 * pr + 1
            pe_pv = [0, 0]
            for kt in range(c.NKT):
                par = kt % 2
                T.wait('ac', k.ac_exp[par])
                T.matmul(PA[:, par * 1024: par * 1024 + 512],
                         S.ktT[0:64, pr, ds(kt, 128)],
                         S.qT[0:64, pr, qbs], start=True, stop=True)
                iB = T.matmul(PA[:, par * 1024 + 512: par * 1024 + 1024],
                              S.ktT[64:128, pr, ds(kt, 128)],
                              S.qT[64:128, pr, qbs], start=True, stop=True)
                T.inc(iB, 'pe')
                pe_s = k.cnt['pe']
                C.wait('pe', pe_s)
                i = C.activation(pTf[:, par * 1024:(par + 1) * 1024],
                                 PA[:, par * 1024:(par + 1) * 1024], AF.Exp,
                                 scale=scale)
                k.ac_exp[par] = C.inc(i, 'ac')
                T.wait('ac', k.ac_exp[par])
                T.matmul(PB[0:65, 0:512], S.vsup[:, kt, 66 * hA:66 * hA + 65],
                         S.pT[:, 2 * par, :], start=(kt == 0), stop=(kt == c.NKT - 1))
                i = T.matmul(PB[0:65, 512:1024], S.vsup[:, kt, 66 * hB:66 * hB + 65],
                             S.pT[:, 2 * par + 1, :], start=(kt == 0), stop=(kt == c.NKT - 1))
            T.inc(i, 'pe')
            V.wait('pe', k.cnt['pe'])
            for hh, reg in ((hA, 0), (hB, 1)):
                dt = hh // 2
                rb = (hh % 2) * 64
                V.inc(V.tensor_add(arena[rb:rb + 64, dt, qbs],
                                   PB[0:64, reg * 512:reg * 512 + 512],
                                   arena[rb:rb + 64, dt, qbs]), 'dv')
                slot = aa_sum_slot(k, hh, qb * c.QBA, c.QBA)
                V.inc(V.tensor_add(slot, PB[64:65, reg * 512:reg * 512 + 512],
                                   slot), 'dv')
            T.wait('dv', k.cnt['dv'])


def emit_super_single(k, scale, kind, qh, sup):
    c, S = k.c, k.S
    V, C, T = k.V, k.C, k.T
    PA, PB = k.PA, k.PB
    QH = c.QH
    arena = S.arena[:].rearrange('p (a q) -> p a q', q=QH)
    pTf = S.pT[:].rearrange('p a b -> p (a b)')
    nqb = c.To // c.QB
    for qb in range(c.NQB):
        qbs = ds(qb, c.QB)
        for kt in range(c.NKT):
            par = kt % 2
            T.wait('ac', k.ac_exp[par])
            T.wait('dv', k.dv_msk[par])
            for dt in range(c.DT):
                i = T.matmul(PA[:, par * 512: par * 512 + 256],
                             S.ktT[:, dt, ds(kt, 128)], S.qT[:, dt, qbs],
                             start=(dt == 0), stop=(dt == c.DT - 1))
            T.inc(i, 'pe')
            C.wait('pe', k.cnt['pe'])
            i = C.activation(pTf[:, kt * 256:(kt + 1) * 256],
                             PA[:, par * 512: par * 512 + 256], AF.Exp, scale=scale)
            k.ac_exp[par] = C.inc(i, 'ac')
            if kind == 'ab':
                V.wait('ac', k.ac_exp[par])
                kdcol = S.kd[:, (sup * c.NKT + kt) * nqb + qh * c.NQB + qb:
                             (sup * c.NKT + kt) * nqb + qh * c.NQB + qb + 1]
                i = V.scalar_tensor_tensor(pTf[:, kt * 256:(kt + 1) * 256],
                                           S.qrow[:, 0:c.QB], kdcol,
                                           pTf[:, kt * 256:(kt + 1) * 256],
                                           ALU.is_le, ALU.mult)
                k.dv_msk[par] = V.inc(i, 'dv')
            if kind == 'ab':
                T.wait('dv', k.dv_msk[par])
            else:
                T.wait('ac', k.ac_exp[par])
            i = T.matmul(PA[0:1, 1024:1280], S.onesb[:, 0:1],
                         pTf[:, kt * 256:(kt + 1) * 256],
                         start=(kt == 0), stop=(kt == c.NKT - 1))
        T.inc(i, 'pe')
        V.wait('pe', k.cnt['pe'])
        V.inc(V.tensor_add(S.rows2[0:1, qb * c.QB:(qb + 1) * c.QB],
                           PA[0:1, 1024:1280],
                           S.rows2[0:1, qb * c.QB:(qb + 1) * c.QB]), 'dv')
        T.wait('dv', k.cnt['dv'])
        T.wait('ac', k.cnt['ac'])
        for kt in range(c.NKT):
            for dt in range(c.DT):
                ps = PA if dt < 4 else PB
                col = (dt % 4) * 512
                i = T.matmul(ps[:, col:col + 256], S.vsup[:, kt, ds(dt, 128)],
                             pTf[:, kt * 256:(kt + 1) * 256],
                             start=(kt == 0), stop=(kt == c.NKT - 1))
        T.inc(i, 'pe')
        V.wait('pe', k.cnt['pe'])
        i = V.tensor_add(arena[:, 0:4, qbs],
                         PA[:].rearrange('p (a q) -> p a q', q=512)[:, :, 0:256],
                         arena[:, 0:4, qbs])
        V.inc(i, 'dv')
        i = V.tensor_add(arena[:, 4:8, qbs],
                         PB[:].rearrange('p (a q) -> p a q', q=512)[:, :, 0:256],
                         arena[:, 4:8, qbs])
        V.inc(i, 'dv')
        k.ac_exp = [0, 0]
        k.dv_msk = [0, 0]
        T.wait('dv', k.cnt['dv'])


def emit_epilogue_proj(k, qh, wname, bname, src, dst, heads):
    c, S, E = k.c, k.S, k.E
    Y, P, V, C, T = k.Y, k.P, k.V, k.C, k.T
    PA, PB = k.PA, k.PB
    QH = c.QH
    qsl = ds(qh, QH)
    arena = S.arena[:].rearrange('p (a q) -> p a q', q=QH)
    barrier(k)
    if heads:
        for hh in range(16):
            V.inc(V.reciprocal(S.rowsc[0:1, 0:QH], aa_sum_slot(k, hh, 0, QH)), 'dv')
            T.wait('dv', k.cnt['dv'])
            for ch in range(QH // 512):
                i = T.matmul(PB[0:64, (ch % 2) * 512:(ch % 2) * 512 + 512],
                             S.ones1r32[0:1, 0:64], S.rowsc[0:1, ds(ch, 512)],
                             start=True, stop=True)
                T.inc(i, 'pe')
                V.wait('pe', k.cnt['pe'])
                rb = (hh % 2) * 64
                V.inc(V.tensor_mul(S.qT[rb:rb + 64, hh // 2, ds(ch, 512)],
                                   arena[rb:rb + 64, hh // 2, ds(ch, 512)],
                                   PB[0:64, (ch % 2) * 512:(ch % 2) * 512 + 512]), 'dv')
            T.wait('dv', k.cnt['dv'])
    else:
        V.inc(V.reciprocal(S.rowsc[0:1, 0:QH], S.rows2[0:1, 0:QH]), 'dv')
        T.wait('dv', k.cnt['dv'])
        for ch in range(QH // 512):
            i = T.matmul(PB[:, (ch % 2) * 512:(ch % 2) * 512 + 512],
                         S.ones1r32[:, :], S.rowsc[0:1, ds(ch, 512)],
                         start=True, stop=True)
            T.inc(i, 'pe')
            V.wait('pe', k.cnt['pe'])
            for dt in range(c.DT):
                V.inc(V.tensor_mul(S.qT[:, dt, ds(ch, 512)],
                                   arena[:, dt, ds(ch, 512)],
                                   PB[:, (ch % 2) * 512:(ch % 2) * 512 + 512]), 'dv')
            T.wait('dv', k.cnt['dv'])
    if bname:
        Y.inc(Y.dma_start(S.cols[:, 8:16], E[bname].ap()), 'dq', 16)
    barrier(k)
    xs = E[src].ap()
    dd = E[dst].ap()
    for g in range(8):
        buf = g % 2
        if g >= 2:
            P.wait('pe', k.pe_qw[buf])
        P.inc(P.dma_start(S.w0[:, buf, 0:1024].rearrange('p (a q) -> p a q', q=128),
                          E[wname].ap()[:, ds(g, 128)]
                          .rearrange('(a p) q -> p a q', p=128)), 'dw%d' % buf, 16)
        Y.wait('dv', k.cnt['dv'])
        Y.wait('sb', k.bc_st)
        Y.inc(Y.dma_start(S.bc[:], xs[ds(g, 128), qsl]), 'sb', 16)
        T.wait('sb', k.cnt['sb'])
        T.wait('dw%d' % buf, k.cnt['dw%d' % buf])
        for ch in range(QH // 512):
            for dt in range(c.DT):
                i = T.matmul(PA[:, (ch % 2) * 512:(ch % 2) * 512 + 512],
                             S.w0[:, buf, ds(dt, 128)], S.qT[:, dt, ds(ch, 512)],
                             start=(dt == 0), stop=(dt == c.DT - 1))
        k.pe_qw[buf] = T.inc(i, 'pe')
        V.wait('pe', k.cnt['pe'])
        V.wait('sb', k.cnt['sb'])
        bcol = S.cols[:, 8 + g:9 + g] if bname else 0.0
        i = V.scalar_tensor_tensor(S.bc[:], PA[:, 0:1024], bcol, S.bc[:],
                                   ALU.add, ALU.add)
        V.inc(i, 'dv')
        Y.wait('dv', k.cnt['dv'])
        k.bc_st = Y.inc(Y.dma_start(dd[ds(g, 128), qsl], S.bc[:]), 'sb', 16)



def emit_ffn_sublayer(k, src, dst):
    c = k.c
    emit_prologue(k, src, 'ffn')
    for t, dsttok in (('q', 'qtok'), ('k', 'ktokF'), ('v', 'vtokF')):
        emit_fff(k, t, 'xtok', 'xn232', dsttok, AF.Relu)
        if t == 'q' and c.dumps_q:
            barrier(k)
            for nm in c.dumps_q:
                if nm.startswith('S.'):
                    sb = getattr(k.S, nm[2:])
                    k.Y.inc(k.Y.dma_start(k.E['dmpq_' + nm[2:]].ap(), sb[:]),
                            'dq', 16)
                else:
                    k.Y.inc(k.Y.dma_start(k.E['dmpq_' + nm].ap(), k.E[nm].ap()),
                            'dq', 16)
                k.Y.wait('dq', k.cnt['dq'])
            barrier(k)
    emit_exchange(k, False)
    for qh in range(c.NQH):
        emit_attention_quarter(k, qh, 'ffn')
        emit_epilogue_y(k, qh)
    emit_fff(k, 'f', 'ytokM', 'yT32', 'ftok', AF.Gelu)
    emit_final_add(k, src, dst)
    barrier(k)


def emit_epilogue_y(k, qh):
    c, S, E = k.c, k.S, k.E
    Y, V, C, T = k.Y, k.V, k.C, k.T
    PA, PB = k.PA, k.PB
    QH = c.QH
    arena = S.arena[:].rearrange('p (a q) -> p a q', q=QH)
    barrier(k)
    V.inc(V.reciprocal(S.rowsc[0:1, 0:QH], S.rows2[0:1, 0:QH]), 'dv')
    for hf in range(QH // 512):
        s_global = (qh * QH + hf * 512) // 512
        tsl = ds(s_global, 512)
        Y.wait('dq', k.cnt['dq'])
        Y.wait('dv', k.cnt['dv'])
        Y.inc(Y.dma_start(S.x32[:], E['xn232'].ap()[:, tsl]
                          .rearrange('(a p) t -> p a t', p=128)), 'dq', 16)
        T.wait('dv', k.cnt['dv'])
        i = T.matmul(PB[:, 0:512], S.ones1r32[:, :], S.rowsc[0:1, ds(hf, 512)],
                     start=True, stop=True)
        T.inc(i, 'pe')
        V.wait('pe', k.cnt['pe'])
        V.wait('dq', k.cnt['dq'])
        for dt in range(c.DT):
            V.inc(V.tensor_mul(arena[:, dt, ds(hf, 512)], arena[:, dt, ds(hf, 512)],
                               PB[:, 0:512]), 'dv')
            V.inc(V.tensor_add(S.x32[:, dt, :], arena[:, dt, ds(hf, 512)],
                               S.x32[:, dt, :]), 'dv')
            V.inc(V.tensor_copy(S.xw[:, dt, :], S.x32[:, dt, :]), 'dv')
        Y.wait('dv', k.cnt['dv'])
        Y.inc(Y.dma_start(E['yT32'].ap()[:, tsl].rearrange('(a p) t -> p a t', p=128),
                          S.x32[:]), 'dq', 16)
        emit_tokmajor(k, 'ytokM', s_global)
    barrier(k)


def emit_final_add(k, src, dst):
    c, S, E = k.c, k.S, k.E
    Y, V, T = k.Y, k.V, k.T
    qt32 = k.S.qT[:].rearrange('p a b -> p (a b)').bitcast(F32)
    barrier(k)
    for s in range(c.ST):
        tsl = ds(s, 512)
        barrier(k)
        Y.inc(Y.dma_start(S.x32[:], E[src].ap()[:, tsl]
                          .rearrange('(a p) t -> p a t', p=128)), 'dq', 16)
        Y.inc(Y.dma_start(qt32.rearrange('p (a t) -> p a t', t=512),
                          E['yT32'].ap()[:, tsl]
                          .rearrange('(a p) t -> p a t', p=128)), 'dq', 16)
        for dt in range(c.DT):
            Y.inc(Y.dma_start(S.xw[:, dt, :], E['ftok'].ap()[tsl, ds(dt, 128)],
                              transpose=True), 'dq', 16)
        V.wait('dq', k.cnt['dq'])
        for dt in range(c.DT):
            V.inc(V.tensor_add(S.x32[:, dt, :], S.x32[:, dt, :],
                               qt32.rearrange('p (a t) -> p a t', t=512)[:, dt, :]), 'dv')
            V.inc(V.tensor_add(S.x32[:, dt, :], S.x32[:, dt, :], S.xw[:, dt, :]), 'dv')
        Y.wait('dv', k.cnt['dv'])
        Y.inc(Y.dma_start(E[dst].ap()[:, tsl].rearrange('(a p) t -> p a t', p=128),
                          S.x32[:]), 'dq', 16)



def emit_fff(k, tree, srctok, src32, dsttok, act):
    c, S, E = k.c, k.S, k.E
    Y, P, V, C, T = k.Y, k.P, k.V, k.C, k.T
    PA, PB = k.PA, k.PB
    AOFF = 0
    arenaF = S.arena[:]
    lgT0 = arenaF[:, 0:c.To]
    lgT1 = arenaF[:, c.To:2 * c.To]
    mask = arenaF[:, 2 * c.To:3 * c.To] if 3 * c.To <= c.DT * c.QH else None
    x32f = S.x32[:].rearrange('p a b -> p (a b)')
    barrier(k)
    P.inc(P.dma_start(S.nwT[:, :, 0:255].rearrange('p a n -> p a n'),
                      E['f%s_nwT' % tree].ap().rearrange('(a p) n -> p a n', p=128)),
          'dg', 16)
    Y.inc(Y.dma_start(S.nb[:], E['f%s_nb' % tree].ap()), 'dq', 16)
    P.inc(P.memset(S.arena[:, 0:2 * c.To], 0.0), 'po')
    barrier(k)

    for s in range(c.ST):
        tsl = ds(s, 512)
        barrier(k)
        Y.inc(Y.dma_start(S.x32[:], E[src32].ap()[:, tsl]
                          .rearrange('(a p) t -> p a t', p=128)), 'dq', 16)
        xr = S.xr[:]
        for hf in range(2):
            hsl = slice(s * 512 + hf * 256, s * 512 + hf * 256 + 256)
            V.wait('dq', k.cnt['dq'])
            T.wait('pe', k.rt_pe[hf])
            for dt in range(c.DT):
                V.inc(V.tensor_copy(xr[:, dt, :], S.x32[:, dt, hf * 256:hf * 256 + 256]), 'dv')
            T.wait('dv', k.cnt['dv'])
            T.wait('ac', k.cnt['ac'])
            for half in range(2):
                ncols = 128 if half == 0 else 127
                for dt in range(c.DT):
                    i = T.matmul(PA[0:ncols, half * 512 + hf * 256: half * 512 + hf * 256 + 256],
                                 S.nwT[:, dt, half * 128: half * 128 + ncols],
                                 xr[:, dt, :],
                                 start=(dt == 0), stop=(dt == c.DT - 1))
                k.rt_pe[hf] = T.inc(i, 'pe')
                V.wait('pe', k.cnt['pe'])
                dstl = lgT0 if half == 0 else lgT1
                i = V.tensor_scalar(dstl[:, hsl] if half == 0 else dstl[0:127, hsl],
                                    PA[:, half * 512 + hf * 256: half * 512 + hf * 256 + 256] if half == 0
                                    else PA[0:127, half * 512 + hf * 256: half * 512 + hf * 256 + 256],
                                    S.nb[:, half:half + 1] if half == 0
                                    else S.nb[0:127, half:half + 1],
                                    None, ALU.add)
                V.inc(i, 'dv')
    barrier(k)

    r_pref = S.rows2[0:1, 0:c.To]
    r_b = S.rowsc[0:1, 0:512]
    NCH = c.To // 512
    V.inc(V.tensor_single_scalar(r_pref, lgT0[0:1, :], 0.0, ALU.is_gt), 'dv')
    gsaved = False
    for lv in range(1, 8):
        off = (1 << lv) - 1
        segs = [(lgT0, off, 0)]
        if lv == 7:
            segs = [(lgT0, 127, 0), (lgT1, -1.0, 1)]
        for chix in range(NCH):
            chs = ds(chix, 512)
            for si, (lg, shift, segi) in enumerate(segs):
                V.inc(V.tensor_scalar_add(S.rowsc[0:1, 1024:1536], r_pref[:, chs],
                                          float(shift)), 'dv')
                T.wait('dv', k.cnt['dv'])
                i = T.matmul(PB[:, 0:512], S.ones1r32[:, :], S.rowsc[0:1, 1024:1536],
                             start=True, stop=True)
                T.inc(i, 'pe')
                V.wait('pe', k.cnt['pe'])
                i = V.scalar_tensor_tensor(x32f[:, chs], PB[:, 0:512], S.iota[:, 0:1],
                                           lg[:, chs],
                                           ALU.is_equal, ALU.mult)
                V.inc(i, 'dv')
                T.wait('dv', k.cnt['dv'])
                i = T.matmul(PB[0:1, 1024:1536], S.ones32c[:],
                             x32f[:, chs],
                             start=(si == 0), stop=(si == len(segs) - 1))
                T.inc(i, 'pe')
            V.wait('pe', k.cnt['pe'])
            V.inc(V.tensor_single_scalar(r_b, PB[0:1, 1024:1536], 0.0,
                                         ALU.is_gt), 'dv')
            V.inc(V.scalar_tensor_tensor(r_pref[:, chs], r_pref[:, chs], 2.0,
                                         r_b, ALU.mult, ALU.add), 'dv')

    barrier(k)
    emit_sort(k, tree)
    emit_fff_tiles(k, tree, srctok, dsttok, act)


def emit_sort(k, tree):
    c, S, E = k.c, k.S, k.E
    Y, P, V, C, T = k.Y, k.P, k.V, k.C, k.T
    PA, PB = k.PA, k.PB
    arenaF = S.arena[:]
    lgT0 = arenaF[:, 0:c.To]
    lgT1 = arenaF[:, c.To:2 * c.To]
    x32f = S.x32[:].rearrange('p a b -> p (a b)')[:, 0:c.To]
    r_leaf = S.rows2[0:1, 0:c.To]
    r_d = S.rows2[32:33, 0:c.To]
    x32row = S.x32[0:1, :, :].rearrange('p a b -> p (a b)')[:, 0:c.To]
    NCH = c.To // 512
    rsc_c = E['rsc'].reshape([8192, 1])
    rsc_r = E['rsc'].reshape([1, 8192])

    for H in range(2):
        for chix in range(NCH):
            chs = ds(chix, 512)
            V.inc(V.tensor_scalar_add(S.rowsc[0:1, 1024:1536], r_leaf[:, chs],
                                      -128.0 * H), 'dv')
            T.wait('dv', k.cnt['dv'])
            i = T.matmul(PB[:, 0:512], S.ones1r32[:, :], S.rowsc[0:1, 1024:1536],
                         start=True, stop=True)
            T.inc(i, 'pe')
            V.wait('pe', k.cnt['pe'])
            V.inc(V.tensor_scalar(lgT0[:, chs], PB[:, 0:512], S.iota[:, 0:1], None,
                                  ALU.is_equal), 'dv')
        V.inc(V.tensor_tensor_scan(x32f, lgT0, lgT0, 0.0, ALU.add, ALU.bypass), 'dv')
        V.inc(V.tensor_copy(S.cols[:, 8 + H:9 + H], x32f[:, c.To - 1:c.To]), 'dv')
        for chix in range(NCH):
            chs = ds(chix, 512)
            V.inc(V.tensor_mul(lgT1[:, chs], lgT0[:, chs], x32f[:, chs]), 'dv')
            T.wait('dv', k.cnt['dv'])
            i = T.matmul(PA[0:1, 1024:1536], S.ones32c[:],
                         lgT1[:, chs], start=True, stop=True)
            T.inc(i, 'pe')
            V.wait('pe', k.cnt['pe'])
            if H == 0:
                V.inc(V.tensor_copy(r_d[:, chs], PA[0:1, 1024:1536]), 'dv')
            else:
                V.inc(V.tensor_add(r_d[:, chs], PA[0:1, 1024:1536], r_d[:, chs]), 'dv')
            T.wait('dv', k.cnt['dv'])
        ncol = S.cols[:, 8 + H:9 + H]
        ti = S.cols[:, 14:15].bitcast(I32)
        V.inc(V.tensor_scalar_add(S.cols[:, 10 + H:11 + H], ncol, 15.0), 'dv')
        V.inc(V.tensor_copy(ti, S.cols[:, 10 + H:11 + H]), 'dv')
        V.inc(V.tensor_single_scalar(ti, ti, -16, ALU.bitwise_and), 'dv')
        V.inc(V.tensor_copy(S.cols[:, 10 + H:11 + H], ti), 'dv')
        Y.wait('dv', k.cnt['dv'])
        Y.inc(Y.dma_start(rsc_c.ap()[H * 128:(H + 1) * 128, :],
                          S.cols[:, 10 + H:11 + H]), 'dq', 16)
    Y.wait('dq', k.cnt['dq'])
    Y.inc(Y.dma_start(S.rowsc[0:1, 0:256], rsc_r.ap()[:, 0:256]), 'dq', 16)
    V.wait('dq', k.cnt['dq'])
    V.inc(V.tensor_tensor_scan(S.rowsc[0:1, 256:512], S.rowsc[0:1, 0:256],
                               S.rowsc[0:1, 0:256], 0.0, ALU.add, ALU.bypass), 'dv')
    V.inc(V.tensor_sub(S.rowsc[0:1, 256:512], S.rowsc[0:1, 256:512],
                       S.rowsc[0:1, 0:256]), 'dv')
    Y.wait('dv', k.cnt['dv'])
    Y.inc(Y.dma_start(rsc_r.ap()[:, 256:512], S.rowsc[0:1, 256:512]), 'dq', 16)
    Y.wait('dq', k.cnt['dq'])
    for H in range(2):
        Y.inc(Y.dma_start(S.cols[:, 12 + H:13 + H],
                          rsc_c.ap()[256 + 128 * H:256 + 128 * H + 128, :]), 'dq', 16)
    V.wait('dq', k.cnt['dq'])
    for H in range(2):
        offcol = S.cols[:, 12 + H:13 + H]
        for chix in range(NCH):
            chs = ds(chix, 512)
            V.inc(V.tensor_scalar_add(S.rowsc[0:1, 1024:1536], r_leaf[:, chs],
                                      -128.0 * H), 'dv')
            T.wait('dv', k.cnt['dv'])
            i = T.matmul(PB[:, 0:512], S.ones1r32[:, :], S.rowsc[0:1, 1024:1536],
                         start=True, stop=True)
            T.inc(i, 'pe')
            V.wait('pe', k.cnt['pe'])
            V.inc(V.tensor_scalar(lgT0[:, chs], PB[:, 0:512], S.iota[:, 0:1], None,
                                  ALU.is_equal), 'dv')
            V.inc(V.tensor_scalar(lgT0[:, chs], lgT0[:, chs], offcol, None, ALU.mult), 'dv')
            T.wait('dv', k.cnt['dv'])
            i = T.matmul(PA[0:1, 1024:1536], S.ones32c[:],
                         lgT0[:, chs], start=True, stop=True)
            T.inc(i, 'pe')
            V.wait('pe', k.cnt['pe'])
            V.inc(V.tensor_add(r_d[:, chs], PA[0:1, 1024:1536], r_d[:, chs]), 'dv')
            T.wait('dv', k.cnt['dv'])
    V.inc(V.tensor_scalar_add(r_leaf, r_d, -1.0), 'dv')
    di = lgT0[0:1, 0:c.To].bitcast(I32)
    V.inc(V.tensor_copy(di, r_leaf), 'dv')
    V.inc(V.tensor_single_scalar(di, di, 127, ALU.bitwise_and), 'dv')
    V.inc(V.tensor_copy(x32row, di), 'dv')
    V.inc(V.tensor_sub(r_leaf, r_leaf, x32row), 'dv')
    V.inc(V.tensor_scalar(r_leaf, r_leaf, 1.0 / 128.0, None, ALU.mult), 'dv')
    V.inc(V.tensor_scalar(x32row, x32row, float(c.NTILE), None, ALU.mult), 'dv')
    V.inc(V.tensor_add(r_leaf, r_leaf, x32row), 'dv')
    nvt = c.To // 128
    Y.wait('dv', k.cnt['dv'])
    Y.inc(Y.dma_start(rsc_r.ap()[:, 0:c.To], r_leaf), 'dq', 16)
    Y.wait('dq', k.cnt['dq'])
    Y.inc(Y.dma_start(S.bc[0:nvt, 0:128], E['rsc'].reshape([64, 128]).ap()[0:nvt, :]), 'dq', 16)
    T.wait('dq', k.cnt['dq'])
    i = T.matmul(PB[:, 0:nvt], S.bc[0:nvt, 0:128],
                 S.iden32[0:nvt, 0:nvt], start=True, stop=True)
    T.inc(i, 'pe')
    V.wait('pe', k.cnt['pe'])
    V.inc(V.tensor_copy(S.dstc[:, 0:nvt], PB[:, 0:nvt]), 'dv')
    P.wait('dv', k.cnt['dv'])
    P.inc(P.dma_start(E['slotTab'].ap(), E['c_oob'].ap()), 'dg', 16)
    P.wait('dg', k.cnt['dg'])
    for i_t in range(nvt):
        P.wait('dg', k.cnt['dg'])
        P.inc(P.indirect_dma_start(
            rsc_i32(k), IndirectOffsetOnAxis(ap=S.dstc[:, i_t:i_t + 1], axis=0),
            S.tki[:, i_t:i_t + 1], None), 'dg', 16)
    P.wait('dg', k.cnt['dg'])
    P.inc(P.dma_start(S.srt[:], E['slotTab'].reshape([128, c.NTILE]).ap()), 'dg', 16)
    for H in range(2):
        offcol = S.cols[:, 12 + H:13 + H]
        T.wait('dv', k.cnt['dv'])
        i = T.matmul(PB[:, 0:c.NCHUNK], S.ones1r32[:, :], S.srow[0:1, 0:c.NCHUNK],
                     start=True, stop=True)
        T.inc(i, 'pe')
        V.wait('pe', k.cnt['pe'])
        V.inc(V.tensor_scalar(lgT0[:, 0:c.NCHUNK], PB[:, 0:c.NCHUNK], offcol, None,
                              ALU.is_ge), 'dv')
        T.wait('dv', k.cnt['dv'])
        i = T.matmul(PA[0:1, 1024:1024 + c.NCHUNK], S.ones32c[:],
                     lgT0[:, 0:c.NCHUNK],
                     start=(H == 0), stop=(H == 1))
        T.inc(i, 'pe')
    V.wait('pe', k.cnt['pe'])
    V.inc(V.tensor_scalar_add(S.rowsc[0:1, 1024:1024 + c.NCHUNK], PA[0:1, 1024:1024 + c.NCHUNK],
                              -1.0), 'dv')
    Y.wait('dv', k.cnt['dv'])
    Y.inc(Y.dma_start(rsc_r.ap()[:, 0:c.NCHUNK], S.rowsc[0:1, 1024:1024 + c.NCHUNK]), 'dq', 16)
    Y.wait('dq', k.cnt['dq'])
    Y.inc(Y.dma_start(S.bc[0:c.NTILE, 0:8], E['rsc'].reshape([1024, 8]).ap()[0:c.NTILE, :]), 'dq', 16)
    V.wait('dq', k.cnt['dq'])
    V.inc(V.tensor_copy(S.ysl[0:c.NTILE, 0, 0:8], S.bc[0:c.NTILE, 0:8]), 'dv')
    T.wait('dv', k.cnt['dv'])
    i = T.matmul(PB[0:8, 0:c.NTILE], S.ysl[0:c.NTILE, 0, 0:8],
                 S.iden[0:c.NTILE, 0:c.NTILE], start=True, stop=True)
    T.inc(i, 'pe')
    V.wait('pe', k.cnt['pe'])
    V.inc(V.tensor_copy(S.clrf[:], PB[0:8, 0:c.NTILE]), 'dv')
    P.wait('dv', k.cnt['dv'])
    P.inc(P.memset(S.clri[:], 255), 'po')
    V.wait('po', k.cnt['po'])
    V.inc(V.tensor_copy(S.clri[0:8, :], S.clrf[:]), 'dv')
    V.inc(V.tensor_copy(S.ysl[0:8, 0, 0:c.NTILE], S.clrf[:]), 'dv')
    T.wait('dv', k.cnt['dv'])
    i = T.matmul(PB[:, 1024:1024 + c.NTILE], S.indic[:], S.ysl[0:8, 0, 0:c.NTILE],
                 start=True, stop=True)
    T.inc(i, 'pe')
    V.wait('pe', k.cnt['pe'])
    V.inc(V.tensor_scalar(S.bc[:, 0:c.NTILE], PB[:, 1024:1024 + c.NTILE], 16.0,
                          S.pmod[:, 0:1], ALU.mult, ALU.add), 'dv')
    V.inc(V.tensor_copy(S.idxw2[:], S.bc[:, 0:c.NTILE]), 'dv')
    cs = min(512, c.NCHUNK)
    for chix in range(c.NCHUNK // cs):
        chs = ds(chix, cs)
        T.wait('dv', k.cnt['dv'])
        i = T.matmul(PB[:, 0:cs], S.ones1r32[:, :], S.rowsc[0:1, 1024:1024 + c.NCHUNK][:, chs],
                     start=True, stop=True)
        T.inc(i, 'pe')
        V.wait('pe', k.cnt['pe'])
        V.inc(V.tensor_scalar(S.bc[:, 0:cs], PB[:, 0:cs], 128.0, S.iota[:, 0:1],
                              ALU.mult, ALU.add), 'dv')
        V.inc(V.tensor_copy(S.idxw1[:, chs], S.bc[:, 0:cs]), 'dv')
    barrier(k)


def rsc_i32(k):
    return k.E['slotTab'].reshape([k.c.SLOTS, 1]).ap()


def emit_fff_tiles(k, tree, srctok, dsttok, act):
    c, S, E = k.c, k.S, k.E
    Y, P, V, C, T = k.Y, k.P, k.V, k.C, k.T
    PA, PB = k.PA, k.PB
    for i_t in range(c.NTILE):
        buf = i_t % 2
        if i_t >= 2:
            P.wait('gx%d' % buf, k.wtok_fff[buf])
        gt = P.inc(P.indirect_dma_start(
            S.gth[:, buf, :], None, E[srctok].ap(),
            IndirectOffsetOnAxis(ap=S.srt[:, i_t:i_t + 1], axis=0),
            bounds_check=k.breg, oob_is_err=False), 'gg%d' % buf, 16)
        P.wait('gg%d' % buf, gt)
        k.wtok_fff[buf] = P.inc(
            P.dma_start(E['xsorted'].ap()[ds(i_t, 128), :], S.gth[:, buf, :]),
            'gx%d' % buf, 16)
    barrier(k)
    dstap = E[dsttok].ap()[:, 0:1024]
    tok_pe = {0: 0, 1: 0}
    tok_ac = {0: 0, 1: 0}
    tok_msk = {0: 0, 1: 0}
    tok_dvy = {0: 0, 1: 0}
    for blk in range(c.NBLK):
        barrier(k)
        for dt in range(c.DT):
            Y.inc(Y.dma_start(S.qT[:, dt, 0:1024],
                              E['xsorted'].ap()[ds(blk, 1024), ds(dt, 128)],
                              transpose=True), 'dq', 16)
        barrier(k)
        for ti in range(8):
            it = blk * 8 + ti
            buf = it % 2
            P.wait('pe', tok_pe[buf])
            gsem = 'gt%d' % buf
            for cc in range(8):
                P.inc(P.indirect_dma_start(
                    S.gth[:, buf, 128 * cc:128 * cc + 128], None,
                    E['w1gB_' + tree].ap(),
                    IndirectOffsetOnAxis(ap=S.idxw1[:, it * 8 + cc:it * 8 + cc + 1], axis=0)), gsem, 16)
            P.inc(P.indirect_dma_start(
                S.w2sb[:, buf, :], None, E['w2gB_' + tree].ap(),
                IndirectOffsetOnAxis(ap=S.idxw2[:, it:it + 1], axis=0)), gsem, 16)
            P.inc(P.indirect_dma_start(
                S.b2sb[:, buf, :], None, E['b2B_' + tree].ap(),
                IndirectOffsetOnAxis(ap=S.clri[:, it:it + 1], axis=0)), gsem, 16)
            P.inc(P.indirect_dma_start(
                S.b1sb[:, buf, :], None, E['f%s_b1' % tree].ap(),
                IndirectOffsetOnAxis(ap=S.idxw2[:, it:it + 1], axis=0)), gsem, 16)
            V.wait(gsem, k.cnt[gsem])
            i = V.tensor_copy(
                S.w1sb[:, buf, :, :].rearrange('p a (b j) -> p a b j', j=16),
                S.gth[:, buf, :].rearrange('p (a b j) -> p b a j', a=8, j=16))
            k.w1cp[buf] = V.inc(i, 'dv')
            T.wait(gsem, k.cnt[gsem])
            T.wait('dv', k.w1cp[buf])
            T.wait('ac', tok_ac[buf])
            for dt in range(c.DT):
                i = T.matmul(PA[:, buf * 512: buf * 512 + 128],
                             S.w1sb[:, buf, dt, :], S.qT[:, dt, ds(ti, 128)],
                             start=(dt == 0), stop=(dt == c.DT - 1))
            T.inc(i, 'pe')
            C.wait('pe', k.cnt['pe'])
            C.wait('gt%d' % buf, k.cnt['gt%d' % buf])
            C.wait('dv', tok_msk[buf])
            i = C.activation(S.hA[:, buf, :], PA[:, buf * 512: buf * 512 + 128],
                             act, bias=S.b1sb[:, buf, :])
            tok_ac[buf] = C.inc(i, 'ac')
            V.wait('ac', k.cnt['ac'])
            V.wait('pe', tok_pe[buf])
            i = V.tensor_mul(S.hT[:, buf, :], S.hA[:, buf, :], S.mdiag[:])
            tok_msk[buf] = V.inc(i, 'dv')
            T.wait('dv', k.cnt['dv'])
            T.wait('dv', tok_dvy[buf])
            pb0 = buf * 1024
            T.matmul(PB[:, pb0:pb0 + 512], S.hT[:, buf, :], S.w2sb[:, buf, 0:512],
                     start=True, stop=False)
            T.matmul(PB[:, pb0 + 512:pb0 + 1024], S.hT[:, buf, :],
                     S.w2sb[:, buf, 512:1024], start=True, stop=False)
            T.matmul(PB[:, pb0:pb0 + 512], S.indic[:], S.b2sb[0:8, buf, 0:512],
                     start=False, stop=True)
            i = T.matmul(PB[:, pb0 + 512:pb0 + 1024], S.indic[:],
                         S.b2sb[0:8, buf, 512:1024], start=False, stop=True)
            tok_pe[buf] = T.inc(i, 'pe')
            V.wait('pe', k.cnt['pe'])
            V.wait('gy%d' % buf, k.ysl_tok[buf])
            i = V.tensor_copy(S.ysl[:, buf, :], PB[:, pb0:pb0 + 1024])
            tok_dvy[buf] = V.inc(i, 'dv')
            P.wait('dv', k.cnt['dv'])
            k.ysl_tok[buf] = P.inc(P.indirect_dma_start(
                dstap, IndirectOffsetOnAxis(ap=S.srt[:, it:it + 1], axis=0),
                S.ysl[:, buf, :], None,
                bounds_check=k.breg, oob_is_err=False), 'gy%d' % buf, 16)
            if c.dbg_serial:
                barrier(k)
    barrier(k)



def kernel(**inputs):
    cfg = Cfg()
    k = build(cfg)
    emit_program(k, script)
    k.es.close()
    in_maps = [prep_core_inputs(inputs, core, cfg) for core in range(8)]
    res = run_bass_kernel_spmd(k.nc, in_maps, core_ids=list(range(8)))
    out = np.empty((4, 8192, 1024), np.float32)
    for core in range(8):
        b, h = core // 2, core % 2
        out[b, h * cfg.To:(h + 1) * cfg.To, :] = res.results[core]['outT'].T
    return out


# version 08
# speedup vs baseline: 1.1067x; 1.1067x over previous
import numpy as np
import ml_dtypes
from contextlib import ExitStack

import concourse.bass as bass
import concourse.mybir as mybir
from concourse.bass_utils import run_bass_kernel_spmd
from concourse.bass_core_rust import IndirectOffsetOnAxis

F32 = mybir.dt.float32
F32R = mybir.dt.float32r
BF16 = mybir.dt.bfloat16
I32 = mybir.dt.int32
AF = mybir.ActivationFunctionType
ALU = mybir.AluOpType
AX = mybir.AxisListType
BF = ml_dtypes.bfloat16
SQD = float(np.sqrt(1024.0))


class Cfg:
    def __init__(self, To=4096, Tf=8192, groups=None, parts=(1, 2, 3, 4),
                 dumps=(), dumps_q=(), dbg_serial=False):
        self.dumps = dumps
        self.dumps_q = dumps_q
        self.dbg_serial = dbg_serial
        self.To = To; self.Tf = Tf
        self.D = 1024; self.DT = 8; self.H = 16
        self.KSUP = min(1024, Tf); self.NSUP = Tf // self.KSUP
        self.NKT = self.KSUP // 128
        self.QH = min(1024, To); self.NQH = To // self.QH
        self.QB = 512
        self.NQB = self.QH // self.QB
        self.QBA = 512
        self.NQBA = self.QH // self.QBA
        self.NNODE = 255; self.NLEAF = 256
        self.SLOTS = ((To + 15 * 256 + 1023) // 1024) * 1024
        self.NTILE = self.SLOTS // 128
        self.NCHUNK = self.SLOTS // 16
        self.NBLK = self.SLOTS // 1024
        self.ST = To // 512
        self.NTT = To // 128
        self.groups = groups if groups is not None else [[0, 1], [2, 3], [4, 5], [6, 7]]
        self.parts = parts


def ds(i, n):
    return slice(i * n, (i + 1) * n)



def prep_core_inputs(inputs, core, cfg):
    c = cfg
    b, h = core // 2, core % 2
    x = np.asarray(inputs['L0_attn_a_x'])
    d = {'xT': np.ascontiguousarray(x[b, h * c.To:(h + 1) * c.To, :].T)}
    d['aa_wqkv'] = np.asarray(inputs['L0_attn_a_w_qkv'])
    d['aa_wout'] = np.asarray(inputs['L0_attn_a_w_out'])
    d['aa_gcol'] = np.ascontiguousarray(
        np.asarray(inputs['L0_attn_a_gamma']).reshape(8, 128).T)
    for nm in 'qkvo':
        W = np.asarray(inputs['L0_attn_b_W' + nm])
        d['ab_w' + nm] = np.ascontiguousarray(W.T)
        bb = np.asarray(inputs['L0_attn_b_b' + nm])
        d['ab_bc' + nm] = np.ascontiguousarray(bb.reshape(8, 128).T)
        d['ab_br' + nm] = np.ascontiguousarray(bb.reshape(1, 1024))
    for t in 'qkvf':
        g = lambda s: np.asarray(inputs['L0_ffn_%s_%s' % (t, s)])
        d['f%s_nwT' % t] = np.ascontiguousarray(g('nw').T)
        nb = g('nb')
        nbp = np.zeros((128, 2), np.float32)
        nbp[:, 0] = nb[0:128]
        nbp[0:127, 1] = nb[128:255]
        d['f%s_nb' % t] = nbp
        d['f%s_w1g' % t] = np.ascontiguousarray(
            g('w1').reshape(256, 8, 128, 16).transpose(0, 2, 1, 3).reshape(256, 128, 128))
        d['f%s_b1' % t] = np.ascontiguousarray(g('b1').reshape(4096, 1))
        d['f%s_w2g' % t] = np.ascontiguousarray(g('w2').reshape(4096, 1024))
        d['f%s_b2' % t] = np.asarray(g('b2'))
    d['c_iota'] = np.arange(128, dtype=np.float32).reshape(128, 1)
    d['c_pmod'] = (np.arange(128, dtype=np.float32) % 16).reshape(128, 1)
    d['c_tokid'] = np.ascontiguousarray(
        np.arange(c.To, dtype=np.float32).reshape(c.To // 128, 128).T)
    d['c_oob'] = np.full((c.SLOTS,), 1 << 20, np.int32)
    jj, ss = np.meshgrid(np.arange(128), np.arange(128), indexing='ij')
    d['c_maskdiag'] = ((jj // 16) == (ss // 16)).astype(BF)
    ind = np.zeros((8, 128), np.float32)
    for cc in range(8):
        ind[cc, cc * 16:(cc + 1) * 16] = 1.0
    d['c_indic8'] = ind.astype(BF)
    nkt, nqb = c.Tf // 128, c.To // c.QB
    kd = np.zeros((128, nkt * nqb), np.float32)
    for kt in range(nkt):
        for qb in range(nqb):
            kd[:, kt * nqb + qb] = np.arange(128) + 128 * kt - (qb * c.QB + h * c.To) + 16
    d['c_kd'] = kd
    d['c_qrow'] = np.tile(np.arange(c.QB, dtype=np.float32), (128, 1))
    d['c_srow'] = (16.0 * np.arange(c.NCHUNK, dtype=np.float32)).reshape(1, c.NCHUNK)
    return d



class NS:
    pass


def build(cfg):
    c = cfg
    nc = bass.Bass("TRN2", target_bir_lowering=False)
    k = NS(); k.c = c; k.nc = nc; k.E = {}

    def dram(name, shape, dt, kind="Internal"):
        k.E[name] = nc.dram_tensor(name, list(shape), dt, kind=kind)

    dram('xT', (c.D, c.To), F32, "ExternalInput")
    dram('aa_wqkv', (c.D, 3072), F32, "ExternalInput")
    dram('aa_wout', (c.D, c.D), F32, "ExternalInput")
    dram('aa_gcol', (128, 8), F32, "ExternalInput")
    for nm in 'qkvo':
        dram('ab_w' + nm, (c.D, c.D), F32, "ExternalInput")
        dram('ab_bc' + nm, (128, 8), F32, "ExternalInput")
        dram('ab_br' + nm, (1, c.D), F32, "ExternalInput")
    for t in 'qkvf':
        dram('f%s_nwT' % t, (c.D, c.NNODE), F32, "ExternalInput")
        dram('f%s_nb' % t, (128, 2), F32, "ExternalInput")
        dram('f%s_w1g' % t, (c.NLEAF, 128, 128), F32, "ExternalInput")
        dram('f%s_b1' % t, (4096, 1), F32, "ExternalInput")
        dram('f%s_w2g' % t, (4096, c.D), F32, "ExternalInput")
        dram('f%s_b2' % t, (c.NLEAF, c.D), F32, "ExternalInput")
    dram('c_iota', (128, 1), F32, "ExternalInput")
    dram('c_pmod', (128, 1), F32, "ExternalInput")
    dram('c_tokid', (128, c.To // 128), F32, "ExternalInput")
    dram('c_oob', (c.SLOTS,), I32, "ExternalInput")
    dram('c_maskdiag', (128, 128), BF16, "ExternalInput")
    dram('c_indic8', (8, 128), BF16, "ExternalInput")
    dram('c_kd', (128, (c.Tf // 128) * (c.To // c.QB)), F32, "ExternalInput")
    dram('c_qrow', (128, c.QB), F32, "ExternalInput")
    dram('c_srow', (1, c.NCHUNK), F32, "ExternalInput")
    dram('outT', (c.D, c.To), F32, "ExternalOutput")

    dram('resid', (c.D, c.To), F32)
    dram('xn232', (c.D, c.To), F32)
    dram('yT32', (c.D, c.To), F32)
    dram('xtok', (c.To, c.D), BF16)
    dram('ytokM', (c.To, c.D), BF16)
    dram('xsorted', (c.SLOTS, c.D), BF16)
    dram('qTd', (c.D, c.To), BF16)
    dram('qtok', (c.To, c.D), BF16)
    dram('ktok', (c.To, 1056), BF16)
    dram('vtok', (c.To, 1056), BF16)
    dram('kfull', (c.Tf, 1056), BF16)
    dram('vfull', (c.Tf, 1056), BF16)
    dram('ktokF', (c.To, 1024), BF16)
    dram('vtokF', (c.To, 1024), BF16)
    dram('kfullF', (c.Tf, 1024), BF16)
    dram('vfullF', (c.Tf, 1024), BF16)
    dram('ftok', (c.To, c.D), BF16)
    dram('slotTab', (c.SLOTS,), I32)
    dram('rsc', (8192,), F32)
    for t in 'qkvf':
        dram('w1gB_' + t, (c.NLEAF * 128, 128), BF16)
        dram('w2gB_' + t, (4096, c.D), BF16)
        dram('b2B_' + t, (c.NLEAF, c.D), BF16)
    for nm in c.dumps:
        h = k.E[nm]
        dram('dmp_' + nm, list(h.shape), h.dtype, "ExternalOutput")
    for nm in c.dumps_q:
        if nm.startswith('S.'):
            continue
        h = k.E[nm]
        dram('dmpq_' + nm, list(h.shape), h.dtype, "ExternalOutput")

    es = ExitStack(); k.es = es

    def sbuf(name, shape, dt):
        return es.enter_context(nc.sbuf_tensor(name, list(shape), dt))

    S = NS(); k.S = S
    QH = c.QH
    S.ktT = sbuf('ktT', (128, c.DT, c.KSUP), BF16)
    S.vsup = sbuf('vsup', (128, c.NKT, 1056), BF16)
    S.qT = sbuf('qT', (128, c.DT, QH), BF16)
    S.arena = sbuf('arena', (128, c.DT * QH), F32)
    S.pT = sbuf('pT', (128, 4, 512), BF16)
    S.x32 = sbuf('x32', (128, c.DT, 512), F32)
    S.xw = sbuf('xw', (128, c.DT, 512), BF16)
    S.ttk = sbuf('ttk', (128, 2, c.D), BF16)
    S.rows2 = sbuf('rows2', (128, max(c.To, 4 * c.QH)), F32)
    S.bc = sbuf('bc', (128, 1024), F32)
    S.cols = sbuf('cols', (128, 16), F32)
    S.w0 = sbuf('w0', (128, 2, 2048), BF16)
    S.onesb = sbuf('onesb', (128, 2), BF16)
    S.ones32c = sbuf('ones32c', (128, 1), F32)
    S.ones1rb = sbuf('ones1rb', (1, 128), BF16)
    S.ones1r32 = sbuf('ones1r32', (1, 128), F32)
    S.iota = sbuf('iota', (128, 1), F32)
    S.pmod = sbuf('pmod', (128, 1), F32)
    S.iden = sbuf('iden', (128, 128), BF16)
    S.iden32 = sbuf('iden32', (128, 128), F32)
    S.rowsc = sbuf('rowsc', (128, 2048), F32)
    S.nwT = sbuf('nwT', (128, c.DT, 256), F32R)
    S.xr = sbuf('xr', (128, c.DT, 256), F32R)
    S.nb = sbuf('nb', (128, 2), F32)
    S.tokid = sbuf('tokid', (128, c.To // 128), F32)
    S.tki = sbuf('tki', (128, c.To // 128), I32)
    S.dstc = sbuf('dstc', (128, c.To // 128), I32)
    S.srt = sbuf('srt', (128, c.NTILE), I32)
    S.idxw1 = sbuf('idxw1', (128, c.NCHUNK), I32)
    S.idxw2 = sbuf('idxw2', (128, c.NTILE), I32)
    S.clri = sbuf('clri', (128, c.NTILE), I32)
    S.clrf = sbuf('clrf', (8, c.NTILE), F32)
    S.gth = sbuf('gth', (128, 2, c.D), BF16)
    S.w1sb = sbuf('w1sb', (128, 2, c.DT, 128), BF16)
    S.w2sb = sbuf('w2sb', (128, 2, c.D), BF16)
    S.b2sb = sbuf('b2sb', (128, 2, c.D), BF16)
    S.b1sb = sbuf('b1sb', (128, 4, 1), F32)
    S.hT = sbuf('hT', (128, 2, 128), BF16)
    S.hA = sbuf('hA', (128, 2, 128), BF16)
    S.ysl = sbuf('ysl', (128, 2, c.D), BF16)
    S.mdiag = sbuf('mdiag', (128, 128), BF16)
    S.indic = sbuf('indic', (8, 128), BF16)
    S.kd = sbuf('kd', (128, (c.Tf // 128) * (c.To // c.QB)), F32)
    S.qrow = sbuf('qrow', (128, c.QB), F32)
    S.srow = sbuf('srow', (1, c.NCHUNK), F32)
    S.brow = sbuf('brow', (128, 2 * c.D), BF16)
    S.vev = sbuf('vev', (128, 544), BF16)
    S.constv = sbuf('constv', (128, 1), F32)
    nc.const_aps.register(F32, 1e-5, S.constv[:, 0:1])

    for nm in c.dumps_q:
        if nm.startswith('S.'):
            t = getattr(S, nm[2:])
            dram('dmpq_' + nm[2:], list(t[:].tensor.shape), t[:].dtype,
                 "ExternalOutput")
    k.PA = es.enter_context(nc.psum_tensor('PA', [128, 2048], F32))
    k.PB = es.enter_context(nc.psum_tensor('PB', [128, 2048], F32))

    k.sems = {}
    for s in ['dq', 'dg', 'cc', 'dw0', 'dw1', 'sp0', 'sp1', 'sv', 'sb',
              'gg0', 'gg1', 'gx0', 'gx1', 'gt0', 'gt1', 'gt2', 'gt3',
              'gy0', 'gy1', 'dk', 'pe', 'ac', 'dv', 'po', 'sq']:
        k.sems[s] = es.enter_context(nc.semaphore(s))
    k.DMASEMS = ('dq', 'dg', 'cc', 'dw0', 'dw1', 'sp0', 'sp1', 'sv', 'sb',
                 'gg0', 'gg1', 'gx0', 'gx1', 'gt0', 'gt1', 'gt2', 'gt3',
                 'gy0', 'gy1', 'dk')
    return k



class EngProxy:
    def __init__(self, k, name):
        self._k = k
        self._name = name

    _NOWAIT = {'wait_ge', 'alloc_register', 'reg_mov', 'free_register', 'nop'}
    _OWN = {'ve': 'dv', 'sc': 'ac'}

    def __getattr__(self, attr):
        k = self._k
        name = self._name

        def call(*args, **kw):
            if k.live == name:
                own = EngProxy._OWN.get(name)
                if own and attr not in EngProxy._NOWAIT and k.cnt[own] > 0:
                    key = (name, own)
                    if k.waited.get(key, -1) < k.cnt[own]:
                        k.engs[name].wait_ge(k.sems[own], k.cnt[own])
                        k.waited[key] = k.cnt[own]
                try:
                    return getattr(k.engs[name], attr)(*args, **kw)
                except Exception as e:
                    raise RuntimeError('engine op failed: %s.%s: %r' % (name, attr, e)) from e
            elif EngProxy._OWN.get(name) and attr not in EngProxy._NOWAIT:
                own = EngProxy._OWN[name]
                k.waited[(name, own)] = max(k.waited.get((name, own), -1), k.cnt[own])
            return None
        return call

    def wait(self, sem, val=None):
        k = self._k
        if val is None:
            val = k.cnt[sem]
        val = min(val, k.cnt[sem])
        if val <= 0:
            return
        if k.live == self._name:
            key = (self._name, sem)
            if k.waited.get(key, -1) < val:
                k.engs[self._name].wait_ge(k.sems[sem], val)
                k.waited[key] = val

    def inc(self, instr, sem, step=1):
        k = self._k
        k.cnt[sem] += step
        if instr is not None:
            instr.then_inc(k.sems[sem], step)
        return k.cnt[sem]


def emit_program(k, scriptfn):
    blk = k.es.enter_context(k.nc.Block())
    k.engs = {}
    k.T = EngProxy(k, 'te'); k.V = EngProxy(k, 've'); k.C = EngProxy(k, 'sc')
    k.P = EngProxy(k, 'po'); k.Y = EngProxy(k, 'sy')

    def replay(name, eng):
        k.engs[name] = eng
        k.live = name
        k.cnt = {s: 0 for s in k.sems}
        k.waited = {}
        scriptfn(k)

    @blk.sync
    def _(e):
        replay('sy', e)

    @blk.scalar
    def _(e):
        replay('sc', e)

    @blk.vector
    def _(e):
        replay('ve', e)

    @blk.tensor
    def _(e):
        replay('te', e)

    @blk.gpsimd
    def _(e):
        replay('po', e)


def barrier(k, engines=None):
    snap = dict(k.cnt)
    own = {'te': 'pe', 've': 'dv', 'sc': 'ac', 'po': 'po', 'sy': 'sq'}
    prox = {'te': k.T, 've': k.V, 'sc': k.C, 'po': k.P, 'sy': k.Y}
    arr = {}
    for e in ('te', 've', 'sc', 'po', 'sy'):
        p = prox[e]
        for s, v in snap.items():
            if v > 0:
                p.wait(s, v)
        arr[e] = p.inc(p.nop(), own[e])
    for e in ('te', 've', 'sc', 'po', 'sy'):
        p = prox[e]
        for e2 in ('te', 've', 'sc', 'po', 'sy'):
            if e2 != e:
                p.wait(own[e2], arr[e2])



def script(k):
    c = k.c
    k.ac_exp = [0, 0]
    k.dv_msk = [0, 0]
    k.pe_qw = [0, 0]
    k.pe_kw = [0, 0]
    k.wtok_fff = [0, 0]
    k.ysl_tok = [0, 0]
    k.w1cp = [0, 0, 0, 0]
    k.qtd_st = [0, 0]
    k.vev_st = 0
    k.bc_st = 0
    k.qp_ev = [0, 0]
    k.qp_evv = [0, 0]
    k.tkm_tok = [0, 0]
    k.tkm_st = 0
    k.rt_pe = [0, 0]
    if k.live == 'po':
        k.breg = k.engs['po'].alloc_register('bnd')
        k.engs['po'].reg_mov(k.breg, c.To - 1)
    else:
        k.breg = None
    emit_init(k)
    src = 'xT'
    for i, p in enumerate(c.parts):
        dst = 'outT' if i == len(c.parts) - 1 else 'resid'
        if p == 1:
            emit_attn_sublayer(k, src, dst, 'aa')
        elif p == 3:
            emit_attn_sublayer(k, src, dst, 'ab')
        else:
            emit_ffn_sublayer(k, src, dst)
        src = 'resid'
    for nm in c.dumps:
        k.Y.inc(k.Y.dma_start(k.E['dmp_' + nm].ap(), k.E[nm].ap()), 'dq', 16)
        k.Y.wait('dq', k.cnt['dq'])
    for s in ('dq', 'dg', 'cc', 'pe', 'ac', 'dv', 'po'):
        if k.cnt[s] > 0:
            k.Y.wait(s, k.cnt[s])


def emit_init(k):
    c, S, E = k.c, k.S, k.E
    Y, P, V = k.Y, k.P, k.V
    for nm, dst in [('c_iota', S.iota), ('c_pmod', S.pmod), ('c_tokid', S.tokid),
                    ('c_maskdiag', S.mdiag), ('c_indic8', S.indic), ('c_kd', S.kd),
                    ('c_qrow', S.qrow), ('c_srow', S.srow)]:
        Y.inc(Y.dma_start(dst[:], E[nm].ap()), 'dq', 16)
    P.inc(P.memset(S.iden[:], 0.0), 'po')
    P.wait('po', k.cnt['po'])
    P.inc(P.affine_select(out=S.iden[:], in_=S.iden[:], compare_op=ALU.not_equal,
                          fill=1.0, base=0, pattern=[[-1, 128]],
                          channel_multiplier=1), 'po')
    P.inc(P.memset(S.iden32[:], 0.0), 'po')
    P.wait('po', k.cnt['po'])
    P.inc(P.affine_select(out=S.iden32[:], in_=S.iden32[:], compare_op=ALU.not_equal,
                          fill=1.0, base=0, pattern=[[-1, 128]],
                          channel_multiplier=1), 'po')
    P.inc(P.memset(S.constv[:], 1e-5), 'po')
    P.inc(P.memset(S.onesb[:], 1.0), 'po')
    P.inc(P.memset(S.ones32c[:], 1.0), 'po')
    P.inc(P.memset(S.ones1rb[:], 1.0), 'po')
    P.inc(P.memset(S.ones1r32[:], 1.0), 'po')
    P.inc(P.memset(S.vev[:], 0.0), 'po')
    P.wait('po', k.cnt['po'])
    for g in range(8):
        P.inc(P.memset(S.vev[:, g * 66 + 64: g * 66 + 65], 1.0), 'po')
    V.wait('dq', k.cnt['dq'])
    V.inc(V.tensor_copy(S.tki[:], S.tokid[:]), 'dv')
    P.inc(P.memset(S.gth[:, 0, 0:((c.To // 128) * 32)], 0.0), 'po')
    P.wait('po', k.cnt['po'])
    for nm in ('ktok', 'vtok'):
        P.inc(P.dma_start(E[nm].ap()[:, 1024:1056]
                          .rearrange('(a p) v -> p a v', p=128),
                          S.gth[:, 0, 0:((c.To // 128) * 32)]
                          .rearrange('p (a v) -> p a v', v=32)), 'dg', 16)
    vflat = S.vsup[:].rearrange('p a b -> p (a b)')
    piece = 4096
    if 2 in c.parts:
        for t in 'qkvf':
            for (src, dstn, n) in [('f%s_w1g' % t, 'w1gB_' + t, c.NLEAF * 128 * 128),
                                   ('f%s_w2g' % t, 'w2gB_' + t, 4096 * c.D),
                                   ('f%s_b2' % t, 'b2B_' + t, c.NLEAF * c.D)]:
                npc = max(n // (128 * piece), 1)
                per = n // 128 // npc
                sap = E[src].reshape([128, n // 128]).ap()
                dap = E[dstn].reshape([128, n // 128]).ap()
                for i in range(npc):
                    buf = vflat[:, (i % 2) * piece:(i % 2) * piece + per]
                    P.wait('dk', k.cnt['dk'])
                    P.inc(P.dma_start(buf, sap[:, ds(i, per)]), 'dk', 16)
                    P.wait('dk', k.cnt['dk'])
                    P.inc(P.dma_start(dap[:, ds(i, per)], buf), 'dk', 16)
    barrier(k)



def emit_prologue(k, src, kind):
    c, S, E = k.c, k.S, k.E
    Y, P, V, C, T = k.Y, k.P, k.V, k.C, k.T
    PA, PB = k.PA, k.PB
    xs = E[src].ap()

    if kind == 'aa':
        Y.inc(Y.dma_start(S.cols[:, 0:8], E['aa_gcol'].ap()), 'dq', 16)
        barrier(k, ('ve',))
        V.inc(V.tensor_scalar_add(S.cols[:, 0:8], S.cols[:, 0:8], 1.0), 'dv')
    if kind == 'ab':
        Y.inc(Y.dma_start(S.cols[:, 0:8], E['ab_bcq'].ap()), 'dq', 16)
        P.inc(P.dma_start(S.brow[0:1, 0:c.D], E['ab_brk'].ap()), 'dg', 16)
        P.inc(P.dma_start(S.brow[0:1, c.D:2 * c.D], E['ab_brv'].ap()), 'dg', 16)

    for s in range(c.ST):
        tsl = ds(s, 512)
        barrier(k)
        Y.inc(Y.dma_start(S.x32[:], xs[:, tsl].rearrange('(a p) t -> p a t', p=128)),
              'dq', 16)
        ldq = k.cnt['dq']
        C.wait('dq', ldq)
        for dt in range(c.DT):
            C.inc(C.activation(S.xw[:, dt, :], S.x32[:, dt, :], AF.Square), 'ac')
        T.wait('ac', k.cnt['ac'])
        T.wait('dq', ldq)
        for dt in range(c.DT):
            T.matmul(PA[0:1, 1024:1536], S.ones32c[:, 0:1], S.x32[:, dt, :],
                     start=(dt == 0), stop=(dt == c.DT - 1))
        for dt in range(c.DT):
            i = T.matmul(PA[0:1, 1536:2048], S.onesb[:, 0:1], S.xw[:, dt, :],
                         start=(dt == 0), stop=(dt == c.DT - 1))
        T.inc(i, 'pe')
        R = S.rowsc
        R2 = S.rows2
        m_ = R[0:1, 0:512]; m2_ = R[0:1, 512:1024]
        fs_ = R[0:1, 1024:1536]; f_ = R[0:1, 1536:2048]
        v_ = R2[0:1, 0:512]; sq_ = R2[0:1, 512:1024]
        r_ = R2[0:1, 1024:1536]; t_ = R2[0:1, 1536:2048]
        V.wait('pe', k.cnt['pe'])
        V.inc(V.tensor_scalar_mul(m_, PA[0:1, 1024:1536], 1.0 / 1024.0), 'dv')
        V.inc(V.tensor_mul(m2_, m_, m_), 'dv')
        V.inc(V.scalar_tensor_tensor(v_, PA[0:1, 1536:2048], 1.0 / 1024.0, m2_,
                                     ALU.mult, ALU.subtract), 'dv')
        C.wait('dv', k.cnt['dv'])
        C.inc(C.activation(sq_, v_, AF.Sqrt, bias=1e-5), 'ac')
        V.wait('ac', k.cnt['ac'])
        V.inc(V.reciprocal(r_, sq_), 'dv')
        if kind == 'aa':
            C.wait('dv', k.cnt['dv'])
            C.inc(C.activation(t_, v_, AF.Sqrt, scale=1024.0), 'ac')
            V.wait('ac', k.cnt['ac'])
            V.inc(V.tensor_mul(t_, t_, r_), 'dv')
            V.inc(V.tensor_scalar_max(t_, t_, 1e-12), 'dv')
            V.inc(V.reciprocal(t_, t_), 'dv')
            V.inc(V.scalar_tensor_tensor(t_, t_, SQD, r_, ALU.mult, ALU.mult), 'dv')
            V.inc(V.tensor_copy(f_, t_), 'dv')
        elif kind == 'ffn':
            C.wait('dv', k.cnt['dv'])
            C.inc(C.activation(t_, v_, AF.Sqrt, scale=1024.0), 'ac')
            V.wait('ac', k.cnt['ac'])
            V.inc(V.scalar_tensor_tensor(t_, t_, 1.0 / float(np.sqrt(1023.0)), r_,
                                         ALU.mult, ALU.mult), 'dv')
            C.wait('dv', k.cnt['dv'])
            C.inc(C.activation(t_, t_, AF.Sqrt, bias=1e-5), 'ac')
            V.wait('ac', k.cnt['ac'])
            V.inc(V.reciprocal(t_, t_), 'dv')
            V.inc(V.tensor_mul(t_, t_, r_), 'dv')
            V.inc(V.tensor_copy(f_, t_), 'dv')
        else:
            V.inc(V.tensor_copy(f_, r_), 'dv')
        T.wait('dv', k.cnt['dv'])
        T.matmul(PB[:, 0:512], S.ones1r32[:, :], S.rowsc[0:1, 0:512], start=True, stop=True)
        i = T.matmul(PB[:, 512:1024], S.ones1r32[:, :], S.rowsc[0:1, 1536:2048], start=True, stop=True)
        T.inc(i, 'pe')
        V.wait('pe', k.cnt['pe'])
        for dt in range(c.DT):
            V.inc(V.scalar_tensor_tensor(S.x32[:, dt, :], S.x32[:, dt, :], 1.0,
                                         PB[:, 0:512], ALU.mult, ALU.subtract), 'dv')
            if kind == 'aa':
                V.inc(V.scalar_tensor_tensor(S.x32[:, dt, :], S.x32[:, dt, :],
                                             S.cols[:, dt:dt + 1], PB[:, 512:1024],
                                             ALU.mult, ALU.mult), 'dv')
            else:
                V.inc(V.scalar_tensor_tensor(S.x32[:, dt, :], S.x32[:, dt, :], 1.0,
                                             PB[:, 512:1024], ALU.mult, ALU.mult), 'dv')
            V.inc(V.tensor_copy(S.xw[:, dt, :], S.x32[:, dt, :]), 'dv')
        if kind == 'ffn':
            Y.wait('dv', k.cnt['dv'])
            Y.inc(Y.dma_start(E['xn232'].ap()[:, tsl].rearrange('(a p) t -> p a t', p=128),
                              S.x32[:]), 'dq', 16)
            emit_tokmajor(k, 'xtok', s)
        else:
            emit_stripe_projections(k, s, kind)
    barrier(k)


def emit_tokmajor(k, dsttok, s):
    c, S, E = k.c, k.S, k.E
    Y, V, T = k.Y, k.V, k.T
    T.wait('dv', k.cnt['dv'])
    for tt in range(4):
        for dt in range(c.DT):
            bank = dt % 2
            T.wait('dv', k.tkm_tok[bank])
            i = T.matmul(PAq(k)[:, bank * 512: bank * 512 + 128],
                         S.xw[:, dt, ds(tt, 128)], S.iden[:], start=True, stop=True)
            T.inc(i, 'pe')
            V.wait('pe', k.cnt['pe'])
            V.wait('dq', k.tkm_st)
            i = V.tensor_copy(S.ttk[:, tt % 2, ds(dt, 128)],
                              PAq(k)[:, bank * 512: bank * 512 + 128])
            k.tkm_tok[bank] = V.inc(i, 'dv')
        if tt % 2 == 1:
            Y.wait('dv', k.cnt['dv'])
            k.tkm_st = Y.inc(
                Y.dma_start(E[dsttok].ap()[s * 512 + (tt - 1) * 128:
                                           s * 512 + (tt + 1) * 128, :]
                            .rearrange('(a p) d -> p a d', p=128), S.ttk[:]),
                'dq', 16)


def PAq(k):
    return k.PA


def emit_stripe_projections(k, s, kind):
    c, S, E = k.c, k.S, k.E
    Y, P, V, C, T = k.Y, k.P, k.V, k.C, k.T
    PA, PB = k.PA, k.PB
    tsl = ds(s, 512)
    w0f = S.w0[:].rearrange('p a b -> p (a b)')

    if kind == 'aa':
        qsrc = 'aa_wqkv'
        kvspec = [('aa_wqkv', 1024, 'ktok', False), ('aa_wqkv', 2048, 'vtok', True)]
    else:
        qsrc = 'ab_wq'
        kvspec = [('ab_wk', 0, 'ktokF', False), ('ab_wv', 0, 'vtokF', False)]

    k.pe_qw[0] = max(k.pe_qw[0], k.pe_kw[0])
    k.pe_qw[1] = max(k.pe_qw[1], k.pe_kw[1])
    for g in range(8):
        buf = g % 2
        if g >= 2:
            P.wait('pe', k.pe_qw[buf])
        else:
            P.wait('pe', k.pe_qw[buf])
        P.inc(P.dma_start(S.w0[:, buf, 0:1024].rearrange('p (a q) -> p a q', q=128),
                          E[qsrc].ap()[:, ds(g, 128)]
                          .rearrange('(a p) q -> p a q', p=128)), 'dw%d' % buf, 16)
        T.wait('dw%d' % buf, k.cnt['dw%d' % buf])
        T.wait('dv', k.cnt['dv'])
        T.wait('ac', k.qp_ev[g % 2])
        T.wait('dv', k.qp_evv[g % 2])
        for dt in range(c.DT):
            i = T.matmul(PB[:, 1024 + (g % 2) * 512: 1024 + (g % 2) * 512 + 512],
                         S.w0[:, buf, ds(dt, 128)], S.xw[:, dt, :],
                         start=(dt == 0), stop=(dt == c.DT - 1))
        k.pe_qw[buf] = T.inc(i, 'pe')
        if kind == 'ab':
            V.wait('pe', k.cnt['pe'])
            V.wait('sp%d' % (g % 2), k.qtd_st[g % 2])
            i = V.tensor_scalar(S.pT[:, 2 * (g % 2), :],
                                PB[:, 1024 + (g % 2) * 512:1024 + (g % 2) * 512 + 512],
                                S.cols[:, g:g + 1], None, ALU.add)
            k.qp_evv[g % 2] = V.inc(i, 'dv')
            Y.wait('dv', k.cnt['dv'])
        else:
            C.wait('pe', k.cnt['pe'])
            C.wait('sp%d' % (g % 2), k.qtd_st[g % 2])
            i = C.activation(S.pT[:, 2 * (g % 2), :],
                             PB[:, 1024 + (g % 2) * 512:1024 + (g % 2) * 512 + 512],
                             AF.Copy)
            k.qp_ev[g % 2] = C.inc(i, 'ac')
            Y.wait('ac', k.cnt['ac'])
        Y.wait('dv', k.cnt['dv'])
        Y.wait('ac', k.cnt['ac'])
        k.qtd_st[g % 2] = Y.inc(
            Y.dma_start(E['qTd'].ap()[ds(g, 128), tsl], S.pT[:, 2 * (g % 2), :]),
            'sp%d' % (g % 2), 16)

    k.pe_kw[0] = max(k.pe_kw[0], k.pe_qw[0])
    k.pe_kw[1] = max(k.pe_kw[1], k.pe_qw[1])
    for (wsrc, coff, dstname, aug) in kvspec:
        for kvg in range(4):
            buf = kvg % 2
            P.wait('pe', k.pe_kw[buf])
            P.inc(P.dma_start(S.w0[:, buf, :].rearrange('p (a q) -> p a q', q=256),
                              E[wsrc].ap()[:, coff + kvg * 256: coff + kvg * 256 + 256]
                              .rearrange('(a p) q -> p a q', p=128)), 'dw%d' % buf, 16)
            wv = S.w0[:, buf, :].rearrange('p (a q) -> p a q', q=256)
            for tt in range(4):
                T.wait('dw%d' % buf, k.cnt['dw%d' % buf])
                T.wait('dv', k.cnt['dv'])
                T.wait('dv', k.qp_evv[tt % 2])
                for dt in range(c.DT):
                    i = T.matmul(PB[:, (tt % 2) * 512:(tt % 2) * 512 + 256],
                                 S.xw[:, dt, ds(tt, 128)], wv[:, dt, :],
                                 start=(dt == 0),
                                 stop=(dt == c.DT - 1 and kind != 'ab'))
                if kind == 'ab':
                    brow = S.brow[0:1, 0:c.D] if dstname == 'ktokF' else S.brow[0:1, c.D:2 * c.D]
                    T.wait('dg', k.cnt['dg'])
                    i = T.matmul(PB[:, (tt % 2) * 512:(tt % 2) * 512 + 256],
                                 S.ones1rb[:, :],
                                 brow[:, kvg * 256:kvg * 256 + 256],
                                 start=False, stop=True)
                T.inc(i, 'pe')
                V.wait('pe', k.cnt['pe'])
                if aug:
                    V.wait('sv', k.vev_st)
                    V.inc(V.tensor_copy(
                        S.vev[:, 0:264].rearrange('p (g e) -> p g e', e=66)[:, :, 0:64],
                        PB[:, (tt % 2) * 512:(tt % 2) * 512 + 256]
                        .rearrange('p (g e) -> p g e', e=64)), 'dv')
                    k.qp_evv[tt % 2] = k.cnt['dv']
                    Y.wait('dv', k.cnt['dv'])
                    k.vev_st = Y.inc(Y.dma_start(
                        E[dstname].ap()[s * 512 + tt * 128: s * 512 + tt * 128 + 128,
                                        kvg * 264: kvg * 264 + 264],
                        S.vev[:, 0:264]), 'sv', 16)
                else:
                    V.wait('sp%d' % (tt % 2), k.qtd_st[tt % 2])
                    V.inc(V.tensor_copy(S.pT[:, 2 * (tt % 2), 0:256],
                                        PB[:, (tt % 2) * 512:(tt % 2) * 512 + 256]), 'dv')
                    k.qp_evv[tt % 2] = k.cnt['dv']
                    Y.wait('dv', k.cnt['dv'])
                    k.qtd_st[tt % 2] = Y.inc(Y.dma_start(
                        E[dstname].ap()[s * 512 + tt * 128: s * 512 + tt * 128 + 128,
                                        kvg * 256: kvg * 256 + 256],
                        S.pT[:, 2 * (tt % 2), 0:256]), 'sp%d' % (tt % 2), 16)
            k.pe_kw[buf] = k.cnt['pe']


def emit_attn_sublayer(k, src, dst, kind):
    c = k.c
    emit_prologue(k, src, kind)
    emit_exchange(k, kind == 'aa')
    for qh in range(c.NQH):
        emit_attention_quarter(k, qh, kind)
        if kind == 'aa':
            emit_epilogue_proj(k, qh, 'aa_wout', None, src, dst, heads=True)
        else:
            emit_epilogue_proj(k, qh, 'ab_wo', 'ab_bco', src, dst, heads=False)
    barrier(k)


def emit_exchange(k, aug):
    c, S, E = k.c, k.S, k.E
    P = k.P
    kk, vv = ('ktok', 'vtok') if aug else ('ktokF', 'vtokF')
    kf, vf = ('kfull', 'vfull') if aug else ('kfullF', 'vfullF')
    barrier(k)
    i = P.collective_compute("AllGather", ALU.bypass, replica_groups=c.groups,
                             ins=[E[kk].ap().opt()], outs=[E[kf].ap().opt()])
    P.inc(i, 'cc')
    i = P.collective_compute("AllGather", ALU.bypass, replica_groups=c.groups,
                             ins=[E[vv].ap().opt()], outs=[E[vf].ap().opt()])
    P.inc(i, 'cc')


def emit_attention_quarter(k, qh, kind):
    c, S, E = k.c, k.S, k.E
    Y, P, V, C, T = k.Y, k.P, k.V, k.C, k.T
    PA, PB = k.PA, k.PB
    QH = c.QH
    qsl = ds(qh, QH)
    scale = 0.125 if kind == 'aa' else 1.0 / 32.0
    arena = S.arena[:].rearrange('p (a q) -> p a q', q=QH)

    barrier(k)
    if kind == 'ffn':
        for dt in range(c.DT):
            Y.inc(Y.dma_start(S.qT[:, dt, :], E['qtok'].ap()[qsl, ds(dt, 128)],
                              transpose=True), 'dq', 16)
    else:
        Y.inc(Y.dma_start(S.qT[:], E['qTd'].ap()[:, qsl]
                          .rearrange('(a p) t -> p a t', p=128)), 'dq', 16)
    P.inc(P.memset(S.arena[:], 0.0), 'po')
    P.inc(P.memset(S.rows2[:], 0.0), 'po')
    P.inc(P.memset(S.rowsc[32:33, :], 0.0), 'po')
    P.inc(P.memset(S.rowsc[64:65, :], 0.0), 'po')
    barrier(k)

    kf, vf = ('kfull', 'vfull') if kind == 'aa' else ('kfullF', 'vfullF')
    for sup in range(c.NSUP):
        ksl = slice(sup * c.KSUP, (sup + 1) * c.KSUP)
        barrier(k)
        for dt in range(c.DT):
            Y.inc(Y.dma_start(S.ktT[:, dt, :], E[kf].ap()[ksl, ds(dt, 128)],
                              transpose=True), 'dq', 16)
        if kind == 'aa':
            Y.inc(Y.dma_start(S.vsup[:], E[vf].ap()[ksl, :]
                              .rearrange('(a p) v -> p a v', p=128)), 'dq', 16)
        else:
            Y.inc(Y.dma_start(S.vsup[:, :, 0:1024], E[vf].ap()[ksl, :]
                              .rearrange('(a p) v -> p a v', p=128)), 'dq', 16)
        barrier(k)
        if kind == 'aa':
            emit_super_aa(k, scale)
        else:
            emit_super_single(k, scale, kind, qh, sup)
        barrier(k)


def aa_sum_slot(k, hh, off, width):
    S, c = k.S, k.c
    QH = c.QH
    if hh < 12:
        return S.rows2[32 * (hh // 4):32 * (hh // 4) + 1,
                       (hh % 4) * QH + off:(hh % 4) * QH + off + width]
    h2 = hh - 12
    return S.rowsc[32 * (h2 // 2 + 1):32 * (h2 // 2 + 1) + 1,
                   (h2 % 2) * 1024 + off:(h2 % 2) * 1024 + off + width]


def emit_super_aa(k, scale):
    c, S = k.c, k.S
    V, C, T = k.V, k.C, k.T
    PA, PB = k.PA, k.PB
    QH = c.QH
    arena = S.arena[:].rearrange('p (a q) -> p a q', q=QH)
    pTf = S.pT[:].rearrange('p a b -> p (a b)')
    for qb in range(c.NQBA):
        qbs = ds(qb, c.QBA)
        for pr in range(8):
            hA, hB = 2 * pr, 2 * pr + 1
            pe_pv = [0, 0]
            for kt in range(c.NKT):
                par = kt % 2
                T.wait('ac', k.ac_exp[par])
                T.matmul(PA[:, par * 1024: par * 1024 + 512],
                         S.ktT[0:64, pr, ds(kt, 128)],
                         S.qT[0:64, pr, qbs], start=True, stop=True)
                iB = T.matmul(PA[:, par * 1024 + 512: par * 1024 + 1024],
                              S.ktT[64:128, pr, ds(kt, 128)],
                              S.qT[64:128, pr, qbs], start=True, stop=True)
                T.inc(iB, 'pe')
                pe_s = k.cnt['pe']
                C.wait('pe', pe_s)
                i = C.activation(pTf[:, par * 1024:(par + 1) * 1024],
                                 PA[:, par * 1024:(par + 1) * 1024], AF.Exp,
                                 scale=scale)
                k.ac_exp[par] = C.inc(i, 'ac')
                T.wait('ac', k.ac_exp[par])
                T.matmul(PB[0:65, 0:512], S.vsup[:, kt, 66 * hA:66 * hA + 65],
                         S.pT[:, 2 * par, :], start=(kt == 0), stop=(kt == c.NKT - 1))
                i = T.matmul(PB[0:65, 512:1024], S.vsup[:, kt, 66 * hB:66 * hB + 65],
                             S.pT[:, 2 * par + 1, :], start=(kt == 0), stop=(kt == c.NKT - 1))
            T.inc(i, 'pe')
            V.wait('pe', k.cnt['pe'])
            for hh, reg in ((hA, 0), (hB, 1)):
                dt = hh // 2
                rb = (hh % 2) * 64
                V.inc(V.tensor_add(arena[rb:rb + 64, dt, qbs],
                                   PB[0:64, reg * 512:reg * 512 + 512],
                                   arena[rb:rb + 64, dt, qbs]), 'dv')
                slot = aa_sum_slot(k, hh, qb * c.QBA, c.QBA)
                V.inc(V.tensor_add(slot, PB[64:65, reg * 512:reg * 512 + 512],
                                   slot), 'dv')
            T.wait('dv', k.cnt['dv'])


def emit_super_single(k, scale, kind, qh, sup):
    c, S = k.c, k.S
    V, C, T = k.V, k.C, k.T
    PA, PB = k.PA, k.PB
    QH = c.QH
    arena = S.arena[:].rearrange('p (a q) -> p a q', q=QH)
    pTf = S.xw[:].rearrange('p a b -> p (a b)')
    nqb = c.To // c.QB
    for qb in range(c.NQB):
        qbs = ds(qb, c.QB)
        for kt in range(c.NKT):
            par = kt % 2
            T.wait('ac', k.ac_exp[par])
            T.wait('dv', k.dv_msk[par])
            for dt in range(c.DT):
                i = T.matmul(PA[:, par * 512: par * 512 + 512],
                             S.ktT[:, dt, ds(kt, 128)], S.qT[:, dt, qbs],
                             start=(dt == 0), stop=(dt == c.DT - 1))
            T.inc(i, 'pe')
            C.wait('pe', k.cnt['pe'])
            i = C.activation(pTf[:, kt * 512:(kt + 1) * 512],
                             PA[:, par * 512: par * 512 + 512], AF.Exp, scale=scale)
            k.ac_exp[par] = C.inc(i, 'ac')
            if kind == 'ab':
                V.wait('ac', k.ac_exp[par])
                kdcol = S.kd[:, (sup * c.NKT + kt) * nqb + qh * c.NQB + qb:
                             (sup * c.NKT + kt) * nqb + qh * c.NQB + qb + 1]
                i = V.scalar_tensor_tensor(pTf[:, kt * 512:(kt + 1) * 512],
                                           S.qrow[:, 0:c.QB], kdcol,
                                           pTf[:, kt * 512:(kt + 1) * 512],
                                           ALU.is_le, ALU.mult)
                k.dv_msk[par] = V.inc(i, 'dv')
            if kind == 'ab':
                T.wait('dv', k.dv_msk[par])
            else:
                T.wait('ac', k.ac_exp[par])
            i = T.matmul(PA[0:1, 1024:1536], S.onesb[:, 0:1],
                         pTf[:, kt * 512:(kt + 1) * 512],
                         start=(kt == 0), stop=(kt == c.NKT - 1))
        T.inc(i, 'pe')
        V.wait('pe', k.cnt['pe'])
        V.inc(V.tensor_add(S.rows2[0:1, qb * c.QB:(qb + 1) * c.QB],
                           PA[0:1, 1024:1536],
                           S.rows2[0:1, qb * c.QB:(qb + 1) * c.QB]), 'dv')
        T.wait('dv', k.cnt['dv'])
        T.wait('ac', k.cnt['ac'])
        for kt in range(c.NKT):
            for dt in range(c.DT):
                ps = PA if dt < 4 else PB
                col = (dt % 4) * 512
                i = T.matmul(ps[:, col:col + 512], S.vsup[:, kt, ds(dt, 128)],
                             pTf[:, kt * 512:(kt + 1) * 512],
                             start=(kt == 0), stop=(kt == c.NKT - 1))
        T.inc(i, 'pe')
        V.wait('pe', k.cnt['pe'])
        i = V.tensor_add(arena[:, 0:4, qbs],
                         PA[:].rearrange('p (a q) -> p a q', q=512),
                         arena[:, 0:4, qbs])
        V.inc(i, 'dv')
        i = V.tensor_add(arena[:, 4:8, qbs],
                         PB[:].rearrange('p (a q) -> p a q', q=512),
                         arena[:, 4:8, qbs])
        V.inc(i, 'dv')
        k.ac_exp = [0, 0]
        k.dv_msk = [0, 0]
        T.wait('dv', k.cnt['dv'])


def emit_epilogue_proj(k, qh, wname, bname, src, dst, heads):
    c, S, E = k.c, k.S, k.E
    Y, P, V, C, T = k.Y, k.P, k.V, k.C, k.T
    PA, PB = k.PA, k.PB
    QH = c.QH
    qsl = ds(qh, QH)
    arena = S.arena[:].rearrange('p (a q) -> p a q', q=QH)
    barrier(k)
    if heads:
        for hh in range(16):
            V.inc(V.reciprocal(S.rowsc[0:1, 0:QH], aa_sum_slot(k, hh, 0, QH)), 'dv')
            T.wait('dv', k.cnt['dv'])
            for ch in range(QH // 512):
                i = T.matmul(PB[0:64, (ch % 2) * 512:(ch % 2) * 512 + 512],
                             S.ones1r32[0:1, 0:64], S.rowsc[0:1, ds(ch, 512)],
                             start=True, stop=True)
                T.inc(i, 'pe')
                V.wait('pe', k.cnt['pe'])
                rb = (hh % 2) * 64
                V.inc(V.tensor_mul(S.qT[rb:rb + 64, hh // 2, ds(ch, 512)],
                                   arena[rb:rb + 64, hh // 2, ds(ch, 512)],
                                   PB[0:64, (ch % 2) * 512:(ch % 2) * 512 + 512]), 'dv')
            T.wait('dv', k.cnt['dv'])
    else:
        V.inc(V.reciprocal(S.rowsc[0:1, 0:QH], S.rows2[0:1, 0:QH]), 'dv')
        T.wait('dv', k.cnt['dv'])
        for ch in range(QH // 512):
            i = T.matmul(PB[:, (ch % 2) * 512:(ch % 2) * 512 + 512],
                         S.ones1r32[:, :], S.rowsc[0:1, ds(ch, 512)],
                         start=True, stop=True)
            T.inc(i, 'pe')
            V.wait('pe', k.cnt['pe'])
            for dt in range(c.DT):
                V.inc(V.tensor_mul(S.qT[:, dt, ds(ch, 512)],
                                   arena[:, dt, ds(ch, 512)],
                                   PB[:, (ch % 2) * 512:(ch % 2) * 512 + 512]), 'dv')
            T.wait('dv', k.cnt['dv'])
    if bname:
        Y.inc(Y.dma_start(S.cols[:, 8:16], E[bname].ap()), 'dq', 16)
    barrier(k)
    xs = E[src].ap()
    dd = E[dst].ap()
    for g in range(8):
        buf = g % 2
        if g >= 2:
            P.wait('pe', k.pe_qw[buf])
        P.inc(P.dma_start(S.w0[:, buf, 0:1024].rearrange('p (a q) -> p a q', q=128),
                          E[wname].ap()[:, ds(g, 128)]
                          .rearrange('(a p) q -> p a q', p=128)), 'dw%d' % buf, 16)
        Y.wait('dv', k.cnt['dv'])
        Y.wait('sb', k.bc_st)
        Y.inc(Y.dma_start(S.bc[:], xs[ds(g, 128), qsl]), 'sb', 16)
        T.wait('sb', k.cnt['sb'])
        T.wait('dw%d' % buf, k.cnt['dw%d' % buf])
        for ch in range(QH // 512):
            for dt in range(c.DT):
                i = T.matmul(PA[:, (ch % 2) * 512:(ch % 2) * 512 + 512],
                             S.w0[:, buf, ds(dt, 128)], S.qT[:, dt, ds(ch, 512)],
                             start=(dt == 0), stop=(dt == c.DT - 1))
        k.pe_qw[buf] = T.inc(i, 'pe')
        V.wait('pe', k.cnt['pe'])
        V.wait('sb', k.cnt['sb'])
        bcol = S.cols[:, 8 + g:9 + g] if bname else 0.0
        i = V.scalar_tensor_tensor(S.bc[:], PA[:, 0:1024], bcol, S.bc[:],
                                   ALU.add, ALU.add)
        V.inc(i, 'dv')
        Y.wait('dv', k.cnt['dv'])
        k.bc_st = Y.inc(Y.dma_start(dd[ds(g, 128), qsl], S.bc[:]), 'sb', 16)



def emit_ffn_sublayer(k, src, dst):
    c = k.c
    emit_prologue(k, src, 'ffn')
    for t, dsttok in (('q', 'qtok'), ('k', 'ktokF'), ('v', 'vtokF')):
        emit_fff(k, t, 'xtok', 'xn232', dsttok, AF.Relu)
        if t == 'q' and c.dumps_q:
            barrier(k)
            for nm in c.dumps_q:
                if nm.startswith('S.'):
                    sb = getattr(k.S, nm[2:])
                    k.Y.inc(k.Y.dma_start(k.E['dmpq_' + nm[2:]].ap(), sb[:]),
                            'dq', 16)
                else:
                    k.Y.inc(k.Y.dma_start(k.E['dmpq_' + nm].ap(), k.E[nm].ap()),
                            'dq', 16)
                k.Y.wait('dq', k.cnt['dq'])
            barrier(k)
    emit_exchange(k, False)
    for qh in range(c.NQH):
        emit_attention_quarter(k, qh, 'ffn')
        emit_epilogue_y(k, qh)
    emit_fff(k, 'f', 'ytokM', 'yT32', 'ftok', AF.Gelu)
    emit_final_add(k, src, dst)
    barrier(k)


def emit_epilogue_y(k, qh):
    c, S, E = k.c, k.S, k.E
    Y, V, C, T = k.Y, k.V, k.C, k.T
    PA, PB = k.PA, k.PB
    QH = c.QH
    arena = S.arena[:].rearrange('p (a q) -> p a q', q=QH)
    barrier(k)
    V.inc(V.reciprocal(S.rowsc[0:1, 0:QH], S.rows2[0:1, 0:QH]), 'dv')
    for hf in range(QH // 512):
        s_global = (qh * QH + hf * 512) // 512
        tsl = ds(s_global, 512)
        Y.wait('dq', k.cnt['dq'])
        Y.wait('dv', k.cnt['dv'])
        Y.inc(Y.dma_start(S.x32[:], E['xn232'].ap()[:, tsl]
                          .rearrange('(a p) t -> p a t', p=128)), 'dq', 16)
        T.wait('dv', k.cnt['dv'])
        i = T.matmul(PB[:, 0:512], S.ones1r32[:, :], S.rowsc[0:1, ds(hf, 512)],
                     start=True, stop=True)
        T.inc(i, 'pe')
        V.wait('pe', k.cnt['pe'])
        V.wait('dq', k.cnt['dq'])
        for dt in range(c.DT):
            V.inc(V.tensor_mul(arena[:, dt, ds(hf, 512)], arena[:, dt, ds(hf, 512)],
                               PB[:, 0:512]), 'dv')
            V.inc(V.tensor_add(S.x32[:, dt, :], arena[:, dt, ds(hf, 512)],
                               S.x32[:, dt, :]), 'dv')
            V.inc(V.tensor_copy(S.xw[:, dt, :], S.x32[:, dt, :]), 'dv')
        Y.wait('dv', k.cnt['dv'])
        Y.inc(Y.dma_start(E['yT32'].ap()[:, tsl].rearrange('(a p) t -> p a t', p=128),
                          S.x32[:]), 'dq', 16)
        emit_tokmajor(k, 'ytokM', s_global)
    barrier(k)


def emit_final_add(k, src, dst):
    c, S, E = k.c, k.S, k.E
    Y, V, T = k.Y, k.V, k.T
    qt32 = k.S.qT[:].rearrange('p a b -> p (a b)').bitcast(F32)
    barrier(k)
    for s in range(c.ST):
        tsl = ds(s, 512)
        barrier(k)
        Y.inc(Y.dma_start(S.x32[:], E[src].ap()[:, tsl]
                          .rearrange('(a p) t -> p a t', p=128)), 'dq', 16)
        Y.inc(Y.dma_start(qt32.rearrange('p (a t) -> p a t', t=512),
                          E['yT32'].ap()[:, tsl]
                          .rearrange('(a p) t -> p a t', p=128)), 'dq', 16)
        for dt in range(c.DT):
            Y.inc(Y.dma_start(S.xw[:, dt, :], E['ftok'].ap()[tsl, ds(dt, 128)],
                              transpose=True), 'dq', 16)
        V.wait('dq', k.cnt['dq'])
        for dt in range(c.DT):
            V.inc(V.tensor_add(S.x32[:, dt, :], S.x32[:, dt, :],
                               qt32.rearrange('p (a t) -> p a t', t=512)[:, dt, :]), 'dv')
            V.inc(V.tensor_add(S.x32[:, dt, :], S.x32[:, dt, :], S.xw[:, dt, :]), 'dv')
        Y.wait('dv', k.cnt['dv'])
        Y.inc(Y.dma_start(E[dst].ap()[:, tsl].rearrange('(a p) t -> p a t', p=128),
                          S.x32[:]), 'dq', 16)



def emit_fff(k, tree, srctok, src32, dsttok, act):
    c, S, E = k.c, k.S, k.E
    Y, P, V, C, T = k.Y, k.P, k.V, k.C, k.T
    PA, PB = k.PA, k.PB
    AOFF = 0
    arenaF = S.arena[:]
    lgT0 = arenaF[:, 0:c.To]
    lgT1 = arenaF[:, c.To:2 * c.To]
    mask = arenaF[:, 2 * c.To:3 * c.To] if 3 * c.To <= c.DT * c.QH else None
    x32f = S.x32[:].rearrange('p a b -> p (a b)')
    barrier(k)
    P.inc(P.dma_start(S.nwT[:, :, 0:255].rearrange('p a n -> p a n'),
                      E['f%s_nwT' % tree].ap().rearrange('(a p) n -> p a n', p=128)),
          'dg', 16)
    Y.inc(Y.dma_start(S.nb[:], E['f%s_nb' % tree].ap()), 'dq', 16)
    P.inc(P.memset(S.arena[:, 0:2 * c.To], 0.0), 'po')
    barrier(k)

    for s in range(c.ST):
        tsl = ds(s, 512)
        barrier(k)
        Y.inc(Y.dma_start(S.x32[:], E[src32].ap()[:, tsl]
                          .rearrange('(a p) t -> p a t', p=128)), 'dq', 16)
        xr = S.xr[:]
        for hf in range(2):
            hsl = slice(s * 512 + hf * 256, s * 512 + hf * 256 + 256)
            V.wait('dq', k.cnt['dq'])
            T.wait('pe', k.rt_pe[hf])
            for dt in range(c.DT):
                V.inc(V.tensor_copy(xr[:, dt, :], S.x32[:, dt, hf * 256:hf * 256 + 256]), 'dv')
            T.wait('dv', k.cnt['dv'])
            T.wait('ac', k.cnt['ac'])
            for half in range(2):
                ncols = 128 if half == 0 else 127
                for dt in range(c.DT):
                    i = T.matmul(PA[0:ncols, half * 512 + hf * 256: half * 512 + hf * 256 + 256],
                                 S.nwT[:, dt, half * 128: half * 128 + ncols],
                                 xr[:, dt, :],
                                 start=(dt == 0), stop=(dt == c.DT - 1))
                k.rt_pe[hf] = T.inc(i, 'pe')
                V.wait('pe', k.cnt['pe'])
                dstl = lgT0 if half == 0 else lgT1
                i = V.tensor_scalar(dstl[:, hsl] if half == 0 else dstl[0:127, hsl],
                                    PA[:, half * 512 + hf * 256: half * 512 + hf * 256 + 256] if half == 0
                                    else PA[0:127, half * 512 + hf * 256: half * 512 + hf * 256 + 256],
                                    S.nb[:, half:half + 1] if half == 0
                                    else S.nb[0:127, half:half + 1],
                                    None, ALU.add)
                V.inc(i, 'dv')
    barrier(k)

    r_pref = S.rows2[0:1, 0:c.To]
    r_b = S.rowsc[0:1, 0:512]
    NCH = c.To // 512
    V.inc(V.tensor_single_scalar(r_pref, lgT0[0:1, :], 0.0, ALU.is_gt), 'dv')
    gsaved = False
    for lv in range(1, 8):
        off = (1 << lv) - 1
        segs = [(lgT0, off, 0)]
        if lv == 7:
            segs = [(lgT0, 127, 0), (lgT1, -1.0, 1)]
        for chix in range(NCH):
            chs = ds(chix, 512)
            for si, (lg, shift, segi) in enumerate(segs):
                V.inc(V.tensor_scalar_add(S.rowsc[0:1, 1024:1536], r_pref[:, chs],
                                          float(shift)), 'dv')
                T.wait('dv', k.cnt['dv'])
                i = T.matmul(PB[:, 0:512], S.ones1r32[:, :], S.rowsc[0:1, 1024:1536],
                             start=True, stop=True)
                T.inc(i, 'pe')
                V.wait('pe', k.cnt['pe'])
                i = V.scalar_tensor_tensor(x32f[:, chs], PB[:, 0:512], S.iota[:, 0:1],
                                           lg[:, chs],
                                           ALU.is_equal, ALU.mult)
                V.inc(i, 'dv')
                T.wait('dv', k.cnt['dv'])
                i = T.matmul(PB[0:1, 1024:1536], S.ones32c[:],
                             x32f[:, chs],
                             start=(si == 0), stop=(si == len(segs) - 1))
                T.inc(i, 'pe')
            V.wait('pe', k.cnt['pe'])
            V.inc(V.tensor_single_scalar(r_b, PB[0:1, 1024:1536], 0.0,
                                         ALU.is_gt), 'dv')
            V.inc(V.scalar_tensor_tensor(r_pref[:, chs], r_pref[:, chs], 2.0,
                                         r_b, ALU.mult, ALU.add), 'dv')

    barrier(k)
    emit_sort(k, tree)
    emit_fff_tiles(k, tree, srctok, dsttok, act)


def emit_sort(k, tree):
    c, S, E = k.c, k.S, k.E
    Y, P, V, C, T = k.Y, k.P, k.V, k.C, k.T
    PA, PB = k.PA, k.PB
    arenaF = S.arena[:]
    lgT0 = arenaF[:, 0:c.To]
    lgT1 = arenaF[:, c.To:2 * c.To]
    x32f = S.x32[:].rearrange('p a b -> p (a b)')[:, 0:c.To]
    r_leaf = S.rows2[0:1, 0:c.To]
    r_d = S.rows2[32:33, 0:c.To]
    x32row = S.x32[0:1, :, :].rearrange('p a b -> p (a b)')[:, 0:c.To]
    NCH = c.To // 512
    rsc_c = E['rsc'].reshape([8192, 1])
    rsc_r = E['rsc'].reshape([1, 8192])

    for H in range(2):
        for chix in range(NCH):
            chs = ds(chix, 512)
            V.inc(V.tensor_scalar_add(S.rowsc[0:1, 1024:1536], r_leaf[:, chs],
                                      -128.0 * H), 'dv')
            T.wait('dv', k.cnt['dv'])
            i = T.matmul(PB[:, 0:512], S.ones1r32[:, :], S.rowsc[0:1, 1024:1536],
                         start=True, stop=True)
            T.inc(i, 'pe')
            V.wait('pe', k.cnt['pe'])
            V.inc(V.tensor_scalar(lgT0[:, chs], PB[:, 0:512], S.iota[:, 0:1], None,
                                  ALU.is_equal), 'dv')
        V.inc(V.tensor_tensor_scan(x32f, lgT0, lgT0, 0.0, ALU.add, ALU.bypass), 'dv')
        V.inc(V.tensor_copy(S.cols[:, 8 + H:9 + H], x32f[:, c.To - 1:c.To]), 'dv')
        for chix in range(NCH):
            chs = ds(chix, 512)
            V.inc(V.tensor_mul(lgT1[:, chs], lgT0[:, chs], x32f[:, chs]), 'dv')
            T.wait('dv', k.cnt['dv'])
            i = T.matmul(PA[0:1, 1024:1536], S.ones32c[:],
                         lgT1[:, chs], start=True, stop=True)
            T.inc(i, 'pe')
            V.wait('pe', k.cnt['pe'])
            if H == 0:
                V.inc(V.tensor_copy(r_d[:, chs], PA[0:1, 1024:1536]), 'dv')
            else:
                V.inc(V.tensor_add(r_d[:, chs], PA[0:1, 1024:1536], r_d[:, chs]), 'dv')
            T.wait('dv', k.cnt['dv'])
        ncol = S.cols[:, 8 + H:9 + H]
        ti = S.cols[:, 14:15].bitcast(I32)
        V.inc(V.tensor_scalar_add(S.cols[:, 10 + H:11 + H], ncol, 15.0), 'dv')
        V.inc(V.tensor_copy(ti, S.cols[:, 10 + H:11 + H]), 'dv')
        V.inc(V.tensor_single_scalar(ti, ti, -16, ALU.bitwise_and), 'dv')
        V.inc(V.tensor_copy(S.cols[:, 10 + H:11 + H], ti), 'dv')
        Y.wait('dv', k.cnt['dv'])
        Y.inc(Y.dma_start(rsc_c.ap()[H * 128:(H + 1) * 128, :],
                          S.cols[:, 10 + H:11 + H]), 'dq', 16)
    Y.wait('dq', k.cnt['dq'])
    Y.inc(Y.dma_start(S.rowsc[0:1, 0:256], rsc_r.ap()[:, 0:256]), 'dq', 16)
    V.wait('dq', k.cnt['dq'])
    V.inc(V.tensor_tensor_scan(S.rowsc[0:1, 256:512], S.rowsc[0:1, 0:256],
                               S.rowsc[0:1, 0:256], 0.0, ALU.add, ALU.bypass), 'dv')
    V.inc(V.tensor_sub(S.rowsc[0:1, 256:512], S.rowsc[0:1, 256:512],
                       S.rowsc[0:1, 0:256]), 'dv')
    Y.wait('dv', k.cnt['dv'])
    Y.inc(Y.dma_start(rsc_r.ap()[:, 256:512], S.rowsc[0:1, 256:512]), 'dq', 16)
    Y.wait('dq', k.cnt['dq'])
    for H in range(2):
        Y.inc(Y.dma_start(S.cols[:, 12 + H:13 + H],
                          rsc_c.ap()[256 + 128 * H:256 + 128 * H + 128, :]), 'dq', 16)
    V.wait('dq', k.cnt['dq'])
    for H in range(2):
        offcol = S.cols[:, 12 + H:13 + H]
        for chix in range(NCH):
            chs = ds(chix, 512)
            V.inc(V.tensor_scalar_add(S.rowsc[0:1, 1024:1536], r_leaf[:, chs],
                                      -128.0 * H), 'dv')
            T.wait('dv', k.cnt['dv'])
            i = T.matmul(PB[:, 0:512], S.ones1r32[:, :], S.rowsc[0:1, 1024:1536],
                         start=True, stop=True)
            T.inc(i, 'pe')
            V.wait('pe', k.cnt['pe'])
            V.inc(V.tensor_scalar(lgT0[:, chs], PB[:, 0:512], S.iota[:, 0:1], None,
                                  ALU.is_equal), 'dv')
            V.inc(V.tensor_scalar(lgT0[:, chs], lgT0[:, chs], offcol, None, ALU.mult), 'dv')
            T.wait('dv', k.cnt['dv'])
            i = T.matmul(PA[0:1, 1024:1536], S.ones32c[:],
                         lgT0[:, chs], start=True, stop=True)
            T.inc(i, 'pe')
            V.wait('pe', k.cnt['pe'])
            V.inc(V.tensor_add(r_d[:, chs], PA[0:1, 1024:1536], r_d[:, chs]), 'dv')
            T.wait('dv', k.cnt['dv'])
    V.inc(V.tensor_scalar_add(r_leaf, r_d, -1.0), 'dv')
    di = lgT0[0:1, 0:c.To].bitcast(I32)
    V.inc(V.tensor_copy(di, r_leaf), 'dv')
    V.inc(V.tensor_single_scalar(di, di, 127, ALU.bitwise_and), 'dv')
    V.inc(V.tensor_copy(x32row, di), 'dv')
    V.inc(V.tensor_sub(r_leaf, r_leaf, x32row), 'dv')
    V.inc(V.tensor_scalar(r_leaf, r_leaf, 1.0 / 128.0, None, ALU.mult), 'dv')
    V.inc(V.tensor_scalar(x32row, x32row, float(c.NTILE), None, ALU.mult), 'dv')
    V.inc(V.tensor_add(r_leaf, r_leaf, x32row), 'dv')
    nvt = c.To // 128
    Y.wait('dv', k.cnt['dv'])
    Y.inc(Y.dma_start(rsc_r.ap()[:, 0:c.To], r_leaf), 'dq', 16)
    Y.wait('dq', k.cnt['dq'])
    Y.inc(Y.dma_start(S.bc[0:nvt, 0:128], E['rsc'].reshape([64, 128]).ap()[0:nvt, :]), 'dq', 16)
    T.wait('dq', k.cnt['dq'])
    i = T.matmul(PB[:, 0:nvt], S.bc[0:nvt, 0:128],
                 S.iden32[0:nvt, 0:nvt], start=True, stop=True)
    T.inc(i, 'pe')
    V.wait('pe', k.cnt['pe'])
    V.inc(V.tensor_copy(S.dstc[:, 0:nvt], PB[:, 0:nvt]), 'dv')
    P.wait('dv', k.cnt['dv'])
    P.inc(P.dma_start(E['slotTab'].ap(), E['c_oob'].ap()), 'dg', 16)
    P.wait('dg', k.cnt['dg'])
    for i_t in range(nvt):
        P.wait('dg', k.cnt['dg'])
        P.inc(P.indirect_dma_start(
            rsc_i32(k), IndirectOffsetOnAxis(ap=S.dstc[:, i_t:i_t + 1], axis=0),
            S.tki[:, i_t:i_t + 1], None), 'dg', 16)
    P.wait('dg', k.cnt['dg'])
    P.inc(P.dma_start(S.srt[:], E['slotTab'].reshape([128, c.NTILE]).ap()), 'dg', 16)
    for H in range(2):
        offcol = S.cols[:, 12 + H:13 + H]
        T.wait('dv', k.cnt['dv'])
        i = T.matmul(PB[:, 0:c.NCHUNK], S.ones1r32[:, :], S.srow[0:1, 0:c.NCHUNK],
                     start=True, stop=True)
        T.inc(i, 'pe')
        V.wait('pe', k.cnt['pe'])
        V.inc(V.tensor_scalar(lgT0[:, 0:c.NCHUNK], PB[:, 0:c.NCHUNK], offcol, None,
                              ALU.is_ge), 'dv')
        T.wait('dv', k.cnt['dv'])
        i = T.matmul(PA[0:1, 1024:1024 + c.NCHUNK], S.ones32c[:],
                     lgT0[:, 0:c.NCHUNK],
                     start=(H == 0), stop=(H == 1))
        T.inc(i, 'pe')
    V.wait('pe', k.cnt['pe'])
    V.inc(V.tensor_scalar_add(S.rowsc[0:1, 1024:1024 + c.NCHUNK], PA[0:1, 1024:1024 + c.NCHUNK],
                              -1.0), 'dv')
    Y.wait('dv', k.cnt['dv'])
    Y.inc(Y.dma_start(rsc_r.ap()[:, 0:c.NCHUNK], S.rowsc[0:1, 1024:1024 + c.NCHUNK]), 'dq', 16)
    Y.wait('dq', k.cnt['dq'])
    Y.inc(Y.dma_start(S.bc[0:c.NTILE, 0:8], E['rsc'].reshape([1024, 8]).ap()[0:c.NTILE, :]), 'dq', 16)
    V.wait('dq', k.cnt['dq'])
    V.inc(V.tensor_copy(S.ysl[0:c.NTILE, 0, 0:8], S.bc[0:c.NTILE, 0:8]), 'dv')
    T.wait('dv', k.cnt['dv'])
    i = T.matmul(PB[0:8, 0:c.NTILE], S.ysl[0:c.NTILE, 0, 0:8],
                 S.iden[0:c.NTILE, 0:c.NTILE], start=True, stop=True)
    T.inc(i, 'pe')
    V.wait('pe', k.cnt['pe'])
    V.inc(V.tensor_copy(S.clrf[:], PB[0:8, 0:c.NTILE]), 'dv')
    P.wait('dv', k.cnt['dv'])
    P.inc(P.memset(S.clri[:], 255), 'po')
    V.wait('po', k.cnt['po'])
    V.inc(V.tensor_copy(S.clri[0:8, :], S.clrf[:]), 'dv')
    V.inc(V.tensor_copy(S.ysl[0:8, 0, 0:c.NTILE], S.clrf[:]), 'dv')
    T.wait('dv', k.cnt['dv'])
    i = T.matmul(PB[:, 1024:1024 + c.NTILE], S.indic[:], S.ysl[0:8, 0, 0:c.NTILE],
                 start=True, stop=True)
    T.inc(i, 'pe')
    V.wait('pe', k.cnt['pe'])
    V.inc(V.tensor_scalar(S.bc[:, 0:c.NTILE], PB[:, 1024:1024 + c.NTILE], 16.0,
                          S.pmod[:, 0:1], ALU.mult, ALU.add), 'dv')
    V.inc(V.tensor_copy(S.idxw2[:], S.bc[:, 0:c.NTILE]), 'dv')
    cs = min(512, c.NCHUNK)
    for chix in range(c.NCHUNK // cs):
        chs = ds(chix, cs)
        T.wait('dv', k.cnt['dv'])
        i = T.matmul(PB[:, 0:cs], S.ones1r32[:, :], S.rowsc[0:1, 1024:1024 + c.NCHUNK][:, chs],
                     start=True, stop=True)
        T.inc(i, 'pe')
        V.wait('pe', k.cnt['pe'])
        V.inc(V.tensor_scalar(S.bc[:, 0:cs], PB[:, 0:cs], 128.0, S.iota[:, 0:1],
                              ALU.mult, ALU.add), 'dv')
        V.inc(V.tensor_copy(S.idxw1[:, chs], S.bc[:, 0:cs]), 'dv')
    barrier(k)


def rsc_i32(k):
    return k.E['slotTab'].reshape([k.c.SLOTS, 1]).ap()


def emit_fff_tiles(k, tree, srctok, dsttok, act):
    c, S, E = k.c, k.S, k.E
    Y, P, V, C, T = k.Y, k.P, k.V, k.C, k.T
    PA, PB = k.PA, k.PB
    for g in range(c.NTILE // 4):
        hb = g % 2
        if g >= 2:
            P.wait('gx%d' % hb, k.wtok_fff[hb])
        for j in range(4):
            i_t = 4 * g + j
            P.inc(P.indirect_dma_start(
                S.qT[:, 4 * hb + j, :], None, E[srctok].ap(),
                IndirectOffsetOnAxis(ap=S.srt[:, i_t:i_t + 1], axis=0),
                bounds_check=k.breg, oob_is_err=False), 'gg%d' % hb, 16)
        P.wait('gg%d' % hb, k.cnt['gg%d' % hb])
        k.wtok_fff[hb] = P.inc(
            P.dma_start(E['xsorted'].ap()[ds(g, 512), :]
                        .rearrange('(j p) d -> p j d', p=128),
                        S.qT[:, 4 * hb:4 * hb + 4, :]),
            'gx%d' % hb, 16)
    barrier(k)
    dstap = E[dsttok].ap()[:, 0:1024]
    tok_pe = [0, 0, 0, 0]
    tok_ac = [0, 0]
    tok_msk = [0, 0]
    tok_dvy = [0, 0]
    y_pe = [0, 0]
    dq_blk = [0] * c.NBLK
    for dt in range(c.DT):
        Y.inc(Y.dma_start(S.qT[:, dt, 0:1024],
                          E['xsorted'].ap()[ds(0, 1024), ds(dt, 128)],
                          transpose=True), 'dq', 16)
    dq_blk[0] = k.cnt['dq']
    for it in range(c.NTILE):
        blk, ti = it // 8, it % 8
        buf = it % 4
        pp = it % 2
        P.wait('pe', tok_pe[buf])
        gsem = 'gt%d' % buf
        for cc in range(8):
            P.inc(P.indirect_dma_start(
                S.ktT[:, buf, 128 * cc:128 * cc + 128], None,
                E['w1gB_' + tree].ap(),
                IndirectOffsetOnAxis(ap=S.idxw1[:, it * 8 + cc:it * 8 + cc + 1], axis=0)), gsem, 16)
        P.inc(P.indirect_dma_start(
            S.vsup[:, buf, 0:1024], None, E['w2gB_' + tree].ap(),
            IndirectOffsetOnAxis(ap=S.idxw2[:, it:it + 1], axis=0)), gsem, 16)
        P.inc(P.indirect_dma_start(
            S.vsup[0:8, 4 + buf, 0:1024], None, E['b2B_' + tree].ap(),
            IndirectOffsetOnAxis(ap=S.clri[0:8, it:it + 1], axis=0)), gsem, 16)
        P.inc(P.indirect_dma_start(
            S.b1sb[:, buf, :], None, E['f%s_b1' % tree].ap(),
            IndirectOffsetOnAxis(ap=S.idxw2[:, it:it + 1], axis=0)), gsem, 16)
        V.wait(gsem, k.cnt[gsem])
        i = V.tensor_copy(
            S.ktT[:, 4 + buf, :].rearrange('p (a b j) -> p a b j', a=8, j=16),
            S.ktT[:, buf, :].rearrange('p (a b j) -> p b a j', a=8, j=16))
        k.w1cp[buf] = V.inc(i, 'dv')
        if ti == 0:
            T.wait('dq', dq_blk[blk])
        T.wait(gsem, k.cnt[gsem])
        T.wait('dv', k.w1cp[buf])
        T.wait('ac', tok_ac[pp])
        for dt in range(c.DT):
            i = T.matmul(PA[:, pp * 512: pp * 512 + 128],
                         S.ktT[:, 4 + buf, ds(dt, 128)], S.qT[:, dt, ds(ti, 128)],
                         start=(dt == 0), stop=(dt == c.DT - 1))
        hpe = T.inc(i, 'pe')
        if ti == 7 and blk + 1 < c.NBLK:
            Y.wait('pe', hpe)
            for dt in range(c.DT):
                Y.inc(Y.dma_start(S.qT[:, dt, 0:1024],
                                  E['xsorted'].ap()[ds(blk + 1, 1024), ds(dt, 128)],
                                  transpose=True), 'dq', 16)
            dq_blk[blk + 1] = k.cnt['dq']
        C.wait('pe', k.cnt['pe'])
        C.wait(gsem, k.cnt[gsem])
        C.wait('dv', tok_msk[pp])
        i = C.activation(S.hA[:, pp, :], PA[:, pp * 512: pp * 512 + 128],
                         act, bias=S.b1sb[:, buf, :])
        tok_ac[pp] = C.inc(i, 'ac')
        V.wait('ac', k.cnt['ac'])
        V.wait('pe', y_pe[pp])
        i = V.tensor_mul(S.hT[:, pp, :], S.hA[:, pp, :], S.mdiag[:])
        tok_msk[pp] = V.inc(i, 'dv')
        T.wait('dv', k.cnt['dv'])
        T.wait('dv', tok_dvy[pp])
        pb0 = pp * 1024
        T.matmul(PB[:, pb0:pb0 + 512], S.hT[:, pp, :], S.vsup[:, buf, 0:512],
                 start=True, stop=False)
        T.matmul(PB[:, pb0 + 512:pb0 + 1024], S.hT[:, pp, :],
                 S.vsup[:, buf, 512:1024], start=True, stop=False)
        T.matmul(PB[:, pb0:pb0 + 512], S.indic[:], S.vsup[0:8, 4 + buf, 0:512],
                 start=False, stop=True)
        i = T.matmul(PB[:, pb0 + 512:pb0 + 1024], S.indic[:],
                     S.vsup[0:8, 4 + buf, 512:1024], start=False, stop=True)
        tok_pe[buf] = y_pe[pp] = T.inc(i, 'pe')
        V.wait('pe', k.cnt['pe'])
        V.wait('gy%d' % pp, k.ysl_tok[pp])
        i = V.tensor_copy(S.ysl[:, pp, :], PB[:, pb0:pb0 + 1024])
        tok_dvy[pp] = V.inc(i, 'dv')
        P.wait('dv', k.cnt['dv'])
        k.ysl_tok[pp] = P.inc(P.indirect_dma_start(
            dstap, IndirectOffsetOnAxis(ap=S.srt[:, it:it + 1], axis=0),
            S.ysl[:, pp, :], None,
            bounds_check=k.breg, oob_is_err=False), 'gy%d' % pp, 16)
        if c.dbg_serial:
            barrier(k)
    barrier(k)



def kernel(**inputs):
    cfg = Cfg()
    k = build(cfg)
    emit_program(k, script)
    k.es.close()
    in_maps = [prep_core_inputs(inputs, core, cfg) for core in range(8)]
    res = run_bass_kernel_spmd(k.nc, in_maps, core_ids=list(range(8)))
    out = np.empty((4, 8192, 1024), np.float32)
    for core in range(8):
        b, h = core // 2, core % 2
        out[b, h * cfg.To:(h + 1) * cfg.To, :] = res.results[core]['outT'].T
    return out


# version 09
# speedup vs baseline: 1.4062x; 1.2706x over previous
import numpy as np
import ml_dtypes
from contextlib import ExitStack

import concourse.bass as bass
import concourse.mybir as mybir
from concourse.bass_utils import run_bass_kernel_spmd
from concourse.bass_core_rust import IndirectOffsetOnAxis

F32 = mybir.dt.float32
F32R = mybir.dt.float32r
BF16 = mybir.dt.bfloat16
I32 = mybir.dt.int32
AF = mybir.ActivationFunctionType
ALU = mybir.AluOpType
AX = mybir.AxisListType
BF = ml_dtypes.bfloat16
SQD = float(np.sqrt(1024.0))


class Cfg:
    def __init__(self, To=4096, Tf=8192, groups=None, parts=(1, 2, 3, 4),
                 dumps=(), dumps_q=(), dbg_serial=False):
        self.dumps = dumps
        self.dumps_q = dumps_q
        self.dbg_serial = dbg_serial
        self.To = To; self.Tf = Tf
        self.D = 1024; self.DT = 8; self.H = 16
        self.KSUP = min(1024, Tf); self.NSUP = Tf // self.KSUP
        self.NKT = self.KSUP // 128
        self.QH = min(1024, To); self.NQH = To // self.QH
        self.QB = 512
        self.NQB = self.QH // self.QB
        self.QBA = 512
        self.NQBA = self.QH // self.QBA
        self.NNODE = 255; self.NLEAF = 256
        self.SLOTS = ((To + 15 * 256 + 1023) // 1024) * 1024
        self.NTILE = self.SLOTS // 128
        self.NCHUNK = self.SLOTS // 16
        self.NBLK = self.SLOTS // 1024
        self.ST = To // 512
        self.NTT = To // 128
        self.groups = groups if groups is not None else [[0, 1], [2, 3], [4, 5], [6, 7]]
        self.parts = parts


def ds(i, n):
    return slice(i * n, (i + 1) * n)



def prep_core_inputs(inputs, core, cfg):
    c = cfg
    b, h = core // 2, core % 2
    x = np.asarray(inputs['L0_attn_a_x'])
    d = {'xT': np.ascontiguousarray(x[b, h * c.To:(h + 1) * c.To, :].T)}
    d['aa_wqkv'] = np.asarray(inputs['L0_attn_a_w_qkv'])
    d['aa_wout'] = np.asarray(inputs['L0_attn_a_w_out'])
    d['aa_gcol'] = np.ascontiguousarray(
        np.asarray(inputs['L0_attn_a_gamma']).reshape(8, 128).T)
    for nm in 'qkvo':
        W = np.asarray(inputs['L0_attn_b_W' + nm])
        d['ab_w' + nm] = np.ascontiguousarray(W.T)
        bb = np.asarray(inputs['L0_attn_b_b' + nm])
        d['ab_bc' + nm] = np.ascontiguousarray(bb.reshape(8, 128).T)
        d['ab_br' + nm] = np.ascontiguousarray(bb.reshape(1, 1024))
    for t in 'qkvf':
        g = lambda s: np.asarray(inputs['L0_ffn_%s_%s' % (t, s)])
        d['f%s_nwT' % t] = np.ascontiguousarray(g('nw').T)
        nb = g('nb')
        nbp = np.zeros((128, 2), np.float32)
        nbp[:, 0] = nb[0:128]
        nbp[0:127, 1] = nb[128:255]
        d['f%s_nb' % t] = nbp
        d['f%s_w1g' % t] = np.ascontiguousarray(
            g('w1').reshape(256, 8, 128, 16).transpose(0, 2, 1, 3).reshape(256, 128, 128))
        d['f%s_b1' % t] = np.ascontiguousarray(g('b1').reshape(4096, 1))
        d['f%s_w2g' % t] = np.ascontiguousarray(g('w2').reshape(4096, 1024))
        d['f%s_b2' % t] = np.asarray(g('b2'))
    d['c_iota'] = np.arange(128, dtype=np.float32).reshape(128, 1)
    d['c_pmod'] = (np.arange(128, dtype=np.float32) % 16).reshape(128, 1)
    d['c_tokid'] = np.ascontiguousarray(
        np.arange(c.To, dtype=np.float32).reshape(c.To // 128, 128).T)
    d['c_oob'] = np.full((c.SLOTS,), 1 << 20, np.int32)
    jj, ss = np.meshgrid(np.arange(128), np.arange(128), indexing='ij')
    d['c_maskdiag'] = ((jj // 16) == (ss // 16)).astype(BF)
    ind = np.zeros((8, 128), np.float32)
    for cc in range(8):
        ind[cc, cc * 16:(cc + 1) * 16] = 1.0
    d['c_indic8'] = ind.astype(BF)
    nkt, nqb = c.Tf // 128, c.To // c.QB
    kd = np.zeros((128, nkt * nqb), np.float32)
    for kt in range(nkt):
        for qb in range(nqb):
            kd[:, kt * nqb + qb] = np.arange(128) + 128 * kt - (qb * c.QB + h * c.To) + 16
    d['c_kd'] = kd
    d['c_qrow'] = np.tile(np.arange(c.QB, dtype=np.float32), (128, 1))
    d['c_srow'] = (16.0 * np.arange(c.NCHUNK, dtype=np.float32)).reshape(1, c.NCHUNK)
    return d



class NS:
    pass


def build(cfg):
    c = cfg
    nc = bass.Bass("TRN2", target_bir_lowering=False)
    k = NS(); k.c = c; k.nc = nc; k.E = {}

    def dram(name, shape, dt, kind="Internal"):
        k.E[name] = nc.dram_tensor(name, list(shape), dt, kind=kind)

    dram('xT', (c.D, c.To), F32, "ExternalInput")
    dram('aa_wqkv', (c.D, 3072), F32, "ExternalInput")
    dram('aa_wout', (c.D, c.D), F32, "ExternalInput")
    dram('aa_gcol', (128, 8), F32, "ExternalInput")
    for nm in 'qkvo':
        dram('ab_w' + nm, (c.D, c.D), F32, "ExternalInput")
        dram('ab_bc' + nm, (128, 8), F32, "ExternalInput")
        dram('ab_br' + nm, (1, c.D), F32, "ExternalInput")
    for t in 'qkvf':
        dram('f%s_nwT' % t, (c.D, c.NNODE), F32, "ExternalInput")
        dram('f%s_nb' % t, (128, 2), F32, "ExternalInput")
        dram('f%s_w1g' % t, (c.NLEAF, 128, 128), F32, "ExternalInput")
        dram('f%s_b1' % t, (4096, 1), F32, "ExternalInput")
        dram('f%s_w2g' % t, (4096, c.D), F32, "ExternalInput")
        dram('f%s_b2' % t, (c.NLEAF, c.D), F32, "ExternalInput")
    dram('c_iota', (128, 1), F32, "ExternalInput")
    dram('c_pmod', (128, 1), F32, "ExternalInput")
    dram('c_tokid', (128, c.To // 128), F32, "ExternalInput")
    dram('c_oob', (c.SLOTS,), I32, "ExternalInput")
    dram('c_maskdiag', (128, 128), BF16, "ExternalInput")
    dram('c_indic8', (8, 128), BF16, "ExternalInput")
    dram('c_kd', (128, (c.Tf // 128) * (c.To // c.QB)), F32, "ExternalInput")
    dram('c_qrow', (128, c.QB), F32, "ExternalInput")
    dram('c_srow', (1, c.NCHUNK), F32, "ExternalInput")
    dram('outT', (c.D, c.To), F32, "ExternalOutput")

    dram('resid', (c.D, c.To), F32)
    dram('xn232', (c.D, c.To), F32)
    dram('yT32', (c.D, c.To), F32)
    dram('xtok', (c.To, c.D), BF16)
    dram('ytokM', (c.To, c.D), BF16)
    dram('xsorted', (c.SLOTS, c.D), BF16)
    dram('qTd', (c.D, c.To), BF16)
    dram('qtok', (c.To, c.D), BF16)
    dram('ktok', (c.To, 1056), BF16)
    dram('vtok', (c.To, 1056), BF16)
    dram('kfull', (c.Tf, 1056), BF16)
    dram('vfull', (c.Tf, 1056), BF16)
    dram('ktokF', (c.To, 1024), BF16)
    dram('vtokF', (c.To, 1024), BF16)
    dram('kfullF', (c.Tf, 1024), BF16)
    dram('vfullF', (c.Tf, 1024), BF16)
    dram('ftok', (c.To, c.D), BF16)
    dram('slotTab', (c.SLOTS,), I32)
    dram('rsc', (8192,), F32)
    for t in 'qkvf':
        dram('w1gB_' + t, (c.NLEAF * 128, 128), BF16)
        dram('w2gB_' + t, (4096, c.D), BF16)
        dram('b2B_' + t, (c.NLEAF, c.D), BF16)
    for nm in c.dumps:
        h = k.E[nm]
        dram('dmp_' + nm, list(h.shape), h.dtype, "ExternalOutput")
    for nm in c.dumps_q:
        if nm.startswith('S.'):
            continue
        h = k.E[nm]
        dram('dmpq_' + nm, list(h.shape), h.dtype, "ExternalOutput")

    es = ExitStack(); k.es = es

    def sbuf(name, shape, dt):
        return es.enter_context(nc.sbuf_tensor(name, list(shape), dt))

    S = NS(); k.S = S
    QH = c.QH
    S.ktT = sbuf('ktT', (128, c.DT, c.KSUP), BF16)
    S.vsup = sbuf('vsup', (128, c.NKT, 1056), BF16)
    S.qT = sbuf('qT', (128, c.DT, QH), BF16)
    S.arena = sbuf('arena', (128, c.DT * QH), F32)
    S.pT = sbuf('pT', (128, 4, 512), BF16)
    S.x32 = sbuf('x32', (128, c.DT, 512), F32)
    S.xw = sbuf('xw', (128, c.DT, 512), BF16)
    S.ttk = sbuf('ttk', (128, 2, c.D), BF16)
    S.rows2 = sbuf('rows2', (128, max(c.To, 4 * c.QH)), F32)
    S.bc = sbuf('bc', (128, 1024), F32)
    S.cols = sbuf('cols', (128, 16), F32)
    S.w0 = sbuf('w0', (128, 2, 2048), BF16)
    S.onesb = sbuf('onesb', (128, 2), BF16)
    S.ones32c = sbuf('ones32c', (128, 1), F32)
    S.ones1rb = sbuf('ones1rb', (1, 128), BF16)
    S.ones1r32 = sbuf('ones1r32', (1, 128), F32)
    S.iota = sbuf('iota', (128, 1), F32)
    S.pmod = sbuf('pmod', (128, 1), F32)
    S.iden = sbuf('iden', (128, 128), BF16)
    S.iden32 = sbuf('iden32', (128, 128), F32)
    S.rowsc = sbuf('rowsc', (128, 2048), F32)
    S.nwT = sbuf('nwT', (128, c.DT, 256), F32R)
    S.xr = sbuf('xr', (128, c.DT, 256), F32R)
    S.nb = sbuf('nb', (128, 2), F32)
    S.tokid = sbuf('tokid', (128, c.To // 128), F32)
    S.tki = sbuf('tki', (128, c.To // 128), I32)
    S.dstc = sbuf('dstc', (128, c.To // 128), I32)
    S.srt = sbuf('srt', (128, c.NTILE), I32)
    S.idxw1 = sbuf('idxw1', (128, c.NCHUNK), I32)
    S.idxw2 = sbuf('idxw2', (128, c.NTILE), I32)
    S.clri = sbuf('clri', (128, c.NTILE), I32)
    S.clrf = sbuf('clrf', (8, c.NTILE), F32)
    S.gth = sbuf('gth', (128, 2, c.D), BF16)
    S.w1sb = sbuf('w1sb', (128, 2, c.DT, 128), BF16)
    S.w2sb = sbuf('w2sb', (128, 2, c.D), BF16)
    S.b2sb = sbuf('b2sb', (128, 2, c.D), BF16)
    S.b1sb = sbuf('b1sb', (128, 4, 1), F32)
    S.hT = sbuf('hT', (128, 2, 128), BF16)
    S.hA = sbuf('hA', (128, 2, 128), BF16)
    S.ysl = sbuf('ysl', (128, 2, c.D), BF16)
    S.mdiag = sbuf('mdiag', (128, 128), BF16)
    S.indic = sbuf('indic', (8, 128), BF16)
    S.kd = sbuf('kd', (128, (c.Tf // 128) * (c.To // c.QB)), F32)
    S.qrow = sbuf('qrow', (128, c.QB), F32)
    S.srow = sbuf('srow', (1, c.NCHUNK), F32)
    S.brow = sbuf('brow', (128, 2 * c.D), BF16)
    S.vev = sbuf('vev', (128, 544), BF16)
    S.constv = sbuf('constv', (128, 1), F32)
    nc.const_aps.register(F32, 1e-5, S.constv[:, 0:1])

    for nm in c.dumps_q:
        if nm.startswith('S.'):
            t = getattr(S, nm[2:])
            dram('dmpq_' + nm[2:], list(t[:].tensor.shape), t[:].dtype,
                 "ExternalOutput")
    k.PA = es.enter_context(nc.psum_tensor('PA', [128, 2048], F32))
    k.PB = es.enter_context(nc.psum_tensor('PB', [128, 2048], F32))

    k.sems = {}
    for s in ['dq', 'dg', 'cc', 'dw0', 'dw1', 'sp0', 'sp1', 'sv', 'sb',
              'gg0', 'gg1', 'gx0', 'gx1', 'gt0', 'gt1', 'gt2', 'gt3',
              'gy0', 'gy1', 'dk', 'pe', 'ac', 'dv', 'po', 'sq']:
        k.sems[s] = es.enter_context(nc.semaphore(s))
    k.DMASEMS = ('dq', 'dg', 'cc', 'dw0', 'dw1', 'sp0', 'sp1', 'sv', 'sb',
                 'gg0', 'gg1', 'gx0', 'gx1', 'gt0', 'gt1', 'gt2', 'gt3',
                 'gy0', 'gy1', 'dk')
    return k



class EngProxy:
    def __init__(self, k, name):
        self._k = k
        self._name = name

    _NOWAIT = {'wait_ge', 'alloc_register', 'reg_mov', 'free_register', 'nop'}
    _OWN = {'ve': 'dv', 'sc': 'ac'}

    def __getattr__(self, attr):
        k = self._k
        name = self._name

        def call(*args, **kw):
            if k.live == name:
                own = EngProxy._OWN.get(name)
                if own and attr not in EngProxy._NOWAIT and k.cnt[own] > 0:
                    key = (name, own)
                    if k.waited.get(key, -1) < k.cnt[own]:
                        k.engs[name].wait_ge(k.sems[own], k.cnt[own])
                        k.waited[key] = k.cnt[own]
                try:
                    return getattr(k.engs[name], attr)(*args, **kw)
                except Exception as e:
                    raise RuntimeError('engine op failed: %s.%s: %r' % (name, attr, e)) from e
            elif EngProxy._OWN.get(name) and attr not in EngProxy._NOWAIT:
                own = EngProxy._OWN[name]
                k.waited[(name, own)] = max(k.waited.get((name, own), -1), k.cnt[own])
            return None
        return call

    def wait(self, sem, val=None):
        k = self._k
        if val is None:
            val = k.cnt[sem]
        val = min(val, k.cnt[sem])
        if val <= 0:
            return
        if k.live == self._name:
            key = (self._name, sem)
            if k.waited.get(key, -1) < val:
                k.engs[self._name].wait_ge(k.sems[sem], val)
                k.waited[key] = val

    def inc(self, instr, sem, step=1):
        k = self._k
        k.cnt[sem] += step
        if instr is not None:
            instr.then_inc(k.sems[sem], step)
        return k.cnt[sem]


def emit_program(k, scriptfn):
    blk = k.es.enter_context(k.nc.Block())
    k.engs = {}
    k.T = EngProxy(k, 'te'); k.V = EngProxy(k, 've'); k.C = EngProxy(k, 'sc')
    k.P = EngProxy(k, 'po'); k.Y = EngProxy(k, 'sy')

    def replay(name, eng):
        k.engs[name] = eng
        k.live = name
        k.cnt = {s: 0 for s in k.sems}
        k.waited = {}
        scriptfn(k)

    @blk.sync
    def _(e):
        replay('sy', e)

    @blk.scalar
    def _(e):
        replay('sc', e)

    @blk.vector
    def _(e):
        replay('ve', e)

    @blk.tensor
    def _(e):
        replay('te', e)

    @blk.gpsimd
    def _(e):
        replay('po', e)


def barrier(k, engines=None):
    snap = dict(k.cnt)
    own = {'te': 'pe', 've': 'dv', 'sc': 'ac', 'po': 'po', 'sy': 'sq'}
    prox = {'te': k.T, 've': k.V, 'sc': k.C, 'po': k.P, 'sy': k.Y}
    arr = {}
    for e in ('te', 've', 'sc', 'po', 'sy'):
        p = prox[e]
        for s, v in snap.items():
            if v > 0:
                p.wait(s, v)
        arr[e] = p.inc(p.nop(), own[e])
    for e in ('te', 've', 'sc', 'po', 'sy'):
        p = prox[e]
        for e2 in ('te', 've', 'sc', 'po', 'sy'):
            if e2 != e:
                p.wait(own[e2], arr[e2])



def script(k):
    c = k.c
    k.ac_exp = [0, 0]
    k.dv_msk = [0, 0]
    k.pe_qw = [0, 0]
    k.pe_kw = [0, 0]
    k.wtok_fff = [0, 0]
    k.ysl_tok = [0, 0]
    k.w1cp = [0, 0, 0, 0]
    k.qtd_st = [0, 0]
    k.vev_st = 0
    k.bc_st = 0
    k.qp_ev = [0, 0]
    k.qp_evv = [0, 0]
    k.tkm_tok = [0, 0]
    k.tkm_st = 0
    k.rt_pe = [0, 0]
    if k.live == 'po':
        k.breg = k.engs['po'].alloc_register('bnd')
        k.engs['po'].reg_mov(k.breg, c.To - 1)
    else:
        k.breg = None
    emit_init(k)
    src = 'xT'
    for i, p in enumerate(c.parts):
        dst = 'outT' if i == len(c.parts) - 1 else 'resid'
        if p == 1:
            emit_attn_sublayer(k, src, dst, 'aa')
        elif p == 3:
            emit_attn_sublayer(k, src, dst, 'ab')
        else:
            emit_ffn_sublayer(k, src, dst)
        src = 'resid'
    for nm in c.dumps:
        k.Y.inc(k.Y.dma_start(k.E['dmp_' + nm].ap(), k.E[nm].ap()), 'dq', 16)
        k.Y.wait('dq', k.cnt['dq'])
    for s in ('dq', 'dg', 'cc', 'pe', 'ac', 'dv', 'po'):
        if k.cnt[s] > 0:
            k.Y.wait(s, k.cnt[s])


def emit_init(k):
    c, S, E = k.c, k.S, k.E
    Y, P, V = k.Y, k.P, k.V
    for nm, dst in [('c_iota', S.iota), ('c_pmod', S.pmod), ('c_tokid', S.tokid),
                    ('c_maskdiag', S.mdiag), ('c_indic8', S.indic), ('c_kd', S.kd),
                    ('c_qrow', S.qrow), ('c_srow', S.srow)]:
        Y.inc(Y.dma_start(dst[:], E[nm].ap()), 'dq', 16)
    P.inc(P.memset(S.iden[:], 0.0), 'po')
    P.wait('po', k.cnt['po'])
    P.inc(P.affine_select(out=S.iden[:], in_=S.iden[:], compare_op=ALU.not_equal,
                          fill=1.0, base=0, pattern=[[-1, 128]],
                          channel_multiplier=1), 'po')
    P.inc(P.memset(S.iden32[:], 0.0), 'po')
    P.wait('po', k.cnt['po'])
    P.inc(P.affine_select(out=S.iden32[:], in_=S.iden32[:], compare_op=ALU.not_equal,
                          fill=1.0, base=0, pattern=[[-1, 128]],
                          channel_multiplier=1), 'po')
    P.inc(P.memset(S.constv[:], 1e-5), 'po')
    P.inc(P.memset(S.onesb[:], 1.0), 'po')
    P.inc(P.memset(S.ones32c[:], 1.0), 'po')
    P.inc(P.memset(S.ones1rb[:], 1.0), 'po')
    P.inc(P.memset(S.ones1r32[:], 1.0), 'po')
    P.inc(P.memset(S.vev[:], 0.0), 'po')
    P.wait('po', k.cnt['po'])
    for g in range(8):
        P.inc(P.memset(S.vev[:, g * 66 + 64: g * 66 + 65], 1.0), 'po')
    V.wait('dq', k.cnt['dq'])
    V.inc(V.tensor_copy(S.tki[:], S.tokid[:]), 'dv')
    P.inc(P.memset(S.gth[:, 0, 0:((c.To // 128) * 32)], 0.0), 'po')
    P.wait('po', k.cnt['po'])
    for nm in ('ktok', 'vtok'):
        P.inc(P.dma_start(E[nm].ap()[:, 1024:1056]
                          .rearrange('(a p) v -> p a v', p=128),
                          S.gth[:, 0, 0:((c.To // 128) * 32)]
                          .rearrange('p (a v) -> p a v', v=32)), 'dg', 16)
    vflat = S.vsup[:].rearrange('p a b -> p (a b)')
    piece = 4096
    if 2 in c.parts:
        for t in 'qkvf':
            for (src, dstn, n) in [('f%s_w1g' % t, 'w1gB_' + t, c.NLEAF * 128 * 128),
                                   ('f%s_w2g' % t, 'w2gB_' + t, 4096 * c.D),
                                   ('f%s_b2' % t, 'b2B_' + t, c.NLEAF * c.D)]:
                npc = max(n // (128 * piece), 1)
                per = n // 128 // npc
                sap = E[src].reshape([128, n // 128]).ap()
                dap = E[dstn].reshape([128, n // 128]).ap()
                for i in range(npc):
                    buf = vflat[:, (i % 2) * piece:(i % 2) * piece + per]
                    P.wait('dk', k.cnt['dk'])
                    P.inc(P.dma_start(buf, sap[:, ds(i, per)]), 'dk', 16)
                    P.wait('dk', k.cnt['dk'])
                    P.inc(P.dma_start(dap[:, ds(i, per)], buf), 'dk', 16)
    barrier(k)



def emit_prologue(k, src, kind):
    c, S, E = k.c, k.S, k.E
    Y, P, V, C, T = k.Y, k.P, k.V, k.C, k.T
    PA, PB = k.PA, k.PB
    xs = E[src].ap()

    if kind == 'aa':
        Y.inc(Y.dma_start(S.cols[:, 0:8], E['aa_gcol'].ap()), 'dq', 16)
        barrier(k, ('ve',))
        V.inc(V.tensor_scalar_add(S.cols[:, 0:8], S.cols[:, 0:8], 1.0), 'dv')
    if kind == 'ab':
        Y.inc(Y.dma_start(S.cols[:, 0:8], E['ab_bcq'].ap()), 'dq', 16)
        P.inc(P.dma_start(S.brow[0:1, 0:c.D], E['ab_brk'].ap()), 'dg', 16)
        P.inc(P.dma_start(S.brow[0:1, c.D:2 * c.D], E['ab_brv'].ap()), 'dg', 16)

    for s in range(c.ST):
        tsl = ds(s, 512)
        barrier(k)
        Y.inc(Y.dma_start(S.x32[:], xs[:, tsl].rearrange('(a p) t -> p a t', p=128)),
              'dq', 16)
        ldq = k.cnt['dq']
        C.wait('dq', ldq)
        for dt in range(c.DT):
            C.inc(C.activation(S.xw[:, dt, :], S.x32[:, dt, :], AF.Square), 'ac')
        T.wait('ac', k.cnt['ac'])
        T.wait('dq', ldq)
        for dt in range(c.DT):
            T.matmul(PA[0:1, 1024:1536], S.ones32c[:, 0:1], S.x32[:, dt, :],
                     start=(dt == 0), stop=(dt == c.DT - 1))
        for dt in range(c.DT):
            i = T.matmul(PA[0:1, 1536:2048], S.onesb[:, 0:1], S.xw[:, dt, :],
                         start=(dt == 0), stop=(dt == c.DT - 1))
        T.inc(i, 'pe')
        R = S.rowsc
        R2 = S.rows2
        m_ = R[0:1, 0:512]; m2_ = R[0:1, 512:1024]
        fs_ = R[0:1, 1024:1536]; f_ = R[0:1, 1536:2048]
        v_ = R2[0:1, 0:512]; sq_ = R2[0:1, 512:1024]
        r_ = R2[0:1, 1024:1536]; t_ = R2[0:1, 1536:2048]
        V.wait('pe', k.cnt['pe'])
        V.inc(V.tensor_scalar_mul(m_, PA[0:1, 1024:1536], 1.0 / 1024.0), 'dv')
        V.inc(V.tensor_mul(m2_, m_, m_), 'dv')
        V.inc(V.scalar_tensor_tensor(v_, PA[0:1, 1536:2048], 1.0 / 1024.0, m2_,
                                     ALU.mult, ALU.subtract), 'dv')
        C.wait('dv', k.cnt['dv'])
        C.inc(C.activation(sq_, v_, AF.Sqrt, bias=1e-5), 'ac')
        V.wait('ac', k.cnt['ac'])
        V.inc(V.reciprocal(r_, sq_), 'dv')
        if kind == 'aa':
            C.wait('dv', k.cnt['dv'])
            C.inc(C.activation(t_, v_, AF.Sqrt, scale=1024.0), 'ac')
            V.wait('ac', k.cnt['ac'])
            V.inc(V.tensor_mul(t_, t_, r_), 'dv')
            V.inc(V.tensor_scalar_max(t_, t_, 1e-12), 'dv')
            V.inc(V.reciprocal(t_, t_), 'dv')
            V.inc(V.scalar_tensor_tensor(t_, t_, SQD, r_, ALU.mult, ALU.mult), 'dv')
            V.inc(V.tensor_copy(f_, t_), 'dv')
        elif kind == 'ffn':
            C.wait('dv', k.cnt['dv'])
            C.inc(C.activation(t_, v_, AF.Sqrt, scale=1024.0), 'ac')
            V.wait('ac', k.cnt['ac'])
            V.inc(V.scalar_tensor_tensor(t_, t_, 1.0 / float(np.sqrt(1023.0)), r_,
                                         ALU.mult, ALU.mult), 'dv')
            C.wait('dv', k.cnt['dv'])
            C.inc(C.activation(t_, t_, AF.Sqrt, bias=1e-5), 'ac')
            V.wait('ac', k.cnt['ac'])
            V.inc(V.reciprocal(t_, t_), 'dv')
            V.inc(V.tensor_mul(t_, t_, r_), 'dv')
            V.inc(V.tensor_copy(f_, t_), 'dv')
        else:
            V.inc(V.tensor_copy(f_, r_), 'dv')
        T.wait('dv', k.cnt['dv'])
        T.matmul(PB[:, 0:512], S.ones1r32[:, :], S.rowsc[0:1, 0:512], start=True, stop=True)
        i = T.matmul(PB[:, 512:1024], S.ones1r32[:, :], S.rowsc[0:1, 1536:2048], start=True, stop=True)
        T.inc(i, 'pe')
        V.wait('pe', k.cnt['pe'])
        for dt in range(c.DT):
            V.inc(V.scalar_tensor_tensor(S.x32[:, dt, :], S.x32[:, dt, :], 1.0,
                                         PB[:, 0:512], ALU.mult, ALU.subtract), 'dv')
            if kind == 'aa':
                V.inc(V.scalar_tensor_tensor(S.x32[:, dt, :], S.x32[:, dt, :],
                                             S.cols[:, dt:dt + 1], PB[:, 512:1024],
                                             ALU.mult, ALU.mult), 'dv')
            else:
                V.inc(V.scalar_tensor_tensor(S.x32[:, dt, :], S.x32[:, dt, :], 1.0,
                                             PB[:, 512:1024], ALU.mult, ALU.mult), 'dv')
            V.inc(V.tensor_copy(S.xw[:, dt, :], S.x32[:, dt, :]), 'dv')
        if kind == 'ffn':
            Y.wait('dv', k.cnt['dv'])
            Y.inc(Y.dma_start(E['xn232'].ap()[:, tsl].rearrange('(a p) t -> p a t', p=128),
                              S.x32[:]), 'dq', 16)
            emit_tokmajor(k, 'xtok', s)
        else:
            emit_stripe_projections(k, s, kind)
    barrier(k)


def emit_tokmajor(k, dsttok, s):
    c, S, E = k.c, k.S, k.E
    Y, V, T = k.Y, k.V, k.T
    T.wait('dv', k.cnt['dv'])
    for tt in range(4):
        for dt in range(c.DT):
            bank = dt % 2
            T.wait('dv', k.tkm_tok[bank])
            i = T.matmul(PAq(k)[:, bank * 512: bank * 512 + 128],
                         S.xw[:, dt, ds(tt, 128)], S.iden[:], start=True, stop=True)
            T.inc(i, 'pe')
            V.wait('pe', k.cnt['pe'])
            V.wait('dq', k.tkm_st)
            i = V.tensor_copy(S.ttk[:, tt % 2, ds(dt, 128)],
                              PAq(k)[:, bank * 512: bank * 512 + 128])
            k.tkm_tok[bank] = V.inc(i, 'dv')
        if tt % 2 == 1:
            Y.wait('dv', k.cnt['dv'])
            k.tkm_st = Y.inc(
                Y.dma_start(E[dsttok].ap()[s * 512 + (tt - 1) * 128:
                                           s * 512 + (tt + 1) * 128, :]
                            .rearrange('(a p) d -> p a d', p=128), S.ttk[:]),
                'dq', 16)


def PAq(k):
    return k.PA


def emit_stripe_projections(k, s, kind):
    c, S, E = k.c, k.S, k.E
    Y, P, V, C, T = k.Y, k.P, k.V, k.C, k.T
    PA, PB = k.PA, k.PB
    tsl = ds(s, 512)
    w0f = S.w0[:].rearrange('p a b -> p (a b)')

    if kind == 'aa':
        qsrc = 'aa_wqkv'
        kvspec = [('aa_wqkv', 1024, 'ktok', False), ('aa_wqkv', 2048, 'vtok', True)]
    else:
        qsrc = 'ab_wq'
        kvspec = [('ab_wk', 0, 'ktokF', False), ('ab_wv', 0, 'vtokF', False)]

    k.pe_qw[0] = max(k.pe_qw[0], k.pe_kw[0])
    k.pe_qw[1] = max(k.pe_qw[1], k.pe_kw[1])
    for g in range(8):
        buf = g % 2
        if g >= 2:
            P.wait('pe', k.pe_qw[buf])
        else:
            P.wait('pe', k.pe_qw[buf])
        P.inc(P.dma_start(S.w0[:, buf, 0:1024].rearrange('p (a q) -> p a q', q=128),
                          E[qsrc].ap()[:, ds(g, 128)]
                          .rearrange('(a p) q -> p a q', p=128)), 'dw%d' % buf, 16)
        T.wait('dw%d' % buf, k.cnt['dw%d' % buf])
        T.wait('dv', k.cnt['dv'])
        T.wait('ac', k.qp_ev[g % 2])
        T.wait('dv', k.qp_evv[g % 2])
        for dt in range(c.DT):
            i = T.matmul(PB[:, 1024 + (g % 2) * 512: 1024 + (g % 2) * 512 + 512],
                         S.w0[:, buf, ds(dt, 128)], S.xw[:, dt, :],
                         start=(dt == 0), stop=(dt == c.DT - 1))
        k.pe_qw[buf] = T.inc(i, 'pe')
        if kind == 'ab':
            V.wait('pe', k.cnt['pe'])
            V.wait('sp%d' % (g % 2), k.qtd_st[g % 2])
            i = V.tensor_scalar(S.pT[:, 2 * (g % 2), :],
                                PB[:, 1024 + (g % 2) * 512:1024 + (g % 2) * 512 + 512],
                                S.cols[:, g:g + 1], None, ALU.add)
            k.qp_evv[g % 2] = V.inc(i, 'dv')
            Y.wait('dv', k.cnt['dv'])
        else:
            C.wait('pe', k.cnt['pe'])
            C.wait('sp%d' % (g % 2), k.qtd_st[g % 2])
            i = C.activation(S.pT[:, 2 * (g % 2), :],
                             PB[:, 1024 + (g % 2) * 512:1024 + (g % 2) * 512 + 512],
                             AF.Copy)
            k.qp_ev[g % 2] = C.inc(i, 'ac')
            Y.wait('ac', k.cnt['ac'])
        Y.wait('dv', k.cnt['dv'])
        Y.wait('ac', k.cnt['ac'])
        k.qtd_st[g % 2] = Y.inc(
            Y.dma_start(E['qTd'].ap()[ds(g, 128), tsl], S.pT[:, 2 * (g % 2), :]),
            'sp%d' % (g % 2), 16)

    k.pe_kw[0] = max(k.pe_kw[0], k.pe_qw[0])
    k.pe_kw[1] = max(k.pe_kw[1], k.pe_qw[1])
    for (wsrc, coff, dstname, aug) in kvspec:
        for kvg in range(4):
            buf = kvg % 2
            P.wait('pe', k.pe_kw[buf])
            P.inc(P.dma_start(S.w0[:, buf, :].rearrange('p (a q) -> p a q', q=256),
                              E[wsrc].ap()[:, coff + kvg * 256: coff + kvg * 256 + 256]
                              .rearrange('(a p) q -> p a q', p=128)), 'dw%d' % buf, 16)
            wv = S.w0[:, buf, :].rearrange('p (a q) -> p a q', q=256)
            for tt in range(4):
                T.wait('dw%d' % buf, k.cnt['dw%d' % buf])
                T.wait('dv', k.cnt['dv'])
                T.wait('dv', k.qp_evv[tt % 2])
                for dt in range(c.DT):
                    i = T.matmul(PB[:, (tt % 2) * 512:(tt % 2) * 512 + 256],
                                 S.xw[:, dt, ds(tt, 128)], wv[:, dt, :],
                                 start=(dt == 0),
                                 stop=(dt == c.DT - 1 and kind != 'ab'))
                if kind == 'ab':
                    brow = S.brow[0:1, 0:c.D] if dstname == 'ktokF' else S.brow[0:1, c.D:2 * c.D]
                    T.wait('dg', k.cnt['dg'])
                    i = T.matmul(PB[:, (tt % 2) * 512:(tt % 2) * 512 + 256],
                                 S.ones1rb[:, :],
                                 brow[:, kvg * 256:kvg * 256 + 256],
                                 start=False, stop=True)
                T.inc(i, 'pe')
                V.wait('pe', k.cnt['pe'])
                if aug:
                    V.wait('sv', k.vev_st)
                    V.inc(V.tensor_copy(
                        S.vev[:, 0:264].rearrange('p (g e) -> p g e', e=66)[:, :, 0:64],
                        PB[:, (tt % 2) * 512:(tt % 2) * 512 + 256]
                        .rearrange('p (g e) -> p g e', e=64)), 'dv')
                    k.qp_evv[tt % 2] = k.cnt['dv']
                    Y.wait('dv', k.cnt['dv'])
                    k.vev_st = Y.inc(Y.dma_start(
                        E[dstname].ap()[s * 512 + tt * 128: s * 512 + tt * 128 + 128,
                                        kvg * 264: kvg * 264 + 264],
                        S.vev[:, 0:264]), 'sv', 16)
                else:
                    V.wait('sp%d' % (tt % 2), k.qtd_st[tt % 2])
                    V.inc(V.tensor_copy(S.pT[:, 2 * (tt % 2), 0:256],
                                        PB[:, (tt % 2) * 512:(tt % 2) * 512 + 256]), 'dv')
                    k.qp_evv[tt % 2] = k.cnt['dv']
                    Y.wait('dv', k.cnt['dv'])
                    k.qtd_st[tt % 2] = Y.inc(Y.dma_start(
                        E[dstname].ap()[s * 512 + tt * 128: s * 512 + tt * 128 + 128,
                                        kvg * 256: kvg * 256 + 256],
                        S.pT[:, 2 * (tt % 2), 0:256]), 'sp%d' % (tt % 2), 16)
            k.pe_kw[buf] = k.cnt['pe']


def emit_attn_sublayer(k, src, dst, kind):
    c = k.c
    emit_prologue(k, src, kind)
    emit_exchange(k, kind == 'aa')
    for qh in range(c.NQH):
        emit_attention_quarter(k, qh, kind)
        if kind == 'aa':
            emit_epilogue_proj(k, qh, 'aa_wout', None, src, dst, heads=True)
        else:
            emit_epilogue_proj(k, qh, 'ab_wo', 'ab_bco', src, dst, heads=False)
    barrier(k)


def emit_exchange(k, aug):
    c, S, E = k.c, k.S, k.E
    P = k.P
    kk, vv = ('ktok', 'vtok') if aug else ('ktokF', 'vtokF')
    kf, vf = ('kfull', 'vfull') if aug else ('kfullF', 'vfullF')
    barrier(k)
    i = P.collective_compute("AllGather", ALU.bypass, replica_groups=c.groups,
                             ins=[E[kk].ap().opt()], outs=[E[kf].ap().opt()])
    P.inc(i, 'cc')
    i = P.collective_compute("AllGather", ALU.bypass, replica_groups=c.groups,
                             ins=[E[vv].ap().opt()], outs=[E[vf].ap().opt()])
    P.inc(i, 'cc')


def emit_attention_quarter(k, qh, kind):
    c, S, E = k.c, k.S, k.E
    Y, P, V, C, T = k.Y, k.P, k.V, k.C, k.T
    PA, PB = k.PA, k.PB
    QH = c.QH
    qsl = ds(qh, QH)
    scale = 0.125 if kind == 'aa' else 1.0 / 32.0
    arena = S.arena[:].rearrange('p (a q) -> p a q', q=QH)

    barrier(k)
    if kind == 'ffn':
        for dt in range(c.DT):
            Y.inc(Y.dma_start(S.qT[:, dt, :], E['qtok'].ap()[qsl, ds(dt, 128)],
                              transpose=True), 'dq', 16)
    else:
        Y.inc(Y.dma_start(S.qT[:], E['qTd'].ap()[:, qsl]
                          .rearrange('(a p) t -> p a t', p=128)), 'dq', 16)
    P.inc(P.memset(S.arena[:], 0.0), 'po')
    P.inc(P.memset(S.rows2[:], 0.0), 'po')
    P.inc(P.memset(S.rowsc[32:33, :], 0.0), 'po')
    P.inc(P.memset(S.rowsc[64:65, :], 0.0), 'po')
    barrier(k)

    kf, vf = ('kfull', 'vfull') if kind == 'aa' else ('kfullF', 'vfullF')
    for sup in range(c.NSUP):
        ksl = slice(sup * c.KSUP, (sup + 1) * c.KSUP)
        barrier(k)
        for dt in range(c.DT):
            Y.inc(Y.dma_start(S.ktT[:, dt, :], E[kf].ap()[ksl, ds(dt, 128)],
                              transpose=True), 'dq', 16)
        if kind == 'aa':
            Y.inc(Y.dma_start(S.vsup[:], E[vf].ap()[ksl, :]
                              .rearrange('(a p) v -> p a v', p=128)), 'dq', 16)
        else:
            Y.inc(Y.dma_start(S.vsup[:, :, 0:1024], E[vf].ap()[ksl, :]
                              .rearrange('(a p) v -> p a v', p=128)), 'dq', 16)
        barrier(k)
        if kind == 'aa':
            emit_super_aa(k, scale)
        else:
            emit_super_single(k, scale, kind, qh, sup)
        barrier(k)


def aa_sum_slot(k, hh, off, width):
    S, c = k.S, k.c
    QH = c.QH
    if hh < 12:
        return S.rows2[32 * (hh // 4):32 * (hh // 4) + 1,
                       (hh % 4) * QH + off:(hh % 4) * QH + off + width]
    h2 = hh - 12
    return S.rowsc[32 * (h2 // 2 + 1):32 * (h2 // 2 + 1) + 1,
                   (h2 % 2) * 1024 + off:(h2 % 2) * 1024 + off + width]


def emit_super_aa(k, scale):
    c, S = k.c, k.S
    V, C, T = k.V, k.C, k.T
    PA, PB = k.PA, k.PB
    QH = c.QH
    arena = S.arena[:].rearrange('p (a q) -> p a q', q=QH)
    pTf = S.pT[:].rearrange('p a b -> p (a b)')
    for qb in range(c.NQBA):
        qbs = ds(qb, c.QBA)
        for pr in range(8):
            hA, hB = 2 * pr, 2 * pr + 1
            def pv(kt):
                par = kt % 2
                T.wait('ac', k.ac_exp[par])
                T.matmul(PB[0:65, 0:512], S.vsup[:, kt, 66 * hA:66 * hA + 65],
                         S.pT[:, 2 * par, :], start=(kt == 0), stop=(kt == c.NKT - 1))
                return T.matmul(PB[0:65, 512:1024],
                                S.vsup[:, kt, 66 * hB:66 * hB + 65],
                                S.pT[:, 2 * par + 1, :],
                                start=(kt == 0), stop=(kt == c.NKT - 1))
            for kt in range(c.NKT):
                par = kt % 2
                T.wait('ac', k.ac_exp[par])
                T.matmul(PA[:, par * 1024: par * 1024 + 512],
                         S.ktT[0:64, pr, ds(kt, 128)],
                         S.qT[0:64, pr, qbs], start=True, stop=True)
                iB = T.matmul(PA[:, par * 1024 + 512: par * 1024 + 1024],
                              S.ktT[64:128, pr, ds(kt, 128)],
                              S.qT[64:128, pr, qbs], start=True, stop=True)
                T.inc(iB, 'pe')
                pe_s = k.cnt['pe']
                C.wait('pe', pe_s)
                i = C.activation(pTf[:, par * 1024:(par + 1) * 1024],
                                 PA[:, par * 1024:(par + 1) * 1024], AF.Exp,
                                 scale=scale)
                k.ac_exp[par] = C.inc(i, 'ac')
                if kt >= 1:
                    pv(kt - 1)
            i = pv(c.NKT - 1)
            T.inc(i, 'pe')
            V.wait('pe', k.cnt['pe'])
            for hh, reg in ((hA, 0), (hB, 1)):
                dt = hh // 2
                rb = (hh % 2) * 64
                V.inc(V.tensor_add(arena[rb:rb + 64, dt, qbs],
                                   PB[0:64, reg * 512:reg * 512 + 512],
                                   arena[rb:rb + 64, dt, qbs]), 'dv')
                slot = aa_sum_slot(k, hh, qb * c.QBA, c.QBA)
                V.inc(V.tensor_add(slot, PB[64:65, reg * 512:reg * 512 + 512],
                                   slot), 'dv')
            T.wait('dv', k.cnt['dv'])


def emit_super_single(k, scale, kind, qh, sup):
    c, S = k.c, k.S
    V, C, T = k.V, k.C, k.T
    PA, PB = k.PA, k.PB
    QH = c.QH
    arena = S.arena[:].rearrange('p (a q) -> p a q', q=QH)
    pTf = S.xw[:].rearrange('p a b -> p (a b)')
    nqb = c.To // c.QB
    for qb in range(c.NQB):
        qbs = ds(qb, c.QB)
        exp_done = [0] * c.NKT
        msk_done = [0] * c.NKT

        def sums(kt):
            if kind == 'ab':
                T.wait('dv', msk_done[kt])
            else:
                T.wait('ac', exp_done[kt])
            return T.matmul(PA[0:1, 1024:1536], S.onesb[:, 0:1],
                            pTf[:, kt * 512:(kt + 1) * 512],
                            start=(kt == 0), stop=(kt == c.NKT - 1))
        for kt in range(c.NKT):
            par = kt % 2
            T.wait('ac', k.ac_exp[par])
            T.wait('dv', k.dv_msk[par])
            for dt in range(c.DT):
                i = T.matmul(PA[:, par * 512: par * 512 + 512],
                             S.ktT[:, dt, ds(kt, 128)], S.qT[:, dt, qbs],
                             start=(dt == 0), stop=(dt == c.DT - 1))
            T.inc(i, 'pe')
            C.wait('pe', k.cnt['pe'])
            i = C.activation(pTf[:, kt * 512:(kt + 1) * 512],
                             PA[:, par * 512: par * 512 + 512], AF.Exp, scale=scale)
            k.ac_exp[par] = exp_done[kt] = C.inc(i, 'ac')
            if kind == 'ab':
                V.wait('ac', k.ac_exp[par])
                kdcol = S.kd[:, (sup * c.NKT + kt) * nqb + qh * c.NQB + qb:
                             (sup * c.NKT + kt) * nqb + qh * c.NQB + qb + 1]
                i = V.scalar_tensor_tensor(pTf[:, kt * 512:(kt + 1) * 512],
                                           S.qrow[:, 0:c.QB], kdcol,
                                           pTf[:, kt * 512:(kt + 1) * 512],
                                           ALU.is_le, ALU.mult)
                k.dv_msk[par] = msk_done[kt] = V.inc(i, 'dv')
            if kt >= 1:
                sums(kt - 1)
        i = sums(c.NKT - 1)
        T.inc(i, 'pe')
        V.wait('pe', k.cnt['pe'])
        V.inc(V.tensor_add(S.rows2[0:1, qb * c.QB:(qb + 1) * c.QB],
                           PA[0:1, 1024:1536],
                           S.rows2[0:1, qb * c.QB:(qb + 1) * c.QB]), 'dv')
        T.wait('dv', k.cnt['dv'])
        T.wait('ac', k.cnt['ac'])
        for kt in range(c.NKT):
            for dt in range(c.DT):
                ps = PA if dt < 4 else PB
                col = (dt % 4) * 512
                i = T.matmul(ps[:, col:col + 512], S.vsup[:, kt, ds(dt, 128)],
                             pTf[:, kt * 512:(kt + 1) * 512],
                             start=(kt == 0), stop=(kt == c.NKT - 1))
        T.inc(i, 'pe')
        V.wait('pe', k.cnt['pe'])
        i = V.tensor_add(arena[:, 0:4, qbs],
                         PA[:].rearrange('p (a q) -> p a q', q=512),
                         arena[:, 0:4, qbs])
        V.inc(i, 'dv')
        i = V.tensor_add(arena[:, 4:8, qbs],
                         PB[:].rearrange('p (a q) -> p a q', q=512),
                         arena[:, 4:8, qbs])
        V.inc(i, 'dv')
        k.ac_exp = [0, 0]
        k.dv_msk = [0, 0]
        T.wait('dv', k.cnt['dv'])


def emit_epilogue_proj(k, qh, wname, bname, src, dst, heads):
    c, S, E = k.c, k.S, k.E
    Y, P, V, C, T = k.Y, k.P, k.V, k.C, k.T
    PA, PB = k.PA, k.PB
    QH = c.QH
    qsl = ds(qh, QH)
    arena = S.arena[:].rearrange('p (a q) -> p a q', q=QH)
    barrier(k)
    if heads:
        for hh in range(16):
            V.inc(V.reciprocal(S.rowsc[0:1, 0:QH], aa_sum_slot(k, hh, 0, QH)), 'dv')
            T.wait('dv', k.cnt['dv'])
            for ch in range(QH // 512):
                i = T.matmul(PB[0:64, (ch % 2) * 512:(ch % 2) * 512 + 512],
                             S.ones1r32[0:1, 0:64], S.rowsc[0:1, ds(ch, 512)],
                             start=True, stop=True)
                T.inc(i, 'pe')
                V.wait('pe', k.cnt['pe'])
                rb = (hh % 2) * 64
                V.inc(V.tensor_mul(S.qT[rb:rb + 64, hh // 2, ds(ch, 512)],
                                   arena[rb:rb + 64, hh // 2, ds(ch, 512)],
                                   PB[0:64, (ch % 2) * 512:(ch % 2) * 512 + 512]), 'dv')
            T.wait('dv', k.cnt['dv'])
    else:
        V.inc(V.reciprocal(S.rowsc[0:1, 0:QH], S.rows2[0:1, 0:QH]), 'dv')
        T.wait('dv', k.cnt['dv'])
        for ch in range(QH // 512):
            i = T.matmul(PB[:, (ch % 2) * 512:(ch % 2) * 512 + 512],
                         S.ones1r32[:, :], S.rowsc[0:1, ds(ch, 512)],
                         start=True, stop=True)
            T.inc(i, 'pe')
            V.wait('pe', k.cnt['pe'])
            for dt in range(c.DT):
                V.inc(V.tensor_mul(S.qT[:, dt, ds(ch, 512)],
                                   arena[:, dt, ds(ch, 512)],
                                   PB[:, (ch % 2) * 512:(ch % 2) * 512 + 512]), 'dv')
            T.wait('dv', k.cnt['dv'])
    if bname:
        Y.inc(Y.dma_start(S.cols[:, 8:16], E[bname].ap()), 'dq', 16)
    barrier(k)
    xs = E[src].ap()
    dd = E[dst].ap()
    for g in range(8):
        buf = g % 2
        if g >= 2:
            P.wait('pe', k.pe_qw[buf])
        P.inc(P.dma_start(S.w0[:, buf, 0:1024].rearrange('p (a q) -> p a q', q=128),
                          E[wname].ap()[:, ds(g, 128)]
                          .rearrange('(a p) q -> p a q', p=128)), 'dw%d' % buf, 16)
        Y.wait('dv', k.cnt['dv'])
        Y.wait('sb', k.bc_st)
        Y.inc(Y.dma_start(S.bc[:], xs[ds(g, 128), qsl]), 'sb', 16)
        T.wait('sb', k.cnt['sb'])
        T.wait('dw%d' % buf, k.cnt['dw%d' % buf])
        for ch in range(QH // 512):
            for dt in range(c.DT):
                i = T.matmul(PA[:, (ch % 2) * 512:(ch % 2) * 512 + 512],
                             S.w0[:, buf, ds(dt, 128)], S.qT[:, dt, ds(ch, 512)],
                             start=(dt == 0), stop=(dt == c.DT - 1))
        k.pe_qw[buf] = T.inc(i, 'pe')
        V.wait('pe', k.cnt['pe'])
        V.wait('sb', k.cnt['sb'])
        bcol = S.cols[:, 8 + g:9 + g] if bname else 0.0
        i = V.scalar_tensor_tensor(S.bc[:], PA[:, 0:1024], bcol, S.bc[:],
                                   ALU.add, ALU.add)
        V.inc(i, 'dv')
        Y.wait('dv', k.cnt['dv'])
        k.bc_st = Y.inc(Y.dma_start(dd[ds(g, 128), qsl], S.bc[:]), 'sb', 16)



def emit_ffn_sublayer(k, src, dst):
    c = k.c
    emit_prologue(k, src, 'ffn')
    for t, dsttok in (('q', 'qtok'), ('k', 'ktokF'), ('v', 'vtokF')):
        emit_fff(k, t, 'xtok', 'xn232', dsttok, AF.Relu)
        if t == 'q' and c.dumps_q:
            barrier(k)
            for nm in c.dumps_q:
                if nm.startswith('S.'):
                    sb = getattr(k.S, nm[2:])
                    k.Y.inc(k.Y.dma_start(k.E['dmpq_' + nm[2:]].ap(), sb[:]),
                            'dq', 16)
                else:
                    k.Y.inc(k.Y.dma_start(k.E['dmpq_' + nm].ap(), k.E[nm].ap()),
                            'dq', 16)
                k.Y.wait('dq', k.cnt['dq'])
            barrier(k)
    emit_exchange(k, False)
    for qh in range(c.NQH):
        emit_attention_quarter(k, qh, 'ffn')
        emit_epilogue_y(k, qh)
    emit_fff(k, 'f', 'ytokM', 'yT32', 'ftok', AF.Gelu)
    emit_final_add(k, src, dst)
    barrier(k)


def emit_epilogue_y(k, qh):
    c, S, E = k.c, k.S, k.E
    Y, V, C, T = k.Y, k.V, k.C, k.T
    PA, PB = k.PA, k.PB
    QH = c.QH
    arena = S.arena[:].rearrange('p (a q) -> p a q', q=QH)
    barrier(k)
    V.inc(V.reciprocal(S.rowsc[0:1, 0:QH], S.rows2[0:1, 0:QH]), 'dv')
    for hf in range(QH // 512):
        s_global = (qh * QH + hf * 512) // 512
        tsl = ds(s_global, 512)
        Y.wait('dq', k.cnt['dq'])
        Y.wait('dv', k.cnt['dv'])
        Y.inc(Y.dma_start(S.x32[:], E['xn232'].ap()[:, tsl]
                          .rearrange('(a p) t -> p a t', p=128)), 'dq', 16)
        T.wait('dv', k.cnt['dv'])
        i = T.matmul(PB[:, 0:512], S.ones1r32[:, :], S.rowsc[0:1, ds(hf, 512)],
                     start=True, stop=True)
        T.inc(i, 'pe')
        V.wait('pe', k.cnt['pe'])
        V.wait('dq', k.cnt['dq'])
        for dt in range(c.DT):
            V.inc(V.tensor_mul(arena[:, dt, ds(hf, 512)], arena[:, dt, ds(hf, 512)],
                               PB[:, 0:512]), 'dv')
            V.inc(V.tensor_add(S.x32[:, dt, :], arena[:, dt, ds(hf, 512)],
                               S.x32[:, dt, :]), 'dv')
            V.inc(V.tensor_copy(S.xw[:, dt, :], S.x32[:, dt, :]), 'dv')
        Y.wait('dv', k.cnt['dv'])
        Y.inc(Y.dma_start(E['yT32'].ap()[:, tsl].rearrange('(a p) t -> p a t', p=128),
                          S.x32[:]), 'dq', 16)
        emit_tokmajor(k, 'ytokM', s_global)
    barrier(k)


def emit_final_add(k, src, dst):
    c, S, E = k.c, k.S, k.E
    Y, V, T = k.Y, k.V, k.T
    qt32 = k.S.qT[:].rearrange('p a b -> p (a b)').bitcast(F32)
    barrier(k)
    for s in range(c.ST):
        tsl = ds(s, 512)
        barrier(k)
        Y.inc(Y.dma_start(S.x32[:], E[src].ap()[:, tsl]
                          .rearrange('(a p) t -> p a t', p=128)), 'dq', 16)
        Y.inc(Y.dma_start(qt32.rearrange('p (a t) -> p a t', t=512),
                          E['yT32'].ap()[:, tsl]
                          .rearrange('(a p) t -> p a t', p=128)), 'dq', 16)
        for dt in range(c.DT):
            Y.inc(Y.dma_start(S.xw[:, dt, :], E['ftok'].ap()[tsl, ds(dt, 128)],
                              transpose=True), 'dq', 16)
        V.wait('dq', k.cnt['dq'])
        for dt in range(c.DT):
            V.inc(V.tensor_add(S.x32[:, dt, :], S.x32[:, dt, :],
                               qt32.rearrange('p (a t) -> p a t', t=512)[:, dt, :]), 'dv')
            V.inc(V.tensor_add(S.x32[:, dt, :], S.x32[:, dt, :], S.xw[:, dt, :]), 'dv')
        Y.wait('dv', k.cnt['dv'])
        Y.inc(Y.dma_start(E[dst].ap()[:, tsl].rearrange('(a p) t -> p a t', p=128),
                          S.x32[:]), 'dq', 16)



def emit_fff(k, tree, srctok, src32, dsttok, act):
    c, S, E = k.c, k.S, k.E
    Y, P, V, C, T = k.Y, k.P, k.V, k.C, k.T
    PA, PB = k.PA, k.PB
    AOFF = 0
    arenaF = S.arena[:]
    lgT0 = arenaF[:, 0:c.To]
    lgT1 = arenaF[:, c.To:2 * c.To]
    mask = arenaF[:, 2 * c.To:3 * c.To] if 3 * c.To <= c.DT * c.QH else None
    x32f = S.x32[:].rearrange('p a b -> p (a b)')
    barrier(k)
    P.inc(P.dma_start(S.nwT[:, :, 0:255].rearrange('p a n -> p a n'),
                      E['f%s_nwT' % tree].ap().rearrange('(a p) n -> p a n', p=128)),
          'dg', 16)
    Y.inc(Y.dma_start(S.nb[:], E['f%s_nb' % tree].ap()), 'dq', 16)
    P.inc(P.memset(S.arena[:, 0:2 * c.To], 0.0), 'po')
    barrier(k)

    for s in range(c.ST):
        tsl = ds(s, 512)
        barrier(k)
        Y.inc(Y.dma_start(S.x32[:], E[src32].ap()[:, tsl]
                          .rearrange('(a p) t -> p a t', p=128)), 'dq', 16)
        xr = S.xr[:]
        for hf in range(2):
            hsl = slice(s * 512 + hf * 256, s * 512 + hf * 256 + 256)
            V.wait('dq', k.cnt['dq'])
            T.wait('pe', k.rt_pe[hf])
            for dt in range(c.DT):
                V.inc(V.tensor_copy(xr[:, dt, :], S.x32[:, dt, hf * 256:hf * 256 + 256]), 'dv')
            T.wait('dv', k.cnt['dv'])
            T.wait('ac', k.cnt['ac'])
            for half in range(2):
                ncols = 128 if half == 0 else 127
                for dt in range(c.DT):
                    i = T.matmul(PA[0:ncols, half * 512 + hf * 256: half * 512 + hf * 256 + 256],
                                 S.nwT[:, dt, half * 128: half * 128 + ncols],
                                 xr[:, dt, :],
                                 start=(dt == 0), stop=(dt == c.DT - 1))
                k.rt_pe[hf] = T.inc(i, 'pe')
                V.wait('pe', k.cnt['pe'])
                dstl = lgT0 if half == 0 else lgT1
                i = V.tensor_scalar(dstl[:, hsl] if half == 0 else dstl[0:127, hsl],
                                    PA[:, half * 512 + hf * 256: half * 512 + hf * 256 + 256] if half == 0
                                    else PA[0:127, half * 512 + hf * 256: half * 512 + hf * 256 + 256],
                                    S.nb[:, half:half + 1] if half == 0
                                    else S.nb[0:127, half:half + 1],
                                    None, ALU.add)
                V.inc(i, 'dv')
    barrier(k)

    r_pref = S.rows2[0:1, 0:c.To]
    r_b = S.rowsc[0:1, 0:512]
    NCH = c.To // 512
    V.inc(V.tensor_single_scalar(r_pref, lgT0[0:1, :], 0.0, ALU.is_gt), 'dv')
    gsaved = False
    for lv in range(1, 8):
        off = (1 << lv) - 1
        segs = [(lgT0, off, 0)]
        if lv == 7:
            segs = [(lgT0, 127, 0), (lgT1, -1.0, 1)]
        for chix in range(NCH):
            chs = ds(chix, 512)
            for si, (lg, shift, segi) in enumerate(segs):
                V.inc(V.tensor_scalar_add(S.rowsc[0:1, 1024:1536], r_pref[:, chs],
                                          float(shift)), 'dv')
                T.wait('dv', k.cnt['dv'])
                i = T.matmul(PB[:, 0:512], S.ones1r32[:, :], S.rowsc[0:1, 1024:1536],
                             start=True, stop=True)
                T.inc(i, 'pe')
                V.wait('pe', k.cnt['pe'])
                i = V.scalar_tensor_tensor(x32f[:, chs], PB[:, 0:512], S.iota[:, 0:1],
                                           lg[:, chs],
                                           ALU.is_equal, ALU.mult)
                V.inc(i, 'dv')
                T.wait('dv', k.cnt['dv'])
                i = T.matmul(PB[0:1, 1024:1536], S.ones32c[:],
                             x32f[:, chs],
                             start=(si == 0), stop=(si == len(segs) - 1))
                T.inc(i, 'pe')
            V.wait('pe', k.cnt['pe'])
            V.inc(V.tensor_single_scalar(r_b, PB[0:1, 1024:1536], 0.0,
                                         ALU.is_gt), 'dv')
            V.inc(V.scalar_tensor_tensor(r_pref[:, chs], r_pref[:, chs], 2.0,
                                         r_b, ALU.mult, ALU.add), 'dv')

    barrier(k)
    emit_sort(k, tree)
    emit_fff_tiles(k, tree, srctok, dsttok, act)


def emit_sort(k, tree):
    c, S, E = k.c, k.S, k.E
    Y, P, V, C, T = k.Y, k.P, k.V, k.C, k.T
    PA, PB = k.PA, k.PB
    arenaF = S.arena[:]
    lgT0 = arenaF[:, 0:c.To]
    lgT1 = arenaF[:, c.To:2 * c.To]
    x32f = S.x32[:].rearrange('p a b -> p (a b)')[:, 0:c.To]
    r_leaf = S.rows2[0:1, 0:c.To]
    r_d = S.rows2[32:33, 0:c.To]
    x32row = S.x32[0:1, :, :].rearrange('p a b -> p (a b)')[:, 0:c.To]
    NCH = c.To // 512
    rsc_c = E['rsc'].reshape([8192, 1])
    rsc_r = E['rsc'].reshape([1, 8192])

    for H in range(2):
        for chix in range(NCH):
            chs = ds(chix, 512)
            V.inc(V.tensor_scalar_add(S.rowsc[0:1, 1024:1536], r_leaf[:, chs],
                                      -128.0 * H), 'dv')
            T.wait('dv', k.cnt['dv'])
            i = T.matmul(PB[:, 0:512], S.ones1r32[:, :], S.rowsc[0:1, 1024:1536],
                         start=True, stop=True)
            T.inc(i, 'pe')
            V.wait('pe', k.cnt['pe'])
            V.inc(V.tensor_scalar(lgT0[:, chs], PB[:, 0:512], S.iota[:, 0:1], None,
                                  ALU.is_equal), 'dv')
        V.inc(V.tensor_tensor_scan(x32f, lgT0, lgT0, 0.0, ALU.add, ALU.bypass), 'dv')
        V.inc(V.tensor_copy(S.cols[:, 8 + H:9 + H], x32f[:, c.To - 1:c.To]), 'dv')
        for chix in range(NCH):
            chs = ds(chix, 512)
            V.inc(V.tensor_mul(lgT1[:, chs], lgT0[:, chs], x32f[:, chs]), 'dv')
            T.wait('dv', k.cnt['dv'])
            i = T.matmul(PA[0:1, 1024:1536], S.ones32c[:],
                         lgT1[:, chs], start=True, stop=True)
            T.inc(i, 'pe')
            V.wait('pe', k.cnt['pe'])
            if H == 0:
                V.inc(V.tensor_copy(r_d[:, chs], PA[0:1, 1024:1536]), 'dv')
            else:
                V.inc(V.tensor_add(r_d[:, chs], PA[0:1, 1024:1536], r_d[:, chs]), 'dv')
            T.wait('dv', k.cnt['dv'])
        ncol = S.cols[:, 8 + H:9 + H]
        ti = S.cols[:, 14:15].bitcast(I32)
        V.inc(V.tensor_scalar_add(S.cols[:, 10 + H:11 + H], ncol, 15.0), 'dv')
        V.inc(V.tensor_copy(ti, S.cols[:, 10 + H:11 + H]), 'dv')
        V.inc(V.tensor_single_scalar(ti, ti, -16, ALU.bitwise_and), 'dv')
        V.inc(V.tensor_copy(S.cols[:, 10 + H:11 + H], ti), 'dv')
        Y.wait('dv', k.cnt['dv'])
        Y.inc(Y.dma_start(rsc_c.ap()[H * 128:(H + 1) * 128, :],
                          S.cols[:, 10 + H:11 + H]), 'dq', 16)
    Y.wait('dq', k.cnt['dq'])
    Y.inc(Y.dma_start(S.rowsc[0:1, 0:256], rsc_r.ap()[:, 0:256]), 'dq', 16)
    V.wait('dq', k.cnt['dq'])
    V.inc(V.tensor_tensor_scan(S.rowsc[0:1, 256:512], S.rowsc[0:1, 0:256],
                               S.rowsc[0:1, 0:256], 0.0, ALU.add, ALU.bypass), 'dv')
    V.inc(V.tensor_sub(S.rowsc[0:1, 256:512], S.rowsc[0:1, 256:512],
                       S.rowsc[0:1, 0:256]), 'dv')
    Y.wait('dv', k.cnt['dv'])
    Y.inc(Y.dma_start(rsc_r.ap()[:, 256:512], S.rowsc[0:1, 256:512]), 'dq', 16)
    Y.wait('dq', k.cnt['dq'])
    for H in range(2):
        Y.inc(Y.dma_start(S.cols[:, 12 + H:13 + H],
                          rsc_c.ap()[256 + 128 * H:256 + 128 * H + 128, :]), 'dq', 16)
    V.wait('dq', k.cnt['dq'])
    for H in range(2):
        offcol = S.cols[:, 12 + H:13 + H]
        for chix in range(NCH):
            chs = ds(chix, 512)
            V.inc(V.tensor_scalar_add(S.rowsc[0:1, 1024:1536], r_leaf[:, chs],
                                      -128.0 * H), 'dv')
            T.wait('dv', k.cnt['dv'])
            i = T.matmul(PB[:, 0:512], S.ones1r32[:, :], S.rowsc[0:1, 1024:1536],
                         start=True, stop=True)
            T.inc(i, 'pe')
            V.wait('pe', k.cnt['pe'])
            V.inc(V.tensor_scalar(lgT0[:, chs], PB[:, 0:512], S.iota[:, 0:1], None,
                                  ALU.is_equal), 'dv')
            V.inc(V.tensor_scalar(lgT0[:, chs], lgT0[:, chs], offcol, None, ALU.mult), 'dv')
            T.wait('dv', k.cnt['dv'])
            i = T.matmul(PA[0:1, 1024:1536], S.ones32c[:],
                         lgT0[:, chs], start=True, stop=True)
            T.inc(i, 'pe')
            V.wait('pe', k.cnt['pe'])
            V.inc(V.tensor_add(r_d[:, chs], PA[0:1, 1024:1536], r_d[:, chs]), 'dv')
            T.wait('dv', k.cnt['dv'])
    V.inc(V.tensor_scalar_add(r_leaf, r_d, -1.0), 'dv')
    di = lgT0[0:1, 0:c.To].bitcast(I32)
    V.inc(V.tensor_copy(di, r_leaf), 'dv')
    V.inc(V.tensor_single_scalar(di, di, 127, ALU.bitwise_and), 'dv')
    V.inc(V.tensor_copy(x32row, di), 'dv')
    V.inc(V.tensor_sub(r_leaf, r_leaf, x32row), 'dv')
    V.inc(V.tensor_scalar(r_leaf, r_leaf, 1.0 / 128.0, None, ALU.mult), 'dv')
    V.inc(V.tensor_scalar(x32row, x32row, float(c.NTILE), None, ALU.mult), 'dv')
    V.inc(V.tensor_add(r_leaf, r_leaf, x32row), 'dv')
    nvt = c.To // 128
    Y.wait('dv', k.cnt['dv'])
    Y.inc(Y.dma_start(rsc_r.ap()[:, 0:c.To], r_leaf), 'dq', 16)
    Y.wait('dq', k.cnt['dq'])
    Y.inc(Y.dma_start(S.bc[0:nvt, 0:128], E['rsc'].reshape([64, 128]).ap()[0:nvt, :]), 'dq', 16)
    T.wait('dq', k.cnt['dq'])
    i = T.matmul(PB[:, 0:nvt], S.bc[0:nvt, 0:128],
                 S.iden32[0:nvt, 0:nvt], start=True, stop=True)
    T.inc(i, 'pe')
    V.wait('pe', k.cnt['pe'])
    V.inc(V.tensor_copy(S.dstc[:, 0:nvt], PB[:, 0:nvt]), 'dv')
    P.wait('dv', k.cnt['dv'])
    P.inc(P.dma_start(E['slotTab'].ap(), E['c_oob'].ap()), 'dg', 16)
    P.wait('dg', k.cnt['dg'])
    for i_t in range(nvt):
        P.wait('dg', k.cnt['dg'])
        P.inc(P.indirect_dma_start(
            rsc_i32(k), IndirectOffsetOnAxis(ap=S.dstc[:, i_t:i_t + 1], axis=0),
            S.tki[:, i_t:i_t + 1], None), 'dg', 16)
    P.wait('dg', k.cnt['dg'])
    P.inc(P.dma_start(S.srt[:], E['slotTab'].reshape([128, c.NTILE]).ap()), 'dg', 16)
    for H in range(2):
        offcol = S.cols[:, 12 + H:13 + H]
        T.wait('dv', k.cnt['dv'])
        i = T.matmul(PB[:, 0:c.NCHUNK], S.ones1r32[:, :], S.srow[0:1, 0:c.NCHUNK],
                     start=True, stop=True)
        T.inc(i, 'pe')
        V.wait('pe', k.cnt['pe'])
        V.inc(V.tensor_scalar(lgT0[:, 0:c.NCHUNK], PB[:, 0:c.NCHUNK], offcol, None,
                              ALU.is_ge), 'dv')
        T.wait('dv', k.cnt['dv'])
        i = T.matmul(PA[0:1, 1024:1024 + c.NCHUNK], S.ones32c[:],
                     lgT0[:, 0:c.NCHUNK],
                     start=(H == 0), stop=(H == 1))
        T.inc(i, 'pe')
    V.wait('pe', k.cnt['pe'])
    V.inc(V.tensor_scalar_add(S.rowsc[0:1, 1024:1024 + c.NCHUNK], PA[0:1, 1024:1024 + c.NCHUNK],
                              -1.0), 'dv')
    Y.wait('dv', k.cnt['dv'])
    Y.inc(Y.dma_start(rsc_r.ap()[:, 0:c.NCHUNK], S.rowsc[0:1, 1024:1024 + c.NCHUNK]), 'dq', 16)
    Y.wait('dq', k.cnt['dq'])
    Y.inc(Y.dma_start(S.bc[0:c.NTILE, 0:8], E['rsc'].reshape([1024, 8]).ap()[0:c.NTILE, :]), 'dq', 16)
    V.wait('dq', k.cnt['dq'])
    V.inc(V.tensor_copy(S.ysl[0:c.NTILE, 0, 0:8], S.bc[0:c.NTILE, 0:8]), 'dv')
    T.wait('dv', k.cnt['dv'])
    i = T.matmul(PB[0:8, 0:c.NTILE], S.ysl[0:c.NTILE, 0, 0:8],
                 S.iden[0:c.NTILE, 0:c.NTILE], start=True, stop=True)
    T.inc(i, 'pe')
    V.wait('pe', k.cnt['pe'])
    V.inc(V.tensor_copy(S.clrf[:], PB[0:8, 0:c.NTILE]), 'dv')
    P.wait('dv', k.cnt['dv'])
    P.inc(P.memset(S.clri[:], 255), 'po')
    V.wait('po', k.cnt['po'])
    V.inc(V.tensor_copy(S.clri[0:8, :], S.clrf[:]), 'dv')
    V.inc(V.tensor_copy(S.ysl[0:8, 0, 0:c.NTILE], S.clrf[:]), 'dv')
    T.wait('dv', k.cnt['dv'])
    i = T.matmul(PB[:, 1024:1024 + c.NTILE], S.indic[:], S.ysl[0:8, 0, 0:c.NTILE],
                 start=True, stop=True)
    T.inc(i, 'pe')
    V.wait('pe', k.cnt['pe'])
    V.inc(V.tensor_scalar(S.bc[:, 0:c.NTILE], PB[:, 1024:1024 + c.NTILE], 16.0,
                          S.pmod[:, 0:1], ALU.mult, ALU.add), 'dv')
    V.inc(V.tensor_copy(S.idxw2[:], S.bc[:, 0:c.NTILE]), 'dv')
    cs = min(512, c.NCHUNK)
    for chix in range(c.NCHUNK // cs):
        chs = ds(chix, cs)
        T.wait('dv', k.cnt['dv'])
        i = T.matmul(PB[:, 0:cs], S.ones1r32[:, :], S.rowsc[0:1, 1024:1024 + c.NCHUNK][:, chs],
                     start=True, stop=True)
        T.inc(i, 'pe')
        V.wait('pe', k.cnt['pe'])
        V.inc(V.tensor_scalar(S.bc[:, 0:cs], PB[:, 0:cs], 128.0, S.iota[:, 0:1],
                              ALU.mult, ALU.add), 'dv')
        V.inc(V.tensor_copy(S.idxw1[:, chs], S.bc[:, 0:cs]), 'dv')
    barrier(k)


def rsc_i32(k):
    return k.E['slotTab'].reshape([k.c.SLOTS, 1]).ap()


def emit_fff_tiles(k, tree, srctok, dsttok, act):
    c, S, E = k.c, k.S, k.E
    Y, P, V, C, T = k.Y, k.P, k.V, k.C, k.T
    PA, PB = k.PA, k.PB
    for g in range(c.NTILE // 4):
        hb = g % 2
        if g >= 2:
            P.wait('gx%d' % hb, k.wtok_fff[hb])
        for j in range(4):
            i_t = 4 * g + j
            P.inc(P.indirect_dma_start(
                S.qT[:, 4 * hb + j, :], None, E[srctok].ap(),
                IndirectOffsetOnAxis(ap=S.srt[:, i_t:i_t + 1], axis=0),
                bounds_check=k.breg, oob_is_err=False), 'gg%d' % hb, 16)
        Y.wait('gg%d' % hb, k.cnt['gg%d' % hb])
        k.wtok_fff[hb] = Y.inc(
            Y.dma_start(E['xsorted'].ap()[ds(g, 512), :]
                        .rearrange('(j p) d -> p j d', p=128),
                        S.qT[:, 4 * hb:4 * hb + 4, :]),
            'gx%d' % hb, 16)
    barrier(k)
    dstap = E[dsttok].ap()[:, 0:1024]
    tok_pe = [0, 0, 0, 0]
    tok_ac = [0, 0]
    tok_msk = [0, 0]
    tok_dvy = [0, 0]
    y_pe = [0, 0]
    dq_blk = [0] * c.NBLK
    for dt in range(c.DT):
        Y.inc(Y.dma_start(S.qT[:, dt, 0:1024],
                          E['xsorted'].ap()[ds(0, 1024), ds(dt, 128)],
                          transpose=True), 'dq', 16)
    dq_blk[0] = k.cnt['dq']

    def gath(it):
        buf = it % 4
        P.wait('pe', tok_pe[buf])
        gsem = 'gt%d' % buf
        for cc in range(8):
            P.inc(P.indirect_dma_start(
                S.ktT[:, buf, 128 * cc:128 * cc + 128], None,
                E['w1gB_' + tree].ap(),
                IndirectOffsetOnAxis(ap=S.idxw1[:, it * 8 + cc:it * 8 + cc + 1], axis=0)), gsem, 16)
        P.inc(P.indirect_dma_start(
            S.vsup[:, buf, 0:1024], None, E['w2gB_' + tree].ap(),
            IndirectOffsetOnAxis(ap=S.idxw2[:, it:it + 1], axis=0)), gsem, 16)
        P.inc(P.indirect_dma_start(
            S.vsup[0:8, 4 + buf, 0:1024], None, E['b2B_' + tree].ap(),
            IndirectOffsetOnAxis(ap=S.clri[0:8, it:it + 1], axis=0)), gsem, 16)
        P.inc(P.indirect_dma_start(
            S.b1sb[:, buf, :], None, E['f%s_b1' % tree].ap(),
            IndirectOffsetOnAxis(ap=S.idxw2[:, it:it + 1], axis=0)), gsem, 16)

    for it in range(min(3, c.NTILE)):
        gath(it)
    for it in range(c.NTILE):
        blk, ti = it // 8, it % 8
        buf = it % 4
        pp = it % 2
        gsem = 'gt%d' % buf
        if it + 3 < c.NTILE:
            gath(it + 3)
        V.wait(gsem, k.cnt[gsem])
        i = V.tensor_copy(
            S.ktT[:, 4 + buf, :].rearrange('p (a b j) -> p a b j', a=8, j=16),
            S.ktT[:, buf, :].rearrange('p (a b j) -> p b a j', a=8, j=16))
        k.w1cp[buf] = V.inc(i, 'dv')
        if ti == 0:
            T.wait('dq', dq_blk[blk])
        T.wait(gsem, k.cnt[gsem])
        T.wait('dv', k.w1cp[buf])
        T.wait('ac', tok_ac[pp])
        for dt in range(c.DT):
            i = T.matmul(PA[:, pp * 512: pp * 512 + 128],
                         S.ktT[:, 4 + buf, ds(dt, 128)], S.qT[:, dt, ds(ti, 128)],
                         start=(dt == 0), stop=(dt == c.DT - 1))
        hpe = T.inc(i, 'pe')
        if ti == 7 and blk + 1 < c.NBLK:
            Y.wait('pe', hpe)
            for dt in range(c.DT):
                Y.inc(Y.dma_start(S.qT[:, dt, 0:1024],
                                  E['xsorted'].ap()[ds(blk + 1, 1024), ds(dt, 128)],
                                  transpose=True), 'dq', 16)
            dq_blk[blk + 1] = k.cnt['dq']
        C.wait('pe', k.cnt['pe'])
        C.wait(gsem, k.cnt[gsem])
        C.wait('dv', tok_msk[pp])
        i = C.activation(S.hA[:, pp, :], PA[:, pp * 512: pp * 512 + 128],
                         act, bias=S.b1sb[:, buf, :])
        tok_ac[pp] = C.inc(i, 'ac')
        V.wait('ac', k.cnt['ac'])
        V.wait('pe', y_pe[pp])
        i = V.tensor_mul(S.hT[:, pp, :], S.hA[:, pp, :], S.mdiag[:])
        tok_msk[pp] = V.inc(i, 'dv')
        T.wait('dv', k.cnt['dv'])
        T.wait('dv', tok_dvy[pp])
        pb0 = pp * 1024
        T.matmul(PB[:, pb0:pb0 + 512], S.hT[:, pp, :], S.vsup[:, buf, 0:512],
                 start=True, stop=False)
        T.matmul(PB[:, pb0 + 512:pb0 + 1024], S.hT[:, pp, :],
                 S.vsup[:, buf, 512:1024], start=True, stop=False)
        T.matmul(PB[:, pb0:pb0 + 512], S.indic[:], S.vsup[0:8, 4 + buf, 0:512],
                 start=False, stop=True)
        i = T.matmul(PB[:, pb0 + 512:pb0 + 1024], S.indic[:],
                     S.vsup[0:8, 4 + buf, 512:1024], start=False, stop=True)
        tok_pe[buf] = y_pe[pp] = T.inc(i, 'pe')
        V.wait('pe', k.cnt['pe'])
        V.wait('gy%d' % pp, k.ysl_tok[pp])
        i = V.tensor_copy(S.ysl[:, pp, :], PB[:, pb0:pb0 + 1024])
        tok_dvy[pp] = V.inc(i, 'dv')
        P.wait('dv', k.cnt['dv'])
        k.ysl_tok[pp] = P.inc(P.indirect_dma_start(
            dstap, IndirectOffsetOnAxis(ap=S.srt[:, it:it + 1], axis=0),
            S.ysl[:, pp, :], None,
            bounds_check=k.breg, oob_is_err=False), 'gy%d' % pp, 16)
        if c.dbg_serial:
            barrier(k)
    barrier(k)



def kernel(**inputs):
    cfg = Cfg()
    k = build(cfg)
    emit_program(k, script)
    k.es.close()
    in_maps = [prep_core_inputs(inputs, core, cfg) for core in range(8)]
    res = run_bass_kernel_spmd(k.nc, in_maps, core_ids=list(range(8)))
    out = np.empty((4, 8192, 1024), np.float32)
    for core in range(8):
        b, h = core // 2, core % 2
        out[b, h * cfg.To:(h + 1) * cfg.To, :] = res.results[core]['outT'].T
    return out


# version 10
# speedup vs baseline: 1.5074x; 1.0720x over previous
import numpy as np
import ml_dtypes
from contextlib import ExitStack

import concourse.bass as bass
import concourse.mybir as mybir
from concourse.bass_utils import run_bass_kernel_spmd
from concourse.bass_core_rust import IndirectOffsetOnAxis

F32 = mybir.dt.float32
F32R = mybir.dt.float32r
BF16 = mybir.dt.bfloat16
I32 = mybir.dt.int32
AF = mybir.ActivationFunctionType
ALU = mybir.AluOpType
AX = mybir.AxisListType
BF = ml_dtypes.bfloat16
SQD = float(np.sqrt(1024.0))


class Cfg:
    def __init__(self, To=4096, Tf=8192, groups=None, parts=(1, 2, 3, 4),
                 dumps=(), dumps_q=(), dbg_serial=False):
        self.dumps = dumps
        self.dumps_q = dumps_q
        self.dbg_serial = dbg_serial
        self.To = To; self.Tf = Tf
        self.D = 1024; self.DT = 8; self.H = 16
        self.KSUP = min(1024, Tf); self.NSUP = Tf // self.KSUP
        self.NKT = self.KSUP // 128
        self.QH = min(1024, To); self.NQH = To // self.QH
        self.QB = 512
        self.NQB = self.QH // self.QB
        self.QBA = 512
        self.NQBA = self.QH // self.QBA
        self.NNODE = 255; self.NLEAF = 256
        self.SLOTS = ((To + 15 * 256 + 1023) // 1024) * 1024
        self.NTILE = self.SLOTS // 128
        self.NCHUNK = self.SLOTS // 16
        self.NBLK = self.SLOTS // 1024
        self.ST = To // 512
        self.NTT = To // 128
        self.groups = groups if groups is not None else [[0, 1], [2, 3], [4, 5], [6, 7]]
        self.parts = parts


def ds(i, n):
    return slice(i * n, (i + 1) * n)



def prep_core_inputs(inputs, core, cfg):
    c = cfg
    b, h = core // 2, core % 2
    x = np.asarray(inputs['L0_attn_a_x'])
    d = {'xT': np.ascontiguousarray(x[b, h * c.To:(h + 1) * c.To, :].T)}
    d['aa_wqkv'] = np.asarray(inputs['L0_attn_a_w_qkv'])
    d['aa_wout'] = np.asarray(inputs['L0_attn_a_w_out'])
    d['aa_gcol'] = np.ascontiguousarray(
        np.asarray(inputs['L0_attn_a_gamma']).reshape(8, 128).T)
    for nm in 'qkvo':
        W = np.asarray(inputs['L0_attn_b_W' + nm])
        d['ab_w' + nm] = np.ascontiguousarray(W.T)
        bb = np.asarray(inputs['L0_attn_b_b' + nm])
        d['ab_bc' + nm] = np.ascontiguousarray(bb.reshape(8, 128).T)
        d['ab_br' + nm] = np.ascontiguousarray(bb.reshape(1, 1024))
    for t in 'qkvf':
        g = lambda s: np.asarray(inputs['L0_ffn_%s_%s' % (t, s)])
        d['f%s_nwT' % t] = np.ascontiguousarray(g('nw').T)
        nb = g('nb')
        nbp = np.zeros((128, 2), np.float32)
        nbp[:, 0] = nb[0:128]
        nbp[0:127, 1] = nb[128:255]
        d['f%s_nb' % t] = nbp
        d['f%s_w1g' % t] = np.ascontiguousarray(
            g('w1').reshape(256, 8, 128, 16).transpose(0, 2, 1, 3).reshape(256, 128, 128))
        d['f%s_b1' % t] = np.ascontiguousarray(g('b1').reshape(4096, 1))
        d['f%s_w2g' % t] = np.ascontiguousarray(np.concatenate(
            [g('w2').reshape(4096, 1024),
             g('b1').reshape(4096, 1).astype(np.float32)], axis=1))
        d['f%s_b2' % t] = np.asarray(g('b2'))
    d['c_iota'] = np.arange(128, dtype=np.float32).reshape(128, 1)
    d['c_pmod'] = (np.arange(128, dtype=np.float32) % 16).reshape(128, 1)
    d['c_tokid'] = np.ascontiguousarray(
        np.arange(c.To, dtype=np.float32).reshape(c.To // 128, 128).T)
    d['c_oob'] = np.full((c.SLOTS,), 1 << 20, np.int32)
    jj, ss = np.meshgrid(np.arange(128), np.arange(128), indexing='ij')
    d['c_maskdiag'] = ((jj // 16) == (ss // 16)).astype(BF)
    ind = np.zeros((8, 128), np.float32)
    for cc in range(8):
        ind[cc, cc * 16:(cc + 1) * 16] = 1.0
    d['c_indic8'] = ind.astype(BF)
    nkt, nqb = c.Tf // 128, c.To // c.QB
    kd = np.zeros((128, nkt * nqb), np.float32)
    for kt in range(nkt):
        for qb in range(nqb):
            kd[:, kt * nqb + qb] = np.arange(128) + 128 * kt - (qb * c.QB + h * c.To) + 16
    d['c_kd'] = kd
    d['c_qrow'] = np.tile(np.arange(c.QB, dtype=np.float32), (128, 1))
    d['c_srow'] = (16.0 * np.arange(c.NCHUNK, dtype=np.float32)).reshape(1, c.NCHUNK)
    return d



class NS:
    pass


def build(cfg):
    c = cfg
    nc = bass.Bass("TRN2", target_bir_lowering=False)
    k = NS(); k.c = c; k.nc = nc; k.E = {}

    def dram(name, shape, dt, kind="Internal"):
        k.E[name] = nc.dram_tensor(name, list(shape), dt, kind=kind)

    dram('xT', (c.D, c.To), F32, "ExternalInput")
    dram('aa_wqkv', (c.D, 3072), F32, "ExternalInput")
    dram('aa_wout', (c.D, c.D), F32, "ExternalInput")
    dram('aa_gcol', (128, 8), F32, "ExternalInput")
    for nm in 'qkvo':
        dram('ab_w' + nm, (c.D, c.D), F32, "ExternalInput")
        dram('ab_bc' + nm, (128, 8), F32, "ExternalInput")
        dram('ab_br' + nm, (1, c.D), F32, "ExternalInput")
    for t in 'qkvf':
        dram('f%s_nwT' % t, (c.D, c.NNODE), F32, "ExternalInput")
        dram('f%s_nb' % t, (128, 2), F32, "ExternalInput")
        dram('f%s_w1g' % t, (c.NLEAF, 128, 128), F32, "ExternalInput")
        dram('f%s_b1' % t, (4096, 1), F32, "ExternalInput")
        dram('f%s_w2g' % t, (4096, c.D + 1), F32, "ExternalInput")
        dram('f%s_b2' % t, (c.NLEAF, c.D), F32, "ExternalInput")
    dram('c_iota', (128, 1), F32, "ExternalInput")
    dram('c_pmod', (128, 1), F32, "ExternalInput")
    dram('c_tokid', (128, c.To // 128), F32, "ExternalInput")
    dram('c_oob', (c.SLOTS,), I32, "ExternalInput")
    dram('c_maskdiag', (128, 128), BF16, "ExternalInput")
    dram('c_indic8', (8, 128), BF16, "ExternalInput")
    dram('c_kd', (128, (c.Tf // 128) * (c.To // c.QB)), F32, "ExternalInput")
    dram('c_qrow', (128, c.QB), F32, "ExternalInput")
    dram('c_srow', (1, c.NCHUNK), F32, "ExternalInput")
    dram('outT', (c.D, c.To), F32, "ExternalOutput")

    dram('resid', (c.D, c.To), F32)
    dram('xn232', (c.D, c.To), F32)
    dram('yT32', (c.D, c.To), F32)
    dram('xtok', (c.To, c.D), BF16)
    dram('ytokM', (c.To, c.D), BF16)
    dram('xsorted', (c.SLOTS, c.D), BF16)
    dram('qTd', (c.D, c.To), BF16)
    dram('qtok', (c.To, c.D), BF16)
    dram('ktok', (c.To, 1056), BF16)
    dram('vtok', (c.To, 1056), BF16)
    dram('kfull', (c.Tf, 1056), BF16)
    dram('vfull', (c.Tf, 1056), BF16)
    dram('ktokF', (c.To, 1024), BF16)
    dram('vtokF', (c.To, 1024), BF16)
    dram('kfullF', (c.Tf, 1024), BF16)
    dram('vfullF', (c.Tf, 1024), BF16)
    dram('ftok', (c.To, c.D), BF16)
    dram('slotTab', (c.SLOTS,), I32)
    dram('rsc', (8192,), F32)
    for t in 'qkvf':
        dram('w1gB_' + t, (c.NLEAF * 128, 128), BF16)
        dram('w2gB_' + t, (4096, c.D + 1), BF16)
        dram('b2B_' + t, (c.NLEAF, c.D), BF16)
    for nm in c.dumps:
        h = k.E[nm]
        dram('dmp_' + nm, list(h.shape), h.dtype, "ExternalOutput")
    for nm in c.dumps_q:
        if nm.startswith('S.'):
            continue
        h = k.E[nm]
        dram('dmpq_' + nm, list(h.shape), h.dtype, "ExternalOutput")

    es = ExitStack(); k.es = es

    def sbuf(name, shape, dt):
        return es.enter_context(nc.sbuf_tensor(name, list(shape), dt))

    S = NS(); k.S = S
    QH = c.QH
    S.ktT = sbuf('ktT', (128, c.DT, c.KSUP), BF16)
    S.vsup = sbuf('vsup', (128, c.NKT, 1056), BF16)
    S.qT = sbuf('qT', (128, c.DT, QH), BF16)
    S.arena = sbuf('arena', (128, c.DT * QH), F32)
    S.pT = sbuf('pT', (128, 4, 512), BF16)
    S.x32 = sbuf('x32', (128, c.DT, 512), F32)
    S.xw = sbuf('xw', (128, c.DT, 512), BF16)
    S.ttk = sbuf('ttk', (128, 2, c.D), BF16)
    S.rows2 = sbuf('rows2', (128, max(c.To, 4 * c.QH)), F32)
    S.bc = sbuf('bc', (128, 1024), F32)
    S.cols = sbuf('cols', (128, 16), F32)
    S.w0 = sbuf('w0', (128, 2, 2048), BF16)
    S.onesb = sbuf('onesb', (128, 2), BF16)
    S.ones32c = sbuf('ones32c', (128, 1), F32)
    S.ones1rb = sbuf('ones1rb', (1, 128), BF16)
    S.ones1r32 = sbuf('ones1r32', (1, 128), F32)
    S.iota = sbuf('iota', (128, 1), F32)
    S.pmod = sbuf('pmod', (128, 1), F32)
    S.iden = sbuf('iden', (128, 128), BF16)
    S.iden32 = sbuf('iden32', (128, 128), F32)
    S.rowsc = sbuf('rowsc', (128, 2048), F32)
    S.nwT = sbuf('nwT', (128, c.DT, 256), F32R)
    S.xr = sbuf('xr', (128, c.DT, 512), F32R)
    S.nb = sbuf('nb', (128, 2), F32)
    S.tokid = sbuf('tokid', (128, c.To // 128), F32)
    S.tki = sbuf('tki', (128, c.To // 128), I32)
    S.dstc = sbuf('dstc', (128, c.To // 128), I32)
    S.srt = sbuf('srt', (128, c.NTILE), I32)
    S.idxw1 = sbuf('idxw1', (128, c.NCHUNK), I32)
    S.idxw2 = sbuf('idxw2', (128, c.NTILE), I32)
    S.clri = sbuf('clri', (128, c.NTILE), I32)
    S.clrf = sbuf('clrf', (8, c.NTILE), F32)
    S.b1sb = sbuf('b1sb', (128, 4, 1), F32)
    S.hT = sbuf('hT', (128, 2, 128), BF16)
    S.hA = sbuf('hA', (128, 2, 128), BF16)
    S.ysl = sbuf('ysl', (128, 2, c.D), BF16)
    S.mdiag = sbuf('mdiag', (128, 128), BF16)
    S.indic = sbuf('indic', (8, 128), BF16)
    S.kd = sbuf('kd', (128, (c.Tf // 128) * (c.To // c.QB)), F32)
    S.qrow = sbuf('qrow', (128, c.QB), F32)
    S.srow = sbuf('srow', (1, c.NCHUNK), F32)
    S.brow = sbuf('brow', (128, 2 * c.D), BF16)
    S.vev = sbuf('vev', (128, 544), BF16)
    S.constv = sbuf('constv', (128, 1), F32)
    nc.const_aps.register(F32, 1e-5, S.constv[:, 0:1])

    for nm in c.dumps_q:
        if nm.startswith('S.'):
            t = getattr(S, nm[2:])
            dram('dmpq_' + nm[2:], list(t[:].tensor.shape), t[:].dtype,
                 "ExternalOutput")
    k.PA = es.enter_context(nc.psum_tensor('PA', [128, 2048], F32))
    k.PB = es.enter_context(nc.psum_tensor('PB', [128, 2048], F32))

    k.sems = {}
    for s in ['dq', 'dg', 'cc', 'dw0', 'dw1', 'sp0', 'sp1', 'sv', 'sb',
              'gg0', 'gg1', 'gx0', 'gx1', 'gt0', 'gt1', 'gt2', 'gt3',
              'gy0', 'gy1', 'dk', 'pe', 'ac', 'dv', 'po', 'sq']:
        k.sems[s] = es.enter_context(nc.semaphore(s))
    k.DMASEMS = ('dq', 'dg', 'cc', 'dw0', 'dw1', 'sp0', 'sp1', 'sv', 'sb',
                 'gg0', 'gg1', 'gx0', 'gx1', 'gt0', 'gt1', 'gt2', 'gt3',
                 'gy0', 'gy1', 'dk')
    return k



class EngProxy:
    def __init__(self, k, name):
        self._k = k
        self._name = name

    _NOWAIT = {'wait_ge', 'alloc_register', 'reg_mov', 'free_register', 'nop'}
    _OWN = {'ve': 'dv', 'sc': 'ac'}

    def __getattr__(self, attr):
        k = self._k
        name = self._name

        def call(*args, **kw):
            if k.live == name:
                own = EngProxy._OWN.get(name)
                if own and attr not in EngProxy._NOWAIT and k.cnt[own] > 0:
                    key = (name, own)
                    if k.waited.get(key, -1) < k.cnt[own]:
                        k.engs[name].wait_ge(k.sems[own], k.cnt[own])
                        k.waited[key] = k.cnt[own]
                try:
                    return getattr(k.engs[name], attr)(*args, **kw)
                except Exception as e:
                    raise RuntimeError('engine op failed: %s.%s: %r' % (name, attr, e)) from e
            elif EngProxy._OWN.get(name) and attr not in EngProxy._NOWAIT:
                own = EngProxy._OWN[name]
                k.waited[(name, own)] = max(k.waited.get((name, own), -1), k.cnt[own])
            return None
        return call

    def wait(self, sem, val=None):
        k = self._k
        if val is None:
            val = k.cnt[sem]
        val = min(val, k.cnt[sem])
        if val <= 0:
            return
        if k.live == self._name:
            key = (self._name, sem)
            if k.waited.get(key, -1) < val:
                k.engs[self._name].wait_ge(k.sems[sem], val)
                k.waited[key] = val

    def inc(self, instr, sem, step=1):
        k = self._k
        k.cnt[sem] += step
        if instr is not None:
            instr.then_inc(k.sems[sem], step)
        return k.cnt[sem]


def emit_program(k, scriptfn):
    blk = k.es.enter_context(k.nc.Block())
    k.engs = {}
    k.T = EngProxy(k, 'te'); k.V = EngProxy(k, 've'); k.C = EngProxy(k, 'sc')
    k.P = EngProxy(k, 'po'); k.Y = EngProxy(k, 'sy')

    def replay(name, eng):
        k.engs[name] = eng
        k.live = name
        k.cnt = {s: 0 for s in k.sems}
        k.waited = {}
        scriptfn(k)

    @blk.sync
    def _(e):
        replay('sy', e)

    @blk.scalar
    def _(e):
        replay('sc', e)

    @blk.vector
    def _(e):
        replay('ve', e)

    @blk.tensor
    def _(e):
        replay('te', e)

    @blk.gpsimd
    def _(e):
        replay('po', e)


def barrier(k, engines=None):
    snap = dict(k.cnt)
    own = {'te': 'pe', 've': 'dv', 'sc': 'ac', 'po': 'po', 'sy': 'sq'}
    prox = {'te': k.T, 've': k.V, 'sc': k.C, 'po': k.P, 'sy': k.Y}
    arr = {}
    for e in ('te', 've', 'sc', 'po', 'sy'):
        p = prox[e]
        for s, v in snap.items():
            if v > 0:
                p.wait(s, v)
        arr[e] = p.inc(p.nop(), own[e])
    for e in ('te', 've', 'sc', 'po', 'sy'):
        p = prox[e]
        for e2 in ('te', 've', 'sc', 'po', 'sy'):
            if e2 != e:
                p.wait(own[e2], arr[e2])



def script(k):
    c = k.c
    k.ac_exp = [0, 0]
    k.dv_msk = [0, 0]
    k.pe_qw = [0, 0]
    k.pe_kw = [0, 0]
    k.wtok_fff = [0, 0]
    k.ysl_tok = [0, 0]
    k.w1cp = [0, 0, 0, 0]
    k.qtd_st = [0, 0]
    k.vev_st = 0
    k.bc_st = 0
    k.qp_ev = [0, 0]
    k.qp_evv = [0, 0]
    k.tkm_tok = [0, 0]
    k.tkm_st = 0
    k.rt_pe = [0, 0]
    if k.live == 'po':
        k.breg = k.engs['po'].alloc_register('bnd')
        k.engs['po'].reg_mov(k.breg, c.To - 1)
    else:
        k.breg = None
    emit_init(k)
    src = 'xT'
    for i, p in enumerate(c.parts):
        dst = 'outT' if i == len(c.parts) - 1 else 'resid'
        if p == 1:
            emit_attn_sublayer(k, src, dst, 'aa')
        elif p == 3:
            emit_attn_sublayer(k, src, dst, 'ab')
        else:
            emit_ffn_sublayer(k, src, dst)
        src = 'resid'
    for nm in c.dumps:
        k.Y.inc(k.Y.dma_start(k.E['dmp_' + nm].ap(), k.E[nm].ap()), 'dq', 16)
        k.Y.wait('dq', k.cnt['dq'])
    for s in ('dq', 'dg', 'cc', 'pe', 'ac', 'dv', 'po'):
        if k.cnt[s] > 0:
            k.Y.wait(s, k.cnt[s])


def emit_init(k):
    c, S, E = k.c, k.S, k.E
    Y, P, V = k.Y, k.P, k.V
    for nm, dst in [('c_iota', S.iota), ('c_pmod', S.pmod), ('c_tokid', S.tokid),
                    ('c_maskdiag', S.mdiag), ('c_indic8', S.indic), ('c_kd', S.kd),
                    ('c_qrow', S.qrow), ('c_srow', S.srow)]:
        Y.inc(Y.dma_start(dst[:], E[nm].ap()), 'dq', 16)
    P.inc(P.memset(S.iden[:], 0.0), 'po')
    P.wait('po', k.cnt['po'])
    P.inc(P.affine_select(out=S.iden[:], in_=S.iden[:], compare_op=ALU.not_equal,
                          fill=1.0, base=0, pattern=[[-1, 128]],
                          channel_multiplier=1), 'po')
    P.inc(P.memset(S.iden32[:], 0.0), 'po')
    P.wait('po', k.cnt['po'])
    P.inc(P.affine_select(out=S.iden32[:], in_=S.iden32[:], compare_op=ALU.not_equal,
                          fill=1.0, base=0, pattern=[[-1, 128]],
                          channel_multiplier=1), 'po')
    P.inc(P.memset(S.constv[:], 1e-5), 'po')
    P.inc(P.memset(S.onesb[:], 1.0), 'po')
    P.inc(P.memset(S.ones32c[:], 1.0), 'po')
    P.inc(P.memset(S.ones1rb[:], 1.0), 'po')
    P.inc(P.memset(S.ones1r32[:], 1.0), 'po')
    P.inc(P.memset(S.vev[:], 0.0), 'po')
    P.wait('po', k.cnt['po'])
    for g in range(8):
        P.inc(P.memset(S.vev[:, g * 66 + 64: g * 66 + 65], 1.0), 'po')
    V.wait('dq', k.cnt['dq'])
    V.inc(V.tensor_copy(S.tki[:], S.tokid[:]), 'dv')
    P.inc(P.memset(S.ysl[:, 0, 0:((c.To // 128) * 32)], 0.0), 'po')
    P.wait('po', k.cnt['po'])
    for nm in ('ktok', 'vtok'):
        P.inc(P.dma_start(E[nm].ap()[:, 1024:1056]
                          .rearrange('(a p) v -> p a v', p=128),
                          S.ysl[:, 0, 0:((c.To // 128) * 32)]
                          .rearrange('p (a v) -> p a v', v=32)), 'dg', 16)
    vflat = S.vsup[:].rearrange('p a b -> p (a b)')
    piece = 4096
    stride = 4224
    if 2 in c.parts:
        for t in 'qkvf':
            for (src, dstn, n) in [('f%s_w1g' % t, 'w1gB_' + t, c.NLEAF * 128 * 128),
                                   ('f%s_w2g' % t, 'w2gB_' + t, 4096 * (c.D + 1)),
                                   ('f%s_b2' % t, 'b2B_' + t, c.NLEAF * c.D)]:
                npc = max(n // (128 * piece), 1)
                per = n // 128 // npc
                assert per <= stride and n == per * npc * 128
                sap = E[src].reshape([128, n // 128]).ap()
                dap = E[dstn].reshape([128, n // 128]).ap()
                for i in range(npc):
                    buf = vflat[:, (i % 2) * stride:(i % 2) * stride + per]
                    P.wait('dk', k.cnt['dk'])
                    P.inc(P.dma_start(buf, sap[:, ds(i, per)]), 'dk', 16)
                    P.wait('dk', k.cnt['dk'])
                    P.inc(P.dma_start(dap[:, ds(i, per)], buf), 'dk', 16)
    barrier(k)



def emit_prologue(k, src, kind):
    c, S, E = k.c, k.S, k.E
    Y, P, V, C, T = k.Y, k.P, k.V, k.C, k.T
    PA, PB = k.PA, k.PB
    xs = E[src].ap()

    if kind == 'aa':
        Y.inc(Y.dma_start(S.cols[:, 0:8], E['aa_gcol'].ap()), 'dq', 16)
        barrier(k, ('ve',))
        V.inc(V.tensor_scalar_add(S.cols[:, 0:8], S.cols[:, 0:8], 1.0), 'dv')
    if kind == 'ab':
        Y.inc(Y.dma_start(S.cols[:, 0:8], E['ab_bcq'].ap()), 'dq', 16)
        P.inc(P.dma_start(S.brow[0:1, 0:c.D], E['ab_brk'].ap()), 'dg', 16)
        P.inc(P.dma_start(S.brow[0:1, c.D:2 * c.D], E['ab_brv'].ap()), 'dg', 16)

    for s in range(c.ST):
        tsl = ds(s, 512)
        barrier(k)
        Y.inc(Y.dma_start(S.x32[:], xs[:, tsl].rearrange('(a p) t -> p a t', p=128)),
              'dq', 16)
        ldq = k.cnt['dq']
        C.wait('dq', ldq)
        for dt in range(c.DT):
            C.inc(C.activation(S.xw[:, dt, :], S.x32[:, dt, :], AF.Square), 'ac')
        T.wait('ac', k.cnt['ac'])
        T.wait('dq', ldq)
        for dt in range(c.DT):
            T.matmul(PA[0:1, 1024:1536], S.ones32c[:, 0:1], S.x32[:, dt, :],
                     start=(dt == 0), stop=(dt == c.DT - 1))
        for dt in range(c.DT):
            i = T.matmul(PA[0:1, 1536:2048], S.onesb[:, 0:1], S.xw[:, dt, :],
                         start=(dt == 0), stop=(dt == c.DT - 1))
        T.inc(i, 'pe')
        R = S.rowsc
        R2 = S.rows2
        m_ = R[0:1, 0:512]; m2_ = R[0:1, 512:1024]
        fs_ = R[0:1, 1024:1536]; f_ = R[0:1, 1536:2048]
        v_ = R2[0:1, 0:512]; sq_ = R2[0:1, 512:1024]
        r_ = R2[0:1, 1024:1536]; t_ = R2[0:1, 1536:2048]
        V.wait('pe', k.cnt['pe'])
        V.inc(V.tensor_scalar_mul(m_, PA[0:1, 1024:1536], 1.0 / 1024.0), 'dv')
        V.inc(V.tensor_mul(m2_, m_, m_), 'dv')
        V.inc(V.scalar_tensor_tensor(v_, PA[0:1, 1536:2048], 1.0 / 1024.0, m2_,
                                     ALU.mult, ALU.subtract), 'dv')
        C.wait('dv', k.cnt['dv'])
        C.inc(C.activation(sq_, v_, AF.Sqrt, bias=1e-5), 'ac')
        V.wait('ac', k.cnt['ac'])
        V.inc(V.reciprocal(r_, sq_), 'dv')
        if kind == 'aa':
            C.wait('dv', k.cnt['dv'])
            C.inc(C.activation(t_, v_, AF.Sqrt, scale=1024.0), 'ac')
            V.wait('ac', k.cnt['ac'])
            V.inc(V.tensor_mul(t_, t_, r_), 'dv')
            V.inc(V.tensor_scalar_max(t_, t_, 1e-12), 'dv')
            V.inc(V.reciprocal(t_, t_), 'dv')
            V.inc(V.scalar_tensor_tensor(t_, t_, SQD, r_, ALU.mult, ALU.mult), 'dv')
            V.inc(V.tensor_copy(f_, t_), 'dv')
        elif kind == 'ffn':
            C.wait('dv', k.cnt['dv'])
            C.inc(C.activation(t_, v_, AF.Sqrt, scale=1024.0), 'ac')
            V.wait('ac', k.cnt['ac'])
            V.inc(V.scalar_tensor_tensor(t_, t_, 1.0 / float(np.sqrt(1023.0)), r_,
                                         ALU.mult, ALU.mult), 'dv')
            C.wait('dv', k.cnt['dv'])
            C.inc(C.activation(t_, t_, AF.Sqrt, bias=1e-5), 'ac')
            V.wait('ac', k.cnt['ac'])
            V.inc(V.reciprocal(t_, t_), 'dv')
            V.inc(V.tensor_mul(t_, t_, r_), 'dv')
            V.inc(V.tensor_copy(f_, t_), 'dv')
        else:
            V.inc(V.tensor_copy(f_, r_), 'dv')
        T.wait('dv', k.cnt['dv'])
        T.matmul(PB[:, 0:512], S.ones1r32[:, :], S.rowsc[0:1, 0:512], start=True, stop=True)
        i = T.matmul(PB[:, 512:1024], S.ones1r32[:, :], S.rowsc[0:1, 1536:2048], start=True, stop=True)
        T.inc(i, 'pe')
        V.wait('pe', k.cnt['pe'])
        for dt in range(c.DT):
            V.inc(V.scalar_tensor_tensor(S.x32[:, dt, :], S.x32[:, dt, :], 1.0,
                                         PB[:, 0:512], ALU.mult, ALU.subtract), 'dv')
            if kind == 'aa':
                V.inc(V.scalar_tensor_tensor(S.x32[:, dt, :], S.x32[:, dt, :],
                                             S.cols[:, dt:dt + 1], PB[:, 512:1024],
                                             ALU.mult, ALU.mult), 'dv')
            else:
                V.inc(V.scalar_tensor_tensor(S.x32[:, dt, :], S.x32[:, dt, :], 1.0,
                                             PB[:, 512:1024], ALU.mult, ALU.mult), 'dv')
            V.inc(V.tensor_copy(S.xw[:, dt, :], S.x32[:, dt, :]), 'dv')
        if kind == 'ffn':
            Y.wait('dv', k.cnt['dv'])
            Y.inc(Y.dma_start(E['xn232'].ap()[:, tsl].rearrange('(a p) t -> p a t', p=128),
                              S.x32[:]), 'dq', 16)
            emit_tokmajor(k, 'xtok', s)
        else:
            emit_stripe_projections(k, s, kind)
    barrier(k)


def emit_tokmajor(k, dsttok, s):
    c, S, E = k.c, k.S, k.E
    Y, V, T = k.Y, k.V, k.T
    T.wait('dv', k.cnt['dv'])
    for tt in range(4):
        for dt in range(c.DT):
            bank = dt % 2
            T.wait('dv', k.tkm_tok[bank])
            i = T.matmul(PAq(k)[:, bank * 512: bank * 512 + 128],
                         S.xw[:, dt, ds(tt, 128)], S.iden[:], start=True, stop=True)
            T.inc(i, 'pe')
            V.wait('pe', k.cnt['pe'])
            V.wait('dq', k.tkm_st)
            i = V.tensor_copy(S.ttk[:, tt % 2, ds(dt, 128)],
                              PAq(k)[:, bank * 512: bank * 512 + 128])
            k.tkm_tok[bank] = V.inc(i, 'dv')
        if tt % 2 == 1:
            Y.wait('dv', k.cnt['dv'])
            k.tkm_st = Y.inc(
                Y.dma_start(E[dsttok].ap()[s * 512 + (tt - 1) * 128:
                                           s * 512 + (tt + 1) * 128, :]
                            .rearrange('(a p) d -> p a d', p=128), S.ttk[:]),
                'dq', 16)


def PAq(k):
    return k.PA


def emit_stripe_projections(k, s, kind):
    c, S, E = k.c, k.S, k.E
    Y, P, V, C, T = k.Y, k.P, k.V, k.C, k.T
    PA, PB = k.PA, k.PB
    tsl = ds(s, 512)
    w0f = S.w0[:].rearrange('p a b -> p (a b)')

    if kind == 'aa':
        qsrc = 'aa_wqkv'
        kvspec = [('aa_wqkv', 1024, 'ktok', False), ('aa_wqkv', 2048, 'vtok', True)]
    else:
        qsrc = 'ab_wq'
        kvspec = [('ab_wk', 0, 'ktokF', False), ('ab_wv', 0, 'vtokF', False)]

    k.pe_qw[0] = max(k.pe_qw[0], k.pe_kw[0])
    k.pe_qw[1] = max(k.pe_qw[1], k.pe_kw[1])
    for g in range(8):
        buf = g % 2
        if g >= 2:
            P.wait('pe', k.pe_qw[buf])
        else:
            P.wait('pe', k.pe_qw[buf])
        P.inc(P.dma_start(S.w0[:, buf, 0:1024].rearrange('p (a q) -> p a q', q=128),
                          E[qsrc].ap()[:, ds(g, 128)]
                          .rearrange('(a p) q -> p a q', p=128)), 'dw%d' % buf, 16)
        T.wait('dw%d' % buf, k.cnt['dw%d' % buf])
        T.wait('dv', k.cnt['dv'])
        T.wait('ac', k.qp_ev[g % 2])
        T.wait('dv', k.qp_evv[g % 2])
        for dt in range(c.DT):
            i = T.matmul(PB[:, 1024 + (g % 2) * 512: 1024 + (g % 2) * 512 + 512],
                         S.w0[:, buf, ds(dt, 128)], S.xw[:, dt, :],
                         start=(dt == 0), stop=(dt == c.DT - 1))
        k.pe_qw[buf] = T.inc(i, 'pe')
        if kind == 'ab':
            V.wait('pe', k.cnt['pe'])
            V.wait('sp%d' % (g % 2), k.qtd_st[g % 2])
            i = V.tensor_scalar(S.pT[:, 2 * (g % 2), :],
                                PB[:, 1024 + (g % 2) * 512:1024 + (g % 2) * 512 + 512],
                                S.cols[:, g:g + 1], None, ALU.add)
            k.qp_evv[g % 2] = V.inc(i, 'dv')
            Y.wait('dv', k.cnt['dv'])
        else:
            C.wait('pe', k.cnt['pe'])
            C.wait('sp%d' % (g % 2), k.qtd_st[g % 2])
            i = C.activation(S.pT[:, 2 * (g % 2), :],
                             PB[:, 1024 + (g % 2) * 512:1024 + (g % 2) * 512 + 512],
                             AF.Copy)
            k.qp_ev[g % 2] = C.inc(i, 'ac')
            Y.wait('ac', k.cnt['ac'])
        Y.wait('dv', k.cnt['dv'])
        Y.wait('ac', k.cnt['ac'])
        k.qtd_st[g % 2] = Y.inc(
            Y.dma_start(E['qTd'].ap()[ds(g, 128), tsl], S.pT[:, 2 * (g % 2), :]),
            'sp%d' % (g % 2), 16)

    k.pe_kw[0] = max(k.pe_kw[0], k.pe_qw[0])
    k.pe_kw[1] = max(k.pe_kw[1], k.pe_qw[1])
    for (wsrc, coff, dstname, aug) in kvspec:
        for kvg in range(4):
            buf = kvg % 2
            P.wait('pe', k.pe_kw[buf])
            P.inc(P.dma_start(S.w0[:, buf, :].rearrange('p (a q) -> p a q', q=256),
                              E[wsrc].ap()[:, coff + kvg * 256: coff + kvg * 256 + 256]
                              .rearrange('(a p) q -> p a q', p=128)), 'dw%d' % buf, 16)
            wv = S.w0[:, buf, :].rearrange('p (a q) -> p a q', q=256)
            for tt in range(4):
                T.wait('dw%d' % buf, k.cnt['dw%d' % buf])
                T.wait('dv', k.cnt['dv'])
                T.wait('dv', k.qp_evv[tt % 2])
                for dt in range(c.DT):
                    i = T.matmul(PB[:, (tt % 2) * 512:(tt % 2) * 512 + 256],
                                 S.xw[:, dt, ds(tt, 128)], wv[:, dt, :],
                                 start=(dt == 0),
                                 stop=(dt == c.DT - 1 and kind != 'ab'))
                if kind == 'ab':
                    brow = S.brow[0:1, 0:c.D] if dstname == 'ktokF' else S.brow[0:1, c.D:2 * c.D]
                    T.wait('dg', k.cnt['dg'])
                    i = T.matmul(PB[:, (tt % 2) * 512:(tt % 2) * 512 + 256],
                                 S.ones1rb[:, :],
                                 brow[:, kvg * 256:kvg * 256 + 256],
                                 start=False, stop=True)
                T.inc(i, 'pe')
                V.wait('pe', k.cnt['pe'])
                if aug:
                    V.wait('sv', k.vev_st)
                    V.inc(V.tensor_copy(
                        S.vev[:, 0:264].rearrange('p (g e) -> p g e', e=66)[:, :, 0:64],
                        PB[:, (tt % 2) * 512:(tt % 2) * 512 + 256]
                        .rearrange('p (g e) -> p g e', e=64)), 'dv')
                    k.qp_evv[tt % 2] = k.cnt['dv']
                    Y.wait('dv', k.cnt['dv'])
                    k.vev_st = Y.inc(Y.dma_start(
                        E[dstname].ap()[s * 512 + tt * 128: s * 512 + tt * 128 + 128,
                                        kvg * 264: kvg * 264 + 264],
                        S.vev[:, 0:264]), 'sv', 16)
                else:
                    V.wait('sp%d' % (tt % 2), k.qtd_st[tt % 2])
                    V.inc(V.tensor_copy(S.pT[:, 2 * (tt % 2), 0:256],
                                        PB[:, (tt % 2) * 512:(tt % 2) * 512 + 256]), 'dv')
                    k.qp_evv[tt % 2] = k.cnt['dv']
                    Y.wait('dv', k.cnt['dv'])
                    k.qtd_st[tt % 2] = Y.inc(Y.dma_start(
                        E[dstname].ap()[s * 512 + tt * 128: s * 512 + tt * 128 + 128,
                                        kvg * 256: kvg * 256 + 256],
                        S.pT[:, 2 * (tt % 2), 0:256]), 'sp%d' % (tt % 2), 16)
            k.pe_kw[buf] = k.cnt['pe']


def emit_attn_sublayer(k, src, dst, kind):
    c = k.c
    emit_prologue(k, src, kind)
    emit_exchange(k, kind == 'aa')
    for qh in range(c.NQH):
        emit_attention_quarter(k, qh, kind)
        if kind == 'aa':
            emit_epilogue_proj(k, qh, 'aa_wout', None, src, dst, heads=True)
        else:
            emit_epilogue_proj(k, qh, 'ab_wo', 'ab_bco', src, dst, heads=False)
    barrier(k)


def emit_exchange(k, aug):
    c, S, E = k.c, k.S, k.E
    P = k.P
    kk, vv = ('ktok', 'vtok') if aug else ('ktokF', 'vtokF')
    kf, vf = ('kfull', 'vfull') if aug else ('kfullF', 'vfullF')
    barrier(k)
    i = P.collective_compute("AllGather", ALU.bypass, replica_groups=c.groups,
                             ins=[E[kk].ap().opt()], outs=[E[kf].ap().opt()])
    P.inc(i, 'cc')
    i = P.collective_compute("AllGather", ALU.bypass, replica_groups=c.groups,
                             ins=[E[vv].ap().opt()], outs=[E[vf].ap().opt()])
    P.inc(i, 'cc')


def emit_attention_quarter(k, qh, kind):
    c, S, E = k.c, k.S, k.E
    Y, P, V, C, T = k.Y, k.P, k.V, k.C, k.T
    PA, PB = k.PA, k.PB
    QH = c.QH
    qsl = ds(qh, QH)
    scale = 0.125 if kind == 'aa' else 1.0 / 32.0
    arena = S.arena[:].rearrange('p (a q) -> p a q', q=QH)

    barrier(k)
    if kind == 'ffn':
        for dt in range(c.DT):
            Y.inc(Y.dma_start(S.qT[:, dt, :], E['qtok'].ap()[qsl, ds(dt, 128)],
                              transpose=True), 'dq', 16)
    else:
        Y.inc(Y.dma_start(S.qT[:], E['qTd'].ap()[:, qsl]
                          .rearrange('(a p) t -> p a t', p=128)), 'dq', 16)
    P.inc(P.memset(S.arena[:], 0.0), 'po')
    P.inc(P.memset(S.rows2[:], 0.0), 'po')
    P.inc(P.memset(S.rowsc[32:33, :], 0.0), 'po')
    P.inc(P.memset(S.rowsc[64:65, :], 0.0), 'po')
    barrier(k)

    kf, vf = ('kfull', 'vfull') if kind == 'aa' else ('kfullF', 'vfullF')
    for sup in range(c.NSUP):
        ksl = slice(sup * c.KSUP, (sup + 1) * c.KSUP)
        barrier(k)
        for dt in range(c.DT):
            Y.inc(Y.dma_start(S.ktT[:, dt, :], E[kf].ap()[ksl, ds(dt, 128)],
                              transpose=True), 'dq', 16)
        if kind == 'aa':
            Y.inc(Y.dma_start(S.vsup[:], E[vf].ap()[ksl, :]
                              .rearrange('(a p) v -> p a v', p=128)), 'dq', 16)
        else:
            Y.inc(Y.dma_start(S.vsup[:, :, 0:1024], E[vf].ap()[ksl, :]
                              .rearrange('(a p) v -> p a v', p=128)), 'dq', 16)
        barrier(k)
        if kind == 'aa':
            emit_super_aa(k, scale)
        else:
            emit_super_single(k, scale, kind, qh, sup)
        barrier(k)


def aa_sum_slot(k, hh, off, width):
    S, c = k.S, k.c
    QH = c.QH
    if hh < 12:
        return S.rows2[32 * (hh // 4):32 * (hh // 4) + 1,
                       (hh % 4) * QH + off:(hh % 4) * QH + off + width]
    h2 = hh - 12
    return S.rowsc[32 * (h2 // 2 + 1):32 * (h2 // 2 + 1) + 1,
                   (h2 % 2) * 1024 + off:(h2 % 2) * 1024 + off + width]


def emit_super_aa(k, scale):
    c, S = k.c, k.S
    V, C, T = k.V, k.C, k.T
    PA, PB = k.PA, k.PB
    QH = c.QH
    arena = S.arena[:].rearrange('p (a q) -> p a q', q=QH)
    pTf = S.pT[:].rearrange('p a b -> p (a b)')
    for qb in range(c.NQBA):
        qbs = ds(qb, c.QBA)
        for pr in range(8):
            hA, hB = 2 * pr, 2 * pr + 1
            def pv(kt):
                par = kt % 2
                T.wait('ac', k.ac_exp[par])
                T.matmul(PB[0:65, 0:512], S.vsup[:, kt, 66 * hA:66 * hA + 65],
                         S.pT[:, 2 * par, :], start=(kt == 0), stop=(kt == c.NKT - 1))
                return T.matmul(PB[0:65, 512:1024],
                                S.vsup[:, kt, 66 * hB:66 * hB + 65],
                                S.pT[:, 2 * par + 1, :],
                                start=(kt == 0), stop=(kt == c.NKT - 1))
            for kt in range(c.NKT):
                par = kt % 2
                T.wait('ac', k.ac_exp[par])
                T.matmul(PA[:, par * 1024: par * 1024 + 512],
                         S.ktT[0:64, pr, ds(kt, 128)],
                         S.qT[0:64, pr, qbs], start=True, stop=True)
                iB = T.matmul(PA[:, par * 1024 + 512: par * 1024 + 1024],
                              S.ktT[64:128, pr, ds(kt, 128)],
                              S.qT[64:128, pr, qbs], start=True, stop=True)
                T.inc(iB, 'pe')
                pe_s = k.cnt['pe']
                C.wait('pe', pe_s)
                i = C.activation(pTf[:, par * 1024:(par + 1) * 1024],
                                 PA[:, par * 1024:(par + 1) * 1024], AF.Exp,
                                 scale=scale)
                k.ac_exp[par] = C.inc(i, 'ac')
                if kt >= 1:
                    pv(kt - 1)
            i = pv(c.NKT - 1)
            T.inc(i, 'pe')
            V.wait('pe', k.cnt['pe'])
            for hh, reg in ((hA, 0), (hB, 1)):
                dt = hh // 2
                rb = (hh % 2) * 64
                V.inc(V.tensor_add(arena[rb:rb + 64, dt, qbs],
                                   PB[0:64, reg * 512:reg * 512 + 512],
                                   arena[rb:rb + 64, dt, qbs]), 'dv')
                slot = aa_sum_slot(k, hh, qb * c.QBA, c.QBA)
                V.inc(V.tensor_add(slot, PB[64:65, reg * 512:reg * 512 + 512],
                                   slot), 'dv')
            T.wait('dv', k.cnt['dv'])


def emit_super_single(k, scale, kind, qh, sup):
    c, S = k.c, k.S
    V, C, T = k.V, k.C, k.T
    PA, PB = k.PA, k.PB
    QH = c.QH
    arena = S.arena[:].rearrange('p (a q) -> p a q', q=QH)
    pTf = S.xw[:].rearrange('p a b -> p (a b)')
    nqb = c.To // c.QB
    for qb in range(c.NQB):
        qbs = ds(qb, c.QB)
        exp_done = [0] * c.NKT
        msk_done = [0] * c.NKT

        def sums(kt):
            if kind == 'ab':
                T.wait('dv', msk_done[kt])
            else:
                T.wait('ac', exp_done[kt])
            return T.matmul(PA[0:1, 1024:1536], S.onesb[:, 0:1],
                            pTf[:, kt * 512:(kt + 1) * 512],
                            start=(kt == 0), stop=(kt == c.NKT - 1))
        for kt in range(c.NKT):
            par = kt % 2
            T.wait('ac', k.ac_exp[par])
            T.wait('dv', k.dv_msk[par])
            for dt in range(c.DT):
                i = T.matmul(PA[:, par * 512: par * 512 + 512],
                             S.ktT[:, dt, ds(kt, 128)], S.qT[:, dt, qbs],
                             start=(dt == 0), stop=(dt == c.DT - 1))
            T.inc(i, 'pe')
            C.wait('pe', k.cnt['pe'])
            i = C.activation(pTf[:, kt * 512:(kt + 1) * 512],
                             PA[:, par * 512: par * 512 + 512], AF.Exp, scale=scale)
            k.ac_exp[par] = exp_done[kt] = C.inc(i, 'ac')
            if kind == 'ab':
                V.wait('ac', k.ac_exp[par])
                kdcol = S.kd[:, (sup * c.NKT + kt) * nqb + qh * c.NQB + qb:
                             (sup * c.NKT + kt) * nqb + qh * c.NQB + qb + 1]
                i = V.scalar_tensor_tensor(pTf[:, kt * 512:(kt + 1) * 512],
                                           S.qrow[:, 0:c.QB], kdcol,
                                           pTf[:, kt * 512:(kt + 1) * 512],
                                           ALU.is_le, ALU.mult)
                k.dv_msk[par] = msk_done[kt] = V.inc(i, 'dv')
            if kt >= 1:
                sums(kt - 1)
        i = sums(c.NKT - 1)
        T.inc(i, 'pe')
        V.wait('pe', k.cnt['pe'])
        V.inc(V.tensor_add(S.rows2[0:1, qb * c.QB:(qb + 1) * c.QB],
                           PA[0:1, 1024:1536],
                           S.rows2[0:1, qb * c.QB:(qb + 1) * c.QB]), 'dv')
        T.wait('dv', k.cnt['dv'])
        T.wait('ac', k.cnt['ac'])
        for kt in range(c.NKT):
            for dt in range(c.DT):
                ps = PA if dt < 4 else PB
                col = (dt % 4) * 512
                i = T.matmul(ps[:, col:col + 512], S.vsup[:, kt, ds(dt, 128)],
                             pTf[:, kt * 512:(kt + 1) * 512],
                             start=(kt == 0), stop=(kt == c.NKT - 1))
        T.inc(i, 'pe')
        V.wait('pe', k.cnt['pe'])
        i = V.tensor_add(arena[:, 0:4, qbs],
                         PA[:].rearrange('p (a q) -> p a q', q=512),
                         arena[:, 0:4, qbs])
        V.inc(i, 'dv')
        i = V.tensor_add(arena[:, 4:8, qbs],
                         PB[:].rearrange('p (a q) -> p a q', q=512),
                         arena[:, 4:8, qbs])
        V.inc(i, 'dv')
        k.ac_exp = [0, 0]
        k.dv_msk = [0, 0]
        T.wait('dv', k.cnt['dv'])


def emit_epilogue_proj(k, qh, wname, bname, src, dst, heads):
    c, S, E = k.c, k.S, k.E
    Y, P, V, C, T = k.Y, k.P, k.V, k.C, k.T
    PA, PB = k.PA, k.PB
    QH = c.QH
    qsl = ds(qh, QH)
    arena = S.arena[:].rearrange('p (a q) -> p a q', q=QH)
    barrier(k)
    if heads:
        for hh in range(16):
            V.inc(V.reciprocal(S.rowsc[0:1, 0:QH], aa_sum_slot(k, hh, 0, QH)), 'dv')
            T.wait('dv', k.cnt['dv'])
            for ch in range(QH // 512):
                i = T.matmul(PB[0:64, (ch % 2) * 512:(ch % 2) * 512 + 512],
                             S.ones1r32[0:1, 0:64], S.rowsc[0:1, ds(ch, 512)],
                             start=True, stop=True)
                T.inc(i, 'pe')
                V.wait('pe', k.cnt['pe'])
                rb = (hh % 2) * 64
                V.inc(V.tensor_mul(S.qT[rb:rb + 64, hh // 2, ds(ch, 512)],
                                   arena[rb:rb + 64, hh // 2, ds(ch, 512)],
                                   PB[0:64, (ch % 2) * 512:(ch % 2) * 512 + 512]), 'dv')
            T.wait('dv', k.cnt['dv'])
    else:
        V.inc(V.reciprocal(S.rowsc[0:1, 0:QH], S.rows2[0:1, 0:QH]), 'dv')
        T.wait('dv', k.cnt['dv'])
        for ch in range(QH // 512):
            i = T.matmul(PB[:, (ch % 2) * 512:(ch % 2) * 512 + 512],
                         S.ones1r32[:, :], S.rowsc[0:1, ds(ch, 512)],
                         start=True, stop=True)
            T.inc(i, 'pe')
            V.wait('pe', k.cnt['pe'])
            for dt in range(c.DT):
                V.inc(V.tensor_mul(S.qT[:, dt, ds(ch, 512)],
                                   arena[:, dt, ds(ch, 512)],
                                   PB[:, (ch % 2) * 512:(ch % 2) * 512 + 512]), 'dv')
            T.wait('dv', k.cnt['dv'])
    if bname:
        Y.inc(Y.dma_start(S.cols[:, 8:16], E[bname].ap()), 'dq', 16)
    barrier(k)
    xs = E[src].ap()
    dd = E[dst].ap()
    for g in range(8):
        buf = g % 2
        if g >= 2:
            P.wait('pe', k.pe_qw[buf])
        P.inc(P.dma_start(S.w0[:, buf, 0:1024].rearrange('p (a q) -> p a q', q=128),
                          E[wname].ap()[:, ds(g, 128)]
                          .rearrange('(a p) q -> p a q', p=128)), 'dw%d' % buf, 16)
        Y.wait('dv', k.cnt['dv'])
        Y.wait('sb', k.bc_st)
        Y.inc(Y.dma_start(S.bc[:], xs[ds(g, 128), qsl]), 'sb', 16)
        T.wait('sb', k.cnt['sb'])
        T.wait('dw%d' % buf, k.cnt['dw%d' % buf])
        for ch in range(QH // 512):
            for dt in range(c.DT):
                i = T.matmul(PA[:, (ch % 2) * 512:(ch % 2) * 512 + 512],
                             S.w0[:, buf, ds(dt, 128)], S.qT[:, dt, ds(ch, 512)],
                             start=(dt == 0), stop=(dt == c.DT - 1))
        k.pe_qw[buf] = T.inc(i, 'pe')
        V.wait('pe', k.cnt['pe'])
        V.wait('sb', k.cnt['sb'])
        bcol = S.cols[:, 8 + g:9 + g] if bname else 0.0
        i = V.scalar_tensor_tensor(S.bc[:], PA[:, 0:1024], bcol, S.bc[:],
                                   ALU.add, ALU.add)
        V.inc(i, 'dv')
        Y.wait('dv', k.cnt['dv'])
        k.bc_st = Y.inc(Y.dma_start(dd[ds(g, 128), qsl], S.bc[:]), 'sb', 16)



def emit_ffn_sublayer(k, src, dst):
    c = k.c
    emit_prologue(k, src, 'ffn')
    for t, dsttok in (('q', 'qtok'), ('k', 'ktokF'), ('v', 'vtokF')):
        emit_fff(k, t, 'xtok', 'xn232', dsttok, AF.Relu)
        if t == 'q' and c.dumps_q:
            barrier(k)
            for nm in c.dumps_q:
                if nm.startswith('S.'):
                    sb = getattr(k.S, nm[2:])
                    k.Y.inc(k.Y.dma_start(k.E['dmpq_' + nm[2:]].ap(), sb[:]),
                            'dq', 16)
                else:
                    k.Y.inc(k.Y.dma_start(k.E['dmpq_' + nm].ap(), k.E[nm].ap()),
                            'dq', 16)
                k.Y.wait('dq', k.cnt['dq'])
            barrier(k)
    emit_exchange(k, False)
    for qh in range(c.NQH):
        emit_attention_quarter(k, qh, 'ffn')
        emit_epilogue_y(k, qh)
    emit_fff(k, 'f', 'ytokM', 'yT32', 'ftok', AF.Gelu)
    emit_final_add(k, src, dst)
    barrier(k)


def emit_epilogue_y(k, qh):
    c, S, E = k.c, k.S, k.E
    Y, V, C, T = k.Y, k.V, k.C, k.T
    PA, PB = k.PA, k.PB
    QH = c.QH
    arena = S.arena[:].rearrange('p (a q) -> p a q', q=QH)
    barrier(k)
    V.inc(V.reciprocal(S.rowsc[0:1, 0:QH], S.rows2[0:1, 0:QH]), 'dv')
    for hf in range(QH // 512):
        s_global = (qh * QH + hf * 512) // 512
        tsl = ds(s_global, 512)
        Y.wait('dq', k.cnt['dq'])
        Y.wait('dv', k.cnt['dv'])
        Y.inc(Y.dma_start(S.x32[:], E['xn232'].ap()[:, tsl]
                          .rearrange('(a p) t -> p a t', p=128)), 'dq', 16)
        T.wait('dv', k.cnt['dv'])
        i = T.matmul(PB[:, 0:512], S.ones1r32[:, :], S.rowsc[0:1, ds(hf, 512)],
                     start=True, stop=True)
        T.inc(i, 'pe')
        V.wait('pe', k.cnt['pe'])
        V.wait('dq', k.cnt['dq'])
        for dt in range(c.DT):
            V.inc(V.tensor_mul(arena[:, dt, ds(hf, 512)], arena[:, dt, ds(hf, 512)],
                               PB[:, 0:512]), 'dv')
            V.inc(V.tensor_add(S.x32[:, dt, :], arena[:, dt, ds(hf, 512)],
                               S.x32[:, dt, :]), 'dv')
            V.inc(V.tensor_copy(S.xw[:, dt, :], S.x32[:, dt, :]), 'dv')
        Y.wait('dv', k.cnt['dv'])
        Y.inc(Y.dma_start(E['yT32'].ap()[:, tsl].rearrange('(a p) t -> p a t', p=128),
                          S.x32[:]), 'dq', 16)
        emit_tokmajor(k, 'ytokM', s_global)
    barrier(k)


def emit_final_add(k, src, dst):
    c, S, E = k.c, k.S, k.E
    Y, V, T = k.Y, k.V, k.T
    qt32 = k.S.qT[:].rearrange('p a b -> p (a b)').bitcast(F32)
    barrier(k)
    for s in range(c.ST):
        tsl = ds(s, 512)
        barrier(k)
        Y.inc(Y.dma_start(S.x32[:], E[src].ap()[:, tsl]
                          .rearrange('(a p) t -> p a t', p=128)), 'dq', 16)
        Y.inc(Y.dma_start(qt32.rearrange('p (a t) -> p a t', t=512),
                          E['yT32'].ap()[:, tsl]
                          .rearrange('(a p) t -> p a t', p=128)), 'dq', 16)
        for dt in range(c.DT):
            Y.inc(Y.dma_start(S.xw[:, dt, :], E['ftok'].ap()[tsl, ds(dt, 128)],
                              transpose=True), 'dq', 16)
        V.wait('dq', k.cnt['dq'])
        for dt in range(c.DT):
            V.inc(V.tensor_add(S.x32[:, dt, :], S.x32[:, dt, :],
                               qt32.rearrange('p (a t) -> p a t', t=512)[:, dt, :]), 'dv')
            V.inc(V.tensor_add(S.x32[:, dt, :], S.x32[:, dt, :], S.xw[:, dt, :]), 'dv')
        Y.wait('dv', k.cnt['dv'])
        Y.inc(Y.dma_start(E[dst].ap()[:, tsl].rearrange('(a p) t -> p a t', p=128),
                          S.x32[:]), 'dq', 16)



def emit_fff(k, tree, srctok, src32, dsttok, act):
    c, S, E = k.c, k.S, k.E
    Y, P, V, C, T = k.Y, k.P, k.V, k.C, k.T
    PA, PB = k.PA, k.PB
    AOFF = 0
    arenaF = S.arena[:]
    lgT0 = arenaF[:, 0:c.To]
    lgT1 = arenaF[:, c.To:2 * c.To]
    mask = arenaF[:, 2 * c.To:3 * c.To] if 3 * c.To <= c.DT * c.QH else None
    x32f = S.x32[:].rearrange('p a b -> p (a b)')
    barrier(k)
    P.inc(P.dma_start(S.nwT[:, :, 0:255].rearrange('p a n -> p a n'),
                      E['f%s_nwT' % tree].ap().rearrange('(a p) n -> p a n', p=128)),
          'dg', 16)
    Y.inc(Y.dma_start(S.nb[:], E['f%s_nb' % tree].ap()), 'dq', 16)
    P.inc(P.memset(S.arena[:, 0:2 * c.To], 0.0), 'po')
    barrier(k)

    for s in range(c.ST):
        tsl = ds(s, 512)
        barrier(k)
        Y.inc(Y.dma_start(S.x32[:], E[src32].ap()[:, tsl]
                          .rearrange('(a p) t -> p a t', p=128)), 'dq', 16)
        xr = S.xr[:]
        V.wait('dq', k.cnt['dq'])
        for dt in range(c.DT):
            V.inc(V.tensor_copy(xr[:, dt, :], S.x32[:, dt, :]), 'dv')
        T.wait('dv', k.cnt['dv'])
        T.wait('ac', k.cnt['ac'])
        for half in range(2):
            ncols = 128 if half == 0 else 127
            for dt in range(c.DT):
                i = T.matmul(PA[0:ncols, half * 512: half * 512 + 512],
                             S.nwT[:, dt, half * 128: half * 128 + ncols],
                             xr[:, dt, :],
                             start=(dt == 0), stop=(dt == c.DT - 1))
            k.rt_pe[half] = T.inc(i, 'pe')
            V.wait('pe', k.cnt['pe'])
            dstl = lgT0 if half == 0 else lgT1
            i = V.tensor_scalar(dstl[:, tsl] if half == 0 else dstl[0:127, tsl],
                                PA[:, half * 512: half * 512 + 512] if half == 0
                                else PA[0:127, half * 512: half * 512 + 512],
                                S.nb[:, half:half + 1] if half == 0
                                else S.nb[0:127, half:half + 1],
                                None, ALU.add)
            V.inc(i, 'dv')
    barrier(k)

    r_pref = S.rows2[0:1, 0:c.To]
    r_b = S.rowsc[0:1, 0:512]
    NCH = c.To // 512
    V.inc(V.tensor_single_scalar(r_pref, lgT0[0:1, :], 0.0, ALU.is_gt), 'dv')
    for lv in range(1, 8):
        off = (1 << lv) - 1
        segs = [(lgT0, float(off))]
        if lv == 7:
            segs = [(lgT0, 127.0), (lgT1, -1.0)]
        units = [(chix, si) for chix in range(NCH) for si in range(len(segs))]
        nu = len(units)
        dv_A = [0] * nu; pe_B = [0] * nu; dv_C = [0] * nu; pe_D = [0] * nu

        def stA(u):
            chix, si = units[u]
            sl = u % 2
            if u >= 2:
                V.wait('pe', pe_B[u - 2])
            V.inc(V.tensor_scalar_add(S.rowsc[0:1, 1024 + sl * 512:1536 + sl * 512],
                                      r_pref[:, ds(chix, 512)], segs[si][1]), 'dv')
            dv_A[u] = k.cnt['dv']

        def stB(u):
            sl = u % 2
            T.wait('dv', dv_A[u])
            i = T.matmul(PB[:, sl * 512:sl * 512 + 512], S.ones1r32[:, :],
                         S.rowsc[0:1, 1024 + sl * 512:1536 + sl * 512],
                         start=True, stop=True)
            T.inc(i, 'pe')
            pe_B[u] = k.cnt['pe']

        def stC(u):
            chix, si = units[u]
            sl = u % 2
            V.wait('pe', pe_B[u])
            if si == 1:
                V.wait('pe', pe_D[u - 1])
            i = V.scalar_tensor_tensor(x32f[:, ds(chix, 512)],
                                       PB[:, sl * 512:sl * 512 + 512],
                                       S.iota[:, 0:1], segs[si][0][:, ds(chix, 512)],
                                       ALU.is_equal, ALU.mult)
            V.inc(i, 'dv')
            dv_C[u] = k.cnt['dv']

        def stD(u):
            chix, si = units[u]
            T.wait('dv', dv_C[u])
            i = T.matmul(PB[0:1, 1024 + (chix % 2) * 512:1536 + (chix % 2) * 512],
                         S.ones32c[:], x32f[:, ds(chix, 512)],
                         start=(si == 0), stop=(si == len(segs) - 1))
            T.inc(i, 'pe')
            pe_D[u] = k.cnt['pe']

        def stE(u):
            chix, si = units[u]
            V.wait('pe', pe_D[u])
            V.inc(V.tensor_single_scalar(r_b, PB[0:1, 1024 + (chix % 2) * 512:
                                                   1536 + (chix % 2) * 512],
                                         0.0, ALU.is_gt), 'dv')
            V.inc(V.scalar_tensor_tensor(r_pref[:, ds(chix, 512)],
                                         r_pref[:, ds(chix, 512)], 2.0,
                                         r_b, ALU.mult, ALU.add), 'dv')

        for step in range(nu + 2):
            if step < nu:
                stA(step); stB(step)
            if 1 <= step <= nu:
                stC(step - 1); stD(step - 1)
            if 2 <= step <= nu + 1:
                u = step - 2
                if units[u][1] == len(segs) - 1:
                    stE(u)

    barrier(k)
    emit_sort(k, tree)
    emit_fff_tiles(k, tree, srctok, dsttok, act)


def emit_sort(k, tree):
    c, S, E = k.c, k.S, k.E
    Y, P, V, C, T = k.Y, k.P, k.V, k.C, k.T
    PA, PB = k.PA, k.PB
    arenaF = S.arena[:]
    lgT0 = arenaF[:, 0:c.To]
    lgT1 = arenaF[:, c.To:2 * c.To]
    x32f = S.x32[:].rearrange('p a b -> p (a b)')[:, 0:c.To]
    r_leaf = S.rows2[0:1, 0:c.To]
    r_d = S.rows2[32:33, 0:c.To]
    x32row = S.x32[0:1, :, :].rearrange('p a b -> p (a b)')[:, 0:c.To]
    NCH = c.To // 512
    rsc_c = E['rsc'].reshape([8192, 1])
    rsc_r = E['rsc'].reshape([1, 8192])

    for H in range(2):
        for chix in range(NCH):
            chs = ds(chix, 512)
            V.inc(V.tensor_scalar_add(S.rowsc[0:1, 1024:1536], r_leaf[:, chs],
                                      -128.0 * H), 'dv')
            T.wait('dv', k.cnt['dv'])
            i = T.matmul(PB[:, 0:512], S.ones1r32[:, :], S.rowsc[0:1, 1024:1536],
                         start=True, stop=True)
            T.inc(i, 'pe')
            V.wait('pe', k.cnt['pe'])
            V.inc(V.tensor_scalar(lgT0[:, chs], PB[:, 0:512], S.iota[:, 0:1], None,
                                  ALU.is_equal), 'dv')
        V.inc(V.tensor_tensor_scan(x32f, lgT0, lgT0, 0.0, ALU.add, ALU.bypass), 'dv')
        V.inc(V.tensor_copy(S.cols[:, 8 + H:9 + H], x32f[:, c.To - 1:c.To]), 'dv')
        for chix in range(NCH):
            chs = ds(chix, 512)
            V.inc(V.tensor_mul(lgT1[:, chs], lgT0[:, chs], x32f[:, chs]), 'dv')
            T.wait('dv', k.cnt['dv'])
            i = T.matmul(PA[0:1, 1024:1536], S.ones32c[:],
                         lgT1[:, chs], start=True, stop=True)
            T.inc(i, 'pe')
            V.wait('pe', k.cnt['pe'])
            if H == 0:
                V.inc(V.tensor_copy(r_d[:, chs], PA[0:1, 1024:1536]), 'dv')
            else:
                V.inc(V.tensor_add(r_d[:, chs], PA[0:1, 1024:1536], r_d[:, chs]), 'dv')
            T.wait('dv', k.cnt['dv'])
        ncol = S.cols[:, 8 + H:9 + H]
        ti = S.cols[:, 14:15].bitcast(I32)
        V.inc(V.tensor_scalar_add(S.cols[:, 10 + H:11 + H], ncol, 15.0), 'dv')
        V.inc(V.tensor_copy(ti, S.cols[:, 10 + H:11 + H]), 'dv')
        V.inc(V.tensor_single_scalar(ti, ti, -16, ALU.bitwise_and), 'dv')
        V.inc(V.tensor_copy(S.cols[:, 10 + H:11 + H], ti), 'dv')
        Y.wait('dv', k.cnt['dv'])
        Y.inc(Y.dma_start(rsc_c.ap()[H * 128:(H + 1) * 128, :],
                          S.cols[:, 10 + H:11 + H]), 'dq', 16)
    Y.wait('dq', k.cnt['dq'])
    Y.inc(Y.dma_start(S.rowsc[0:1, 0:256], rsc_r.ap()[:, 0:256]), 'dq', 16)
    V.wait('dq', k.cnt['dq'])
    V.inc(V.tensor_tensor_scan(S.rowsc[0:1, 256:512], S.rowsc[0:1, 0:256],
                               S.rowsc[0:1, 0:256], 0.0, ALU.add, ALU.bypass), 'dv')
    V.inc(V.tensor_sub(S.rowsc[0:1, 256:512], S.rowsc[0:1, 256:512],
                       S.rowsc[0:1, 0:256]), 'dv')
    Y.wait('dv', k.cnt['dv'])
    Y.inc(Y.dma_start(rsc_r.ap()[:, 256:512], S.rowsc[0:1, 256:512]), 'dq', 16)
    Y.wait('dq', k.cnt['dq'])
    for H in range(2):
        Y.inc(Y.dma_start(S.cols[:, 12 + H:13 + H],
                          rsc_c.ap()[256 + 128 * H:256 + 128 * H + 128, :]), 'dq', 16)
    V.wait('dq', k.cnt['dq'])
    for H in range(2):
        offcol = S.cols[:, 12 + H:13 + H]
        for chix in range(NCH):
            chs = ds(chix, 512)
            V.inc(V.tensor_scalar_add(S.rowsc[0:1, 1024:1536], r_leaf[:, chs],
                                      -128.0 * H), 'dv')
            T.wait('dv', k.cnt['dv'])
            i = T.matmul(PB[:, 0:512], S.ones1r32[:, :], S.rowsc[0:1, 1024:1536],
                         start=True, stop=True)
            T.inc(i, 'pe')
            V.wait('pe', k.cnt['pe'])
            V.inc(V.tensor_scalar(lgT0[:, chs], PB[:, 0:512], S.iota[:, 0:1], None,
                                  ALU.is_equal), 'dv')
            V.inc(V.tensor_scalar(lgT0[:, chs], lgT0[:, chs], offcol, None, ALU.mult), 'dv')
            T.wait('dv', k.cnt['dv'])
            i = T.matmul(PA[0:1, 1024:1536], S.ones32c[:],
                         lgT0[:, chs], start=True, stop=True)
            T.inc(i, 'pe')
            V.wait('pe', k.cnt['pe'])
            V.inc(V.tensor_add(r_d[:, chs], PA[0:1, 1024:1536], r_d[:, chs]), 'dv')
            T.wait('dv', k.cnt['dv'])
    V.inc(V.tensor_scalar_add(r_leaf, r_d, -1.0), 'dv')
    di = lgT0[0:1, 0:c.To].bitcast(I32)
    V.inc(V.tensor_copy(di, r_leaf), 'dv')
    V.inc(V.tensor_single_scalar(di, di, 127, ALU.bitwise_and), 'dv')
    V.inc(V.tensor_copy(x32row, di), 'dv')
    V.inc(V.tensor_sub(r_leaf, r_leaf, x32row), 'dv')
    V.inc(V.tensor_scalar(r_leaf, r_leaf, 1.0 / 128.0, None, ALU.mult), 'dv')
    V.inc(V.tensor_scalar(x32row, x32row, float(c.NTILE), None, ALU.mult), 'dv')
    V.inc(V.tensor_add(r_leaf, r_leaf, x32row), 'dv')
    nvt = c.To // 128
    Y.wait('dv', k.cnt['dv'])
    Y.inc(Y.dma_start(rsc_r.ap()[:, 0:c.To], r_leaf), 'dq', 16)
    Y.wait('dq', k.cnt['dq'])
    Y.inc(Y.dma_start(S.bc[0:nvt, 0:128], E['rsc'].reshape([64, 128]).ap()[0:nvt, :]), 'dq', 16)
    T.wait('dq', k.cnt['dq'])
    i = T.matmul(PB[:, 0:nvt], S.bc[0:nvt, 0:128],
                 S.iden32[0:nvt, 0:nvt], start=True, stop=True)
    T.inc(i, 'pe')
    V.wait('pe', k.cnt['pe'])
    V.inc(V.tensor_copy(S.dstc[:, 0:nvt], PB[:, 0:nvt]), 'dv')
    P.wait('dv', k.cnt['dv'])
    P.inc(P.dma_start(E['slotTab'].ap(), E['c_oob'].ap()), 'dg', 16)
    P.wait('dg', k.cnt['dg'])
    for i_t in range(nvt):
        P.wait('dg', k.cnt['dg'])
        P.inc(P.indirect_dma_start(
            rsc_i32(k), IndirectOffsetOnAxis(ap=S.dstc[:, i_t:i_t + 1], axis=0),
            S.tki[:, i_t:i_t + 1], None), 'dg', 16)
    P.wait('dg', k.cnt['dg'])
    P.inc(P.dma_start(S.srt[:], E['slotTab'].reshape([128, c.NTILE]).ap()), 'dg', 16)
    for H in range(2):
        offcol = S.cols[:, 12 + H:13 + H]
        T.wait('dv', k.cnt['dv'])
        i = T.matmul(PB[:, 0:c.NCHUNK], S.ones1r32[:, :], S.srow[0:1, 0:c.NCHUNK],
                     start=True, stop=True)
        T.inc(i, 'pe')
        V.wait('pe', k.cnt['pe'])
        V.inc(V.tensor_scalar(lgT0[:, 0:c.NCHUNK], PB[:, 0:c.NCHUNK], offcol, None,
                              ALU.is_ge), 'dv')
        T.wait('dv', k.cnt['dv'])
        i = T.matmul(PA[0:1, 1024:1024 + c.NCHUNK], S.ones32c[:],
                     lgT0[:, 0:c.NCHUNK],
                     start=(H == 0), stop=(H == 1))
        T.inc(i, 'pe')
    V.wait('pe', k.cnt['pe'])
    V.inc(V.tensor_scalar_add(S.rowsc[0:1, 1024:1024 + c.NCHUNK], PA[0:1, 1024:1024 + c.NCHUNK],
                              -1.0), 'dv')
    Y.wait('dv', k.cnt['dv'])
    Y.inc(Y.dma_start(rsc_r.ap()[:, 0:c.NCHUNK], S.rowsc[0:1, 1024:1024 + c.NCHUNK]), 'dq', 16)
    Y.wait('dq', k.cnt['dq'])
    Y.inc(Y.dma_start(S.bc[0:c.NTILE, 0:8], E['rsc'].reshape([1024, 8]).ap()[0:c.NTILE, :]), 'dq', 16)
    V.wait('dq', k.cnt['dq'])
    V.inc(V.tensor_copy(S.ysl[0:c.NTILE, 0, 0:8], S.bc[0:c.NTILE, 0:8]), 'dv')
    T.wait('dv', k.cnt['dv'])
    i = T.matmul(PB[0:8, 0:c.NTILE], S.ysl[0:c.NTILE, 0, 0:8],
                 S.iden[0:c.NTILE, 0:c.NTILE], start=True, stop=True)
    T.inc(i, 'pe')
    V.wait('pe', k.cnt['pe'])
    V.inc(V.tensor_copy(S.clrf[:], PB[0:8, 0:c.NTILE]), 'dv')
    P.wait('dv', k.cnt['dv'])
    P.inc(P.memset(S.clri[:], 255), 'po')
    V.wait('po', k.cnt['po'])
    V.inc(V.tensor_copy(S.clri[0:8, :], S.clrf[:]), 'dv')
    V.inc(V.tensor_copy(S.ysl[0:8, 0, 0:c.NTILE], S.clrf[:]), 'dv')
    T.wait('dv', k.cnt['dv'])
    i = T.matmul(PB[:, 1024:1024 + c.NTILE], S.indic[:], S.ysl[0:8, 0, 0:c.NTILE],
                 start=True, stop=True)
    T.inc(i, 'pe')
    V.wait('pe', k.cnt['pe'])
    V.inc(V.tensor_scalar(S.bc[:, 0:c.NTILE], PB[:, 1024:1024 + c.NTILE], 16.0,
                          S.pmod[:, 0:1], ALU.mult, ALU.add), 'dv')
    V.inc(V.tensor_copy(S.idxw2[:], S.bc[:, 0:c.NTILE]), 'dv')
    cs = min(512, c.NCHUNK)
    for chix in range(c.NCHUNK // cs):
        chs = ds(chix, cs)
        T.wait('dv', k.cnt['dv'])
        i = T.matmul(PB[:, 0:cs], S.ones1r32[:, :], S.rowsc[0:1, 1024:1024 + c.NCHUNK][:, chs],
                     start=True, stop=True)
        T.inc(i, 'pe')
        V.wait('pe', k.cnt['pe'])
        V.inc(V.tensor_scalar(S.bc[:, 0:cs], PB[:, 0:cs], 128.0, S.iota[:, 0:1],
                              ALU.mult, ALU.add), 'dv')
        V.inc(V.tensor_copy(S.idxw1[:, chs], S.bc[:, 0:cs]), 'dv')
    barrier(k)


def rsc_i32(k):
    return k.E['slotTab'].reshape([k.c.SLOTS, 1]).ap()


def emit_fff_tiles(k, tree, srctok, dsttok, act):
    c, S, E = k.c, k.S, k.E
    Y, P, V, C, T = k.Y, k.P, k.V, k.C, k.T
    PA, PB = k.PA, k.PB
    for g in range(c.NTILE // 4):
        hb = g % 2
        if g >= 2:
            P.wait('gx%d' % hb, k.wtok_fff[hb])
        for j in range(4):
            i_t = 4 * g + j
            P.inc(P.indirect_dma_start(
                S.qT[:, 4 * hb + j, :], None, E[srctok].ap(),
                IndirectOffsetOnAxis(ap=S.srt[:, i_t:i_t + 1], axis=0),
                bounds_check=k.breg, oob_is_err=False), 'gg%d' % hb, 16)
        Y.wait('gg%d' % hb, k.cnt['gg%d' % hb])
        k.wtok_fff[hb] = Y.inc(
            Y.dma_start(E['xsorted'].ap()[ds(g, 512), :]
                        .rearrange('(j p) d -> p j d', p=128),
                        S.qT[:, 4 * hb:4 * hb + 4, :]),
            'gx%d' % hb, 16)
    barrier(k)
    dstap = E[dsttok].ap()[:, 0:1024]
    tok_pe = [0, 0, 0, 0]
    tok_ac = [0, 0]
    tok_msk = [0, 0]
    tok_dvy = [0, 0]
    y_pe = [0, 0]
    dq_blk = [0] * c.NBLK
    for dt in range(c.DT):
        Y.inc(Y.dma_start(S.qT[:, dt, 0:1024],
                          E['xsorted'].ap()[ds(0, 1024), ds(dt, 128)],
                          transpose=True), 'dq', 16)
    dq_blk[0] = k.cnt['dq']

    def gath(it):
        buf = it % 4
        P.wait('pe', tok_pe[buf])
        gsem = 'gt%d' % buf
        for cc in range(8):
            P.inc(P.indirect_dma_start(
                S.ktT[:, buf, 128 * cc:128 * cc + 128], None,
                E['w1gB_' + tree].ap(),
                IndirectOffsetOnAxis(ap=S.idxw1[:, it * 8 + cc:it * 8 + cc + 1], axis=0)), gsem, 16)
        P.inc(P.indirect_dma_start(
            S.vsup[:, buf, 0:1025], None, E['w2gB_' + tree].ap(),
            IndirectOffsetOnAxis(ap=S.idxw2[:, it:it + 1], axis=0)), gsem, 16)
        P.inc(P.indirect_dma_start(
            S.vsup[0:8, 4 + buf, 0:1024], None, E['b2B_' + tree].ap(),
            IndirectOffsetOnAxis(ap=S.clri[0:8, it:it + 1], axis=0)), gsem, 16)

    for it in range(min(3, c.NTILE)):
        gath(it)
    for it in range(c.NTILE):
        blk, ti = it // 8, it % 8
        buf = it % 4
        pp = it % 2
        gsem = 'gt%d' % buf
        if it + 3 < c.NTILE:
            gath(it + 3)
        V.wait(gsem, k.cnt[gsem])
        i = V.tensor_copy(
            S.ktT[:, 4 + buf, :].rearrange('p (a b j) -> p a b j', a=8, j=16),
            S.ktT[:, buf, :].rearrange('p (a b j) -> p b a j', a=8, j=16))
        V.inc(i, 'dv')
        i = V.tensor_copy(S.b1sb[:, buf, :], S.vsup[:, buf, 1024:1025])
        k.w1cp[buf] = V.inc(i, 'dv')
        if ti == 0:
            T.wait('dq', dq_blk[blk])
        T.wait(gsem, k.cnt[gsem])
        T.wait('dv', k.w1cp[buf])
        T.wait('ac', tok_ac[pp])
        for dt in range(c.DT):
            i = T.matmul(PA[:, pp * 512: pp * 512 + 128],
                         S.ktT[:, 4 + buf, ds(dt, 128)], S.qT[:, dt, ds(ti, 128)],
                         start=(dt == 0), stop=(dt == c.DT - 1))
        hpe = T.inc(i, 'pe')
        if ti == 7 and blk + 1 < c.NBLK:
            Y.wait('pe', hpe)
            for dt in range(c.DT):
                Y.inc(Y.dma_start(S.qT[:, dt, 0:1024],
                                  E['xsorted'].ap()[ds(blk + 1, 1024), ds(dt, 128)],
                                  transpose=True), 'dq', 16)
            dq_blk[blk + 1] = k.cnt['dq']
        C.wait('pe', k.cnt['pe'])
        C.wait(gsem, k.cnt[gsem])
        C.wait('dv', k.w1cp[buf])
        C.wait('dv', tok_msk[pp])
        i = C.activation(S.hA[:, pp, :], PA[:, pp * 512: pp * 512 + 128],
                         act, bias=S.b1sb[:, buf, :])
        tok_ac[pp] = C.inc(i, 'ac')
        V.wait('ac', k.cnt['ac'])
        V.wait('pe', y_pe[pp])
        i = V.tensor_mul(S.hT[:, pp, :], S.hA[:, pp, :], S.mdiag[:])
        tok_msk[pp] = V.inc(i, 'dv')
        T.wait('dv', k.cnt['dv'])
        T.wait('dv', tok_dvy[pp])
        pb0 = pp * 1024
        T.matmul(PB[:, pb0:pb0 + 512], S.hT[:, pp, :], S.vsup[:, buf, 0:512],
                 start=True, stop=False)
        T.matmul(PB[:, pb0 + 512:pb0 + 1024], S.hT[:, pp, :],
                 S.vsup[:, buf, 512:1024], start=True, stop=False)
        T.matmul(PB[:, pb0:pb0 + 512], S.indic[:], S.vsup[0:8, 4 + buf, 0:512],
                 start=False, stop=True)
        i = T.matmul(PB[:, pb0 + 512:pb0 + 1024], S.indic[:],
                     S.vsup[0:8, 4 + buf, 512:1024], start=False, stop=True)
        tok_pe[buf] = y_pe[pp] = T.inc(i, 'pe')
        V.wait('pe', k.cnt['pe'])
        V.wait('gy%d' % pp, k.ysl_tok[pp])
        i = V.tensor_copy(S.ysl[:, pp, :], PB[:, pb0:pb0 + 1024])
        tok_dvy[pp] = V.inc(i, 'dv')
        P.wait('dv', k.cnt['dv'])
        k.ysl_tok[pp] = P.inc(P.indirect_dma_start(
            dstap, IndirectOffsetOnAxis(ap=S.srt[:, it:it + 1], axis=0),
            S.ysl[:, pp, :], None,
            bounds_check=k.breg, oob_is_err=False), 'gy%d' % pp, 16)
        if c.dbg_serial:
            barrier(k)
    barrier(k)



def kernel(**inputs):
    cfg = Cfg()
    k = build(cfg)
    emit_program(k, script)
    k.es.close()
    in_maps = [prep_core_inputs(inputs, core, cfg) for core in range(8)]
    res = run_bass_kernel_spmd(k.nc, in_maps, core_ids=list(range(8)))
    out = np.empty((4, 8192, 1024), np.float32)
    for core in range(8):
        b, h = core // 2, core % 2
        out[b, h * cfg.To:(h + 1) * cfg.To, :] = res.results[core]['outT'].T
    return out


# version 11
# speedup vs baseline: 1.5149x; 1.0050x over previous
import numpy as np
import ml_dtypes
from contextlib import ExitStack

import concourse.bass as bass
import concourse.mybir as mybir
from concourse.bass_utils import run_bass_kernel_spmd
from concourse.bass_core_rust import IndirectOffsetOnAxis

F32 = mybir.dt.float32
F32R = mybir.dt.float32r
BF16 = mybir.dt.bfloat16
I32 = mybir.dt.int32
AF = mybir.ActivationFunctionType
ALU = mybir.AluOpType
AX = mybir.AxisListType
BF = ml_dtypes.bfloat16
SQD = float(np.sqrt(1024.0))


class Cfg:
    def __init__(self, To=4096, Tf=8192, groups=None, parts=(1, 2, 3, 4),
                 dumps=(), dumps_q=(), dbg_serial=False):
        self.dumps = dumps
        self.dumps_q = dumps_q
        self.dbg_serial = dbg_serial
        self.To = To; self.Tf = Tf
        self.D = 1024; self.DT = 8; self.H = 16
        self.KSUP = min(1024, Tf); self.NSUP = Tf // self.KSUP
        self.NKT = self.KSUP // 128
        self.QH = min(1024, To); self.NQH = To // self.QH
        self.QB = 512
        self.NQB = self.QH // self.QB
        self.QBA = 512
        self.NQBA = self.QH // self.QBA
        self.NNODE = 255; self.NLEAF = 256
        self.SLOTS = ((To + 15 * 256 + 1023) // 1024) * 1024
        self.NTILE = self.SLOTS // 128
        self.NCHUNK = self.SLOTS // 16
        self.NBLK = self.SLOTS // 1024
        self.ST = To // 512
        self.NTT = To // 128
        self.groups = groups if groups is not None else [[0, 1], [2, 3], [4, 5], [6, 7]]
        self.parts = parts


def ds(i, n):
    return slice(i * n, (i + 1) * n)



def prep_core_inputs(inputs, core, cfg):
    c = cfg
    b, h = core // 2, core % 2
    x = np.asarray(inputs['L0_attn_a_x'])
    d = {'xT': np.ascontiguousarray(x[b, h * c.To:(h + 1) * c.To, :].T)}
    d['aa_wqkv'] = np.asarray(inputs['L0_attn_a_w_qkv'])
    d['aa_wout'] = np.asarray(inputs['L0_attn_a_w_out'])
    d['aa_gcol'] = np.ascontiguousarray(
        np.asarray(inputs['L0_attn_a_gamma']).reshape(8, 128).T)
    for nm in 'qkvo':
        W = np.asarray(inputs['L0_attn_b_W' + nm])
        d['ab_w' + nm] = np.ascontiguousarray(W.T)
        bb = np.asarray(inputs['L0_attn_b_b' + nm])
        d['ab_bc' + nm] = np.ascontiguousarray(bb.reshape(8, 128).T)
        d['ab_br' + nm] = np.ascontiguousarray(bb.reshape(1, 1024))
    for t in 'qkvf':
        g = lambda s: np.asarray(inputs['L0_ffn_%s_%s' % (t, s)])
        d['f%s_nwT' % t] = np.ascontiguousarray(g('nw').T)
        nb = g('nb')
        nbp = np.zeros((128, 2), np.float32)
        nbp[:, 0] = nb[0:128]
        nbp[0:127, 1] = nb[128:255]
        d['f%s_nb' % t] = nbp
        d['f%s_w1g' % t] = np.ascontiguousarray(
            g('w1').reshape(256, 8, 128, 16).transpose(0, 2, 1, 3).reshape(256, 128, 128))
        d['f%s_b1' % t] = np.ascontiguousarray(g('b1').reshape(4096, 1))
        d['f%s_w2g' % t] = np.ascontiguousarray(np.concatenate(
            [g('w2').reshape(4096, 1024),
             g('b1').reshape(4096, 1).astype(np.float32)], axis=1))
        d['f%s_b2' % t] = np.asarray(g('b2'))
    d['c_iota'] = np.arange(128, dtype=np.float32).reshape(128, 1)
    d['c_pmod'] = (np.arange(128, dtype=np.float32) % 16).reshape(128, 1)
    d['c_tokid'] = np.ascontiguousarray(
        np.arange(c.To, dtype=np.float32).reshape(c.To // 128, 128).T)
    d['c_oob'] = np.full((c.SLOTS,), 1 << 20, np.int32)
    jj, ss = np.meshgrid(np.arange(128), np.arange(128), indexing='ij')
    d['c_maskdiag'] = ((jj // 16) == (ss // 16)).astype(BF)
    ind = np.zeros((8, 128), np.float32)
    for cc in range(8):
        ind[cc, cc * 16:(cc + 1) * 16] = 1.0
    d['c_indic8'] = ind.astype(BF)
    nkt, nqb = c.Tf // 128, c.To // c.QB
    kd = np.zeros((128, nkt * nqb), np.float32)
    for kt in range(nkt):
        for qb in range(nqb):
            kd[:, kt * nqb + qb] = np.arange(128) + 128 * kt - (qb * c.QB + h * c.To) + 16
    d['c_kd'] = kd
    d['c_qrow'] = np.tile(np.arange(c.QB, dtype=np.float32), (128, 1))
    d['c_srow'] = (16.0 * np.arange(c.NCHUNK, dtype=np.float32)).reshape(1, c.NCHUNK)
    return d



class NS:
    pass


def build(cfg):
    c = cfg
    nc = bass.Bass("TRN2", target_bir_lowering=False)
    k = NS(); k.c = c; k.nc = nc; k.E = {}

    def dram(name, shape, dt, kind="Internal"):
        k.E[name] = nc.dram_tensor(name, list(shape), dt, kind=kind)

    dram('xT', (c.D, c.To), F32, "ExternalInput")
    dram('aa_wqkv', (c.D, 3072), F32, "ExternalInput")
    dram('aa_wout', (c.D, c.D), F32, "ExternalInput")
    dram('aa_gcol', (128, 8), F32, "ExternalInput")
    for nm in 'qkvo':
        dram('ab_w' + nm, (c.D, c.D), F32, "ExternalInput")
        dram('ab_bc' + nm, (128, 8), F32, "ExternalInput")
        dram('ab_br' + nm, (1, c.D), F32, "ExternalInput")
    for t in 'qkvf':
        dram('f%s_nwT' % t, (c.D, c.NNODE), F32, "ExternalInput")
        dram('f%s_nb' % t, (128, 2), F32, "ExternalInput")
        dram('f%s_w1g' % t, (c.NLEAF, 128, 128), F32, "ExternalInput")
        dram('f%s_b1' % t, (4096, 1), F32, "ExternalInput")
        dram('f%s_w2g' % t, (4096, c.D + 1), F32, "ExternalInput")
        dram('f%s_b2' % t, (c.NLEAF, c.D), F32, "ExternalInput")
    dram('c_iota', (128, 1), F32, "ExternalInput")
    dram('c_pmod', (128, 1), F32, "ExternalInput")
    dram('c_tokid', (128, c.To // 128), F32, "ExternalInput")
    dram('c_oob', (c.SLOTS,), I32, "ExternalInput")
    dram('c_maskdiag', (128, 128), BF16, "ExternalInput")
    dram('c_indic8', (8, 128), BF16, "ExternalInput")
    dram('c_kd', (128, (c.Tf // 128) * (c.To // c.QB)), F32, "ExternalInput")
    dram('c_qrow', (128, c.QB), F32, "ExternalInput")
    dram('c_srow', (1, c.NCHUNK), F32, "ExternalInput")
    dram('outT', (c.D, c.To), F32, "ExternalOutput")

    dram('resid', (c.D, c.To), F32)
    dram('xn232', (c.D, c.To), F32)
    dram('yT32', (c.D, c.To), F32)
    dram('xtok', (c.To, c.D), BF16)
    dram('ytokM', (c.To, c.D), BF16)
    dram('xsorted', (c.SLOTS, c.D), BF16)
    dram('qTd', (c.D, c.To), BF16)
    dram('qtok', (c.To, c.D), BF16)
    dram('ktok', (c.To, 1056), BF16)
    dram('vtok', (c.To, 1056), BF16)
    dram('kfull', (c.Tf, 1056), BF16)
    dram('vfull', (c.Tf, 1056), BF16)
    dram('ktokF', (c.To, 1024), BF16)
    dram('vtokF', (c.To, 1024), BF16)
    dram('kfullF', (c.Tf, 1024), BF16)
    dram('vfullF', (c.Tf, 1024), BF16)
    dram('ftok', (c.To, c.D), BF16)
    dram('slotTab', (c.SLOTS,), I32)
    dram('rsc', (8192,), F32)
    for t in 'qkvf':
        dram('w1gB_' + t, (c.NLEAF * 128, 128), BF16)
        dram('w2gB_' + t, (4096, c.D + 1), BF16)
        dram('b2B_' + t, (c.NLEAF, c.D), BF16)
    for nm in c.dumps:
        h = k.E[nm]
        dram('dmp_' + nm, list(h.shape), h.dtype, "ExternalOutput")
    for nm in c.dumps_q:
        if nm.startswith('S.'):
            continue
        h = k.E[nm]
        dram('dmpq_' + nm, list(h.shape), h.dtype, "ExternalOutput")

    es = ExitStack(); k.es = es

    def sbuf(name, shape, dt):
        return es.enter_context(nc.sbuf_tensor(name, list(shape), dt))

    S = NS(); k.S = S
    QH = c.QH
    S.ktT = sbuf('ktT', (128, c.DT, c.KSUP), BF16)
    S.vsup = sbuf('vsup', (128, c.NKT, 1056), BF16)
    S.qT = sbuf('qT', (128, c.DT, QH), BF16)
    S.arena = sbuf('arena', (128, c.DT * QH), F32)
    S.pT = sbuf('pT', (128, 4, 512), BF16)
    S.x32 = sbuf('x32', (128, c.DT, 512), F32)
    S.xw = sbuf('xw', (128, c.DT, 512), BF16)
    S.ttk = sbuf('ttk', (128, 2, c.D), BF16)
    S.rows2 = sbuf('rows2', (128, max(c.To, 4 * c.QH)), F32)
    S.bc = sbuf('bc', (128, 1024), F32)
    S.cols = sbuf('cols', (128, 16), F32)
    S.w0 = sbuf('w0', (128, 2, 2048), BF16)
    S.onesb = sbuf('onesb', (128, 2), BF16)
    S.ones32c = sbuf('ones32c', (128, 1), F32)
    S.ones1rb = sbuf('ones1rb', (1, 128), BF16)
    S.ones1r32 = sbuf('ones1r32', (1, 128), F32)
    S.iota = sbuf('iota', (128, 1), F32)
    S.pmod = sbuf('pmod', (128, 1), F32)
    S.iden = sbuf('iden', (128, 128), BF16)
    S.iden32 = sbuf('iden32', (128, 128), F32)
    S.rowsc = sbuf('rowsc', (128, 2048), F32)
    S.nwT = sbuf('nwT', (128, c.DT, 256), F32R)
    S.xr = sbuf('xr', (128, c.DT, 512), F32R)
    S.nb = sbuf('nb', (128, 2), F32)
    S.tokid = sbuf('tokid', (128, c.To // 128), F32)
    S.tki = sbuf('tki', (128, c.To // 128), I32)
    S.dstc = sbuf('dstc', (128, c.To // 128), I32)
    S.srt = sbuf('srt', (128, c.NTILE), I32)
    S.idxw1 = sbuf('idxw1', (128, c.NCHUNK), I32)
    S.idxw2 = sbuf('idxw2', (128, c.NTILE), I32)
    S.clri = sbuf('clri', (128, c.NTILE), I32)
    S.clrf = sbuf('clrf', (8, c.NTILE), F32)
    S.b1sb = sbuf('b1sb', (128, 4, 1), F32)
    S.hT = sbuf('hT', (128, 2, 128), BF16)
    S.hA = sbuf('hA', (128, 2, 128), BF16)
    S.ysl = sbuf('ysl', (128, 2, c.D), BF16)
    S.mdiag = sbuf('mdiag', (128, 128), BF16)
    S.indic = sbuf('indic', (8, 128), BF16)
    S.kd = sbuf('kd', (128, (c.Tf // 128) * (c.To // c.QB)), F32)
    S.qrow = sbuf('qrow', (128, c.QB), F32)
    S.srow = sbuf('srow', (1, c.NCHUNK), F32)
    S.brow = sbuf('brow', (128, 2 * c.D), BF16)
    S.vev = sbuf('vev', (128, 544), BF16)
    S.constv = sbuf('constv', (128, 1), F32)
    nc.const_aps.register(F32, 1e-5, S.constv[:, 0:1])

    for nm in c.dumps_q:
        if nm.startswith('S.'):
            t = getattr(S, nm[2:])
            dram('dmpq_' + nm[2:], list(t[:].tensor.shape), t[:].dtype,
                 "ExternalOutput")
    k.PA = es.enter_context(nc.psum_tensor('PA', [128, 2048], F32))
    k.PB = es.enter_context(nc.psum_tensor('PB', [128, 2048], F32))

    k.sems = {}
    for s in ['dq', 'dg', 'cc', 'dw0', 'dw1', 'sp0', 'sp1', 'sv', 'sb',
              'gg0', 'gg1', 'gx0', 'gx1', 'gt0', 'gt1', 'gt2', 'gt3',
              'gy0', 'gy1', 'dk', 'pe', 'ac', 'dv', 'po', 'sq']:
        k.sems[s] = es.enter_context(nc.semaphore(s))
    k.DMASEMS = ('dq', 'dg', 'cc', 'dw0', 'dw1', 'sp0', 'sp1', 'sv', 'sb',
                 'gg0', 'gg1', 'gx0', 'gx1', 'gt0', 'gt1', 'gt2', 'gt3',
                 'gy0', 'gy1', 'dk')
    return k



class EngProxy:
    def __init__(self, k, name):
        self._k = k
        self._name = name

    _NOWAIT = {'wait_ge', 'alloc_register', 'reg_mov', 'free_register', 'nop'}
    _OWN = {'ve': 'dv', 'sc': 'ac'}

    def __getattr__(self, attr):
        k = self._k
        name = self._name

        def call(*args, **kw):
            if k.live == name:
                own = EngProxy._OWN.get(name)
                if own and attr not in EngProxy._NOWAIT and k.cnt[own] > 0:
                    key = (name, own)
                    if k.waited.get(key, -1) < k.cnt[own]:
                        k.engs[name].wait_ge(k.sems[own], k.cnt[own])
                        k.waited[key] = k.cnt[own]
                try:
                    return getattr(k.engs[name], attr)(*args, **kw)
                except Exception as e:
                    raise RuntimeError('engine op failed: %s.%s: %r' % (name, attr, e)) from e
            elif EngProxy._OWN.get(name) and attr not in EngProxy._NOWAIT:
                own = EngProxy._OWN[name]
                k.waited[(name, own)] = max(k.waited.get((name, own), -1), k.cnt[own])
            return None
        return call

    def wait(self, sem, val=None):
        k = self._k
        if val is None:
            val = k.cnt[sem]
        val = min(val, k.cnt[sem])
        if val <= 0:
            return
        if k.live == self._name:
            key = (self._name, sem)
            if k.waited.get(key, -1) < val:
                k.engs[self._name].wait_ge(k.sems[sem], val)
                k.waited[key] = val

    def inc(self, instr, sem, step=1):
        k = self._k
        k.cnt[sem] += step
        if instr is not None:
            instr.then_inc(k.sems[sem], step)
        return k.cnt[sem]


def emit_program(k, scriptfn):
    blk = k.es.enter_context(k.nc.Block())
    k.engs = {}
    k.T = EngProxy(k, 'te'); k.V = EngProxy(k, 've'); k.C = EngProxy(k, 'sc')
    k.P = EngProxy(k, 'po'); k.Y = EngProxy(k, 'sy')

    def replay(name, eng):
        k.engs[name] = eng
        k.live = name
        k.cnt = {s: 0 for s in k.sems}
        k.waited = {}
        scriptfn(k)

    @blk.sync
    def _(e):
        replay('sy', e)

    @blk.scalar
    def _(e):
        replay('sc', e)

    @blk.vector
    def _(e):
        replay('ve', e)

    @blk.tensor
    def _(e):
        replay('te', e)

    @blk.gpsimd
    def _(e):
        replay('po', e)


def barrier(k, engines=None):
    snap = dict(k.cnt)
    snap.pop('cc', None)
    own = {'te': 'pe', 've': 'dv', 'sc': 'ac', 'po': 'po', 'sy': 'sq'}
    prox = {'te': k.T, 've': k.V, 'sc': k.C, 'po': k.P, 'sy': k.Y}
    arr = {}
    for e in ('te', 've', 'sc', 'po', 'sy'):
        p = prox[e]
        for s, v in snap.items():
            if v > 0:
                p.wait(s, v)
        arr[e] = p.inc(p.nop(), own[e])
    for e in ('te', 've', 'sc', 'po', 'sy'):
        p = prox[e]
        for e2 in ('te', 've', 'sc', 'po', 'sy'):
            if e2 != e:
                p.wait(own[e2], arr[e2])



def script(k):
    c = k.c
    k.ac_exp = [0, 0]
    k.dv_msk = [0, 0]
    k.pe_qw = [0, 0]
    k.pe_kw = [0, 0]
    k.wtok_fff = [0, 0]
    k.ysl_tok = [0, 0]
    k.w1cp = [0, 0, 0, 0]
    k.qtd_st = [0, 0]
    k.vev_st = 0
    k.bc_st = 0
    k.qp_ev = [0, 0]
    k.qp_evv = [0, 0]
    k.tkm_tok = [0, 0]
    k.tkm_st = 0
    k.rt_pe = [0, 0]
    if k.live == 'po':
        k.breg = k.engs['po'].alloc_register('bnd')
        k.engs['po'].reg_mov(k.breg, c.To - 1)
    else:
        k.breg = None
    emit_init(k)
    src = 'xT'
    for i, p in enumerate(c.parts):
        dst = 'outT' if i == len(c.parts) - 1 else 'resid'
        if p == 1:
            emit_attn_sublayer(k, src, dst, 'aa')
        elif p == 3:
            emit_attn_sublayer(k, src, dst, 'ab')
        else:
            emit_ffn_sublayer(k, src, dst)
        src = 'resid'
    for nm in c.dumps:
        k.Y.inc(k.Y.dma_start(k.E['dmp_' + nm].ap(), k.E[nm].ap()), 'dq', 16)
        k.Y.wait('dq', k.cnt['dq'])
    for s in ('dq', 'dg', 'cc', 'pe', 'ac', 'dv', 'po'):
        if k.cnt[s] > 0:
            k.Y.wait(s, k.cnt[s])


def emit_init(k):
    c, S, E = k.c, k.S, k.E
    Y, P, V = k.Y, k.P, k.V
    for nm, dst in [('c_iota', S.iota), ('c_pmod', S.pmod), ('c_tokid', S.tokid),
                    ('c_maskdiag', S.mdiag), ('c_indic8', S.indic), ('c_kd', S.kd),
                    ('c_qrow', S.qrow), ('c_srow', S.srow)]:
        Y.inc(Y.dma_start(dst[:], E[nm].ap()), 'dq', 16)
    P.inc(P.memset(S.iden[:], 0.0), 'po')
    P.wait('po', k.cnt['po'])
    P.inc(P.affine_select(out=S.iden[:], in_=S.iden[:], compare_op=ALU.not_equal,
                          fill=1.0, base=0, pattern=[[-1, 128]],
                          channel_multiplier=1), 'po')
    P.inc(P.memset(S.iden32[:], 0.0), 'po')
    P.wait('po', k.cnt['po'])
    P.inc(P.affine_select(out=S.iden32[:], in_=S.iden32[:], compare_op=ALU.not_equal,
                          fill=1.0, base=0, pattern=[[-1, 128]],
                          channel_multiplier=1), 'po')
    P.inc(P.memset(S.constv[:], 1e-5), 'po')
    P.inc(P.memset(S.onesb[:], 1.0), 'po')
    P.inc(P.memset(S.ones32c[:], 1.0), 'po')
    P.inc(P.memset(S.ones1rb[:], 1.0), 'po')
    P.inc(P.memset(S.ones1r32[:], 1.0), 'po')
    P.inc(P.memset(S.vev[:], 0.0), 'po')
    P.wait('po', k.cnt['po'])
    for g in range(8):
        P.inc(P.memset(S.vev[:, g * 66 + 64: g * 66 + 65], 1.0), 'po')
    V.wait('dq', k.cnt['dq'])
    V.inc(V.tensor_copy(S.tki[:], S.tokid[:]), 'dv')
    P.inc(P.memset(S.ysl[:, 0, 0:((c.To // 128) * 32)], 0.0), 'po')
    P.wait('po', k.cnt['po'])
    for nm in ('ktok', 'vtok'):
        P.inc(P.dma_start(E[nm].ap()[:, 1024:1056]
                          .rearrange('(a p) v -> p a v', p=128),
                          S.ysl[:, 0, 0:((c.To // 128) * 32)]
                          .rearrange('p (a v) -> p a v', v=32)), 'dg', 16)
    vflat = S.vsup[:].rearrange('p a b -> p (a b)')
    piece = 4096
    stride = 4224
    if 2 in c.parts:
        for t in 'qkvf':
            for (src, dstn, n) in [('f%s_w1g' % t, 'w1gB_' + t, c.NLEAF * 128 * 128),
                                   ('f%s_w2g' % t, 'w2gB_' + t, 4096 * (c.D + 1)),
                                   ('f%s_b2' % t, 'b2B_' + t, c.NLEAF * c.D)]:
                npc = max(n // (128 * piece), 1)
                per = n // 128 // npc
                assert per <= stride and n == per * npc * 128
                sap = E[src].reshape([128, n // 128]).ap()
                dap = E[dstn].reshape([128, n // 128]).ap()
                for i in range(npc):
                    buf = vflat[:, (i % 2) * stride:(i % 2) * stride + per]
                    P.wait('dk', k.cnt['dk'])
                    P.inc(P.dma_start(buf, sap[:, ds(i, per)]), 'dk', 16)
                    P.wait('dk', k.cnt['dk'])
                    P.inc(P.dma_start(dap[:, ds(i, per)], buf), 'dk', 16)
    barrier(k)



def emit_prologue(k, src, kind):
    c, S, E = k.c, k.S, k.E
    Y, P, V, C, T = k.Y, k.P, k.V, k.C, k.T
    PA, PB = k.PA, k.PB
    xs = E[src].ap()

    if kind == 'aa':
        Y.inc(Y.dma_start(S.cols[:, 0:8], E['aa_gcol'].ap()), 'dq', 16)
        barrier(k, ('ve',))
        V.inc(V.tensor_scalar_add(S.cols[:, 0:8], S.cols[:, 0:8], 1.0), 'dv')
    if kind == 'ab':
        Y.inc(Y.dma_start(S.cols[:, 0:8], E['ab_bcq'].ap()), 'dq', 16)
        P.inc(P.dma_start(S.brow[0:1, 0:c.D], E['ab_brk'].ap()), 'dg', 16)
        P.inc(P.dma_start(S.brow[0:1, c.D:2 * c.D], E['ab_brv'].ap()), 'dg', 16)

    for s in range(c.ST):
        tsl = ds(s, 512)
        barrier(k)
        Y.inc(Y.dma_start(S.x32[:], xs[:, tsl].rearrange('(a p) t -> p a t', p=128)),
              'dq', 16)
        ldq = k.cnt['dq']
        C.wait('dq', ldq)
        for dt in range(c.DT):
            C.inc(C.activation(S.xw[:, dt, :], S.x32[:, dt, :], AF.Square), 'ac')
        T.wait('ac', k.cnt['ac'])
        T.wait('dq', ldq)
        for dt in range(c.DT):
            T.matmul(PA[0:1, 1024:1536], S.ones32c[:, 0:1], S.x32[:, dt, :],
                     start=(dt == 0), stop=(dt == c.DT - 1))
        for dt in range(c.DT):
            i = T.matmul(PA[0:1, 1536:2048], S.onesb[:, 0:1], S.xw[:, dt, :],
                         start=(dt == 0), stop=(dt == c.DT - 1))
        T.inc(i, 'pe')
        R = S.rowsc
        R2 = S.rows2
        m_ = R[0:1, 0:512]; m2_ = R[0:1, 512:1024]
        fs_ = R[0:1, 1024:1536]; f_ = R[0:1, 1536:2048]
        v_ = R2[0:1, 0:512]; sq_ = R2[0:1, 512:1024]
        r_ = R2[0:1, 1024:1536]; t_ = R2[0:1, 1536:2048]
        V.wait('pe', k.cnt['pe'])
        V.inc(V.tensor_scalar_mul(m_, PA[0:1, 1024:1536], 1.0 / 1024.0), 'dv')
        V.inc(V.tensor_mul(m2_, m_, m_), 'dv')
        V.inc(V.scalar_tensor_tensor(v_, PA[0:1, 1536:2048], 1.0 / 1024.0, m2_,
                                     ALU.mult, ALU.subtract), 'dv')
        C.wait('dv', k.cnt['dv'])
        C.inc(C.activation(sq_, v_, AF.Sqrt, bias=1e-5), 'ac')
        V.wait('ac', k.cnt['ac'])
        V.inc(V.reciprocal(r_, sq_), 'dv')
        if kind == 'aa':
            C.wait('dv', k.cnt['dv'])
            C.inc(C.activation(t_, v_, AF.Sqrt, scale=1024.0), 'ac')
            V.wait('ac', k.cnt['ac'])
            V.inc(V.tensor_mul(t_, t_, r_), 'dv')
            V.inc(V.tensor_scalar_max(t_, t_, 1e-12), 'dv')
            V.inc(V.reciprocal(t_, t_), 'dv')
            V.inc(V.scalar_tensor_tensor(t_, t_, SQD, r_, ALU.mult, ALU.mult), 'dv')
            V.inc(V.tensor_copy(f_, t_), 'dv')
        elif kind == 'ffn':
            C.wait('dv', k.cnt['dv'])
            C.inc(C.activation(t_, v_, AF.Sqrt, scale=1024.0), 'ac')
            V.wait('ac', k.cnt['ac'])
            V.inc(V.scalar_tensor_tensor(t_, t_, 1.0 / float(np.sqrt(1023.0)), r_,
                                         ALU.mult, ALU.mult), 'dv')
            C.wait('dv', k.cnt['dv'])
            C.inc(C.activation(t_, t_, AF.Sqrt, bias=1e-5), 'ac')
            V.wait('ac', k.cnt['ac'])
            V.inc(V.reciprocal(t_, t_), 'dv')
            V.inc(V.tensor_mul(t_, t_, r_), 'dv')
            V.inc(V.tensor_copy(f_, t_), 'dv')
        else:
            V.inc(V.tensor_copy(f_, r_), 'dv')
        T.wait('dv', k.cnt['dv'])
        T.matmul(PB[:, 0:512], S.ones1r32[:, :], S.rowsc[0:1, 0:512], start=True, stop=True)
        i = T.matmul(PB[:, 512:1024], S.ones1r32[:, :], S.rowsc[0:1, 1536:2048], start=True, stop=True)
        T.inc(i, 'pe')
        V.wait('pe', k.cnt['pe'])
        for dt in range(c.DT):
            V.inc(V.scalar_tensor_tensor(S.x32[:, dt, :], S.x32[:, dt, :], 1.0,
                                         PB[:, 0:512], ALU.mult, ALU.subtract), 'dv')
            if kind == 'aa':
                V.inc(V.scalar_tensor_tensor(S.x32[:, dt, :], S.x32[:, dt, :],
                                             S.cols[:, dt:dt + 1], PB[:, 512:1024],
                                             ALU.mult, ALU.mult), 'dv')
            else:
                V.inc(V.scalar_tensor_tensor(S.x32[:, dt, :], S.x32[:, dt, :], 1.0,
                                             PB[:, 512:1024], ALU.mult, ALU.mult), 'dv')
            V.inc(V.tensor_copy(S.xw[:, dt, :], S.x32[:, dt, :]), 'dv')
        if kind == 'ffn':
            Y.wait('dv', k.cnt['dv'])
            Y.inc(Y.dma_start(E['xn232'].ap()[:, tsl].rearrange('(a p) t -> p a t', p=128),
                              S.x32[:]), 'dq', 16)
            emit_tokmajor(k, 'xtok', s)
        else:
            emit_stripe_projections(k, s, kind)
    barrier(k)


def emit_tokmajor(k, dsttok, s):
    c, S, E = k.c, k.S, k.E
    Y, V, T = k.Y, k.V, k.T
    T.wait('dv', k.cnt['dv'])
    for tt in range(4):
        for dt in range(c.DT):
            bank = dt % 2
            T.wait('dv', k.tkm_tok[bank])
            i = T.matmul(PAq(k)[:, bank * 512: bank * 512 + 128],
                         S.xw[:, dt, ds(tt, 128)], S.iden[:], start=True, stop=True)
            T.inc(i, 'pe')
            V.wait('pe', k.cnt['pe'])
            V.wait('dq', k.tkm_st)
            i = V.tensor_copy(S.ttk[:, tt % 2, ds(dt, 128)],
                              PAq(k)[:, bank * 512: bank * 512 + 128])
            k.tkm_tok[bank] = V.inc(i, 'dv')
        if tt % 2 == 1:
            Y.wait('dv', k.cnt['dv'])
            k.tkm_st = Y.inc(
                Y.dma_start(E[dsttok].ap()[s * 512 + (tt - 1) * 128:
                                           s * 512 + (tt + 1) * 128, :]
                            .rearrange('(a p) d -> p a d', p=128), S.ttk[:]),
                'dq', 16)


def PAq(k):
    return k.PA


def emit_stripe_projections(k, s, kind):
    c, S, E = k.c, k.S, k.E
    Y, P, V, C, T = k.Y, k.P, k.V, k.C, k.T
    PA, PB = k.PA, k.PB
    tsl = ds(s, 512)
    w0f = S.w0[:].rearrange('p a b -> p (a b)')

    if kind == 'aa':
        qsrc = 'aa_wqkv'
        kvspec = [('aa_wqkv', 1024, 'ktok', False), ('aa_wqkv', 2048, 'vtok', True)]
    else:
        qsrc = 'ab_wq'
        kvspec = [('ab_wk', 0, 'ktokF', False), ('ab_wv', 0, 'vtokF', False)]

    k.pe_qw[0] = max(k.pe_qw[0], k.pe_kw[0])
    k.pe_qw[1] = max(k.pe_qw[1], k.pe_kw[1])
    for g in range(8):
        buf = g % 2
        if g >= 2:
            P.wait('pe', k.pe_qw[buf])
        else:
            P.wait('pe', k.pe_qw[buf])
        P.inc(P.dma_start(S.w0[:, buf, 0:1024].rearrange('p (a q) -> p a q', q=128),
                          E[qsrc].ap()[:, ds(g, 128)]
                          .rearrange('(a p) q -> p a q', p=128)), 'dw%d' % buf, 16)
        T.wait('dw%d' % buf, k.cnt['dw%d' % buf])
        T.wait('dv', k.cnt['dv'])
        T.wait('ac', k.qp_ev[g % 2])
        T.wait('dv', k.qp_evv[g % 2])
        for dt in range(c.DT):
            i = T.matmul(PB[:, 1024 + (g % 2) * 512: 1024 + (g % 2) * 512 + 512],
                         S.w0[:, buf, ds(dt, 128)], S.xw[:, dt, :],
                         start=(dt == 0), stop=(dt == c.DT - 1))
        k.pe_qw[buf] = T.inc(i, 'pe')
        if kind == 'ab':
            V.wait('pe', k.cnt['pe'])
            V.wait('sp%d' % (g % 2), k.qtd_st[g % 2])
            i = V.tensor_scalar(S.pT[:, 2 * (g % 2), :],
                                PB[:, 1024 + (g % 2) * 512:1024 + (g % 2) * 512 + 512],
                                S.cols[:, g:g + 1], None, ALU.add)
            k.qp_evv[g % 2] = V.inc(i, 'dv')
            Y.wait('dv', k.cnt['dv'])
        else:
            C.wait('pe', k.cnt['pe'])
            C.wait('sp%d' % (g % 2), k.qtd_st[g % 2])
            i = C.activation(S.pT[:, 2 * (g % 2), :],
                             PB[:, 1024 + (g % 2) * 512:1024 + (g % 2) * 512 + 512],
                             AF.Copy)
            k.qp_ev[g % 2] = C.inc(i, 'ac')
            Y.wait('ac', k.cnt['ac'])
        Y.wait('dv', k.cnt['dv'])
        Y.wait('ac', k.cnt['ac'])
        k.qtd_st[g % 2] = Y.inc(
            Y.dma_start(E['qTd'].ap()[ds(g, 128), tsl], S.pT[:, 2 * (g % 2), :]),
            'sp%d' % (g % 2), 16)

    k.pe_kw[0] = max(k.pe_kw[0], k.pe_qw[0])
    k.pe_kw[1] = max(k.pe_kw[1], k.pe_qw[1])
    for (wsrc, coff, dstname, aug) in kvspec:
        for kvg in range(4):
            buf = kvg % 2
            P.wait('pe', k.pe_kw[buf])
            P.inc(P.dma_start(S.w0[:, buf, :].rearrange('p (a q) -> p a q', q=256),
                              E[wsrc].ap()[:, coff + kvg * 256: coff + kvg * 256 + 256]
                              .rearrange('(a p) q -> p a q', p=128)), 'dw%d' % buf, 16)
            wv = S.w0[:, buf, :].rearrange('p (a q) -> p a q', q=256)
            for tt in range(4):
                T.wait('dw%d' % buf, k.cnt['dw%d' % buf])
                T.wait('dv', k.cnt['dv'])
                T.wait('dv', k.qp_evv[tt % 2])
                for dt in range(c.DT):
                    i = T.matmul(PB[:, (tt % 2) * 512:(tt % 2) * 512 + 256],
                                 S.xw[:, dt, ds(tt, 128)], wv[:, dt, :],
                                 start=(dt == 0),
                                 stop=(dt == c.DT - 1 and kind != 'ab'))
                if kind == 'ab':
                    brow = S.brow[0:1, 0:c.D] if dstname == 'ktokF' else S.brow[0:1, c.D:2 * c.D]
                    T.wait('dg', k.cnt['dg'])
                    i = T.matmul(PB[:, (tt % 2) * 512:(tt % 2) * 512 + 256],
                                 S.ones1rb[:, :],
                                 brow[:, kvg * 256:kvg * 256 + 256],
                                 start=False, stop=True)
                T.inc(i, 'pe')
                V.wait('pe', k.cnt['pe'])
                if aug:
                    V.wait('sv', k.vev_st)
                    V.inc(V.tensor_copy(
                        S.vev[:, 0:264].rearrange('p (g e) -> p g e', e=66)[:, :, 0:64],
                        PB[:, (tt % 2) * 512:(tt % 2) * 512 + 256]
                        .rearrange('p (g e) -> p g e', e=64)), 'dv')
                    k.qp_evv[tt % 2] = k.cnt['dv']
                    Y.wait('dv', k.cnt['dv'])
                    k.vev_st = Y.inc(Y.dma_start(
                        E[dstname].ap()[s * 512 + tt * 128: s * 512 + tt * 128 + 128,
                                        kvg * 264: kvg * 264 + 264],
                        S.vev[:, 0:264]), 'sv', 16)
                else:
                    V.wait('sp%d' % (tt % 2), k.qtd_st[tt % 2])
                    V.inc(V.tensor_copy(S.pT[:, 2 * (tt % 2), 0:256],
                                        PB[:, (tt % 2) * 512:(tt % 2) * 512 + 256]), 'dv')
                    k.qp_evv[tt % 2] = k.cnt['dv']
                    Y.wait('dv', k.cnt['dv'])
                    k.qtd_st[tt % 2] = Y.inc(Y.dma_start(
                        E[dstname].ap()[s * 512 + tt * 128: s * 512 + tt * 128 + 128,
                                        kvg * 256: kvg * 256 + 256],
                        S.pT[:, 2 * (tt % 2), 0:256]), 'sp%d' % (tt % 2), 16)
            k.pe_kw[buf] = k.cnt['pe']


def emit_attn_sublayer(k, src, dst, kind):
    c = k.c
    emit_prologue(k, src, kind)
    emit_exchange(k, kind == 'aa')
    for qh in range(c.NQH):
        emit_attention_quarter(k, qh, kind)
        if kind == 'aa':
            emit_epilogue_proj(k, qh, 'aa_wout', None, src, dst, heads=True)
        else:
            emit_epilogue_proj(k, qh, 'ab_wo', 'ab_bco', src, dst, heads=False)
    barrier(k)


def emit_exchange(k, aug):
    c, S, E = k.c, k.S, k.E
    P = k.P
    kk, vv = ('ktok', 'vtok') if aug else ('ktokF', 'vtokF')
    kf, vf = ('kfull', 'vfull') if aug else ('kfullF', 'vfullF')
    barrier(k)
    i = P.collective_compute("AllGather", ALU.bypass, replica_groups=c.groups,
                             ins=[E[kk].ap().opt()], outs=[E[kf].ap().opt()])
    P.inc(i, 'cc')
    i = P.collective_compute("AllGather", ALU.bypass, replica_groups=c.groups,
                             ins=[E[vv].ap().opt()], outs=[E[vf].ap().opt()])
    P.inc(i, 'cc')


def emit_attention_quarter(k, qh, kind):
    c, S, E = k.c, k.S, k.E
    Y, P, V, C, T = k.Y, k.P, k.V, k.C, k.T
    PA, PB = k.PA, k.PB
    QH = c.QH
    qsl = ds(qh, QH)
    scale = 0.125 if kind == 'aa' else 1.0 / 32.0
    arena = S.arena[:].rearrange('p (a q) -> p a q', q=QH)

    barrier(k)
    if kind == 'ffn':
        for dt in range(c.DT):
            Y.inc(Y.dma_start(S.qT[:, dt, :], E['qtok'].ap()[qsl, ds(dt, 128)],
                              transpose=True), 'dq', 16)
    else:
        Y.inc(Y.dma_start(S.qT[:], E['qTd'].ap()[:, qsl]
                          .rearrange('(a p) t -> p a t', p=128)), 'dq', 16)
    P.inc(P.memset(S.arena[:], 0.0), 'po')
    P.inc(P.memset(S.rows2[:], 0.0), 'po')
    P.inc(P.memset(S.rowsc[32:33, :], 0.0), 'po')
    P.inc(P.memset(S.rowsc[64:65, :], 0.0), 'po')
    barrier(k)

    kf, vf = ('kfull', 'vfull') if kind == 'aa' else ('kfullF', 'vfullF')
    Y.wait('cc', k.cnt['cc'])
    for sup in range(c.NSUP):
        ksl = slice(sup * c.KSUP, (sup + 1) * c.KSUP)
        barrier(k)
        for dt in range(c.DT):
            Y.inc(Y.dma_start(S.ktT[:, dt, :], E[kf].ap()[ksl, ds(dt, 128)],
                              transpose=True), 'dq', 16)
        if kind == 'aa':
            Y.inc(Y.dma_start(S.vsup[:], E[vf].ap()[ksl, :]
                              .rearrange('(a p) v -> p a v', p=128)), 'dq', 16)
        else:
            Y.inc(Y.dma_start(S.vsup[:, :, 0:1024], E[vf].ap()[ksl, :]
                              .rearrange('(a p) v -> p a v', p=128)), 'dq', 16)
        barrier(k)
        if kind == 'aa':
            emit_super_aa(k, scale)
        else:
            emit_super_single(k, scale, kind, qh, sup)
        barrier(k)


def aa_sum_slot(k, hh, off, width):
    S, c = k.S, k.c
    QH = c.QH
    if hh < 12:
        return S.rows2[32 * (hh // 4):32 * (hh // 4) + 1,
                       (hh % 4) * QH + off:(hh % 4) * QH + off + width]
    h2 = hh - 12
    return S.rowsc[32 * (h2 // 2 + 1):32 * (h2 // 2 + 1) + 1,
                   (h2 % 2) * 1024 + off:(h2 % 2) * 1024 + off + width]


def emit_super_aa(k, scale):
    c, S = k.c, k.S
    V, C, T = k.V, k.C, k.T
    PA, PB = k.PA, k.PB
    QH = c.QH
    arena = S.arena[:].rearrange('p (a q) -> p a q', q=QH)
    pTf = S.pT[:].rearrange('p a b -> p (a b)')
    for qb in range(c.NQBA):
        qbs = ds(qb, c.QBA)
        for pr in range(8):
            hA, hB = 2 * pr, 2 * pr + 1
            def pv(kt):
                par = kt % 2
                T.wait('ac', k.ac_exp[par])
                T.matmul(PB[0:65, 0:512], S.vsup[:, kt, 66 * hA:66 * hA + 65],
                         S.pT[:, 2 * par, :], start=(kt == 0), stop=(kt == c.NKT - 1))
                return T.matmul(PB[0:65, 512:1024],
                                S.vsup[:, kt, 66 * hB:66 * hB + 65],
                                S.pT[:, 2 * par + 1, :],
                                start=(kt == 0), stop=(kt == c.NKT - 1))
            for kt in range(c.NKT):
                par = kt % 2
                T.wait('ac', k.ac_exp[par])
                T.matmul(PA[:, par * 1024: par * 1024 + 512],
                         S.ktT[0:64, pr, ds(kt, 128)],
                         S.qT[0:64, pr, qbs], start=True, stop=True)
                iB = T.matmul(PA[:, par * 1024 + 512: par * 1024 + 1024],
                              S.ktT[64:128, pr, ds(kt, 128)],
                              S.qT[64:128, pr, qbs], start=True, stop=True)
                T.inc(iB, 'pe')
                pe_s = k.cnt['pe']
                C.wait('pe', pe_s)
                i = C.activation(pTf[:, par * 1024:(par + 1) * 1024],
                                 PA[:, par * 1024:(par + 1) * 1024], AF.Exp,
                                 scale=scale)
                k.ac_exp[par] = C.inc(i, 'ac')
                if kt >= 1:
                    pv(kt - 1)
            i = pv(c.NKT - 1)
            T.inc(i, 'pe')
            V.wait('pe', k.cnt['pe'])
            for hh, reg in ((hA, 0), (hB, 1)):
                dt = hh // 2
                rb = (hh % 2) * 64
                V.inc(V.tensor_add(arena[rb:rb + 64, dt, qbs],
                                   PB[0:64, reg * 512:reg * 512 + 512],
                                   arena[rb:rb + 64, dt, qbs]), 'dv')
                slot = aa_sum_slot(k, hh, qb * c.QBA, c.QBA)
                V.inc(V.tensor_add(slot, PB[64:65, reg * 512:reg * 512 + 512],
                                   slot), 'dv')
            T.wait('dv', k.cnt['dv'])


def emit_super_single(k, scale, kind, qh, sup):
    c, S = k.c, k.S
    V, C, T = k.V, k.C, k.T
    PA, PB = k.PA, k.PB
    QH = c.QH
    arena = S.arena[:].rearrange('p (a q) -> p a q', q=QH)
    pTf = S.xw[:].rearrange('p a b -> p (a b)')
    nqb = c.To // c.QB
    for qb in range(c.NQB):
        qbs = ds(qb, c.QB)
        exp_done = [0] * c.NKT
        msk_done = [0] * c.NKT

        def sums(kt):
            if kind == 'ab':
                T.wait('dv', msk_done[kt])
            else:
                T.wait('ac', exp_done[kt])
            return T.matmul(PA[0:1, 1024:1536], S.onesb[:, 0:1],
                            pTf[:, kt * 512:(kt + 1) * 512],
                            start=(kt == 0), stop=(kt == c.NKT - 1))
        for kt in range(c.NKT):
            par = kt % 2
            T.wait('ac', k.ac_exp[par])
            T.wait('dv', k.dv_msk[par])
            for dt in range(c.DT):
                i = T.matmul(PA[:, par * 512: par * 512 + 512],
                             S.ktT[:, dt, ds(kt, 128)], S.qT[:, dt, qbs],
                             start=(dt == 0), stop=(dt == c.DT - 1))
            T.inc(i, 'pe')
            C.wait('pe', k.cnt['pe'])
            i = C.activation(pTf[:, kt * 512:(kt + 1) * 512],
                             PA[:, par * 512: par * 512 + 512], AF.Exp, scale=scale)
            k.ac_exp[par] = exp_done[kt] = C.inc(i, 'ac')
            if kind == 'ab':
                V.wait('ac', k.ac_exp[par])
                kdcol = S.kd[:, (sup * c.NKT + kt) * nqb + qh * c.NQB + qb:
                             (sup * c.NKT + kt) * nqb + qh * c.NQB + qb + 1]
                i = V.scalar_tensor_tensor(pTf[:, kt * 512:(kt + 1) * 512],
                                           S.qrow[:, 0:c.QB], kdcol,
                                           pTf[:, kt * 512:(kt + 1) * 512],
                                           ALU.is_le, ALU.mult)
                k.dv_msk[par] = msk_done[kt] = V.inc(i, 'dv')
            if kt >= 1:
                sums(kt - 1)
        i = sums(c.NKT - 1)
        T.inc(i, 'pe')
        V.wait('pe', k.cnt['pe'])
        V.inc(V.tensor_add(S.rows2[0:1, qb * c.QB:(qb + 1) * c.QB],
                           PA[0:1, 1024:1536],
                           S.rows2[0:1, qb * c.QB:(qb + 1) * c.QB]), 'dv')
        T.wait('dv', k.cnt['dv'])
        T.wait('ac', k.cnt['ac'])
        for kt in range(c.NKT):
            for dt in range(c.DT):
                ps = PA if dt < 4 else PB
                col = (dt % 4) * 512
                i = T.matmul(ps[:, col:col + 512], S.vsup[:, kt, ds(dt, 128)],
                             pTf[:, kt * 512:(kt + 1) * 512],
                             start=(kt == 0), stop=(kt == c.NKT - 1))
        T.inc(i, 'pe')
        V.wait('pe', k.cnt['pe'])
        i = V.tensor_add(arena[:, 0:4, qbs],
                         PA[:].rearrange('p (a q) -> p a q', q=512),
                         arena[:, 0:4, qbs])
        V.inc(i, 'dv')
        i = V.tensor_add(arena[:, 4:8, qbs],
                         PB[:].rearrange('p (a q) -> p a q', q=512),
                         arena[:, 4:8, qbs])
        V.inc(i, 'dv')
        k.ac_exp = [0, 0]
        k.dv_msk = [0, 0]
        T.wait('dv', k.cnt['dv'])


def emit_epilogue_proj(k, qh, wname, bname, src, dst, heads):
    c, S, E = k.c, k.S, k.E
    Y, P, V, C, T = k.Y, k.P, k.V, k.C, k.T
    PA, PB = k.PA, k.PB
    QH = c.QH
    qsl = ds(qh, QH)
    arena = S.arena[:].rearrange('p (a q) -> p a q', q=QH)
    barrier(k)
    if heads:
        for hh in range(16):
            V.inc(V.reciprocal(S.rowsc[0:1, 0:QH], aa_sum_slot(k, hh, 0, QH)), 'dv')
            T.wait('dv', k.cnt['dv'])
            for ch in range(QH // 512):
                i = T.matmul(PB[0:64, (ch % 2) * 512:(ch % 2) * 512 + 512],
                             S.ones1r32[0:1, 0:64], S.rowsc[0:1, ds(ch, 512)],
                             start=True, stop=True)
                T.inc(i, 'pe')
                V.wait('pe', k.cnt['pe'])
                rb = (hh % 2) * 64
                V.inc(V.tensor_mul(S.qT[rb:rb + 64, hh // 2, ds(ch, 512)],
                                   arena[rb:rb + 64, hh // 2, ds(ch, 512)],
                                   PB[0:64, (ch % 2) * 512:(ch % 2) * 512 + 512]), 'dv')
            T.wait('dv', k.cnt['dv'])
    else:
        V.inc(V.reciprocal(S.rowsc[0:1, 0:QH], S.rows2[0:1, 0:QH]), 'dv')
        T.wait('dv', k.cnt['dv'])
        for ch in range(QH // 512):
            i = T.matmul(PB[:, (ch % 2) * 512:(ch % 2) * 512 + 512],
                         S.ones1r32[:, :], S.rowsc[0:1, ds(ch, 512)],
                         start=True, stop=True)
            T.inc(i, 'pe')
            V.wait('pe', k.cnt['pe'])
            for dt in range(c.DT):
                V.inc(V.tensor_mul(S.qT[:, dt, ds(ch, 512)],
                                   arena[:, dt, ds(ch, 512)],
                                   PB[:, (ch % 2) * 512:(ch % 2) * 512 + 512]), 'dv')
            T.wait('dv', k.cnt['dv'])
    if bname:
        Y.inc(Y.dma_start(S.cols[:, 8:16], E[bname].ap()), 'dq', 16)
    barrier(k)
    xs = E[src].ap()
    dd = E[dst].ap()
    for g in range(8):
        buf = g % 2
        if g >= 2:
            P.wait('pe', k.pe_qw[buf])
        P.inc(P.dma_start(S.w0[:, buf, 0:1024].rearrange('p (a q) -> p a q', q=128),
                          E[wname].ap()[:, ds(g, 128)]
                          .rearrange('(a p) q -> p a q', p=128)), 'dw%d' % buf, 16)
        Y.wait('dv', k.cnt['dv'])
        Y.wait('sb', k.bc_st)
        Y.inc(Y.dma_start(S.bc[:], xs[ds(g, 128), qsl]), 'sb', 16)
        T.wait('sb', k.cnt['sb'])
        T.wait('dw%d' % buf, k.cnt['dw%d' % buf])
        for ch in range(QH // 512):
            for dt in range(c.DT):
                i = T.matmul(PA[:, (ch % 2) * 512:(ch % 2) * 512 + 512],
                             S.w0[:, buf, ds(dt, 128)], S.qT[:, dt, ds(ch, 512)],
                             start=(dt == 0), stop=(dt == c.DT - 1))
        k.pe_qw[buf] = T.inc(i, 'pe')
        V.wait('pe', k.cnt['pe'])
        V.wait('sb', k.cnt['sb'])
        bcol = S.cols[:, 8 + g:9 + g] if bname else 0.0
        i = V.scalar_tensor_tensor(S.bc[:], PA[:, 0:1024], bcol, S.bc[:],
                                   ALU.add, ALU.add)
        V.inc(i, 'dv')
        Y.wait('dv', k.cnt['dv'])
        k.bc_st = Y.inc(Y.dma_start(dd[ds(g, 128), qsl], S.bc[:]), 'sb', 16)



def emit_ffn_sublayer(k, src, dst):
    c = k.c
    emit_prologue(k, src, 'ffn')
    for t, dsttok in (('k', 'ktokF'), ('v', 'vtokF'), ('q', 'qtok')):
        if t == 'q':
            emit_exchange(k, False)
        emit_fff(k, t, 'xtok', 'xn232', dsttok, AF.Relu)
        if t == 'q' and c.dumps_q:
            barrier(k)
            for nm in c.dumps_q:
                if nm.startswith('S.'):
                    sb = getattr(k.S, nm[2:])
                    k.Y.inc(k.Y.dma_start(k.E['dmpq_' + nm[2:]].ap(), sb[:]),
                            'dq', 16)
                else:
                    k.Y.inc(k.Y.dma_start(k.E['dmpq_' + nm].ap(), k.E[nm].ap()),
                            'dq', 16)
                k.Y.wait('dq', k.cnt['dq'])
            barrier(k)
    for qh in range(c.NQH):
        emit_attention_quarter(k, qh, 'ffn')
        emit_epilogue_y(k, qh)
    emit_fff(k, 'f', 'ytokM', 'yT32', 'ftok', AF.Gelu)
    emit_final_add(k, src, dst)
    barrier(k)


def emit_epilogue_y(k, qh):
    c, S, E = k.c, k.S, k.E
    Y, V, C, T = k.Y, k.V, k.C, k.T
    PA, PB = k.PA, k.PB
    QH = c.QH
    arena = S.arena[:].rearrange('p (a q) -> p a q', q=QH)
    barrier(k)
    V.inc(V.reciprocal(S.rowsc[0:1, 0:QH], S.rows2[0:1, 0:QH]), 'dv')
    for hf in range(QH // 512):
        s_global = (qh * QH + hf * 512) // 512
        tsl = ds(s_global, 512)
        Y.wait('dq', k.cnt['dq'])
        Y.wait('dv', k.cnt['dv'])
        Y.inc(Y.dma_start(S.x32[:], E['xn232'].ap()[:, tsl]
                          .rearrange('(a p) t -> p a t', p=128)), 'dq', 16)
        T.wait('dv', k.cnt['dv'])
        i = T.matmul(PB[:, 0:512], S.ones1r32[:, :], S.rowsc[0:1, ds(hf, 512)],
                     start=True, stop=True)
        T.inc(i, 'pe')
        V.wait('pe', k.cnt['pe'])
        V.wait('dq', k.cnt['dq'])
        for dt in range(c.DT):
            V.inc(V.tensor_mul(arena[:, dt, ds(hf, 512)], arena[:, dt, ds(hf, 512)],
                               PB[:, 0:512]), 'dv')
            V.inc(V.tensor_add(S.x32[:, dt, :], arena[:, dt, ds(hf, 512)],
                               S.x32[:, dt, :]), 'dv')
            V.inc(V.tensor_copy(S.xw[:, dt, :], S.x32[:, dt, :]), 'dv')
        Y.wait('dv', k.cnt['dv'])
        Y.inc(Y.dma_start(E['yT32'].ap()[:, tsl].rearrange('(a p) t -> p a t', p=128),
                          S.x32[:]), 'dq', 16)
        emit_tokmajor(k, 'ytokM', s_global)
    barrier(k)


def emit_final_add(k, src, dst):
    c, S, E = k.c, k.S, k.E
    Y, V, T = k.Y, k.V, k.T
    qt32 = k.S.qT[:].rearrange('p a b -> p (a b)').bitcast(F32)
    barrier(k)
    for s in range(c.ST):
        tsl = ds(s, 512)
        barrier(k)
        Y.inc(Y.dma_start(S.x32[:], E[src].ap()[:, tsl]
                          .rearrange('(a p) t -> p a t', p=128)), 'dq', 16)
        Y.inc(Y.dma_start(qt32.rearrange('p (a t) -> p a t', t=512),
                          E['yT32'].ap()[:, tsl]
                          .rearrange('(a p) t -> p a t', p=128)), 'dq', 16)
        for dt in range(c.DT):
            Y.inc(Y.dma_start(S.xw[:, dt, :], E['ftok'].ap()[tsl, ds(dt, 128)],
                              transpose=True), 'dq', 16)
        V.wait('dq', k.cnt['dq'])
        for dt in range(c.DT):
            V.inc(V.tensor_add(S.x32[:, dt, :], S.x32[:, dt, :],
                               qt32.rearrange('p (a t) -> p a t', t=512)[:, dt, :]), 'dv')
            V.inc(V.tensor_add(S.x32[:, dt, :], S.x32[:, dt, :], S.xw[:, dt, :]), 'dv')
        Y.wait('dv', k.cnt['dv'])
        Y.inc(Y.dma_start(E[dst].ap()[:, tsl].rearrange('(a p) t -> p a t', p=128),
                          S.x32[:]), 'dq', 16)



def emit_fff(k, tree, srctok, src32, dsttok, act):
    c, S, E = k.c, k.S, k.E
    Y, P, V, C, T = k.Y, k.P, k.V, k.C, k.T
    PA, PB = k.PA, k.PB
    AOFF = 0
    arenaF = S.arena[:]
    lgT0 = arenaF[:, 0:c.To]
    lgT1 = arenaF[:, c.To:2 * c.To]
    mask = arenaF[:, 2 * c.To:3 * c.To] if 3 * c.To <= c.DT * c.QH else None
    x32f = S.x32[:].rearrange('p a b -> p (a b)')
    barrier(k)
    P.inc(P.dma_start(S.nwT[:, :, 0:255].rearrange('p a n -> p a n'),
                      E['f%s_nwT' % tree].ap().rearrange('(a p) n -> p a n', p=128)),
          'dg', 16)
    Y.inc(Y.dma_start(S.nb[:], E['f%s_nb' % tree].ap()), 'dq', 16)
    P.inc(P.memset(S.arena[:, 0:2 * c.To], 0.0), 'po')
    barrier(k)

    for s in range(c.ST):
        tsl = ds(s, 512)
        barrier(k)
        Y.inc(Y.dma_start(S.x32[:], E[src32].ap()[:, tsl]
                          .rearrange('(a p) t -> p a t', p=128)), 'dq', 16)
        xr = S.xr[:]
        V.wait('dq', k.cnt['dq'])
        for dt in range(c.DT):
            V.inc(V.tensor_copy(xr[:, dt, :], S.x32[:, dt, :]), 'dv')
        T.wait('dv', k.cnt['dv'])
        T.wait('ac', k.cnt['ac'])
        for half in range(2):
            ncols = 128 if half == 0 else 127
            for dt in range(c.DT):
                i = T.matmul(PA[0:ncols, half * 512: half * 512 + 512],
                             S.nwT[:, dt, half * 128: half * 128 + ncols],
                             xr[:, dt, :],
                             start=(dt == 0), stop=(dt == c.DT - 1))
            k.rt_pe[half] = T.inc(i, 'pe')
            V.wait('pe', k.cnt['pe'])
            dstl = lgT0 if half == 0 else lgT1
            i = V.tensor_scalar(dstl[:, tsl] if half == 0 else dstl[0:127, tsl],
                                PA[:, half * 512: half * 512 + 512] if half == 0
                                else PA[0:127, half * 512: half * 512 + 512],
                                S.nb[:, half:half + 1] if half == 0
                                else S.nb[0:127, half:half + 1],
                                None, ALU.add)
            V.inc(i, 'dv')
    barrier(k)

    r_pref = S.rows2[0:1, 0:c.To]
    r_b = S.rowsc[0:1, 0:512]
    NCH = c.To // 512
    V.inc(V.tensor_single_scalar(r_pref, lgT0[0:1, :], 0.0, ALU.is_gt), 'dv')
    for lv in range(1, 8):
        off = (1 << lv) - 1
        segs = [(lgT0, float(off))]
        if lv == 7:
            segs = [(lgT0, 127.0), (lgT1, -1.0)]
        units = [(chix, si) for chix in range(NCH) for si in range(len(segs))]
        nu = len(units)
        dv_A = [0] * nu; pe_B = [0] * nu; dv_C = [0] * nu; pe_D = [0] * nu

        def stA(u):
            chix, si = units[u]
            sl = u % 2
            if u >= 2:
                V.wait('pe', pe_B[u - 2])
            V.inc(V.tensor_scalar_add(S.rowsc[0:1, 1024 + sl * 512:1536 + sl * 512],
                                      r_pref[:, ds(chix, 512)], segs[si][1]), 'dv')
            dv_A[u] = k.cnt['dv']

        def stB(u):
            sl = u % 2
            T.wait('dv', dv_A[u])
            i = T.matmul(PB[:, sl * 512:sl * 512 + 512], S.ones1r32[:, :],
                         S.rowsc[0:1, 1024 + sl * 512:1536 + sl * 512],
                         start=True, stop=True)
            T.inc(i, 'pe')
            pe_B[u] = k.cnt['pe']

        def stC(u):
            chix, si = units[u]
            sl = u % 2
            V.wait('pe', pe_B[u])
            if si == 1:
                V.wait('pe', pe_D[u - 1])
            i = V.scalar_tensor_tensor(x32f[:, ds(chix, 512)],
                                       PB[:, sl * 512:sl * 512 + 512],
                                       S.iota[:, 0:1], segs[si][0][:, ds(chix, 512)],
                                       ALU.is_equal, ALU.mult)
            V.inc(i, 'dv')
            dv_C[u] = k.cnt['dv']

        def stD(u):
            chix, si = units[u]
            T.wait('dv', dv_C[u])
            i = T.matmul(PB[0:1, 1024 + (chix % 2) * 512:1536 + (chix % 2) * 512],
                         S.ones32c[:], x32f[:, ds(chix, 512)],
                         start=(si == 0), stop=(si == len(segs) - 1))
            T.inc(i, 'pe')
            pe_D[u] = k.cnt['pe']

        def stE(u):
            chix, si = units[u]
            V.wait('pe', pe_D[u])
            V.inc(V.tensor_single_scalar(r_b, PB[0:1, 1024 + (chix % 2) * 512:
                                                   1536 + (chix % 2) * 512],
                                         0.0, ALU.is_gt), 'dv')
            V.inc(V.scalar_tensor_tensor(r_pref[:, ds(chix, 512)],
                                         r_pref[:, ds(chix, 512)], 2.0,
                                         r_b, ALU.mult, ALU.add), 'dv')

        for step in range(nu + 2):
            if step < nu:
                stA(step); stB(step)
            if 1 <= step <= nu:
                stC(step - 1); stD(step - 1)
            if 2 <= step <= nu + 1:
                u = step - 2
                if units[u][1] == len(segs) - 1:
                    stE(u)

    barrier(k)
    emit_sort(k, tree)
    emit_fff_tiles(k, tree, srctok, dsttok, act)


def emit_sort(k, tree):
    c, S, E = k.c, k.S, k.E
    Y, P, V, C, T = k.Y, k.P, k.V, k.C, k.T
    PA, PB = k.PA, k.PB
    arenaF = S.arena[:]
    lgT0 = arenaF[:, 0:c.To]
    lgT1 = arenaF[:, c.To:2 * c.To]
    x32f = S.x32[:].rearrange('p a b -> p (a b)')[:, 0:c.To]
    r_leaf = S.rows2[0:1, 0:c.To]
    r_d = S.rows2[32:33, 0:c.To]
    x32row = S.x32[0:1, :, :].rearrange('p a b -> p (a b)')[:, 0:c.To]
    NCH = c.To // 512
    rsc_c = E['rsc'].reshape([8192, 1])
    rsc_r = E['rsc'].reshape([1, 8192])

    for H in range(2):
        for chix in range(NCH):
            chs = ds(chix, 512)
            V.inc(V.tensor_scalar_add(S.rowsc[0:1, 1024:1536], r_leaf[:, chs],
                                      -128.0 * H), 'dv')
            T.wait('dv', k.cnt['dv'])
            i = T.matmul(PB[:, 0:512], S.ones1r32[:, :], S.rowsc[0:1, 1024:1536],
                         start=True, stop=True)
            T.inc(i, 'pe')
            V.wait('pe', k.cnt['pe'])
            V.inc(V.tensor_scalar(lgT0[:, chs], PB[:, 0:512], S.iota[:, 0:1], None,
                                  ALU.is_equal), 'dv')
        V.inc(V.tensor_tensor_scan(x32f, lgT0, lgT0, 0.0, ALU.add, ALU.bypass), 'dv')
        V.inc(V.tensor_copy(S.cols[:, 8 + H:9 + H], x32f[:, c.To - 1:c.To]), 'dv')
        for chix in range(NCH):
            chs = ds(chix, 512)
            V.inc(V.tensor_mul(lgT1[:, chs], lgT0[:, chs], x32f[:, chs]), 'dv')
            T.wait('dv', k.cnt['dv'])
            i = T.matmul(PA[0:1, 1024:1536], S.ones32c[:],
                         lgT1[:, chs], start=True, stop=True)
            T.inc(i, 'pe')
            V.wait('pe', k.cnt['pe'])
            if H == 0:
                V.inc(V.tensor_copy(r_d[:, chs], PA[0:1, 1024:1536]), 'dv')
            else:
                V.inc(V.tensor_add(r_d[:, chs], PA[0:1, 1024:1536], r_d[:, chs]), 'dv')
            T.wait('dv', k.cnt['dv'])
        ncol = S.cols[:, 8 + H:9 + H]
        ti = S.cols[:, 14:15].bitcast(I32)
        V.inc(V.tensor_scalar_add(S.cols[:, 10 + H:11 + H], ncol, 15.0), 'dv')
        V.inc(V.tensor_copy(ti, S.cols[:, 10 + H:11 + H]), 'dv')
        V.inc(V.tensor_single_scalar(ti, ti, -16, ALU.bitwise_and), 'dv')
        V.inc(V.tensor_copy(S.cols[:, 10 + H:11 + H], ti), 'dv')
        Y.wait('dv', k.cnt['dv'])
        Y.inc(Y.dma_start(rsc_c.ap()[H * 128:(H + 1) * 128, :],
                          S.cols[:, 10 + H:11 + H]), 'dq', 16)
    Y.wait('dq', k.cnt['dq'])
    Y.inc(Y.dma_start(S.rowsc[0:1, 0:256], rsc_r.ap()[:, 0:256]), 'dq', 16)
    V.wait('dq', k.cnt['dq'])
    V.inc(V.tensor_tensor_scan(S.rowsc[0:1, 256:512], S.rowsc[0:1, 0:256],
                               S.rowsc[0:1, 0:256], 0.0, ALU.add, ALU.bypass), 'dv')
    V.inc(V.tensor_sub(S.rowsc[0:1, 256:512], S.rowsc[0:1, 256:512],
                       S.rowsc[0:1, 0:256]), 'dv')
    Y.wait('dv', k.cnt['dv'])
    Y.inc(Y.dma_start(rsc_r.ap()[:, 256:512], S.rowsc[0:1, 256:512]), 'dq', 16)
    Y.wait('dq', k.cnt['dq'])
    for H in range(2):
        Y.inc(Y.dma_start(S.cols[:, 12 + H:13 + H],
                          rsc_c.ap()[256 + 128 * H:256 + 128 * H + 128, :]), 'dq', 16)
    V.wait('dq', k.cnt['dq'])
    for H in range(2):
        offcol = S.cols[:, 12 + H:13 + H]
        for chix in range(NCH):
            chs = ds(chix, 512)
            V.inc(V.tensor_scalar_add(S.rowsc[0:1, 1024:1536], r_leaf[:, chs],
                                      -128.0 * H), 'dv')
            T.wait('dv', k.cnt['dv'])
            i = T.matmul(PB[:, 0:512], S.ones1r32[:, :], S.rowsc[0:1, 1024:1536],
                         start=True, stop=True)
            T.inc(i, 'pe')
            V.wait('pe', k.cnt['pe'])
            V.inc(V.tensor_scalar(lgT0[:, chs], PB[:, 0:512], S.iota[:, 0:1], None,
                                  ALU.is_equal), 'dv')
            V.inc(V.tensor_scalar(lgT0[:, chs], lgT0[:, chs], offcol, None, ALU.mult), 'dv')
            T.wait('dv', k.cnt['dv'])
            i = T.matmul(PA[0:1, 1024:1536], S.ones32c[:],
                         lgT0[:, chs], start=True, stop=True)
            T.inc(i, 'pe')
            V.wait('pe', k.cnt['pe'])
            V.inc(V.tensor_add(r_d[:, chs], PA[0:1, 1024:1536], r_d[:, chs]), 'dv')
            T.wait('dv', k.cnt['dv'])
    V.inc(V.tensor_scalar_add(r_leaf, r_d, -1.0), 'dv')
    di = lgT0[0:1, 0:c.To].bitcast(I32)
    V.inc(V.tensor_copy(di, r_leaf), 'dv')
    V.inc(V.tensor_single_scalar(di, di, 127, ALU.bitwise_and), 'dv')
    V.inc(V.tensor_copy(x32row, di), 'dv')
    V.inc(V.tensor_sub(r_leaf, r_leaf, x32row), 'dv')
    V.inc(V.tensor_scalar(r_leaf, r_leaf, 1.0 / 128.0, None, ALU.mult), 'dv')
    V.inc(V.tensor_scalar(x32row, x32row, float(c.NTILE), None, ALU.mult), 'dv')
    V.inc(V.tensor_add(r_leaf, r_leaf, x32row), 'dv')
    nvt = c.To // 128
    Y.wait('dv', k.cnt['dv'])
    Y.inc(Y.dma_start(rsc_r.ap()[:, 0:c.To], r_leaf), 'dq', 16)
    Y.wait('dq', k.cnt['dq'])
    Y.inc(Y.dma_start(S.bc[0:nvt, 0:128], E['rsc'].reshape([64, 128]).ap()[0:nvt, :]), 'dq', 16)
    T.wait('dq', k.cnt['dq'])
    i = T.matmul(PB[:, 0:nvt], S.bc[0:nvt, 0:128],
                 S.iden32[0:nvt, 0:nvt], start=True, stop=True)
    T.inc(i, 'pe')
    V.wait('pe', k.cnt['pe'])
    V.inc(V.tensor_copy(S.dstc[:, 0:nvt], PB[:, 0:nvt]), 'dv')
    P.wait('dv', k.cnt['dv'])
    P.inc(P.dma_start(E['slotTab'].ap(), E['c_oob'].ap()), 'dg', 16)
    P.wait('dg', k.cnt['dg'])
    for i_t in range(nvt):
        P.wait('dg', k.cnt['dg'])
        P.inc(P.indirect_dma_start(
            rsc_i32(k), IndirectOffsetOnAxis(ap=S.dstc[:, i_t:i_t + 1], axis=0),
            S.tki[:, i_t:i_t + 1], None), 'dg', 16)
    P.wait('dg', k.cnt['dg'])
    P.inc(P.dma_start(S.srt[:], E['slotTab'].reshape([128, c.NTILE]).ap()), 'dg', 16)
    for H in range(2):
        offcol = S.cols[:, 12 + H:13 + H]
        T.wait('dv', k.cnt['dv'])
        i = T.matmul(PB[:, 0:c.NCHUNK], S.ones1r32[:, :], S.srow[0:1, 0:c.NCHUNK],
                     start=True, stop=True)
        T.inc(i, 'pe')
        V.wait('pe', k.cnt['pe'])
        V.inc(V.tensor_scalar(lgT0[:, 0:c.NCHUNK], PB[:, 0:c.NCHUNK], offcol, None,
                              ALU.is_ge), 'dv')
        T.wait('dv', k.cnt['dv'])
        i = T.matmul(PA[0:1, 1024:1024 + c.NCHUNK], S.ones32c[:],
                     lgT0[:, 0:c.NCHUNK],
                     start=(H == 0), stop=(H == 1))
        T.inc(i, 'pe')
    V.wait('pe', k.cnt['pe'])
    V.inc(V.tensor_scalar_add(S.rowsc[0:1, 1024:1024 + c.NCHUNK], PA[0:1, 1024:1024 + c.NCHUNK],
                              -1.0), 'dv')
    Y.wait('dv', k.cnt['dv'])
    Y.inc(Y.dma_start(rsc_r.ap()[:, 0:c.NCHUNK], S.rowsc[0:1, 1024:1024 + c.NCHUNK]), 'dq', 16)
    Y.wait('dq', k.cnt['dq'])
    Y.inc(Y.dma_start(S.bc[0:c.NTILE, 0:8], E['rsc'].reshape([1024, 8]).ap()[0:c.NTILE, :]), 'dq', 16)
    V.wait('dq', k.cnt['dq'])
    V.inc(V.tensor_copy(S.ysl[0:c.NTILE, 0, 0:8], S.bc[0:c.NTILE, 0:8]), 'dv')
    T.wait('dv', k.cnt['dv'])
    i = T.matmul(PB[0:8, 0:c.NTILE], S.ysl[0:c.NTILE, 0, 0:8],
                 S.iden[0:c.NTILE, 0:c.NTILE], start=True, stop=True)
    T.inc(i, 'pe')
    V.wait('pe', k.cnt['pe'])
    V.inc(V.tensor_copy(S.clrf[:], PB[0:8, 0:c.NTILE]), 'dv')
    P.wait('dv', k.cnt['dv'])
    P.inc(P.memset(S.clri[:], 255), 'po')
    V.wait('po', k.cnt['po'])
    V.inc(V.tensor_copy(S.clri[0:8, :], S.clrf[:]), 'dv')
    V.inc(V.tensor_copy(S.ysl[0:8, 0, 0:c.NTILE], S.clrf[:]), 'dv')
    T.wait('dv', k.cnt['dv'])
    i = T.matmul(PB[:, 1024:1024 + c.NTILE], S.indic[:], S.ysl[0:8, 0, 0:c.NTILE],
                 start=True, stop=True)
    T.inc(i, 'pe')
    V.wait('pe', k.cnt['pe'])
    V.inc(V.tensor_scalar(S.bc[:, 0:c.NTILE], PB[:, 1024:1024 + c.NTILE], 16.0,
                          S.pmod[:, 0:1], ALU.mult, ALU.add), 'dv')
    V.inc(V.tensor_copy(S.idxw2[:], S.bc[:, 0:c.NTILE]), 'dv')
    cs = min(512, c.NCHUNK)
    for chix in range(c.NCHUNK // cs):
        chs = ds(chix, cs)
        T.wait('dv', k.cnt['dv'])
        i = T.matmul(PB[:, 0:cs], S.ones1r32[:, :], S.rowsc[0:1, 1024:1024 + c.NCHUNK][:, chs],
                     start=True, stop=True)
        T.inc(i, 'pe')
        V.wait('pe', k.cnt['pe'])
        V.inc(V.tensor_scalar(S.bc[:, 0:cs], PB[:, 0:cs], 128.0, S.iota[:, 0:1],
                              ALU.mult, ALU.add), 'dv')
        V.inc(V.tensor_copy(S.idxw1[:, chs], S.bc[:, 0:cs]), 'dv')
    barrier(k)


def rsc_i32(k):
    return k.E['slotTab'].reshape([k.c.SLOTS, 1]).ap()


def emit_fff_tiles(k, tree, srctok, dsttok, act):
    c, S, E = k.c, k.S, k.E
    Y, P, V, C, T = k.Y, k.P, k.V, k.C, k.T
    PA, PB = k.PA, k.PB
    for g in range(c.NTILE // 4):
        hb = g % 2
        if g >= 2:
            P.wait('gx%d' % hb, k.wtok_fff[hb])
        for j in range(4):
            i_t = 4 * g + j
            P.inc(P.indirect_dma_start(
                S.qT[:, 4 * hb + j, :], None, E[srctok].ap(),
                IndirectOffsetOnAxis(ap=S.srt[:, i_t:i_t + 1], axis=0),
                bounds_check=k.breg, oob_is_err=False), 'gg%d' % hb, 16)
        Y.wait('gg%d' % hb, k.cnt['gg%d' % hb])
        k.wtok_fff[hb] = Y.inc(
            Y.dma_start(E['xsorted'].ap()[ds(g, 512), :]
                        .rearrange('(j p) d -> p j d', p=128),
                        S.qT[:, 4 * hb:4 * hb + 4, :]),
            'gx%d' % hb, 16)
    barrier(k)
    dstap = E[dsttok].ap()[:, 0:1024]
    tok_pe = [0, 0, 0, 0]
    tok_ac = [0, 0]
    tok_msk = [0, 0]
    tok_dvy = [0, 0]
    y_pe = [0, 0]
    dq_blk = [0] * c.NBLK
    for dt in range(c.DT):
        Y.inc(Y.dma_start(S.qT[:, dt, 0:1024],
                          E['xsorted'].ap()[ds(0, 1024), ds(dt, 128)],
                          transpose=True), 'dq', 16)
    dq_blk[0] = k.cnt['dq']

    def gath(it):
        buf = it % 4
        P.wait('pe', tok_pe[buf])
        gsem = 'gt%d' % buf
        for cc in range(8):
            P.inc(P.indirect_dma_start(
                S.ktT[:, buf, 128 * cc:128 * cc + 128], None,
                E['w1gB_' + tree].ap(),
                IndirectOffsetOnAxis(ap=S.idxw1[:, it * 8 + cc:it * 8 + cc + 1], axis=0)), gsem, 16)
        P.inc(P.indirect_dma_start(
            S.vsup[:, buf, 0:1025], None, E['w2gB_' + tree].ap(),
            IndirectOffsetOnAxis(ap=S.idxw2[:, it:it + 1], axis=0)), gsem, 16)
        P.inc(P.indirect_dma_start(
            S.vsup[0:8, 4 + buf, 0:1024], None, E['b2B_' + tree].ap(),
            IndirectOffsetOnAxis(ap=S.clri[0:8, it:it + 1], axis=0)), gsem, 16)

    for it in range(min(3, c.NTILE)):
        gath(it)
    for it in range(c.NTILE):
        blk, ti = it // 8, it % 8
        buf = it % 4
        pp = it % 2
        gsem = 'gt%d' % buf
        if it + 3 < c.NTILE:
            gath(it + 3)
        V.wait(gsem, k.cnt[gsem])
        i = V.tensor_copy(
            S.ktT[:, 4 + buf, :].rearrange('p (a b j) -> p a b j', a=8, j=16),
            S.ktT[:, buf, :].rearrange('p (a b j) -> p b a j', a=8, j=16))
        V.inc(i, 'dv')
        i = V.tensor_copy(S.b1sb[:, buf, :], S.vsup[:, buf, 1024:1025])
        k.w1cp[buf] = V.inc(i, 'dv')
        if ti == 0:
            T.wait('dq', dq_blk[blk])
        T.wait(gsem, k.cnt[gsem])
        T.wait('dv', k.w1cp[buf])
        T.wait('ac', tok_ac[pp])
        for dt in range(c.DT):
            i = T.matmul(PA[:, pp * 512: pp * 512 + 128],
                         S.ktT[:, 4 + buf, ds(dt, 128)], S.qT[:, dt, ds(ti, 128)],
                         start=(dt == 0), stop=(dt == c.DT - 1))
        hpe = T.inc(i, 'pe')
        if ti == 7 and blk + 1 < c.NBLK:
            Y.wait('pe', hpe)
            for dt in range(c.DT):
                Y.inc(Y.dma_start(S.qT[:, dt, 0:1024],
                                  E['xsorted'].ap()[ds(blk + 1, 1024), ds(dt, 128)],
                                  transpose=True), 'dq', 16)
            dq_blk[blk + 1] = k.cnt['dq']
        C.wait('pe', k.cnt['pe'])
        C.wait(gsem, k.cnt[gsem])
        C.wait('dv', k.w1cp[buf])
        C.wait('dv', tok_msk[pp])
        i = C.activation(S.hA[:, pp, :], PA[:, pp * 512: pp * 512 + 128],
                         act, bias=S.b1sb[:, buf, :])
        tok_ac[pp] = C.inc(i, 'ac')
        V.wait('ac', k.cnt['ac'])
        V.wait('pe', y_pe[pp])
        i = V.tensor_mul(S.hT[:, pp, :], S.hA[:, pp, :], S.mdiag[:])
        tok_msk[pp] = V.inc(i, 'dv')
        T.wait('dv', k.cnt['dv'])
        T.wait('dv', tok_dvy[pp])
        pb0 = pp * 1024
        T.matmul(PB[:, pb0:pb0 + 512], S.hT[:, pp, :], S.vsup[:, buf, 0:512],
                 start=True, stop=False)
        T.matmul(PB[:, pb0 + 512:pb0 + 1024], S.hT[:, pp, :],
                 S.vsup[:, buf, 512:1024], start=True, stop=False)
        T.matmul(PB[:, pb0:pb0 + 512], S.indic[:], S.vsup[0:8, 4 + buf, 0:512],
                 start=False, stop=True)
        i = T.matmul(PB[:, pb0 + 512:pb0 + 1024], S.indic[:],
                     S.vsup[0:8, 4 + buf, 512:1024], start=False, stop=True)
        tok_pe[buf] = y_pe[pp] = T.inc(i, 'pe')
        V.wait('pe', k.cnt['pe'])
        V.wait('gy%d' % pp, k.ysl_tok[pp])
        i = V.tensor_copy(S.ysl[:, pp, :], PB[:, pb0:pb0 + 1024])
        tok_dvy[pp] = V.inc(i, 'dv')
        P.wait('dv', k.cnt['dv'])
        k.ysl_tok[pp] = P.inc(P.indirect_dma_start(
            dstap, IndirectOffsetOnAxis(ap=S.srt[:, it:it + 1], axis=0),
            S.ysl[:, pp, :], None,
            bounds_check=k.breg, oob_is_err=False), 'gy%d' % pp, 16)
        if c.dbg_serial:
            barrier(k)
    barrier(k)



def kernel(**inputs):
    cfg = Cfg()
    k = build(cfg)
    emit_program(k, script)
    k.es.close()
    in_maps = [prep_core_inputs(inputs, core, cfg) for core in range(8)]
    res = run_bass_kernel_spmd(k.nc, in_maps, core_ids=list(range(8)))
    out = np.empty((4, 8192, 1024), np.float32)
    for core in range(8):
        b, h = core // 2, core % 2
        out[b, h * cfg.To:(h + 1) * cfg.To, :] = res.results[core]['outT'].T
    return out


# version 12
# speedup vs baseline: 1.5805x; 1.0433x over previous
import numpy as np
import ml_dtypes
from contextlib import ExitStack

import concourse.bass as bass
import concourse.mybir as mybir
from concourse.bass_utils import run_bass_kernel_spmd
from concourse.bass_core_rust import IndirectOffsetOnAxis

F32 = mybir.dt.float32
F32R = mybir.dt.float32r
BF16 = mybir.dt.bfloat16
I32 = mybir.dt.int32
AF = mybir.ActivationFunctionType
ALU = mybir.AluOpType
AX = mybir.AxisListType
BF = ml_dtypes.bfloat16
SQD = float(np.sqrt(1024.0))


class Cfg:
    def __init__(self, To=4096, Tf=8192, groups=None, parts=(1, 2, 3, 4),
                 dumps=(), dumps_q=(), dbg_serial=False):
        self.dumps = dumps
        self.dumps_q = dumps_q
        self.dbg_serial = dbg_serial
        self.To = To; self.Tf = Tf
        self.D = 1024; self.DT = 8; self.H = 16
        self.KSUP = min(1024, Tf); self.NSUP = Tf // self.KSUP
        self.NKT = self.KSUP // 128
        self.QH = min(1024, To); self.NQH = To // self.QH
        self.QB = 512
        self.NQB = self.QH // self.QB
        self.QBA = 512
        self.NQBA = self.QH // self.QBA
        self.NNODE = 255; self.NLEAF = 256
        self.SLOTS = ((To + 15 * 256 + 1023) // 1024) * 1024
        self.NTILE = self.SLOTS // 128
        self.NCHUNK = self.SLOTS // 16
        self.NBLK = self.SLOTS // 1024
        self.ST = To // 512
        self.NTT = To // 128
        self.groups = groups if groups is not None else [[0, 1], [2, 3], [4, 5], [6, 7]]
        self.parts = parts


def ds(i, n):
    return slice(i * n, (i + 1) * n)



def prep_core_inputs(inputs, core, cfg):
    c = cfg
    b, h = core // 2, core % 2
    x = np.asarray(inputs['L0_attn_a_x'])
    d = {'xT': np.ascontiguousarray(x[b, h * c.To:(h + 1) * c.To, :].T)}
    d['aa_wqkv'] = np.asarray(inputs['L0_attn_a_w_qkv'])
    d['aa_wout'] = np.asarray(inputs['L0_attn_a_w_out'])
    d['aa_gcol'] = np.ascontiguousarray(
        np.asarray(inputs['L0_attn_a_gamma']).reshape(8, 128).T)
    for nm in 'qkvo':
        W = np.asarray(inputs['L0_attn_b_W' + nm])
        d['ab_w' + nm] = np.ascontiguousarray(W.T)
        bb = np.asarray(inputs['L0_attn_b_b' + nm])
        d['ab_bc' + nm] = np.ascontiguousarray(bb.reshape(8, 128).T)
        d['ab_br' + nm] = np.ascontiguousarray(bb.reshape(1, 1024))
    for t in 'qkvf':
        g = lambda s: np.asarray(inputs['L0_ffn_%s_%s' % (t, s)])
        d['f%s_nwT' % t] = np.ascontiguousarray(g('nw').T)
        nb = g('nb')
        nbp = np.zeros((128, 2), np.float32)
        nbp[:, 0] = nb[0:128]
        nbp[0:127, 1] = nb[128:255]
        d['f%s_nb' % t] = nbp
        d['f%s_w1g' % t] = np.ascontiguousarray(
            g('w1').reshape(256, 8, 128, 16).transpose(0, 2, 1, 3).reshape(256, 128, 128))
        d['f%s_b1' % t] = np.ascontiguousarray(g('b1').reshape(4096, 1))
        d['f%s_w2g' % t] = np.ascontiguousarray(np.concatenate(
            [g('w2').reshape(4096, 1024),
             g('b1').reshape(4096, 1).astype(np.float32)], axis=1))
        d['f%s_b2' % t] = np.asarray(g('b2'))
    d['c_iota'] = np.arange(128, dtype=np.float32).reshape(128, 1)
    d['c_pmod'] = (np.arange(128, dtype=np.float32) % 16).reshape(128, 1)
    d['c_tokid'] = np.ascontiguousarray(
        np.arange(c.To, dtype=np.float32).reshape(c.To // 128, 128).T)
    d['c_oob'] = np.full((c.SLOTS,), 1 << 20, np.int32)
    jj, ss = np.meshgrid(np.arange(128), np.arange(128), indexing='ij')
    d['c_maskdiag'] = ((jj // 16) == (ss // 16)).astype(BF)
    ind = np.zeros((8, 128), np.float32)
    for cc in range(8):
        ind[cc, cc * 16:(cc + 1) * 16] = 1.0
    d['c_indic8'] = ind.astype(BF)
    nkt, nqb = c.Tf // 128, c.To // c.QB
    kd = np.zeros((128, nkt * nqb), np.float32)
    for kt in range(nkt):
        for qb in range(nqb):
            kd[:, kt * nqb + qb] = np.arange(128) + 128 * kt - (qb * c.QB + h * c.To) + 16
    d['c_kd'] = kd
    d['c_qrow'] = np.tile(np.arange(c.QB, dtype=np.float32), (128, 1))
    d['c_srow'] = (16.0 * np.arange(c.NCHUNK, dtype=np.float32)).reshape(1, c.NCHUNK)
    return d



class NS:
    pass


def build(cfg):
    c = cfg
    nc = bass.Bass("TRN2", target_bir_lowering=False)
    k = NS(); k.c = c; k.nc = nc; k.E = {}

    def dram(name, shape, dt, kind="Internal"):
        k.E[name] = nc.dram_tensor(name, list(shape), dt, kind=kind)

    dram('xT', (c.D, c.To), F32, "ExternalInput")
    dram('aa_wqkv', (c.D, 3072), F32, "ExternalInput")
    dram('aa_wout', (c.D, c.D), F32, "ExternalInput")
    dram('aa_gcol', (128, 8), F32, "ExternalInput")
    for nm in 'qkvo':
        dram('ab_w' + nm, (c.D, c.D), F32, "ExternalInput")
        dram('ab_bc' + nm, (128, 8), F32, "ExternalInput")
        dram('ab_br' + nm, (1, c.D), F32, "ExternalInput")
    for t in 'qkvf':
        dram('f%s_nwT' % t, (c.D, c.NNODE), F32, "ExternalInput")
        dram('f%s_nb' % t, (128, 2), F32, "ExternalInput")
        dram('f%s_w1g' % t, (c.NLEAF, 128, 128), F32, "ExternalInput")
        dram('f%s_b1' % t, (4096, 1), F32, "ExternalInput")
        dram('f%s_w2g' % t, (4096, c.D + 1), F32, "ExternalInput")
        dram('f%s_b2' % t, (c.NLEAF, c.D), F32, "ExternalInput")
    dram('c_iota', (128, 1), F32, "ExternalInput")
    dram('c_pmod', (128, 1), F32, "ExternalInput")
    dram('c_tokid', (128, c.To // 128), F32, "ExternalInput")
    dram('c_oob', (c.SLOTS,), I32, "ExternalInput")
    dram('c_maskdiag', (128, 128), BF16, "ExternalInput")
    dram('c_indic8', (8, 128), BF16, "ExternalInput")
    dram('c_kd', (128, (c.Tf // 128) * (c.To // c.QB)), F32, "ExternalInput")
    dram('c_qrow', (128, c.QB), F32, "ExternalInput")
    dram('c_srow', (1, c.NCHUNK), F32, "ExternalInput")
    dram('outT', (c.D, c.To), F32, "ExternalOutput")

    dram('resid', (c.D, c.To), F32)
    dram('xn232', (c.D, c.To), F32)
    dram('yT32', (c.D, c.To), F32)
    dram('xtok', (c.To, c.D), BF16)
    dram('ytokM', (c.To, c.D), BF16)
    dram('xsorted', (c.SLOTS, c.D), BF16)
    dram('qTd', (c.D, c.To), BF16)
    dram('qtok', (c.To, c.D), BF16)
    dram('ktok', (c.To, 1056), BF16)
    dram('vtok', (c.To, 1056), BF16)
    dram('kfull', (c.Tf, 1056), BF16)
    dram('vfull', (c.Tf, 1056), BF16)
    dram('ktokF', (c.To, 1024), BF16)
    dram('vtokF', (c.To, 1024), BF16)
    dram('kfullF', (c.Tf, 1024), BF16)
    dram('vfullF', (c.Tf, 1024), BF16)
    dram('ftok', (c.To, c.D), BF16)
    dram('slotTab', (c.SLOTS,), I32)
    dram('rsc', (8192,), F32)
    for t in 'qkvf':
        dram('w1gB_' + t, (c.NLEAF * 128, 128), BF16)
        dram('w2gB_' + t, (4096, c.D + 1), BF16)
        dram('b2B_' + t, (c.NLEAF, c.D), BF16)
    for nm in c.dumps:
        h = k.E[nm]
        dram('dmp_' + nm, list(h.shape), h.dtype, "ExternalOutput")
    for nm in c.dumps_q:
        if nm.startswith('S.'):
            continue
        h = k.E[nm]
        dram('dmpq_' + nm, list(h.shape), h.dtype, "ExternalOutput")

    es = ExitStack(); k.es = es

    def sbuf(name, shape, dt):
        return es.enter_context(nc.sbuf_tensor(name, list(shape), dt))

    S = NS(); k.S = S
    QH = c.QH
    S.ktT = sbuf('ktT', (128, c.DT, c.KSUP), BF16)
    S.vsup = sbuf('vsup', (128, c.NKT, 1056), BF16)
    S.qT = sbuf('qT', (128, c.DT, QH), BF16)
    S.arena = sbuf('arena', (128, c.DT * QH), F32)
    S.pT = sbuf('pT', (128, 4, 512), BF16)
    S.x32 = sbuf('x32', (128, c.DT, 512), F32)
    S.xw = sbuf('xw', (128, c.DT, 512), BF16)
    S.ttk = sbuf('ttk', (128, 2, c.D), BF16)
    S.rows2 = sbuf('rows2', (128, max(c.To, 4 * c.QH)), F32)
    S.bc = sbuf('bc', (128, 1024), F32)
    S.cols = sbuf('cols', (128, 16), F32)
    S.w0 = sbuf('w0', (128, 2, 2048), BF16)
    S.onesb = sbuf('onesb', (128, 2), BF16)
    S.ones32c = sbuf('ones32c', (128, 1), F32)
    S.ones1rb = sbuf('ones1rb', (1, 128), BF16)
    S.ones1r32 = sbuf('ones1r32', (1, 128), F32)
    S.iota = sbuf('iota', (128, 1), F32)
    S.pmod = sbuf('pmod', (128, 1), F32)
    S.iden = sbuf('iden', (128, 128), BF16)
    S.iden32 = sbuf('iden32', (128, 128), F32)
    S.rowsc = sbuf('rowsc', (128, 2048), F32)
    S.nwT = sbuf('nwT', (128, c.DT, 256), F32R)
    S.xr = sbuf('xr', (128, c.DT, 512), F32R)
    S.nb = sbuf('nb', (128, 2), F32)
    S.tokid = sbuf('tokid', (128, c.To // 128), F32)
    S.tki = sbuf('tki', (128, c.To // 128), I32)
    S.dstc = sbuf('dstc', (128, c.To // 128), I32)
    S.srt = sbuf('srt', (128, c.NTILE), I32)
    S.idxw1 = sbuf('idxw1', (128, c.NCHUNK), I32)
    S.idxw2 = sbuf('idxw2', (128, c.NTILE), I32)
    S.clri = sbuf('clri', (128, c.NTILE), I32)
    S.clrf = sbuf('clrf', (8, c.NTILE), F32)
    S.b1sb = sbuf('b1sb', (128, 4, 1), F32)
    S.hT = sbuf('hT', (128, 2, 128), BF16)
    S.hA = sbuf('hA', (128, 2, 128), BF16)
    S.ysl = sbuf('ysl', (128, 2, c.D), BF16)
    S.mdiag = sbuf('mdiag', (128, 128), BF16)
    S.indic = sbuf('indic', (8, 128), BF16)
    S.kd = sbuf('kd', (128, (c.Tf // 128) * (c.To // c.QB)), F32)
    S.qrow = sbuf('qrow', (128, c.QB), F32)
    S.srow = sbuf('srow', (1, c.NCHUNK), F32)
    S.brow = sbuf('brow', (128, 2 * c.D), BF16)
    S.vev = sbuf('vev', (128, 544), BF16)
    S.constv = sbuf('constv', (128, 1), F32)
    nc.const_aps.register(F32, 1e-5, S.constv[:, 0:1])

    for nm in c.dumps_q:
        if nm.startswith('S.'):
            t = getattr(S, nm[2:])
            dram('dmpq_' + nm[2:], list(t[:].tensor.shape), t[:].dtype,
                 "ExternalOutput")
    k.PA = es.enter_context(nc.psum_tensor('PA', [128, 2048], F32))
    k.PB = es.enter_context(nc.psum_tensor('PB', [128, 2048], F32))

    k.sems = {}
    for s in ['dq', 'dg', 'cc', 'dw0', 'dw1', 'sp0', 'sp1', 'sv', 'sb', 'sr',
              'gg0', 'gg1', 'gx0', 'gx1', 'gt0', 'gt1', 'gt2', 'gt3',
              'gy0', 'gy1', 'dk', 'pe', 'ac', 'dv', 'po', 'sq']:
        k.sems[s] = es.enter_context(nc.semaphore(s))
    k.DMASEMS = ('dq', 'dg', 'cc', 'dw0', 'dw1', 'sp0', 'sp1', 'sv', 'sb', 'sr',
                 'gg0', 'gg1', 'gx0', 'gx1', 'gt0', 'gt1', 'gt2', 'gt3',
                 'gy0', 'gy1', 'dk')
    return k



class EngProxy:
    def __init__(self, k, name):
        self._k = k
        self._name = name

    _NOWAIT = {'wait_ge', 'alloc_register', 'reg_mov', 'free_register', 'nop'}
    _OWN = {'ve': 'dv', 'sc': 'ac'}

    def __getattr__(self, attr):
        k = self._k
        name = self._name

        def call(*args, **kw):
            if k.live == name:
                own = EngProxy._OWN.get(name)
                if own and attr not in EngProxy._NOWAIT and k.cnt[own] > 0:
                    key = (name, own)
                    if k.waited.get(key, -1) < k.cnt[own]:
                        k.engs[name].wait_ge(k.sems[own], k.cnt[own])
                        k.waited[key] = k.cnt[own]
                try:
                    return getattr(k.engs[name], attr)(*args, **kw)
                except Exception as e:
                    raise RuntimeError('engine op failed: %s.%s: %r' % (name, attr, e)) from e
            elif EngProxy._OWN.get(name) and attr not in EngProxy._NOWAIT:
                own = EngProxy._OWN[name]
                k.waited[(name, own)] = max(k.waited.get((name, own), -1), k.cnt[own])
            return None
        return call

    def wait(self, sem, val=None):
        k = self._k
        if val is None:
            val = k.cnt[sem]
        val = min(val, k.cnt[sem])
        if val <= 0:
            return
        if k.live == self._name:
            key = (self._name, sem)
            if k.waited.get(key, -1) < val:
                k.engs[self._name].wait_ge(k.sems[sem], val)
                k.waited[key] = val

    def inc(self, instr, sem, step=1):
        k = self._k
        k.cnt[sem] += step
        if instr is not None:
            instr.then_inc(k.sems[sem], step)
        return k.cnt[sem]


def emit_program(k, scriptfn):
    blk = k.es.enter_context(k.nc.Block())
    k.engs = {}
    k.T = EngProxy(k, 'te'); k.V = EngProxy(k, 've'); k.C = EngProxy(k, 'sc')
    k.P = EngProxy(k, 'po'); k.Y = EngProxy(k, 'sy')

    def replay(name, eng):
        k.engs[name] = eng
        k.live = name
        k.cnt = {s: 0 for s in k.sems}
        k.waited = {}
        scriptfn(k)

    @blk.sync
    def _(e):
        replay('sy', e)

    @blk.scalar
    def _(e):
        replay('sc', e)

    @blk.vector
    def _(e):
        replay('ve', e)

    @blk.tensor
    def _(e):
        replay('te', e)

    @blk.gpsimd
    def _(e):
        replay('po', e)


def barrier(k, engines=None):
    snap = dict(k.cnt)
    snap.pop('cc', None)
    own = {'te': 'pe', 've': 'dv', 'sc': 'ac', 'po': 'po', 'sy': 'sq'}
    prox = {'te': k.T, 've': k.V, 'sc': k.C, 'po': k.P, 'sy': k.Y}
    arr = {}
    for e in ('te', 've', 'sc', 'po', 'sy'):
        p = prox[e]
        for s, v in snap.items():
            if v > 0:
                p.wait(s, v)
        arr[e] = p.inc(p.nop(), own[e])
    for e in ('te', 've', 'sc', 'po', 'sy'):
        p = prox[e]
        for e2 in ('te', 've', 'sc', 'po', 'sy'):
            if e2 != e:
                p.wait(own[e2], arr[e2])



def script(k):
    c = k.c
    k.ac_exp = [0, 0]
    k.dv_msk = [0, 0]
    k.pe_qw = [0, 0]
    k.pe_kw = [0, 0]
    k.wtok_fff = [0, 0]
    k.ysl_tok = [0, 0]
    k.w1cp = [0, 0, 0, 0]
    k.qtd_st = [0, 0]
    k.vev_st = 0
    k.bc_st = 0
    k.qp_ev = [0, 0]
    k.qp_evv = [0, 0]
    k.tkm_tok = [0, 0]
    k.tkm_st = 0
    k.rt_pe = [0, 0]
    if k.live == 'po':
        k.breg = k.engs['po'].alloc_register('bnd')
        k.engs['po'].reg_mov(k.breg, c.To - 1)
    else:
        k.breg = None
    emit_init(k)
    src = 'xT'
    for i, p in enumerate(c.parts):
        dst = 'outT' if i == len(c.parts) - 1 else 'resid'
        if p == 1:
            emit_attn_sublayer(k, src, dst, 'aa')
        elif p == 3:
            emit_attn_sublayer(k, src, dst, 'ab')
        else:
            emit_ffn_sublayer(k, src, dst)
        src = 'resid'
    for nm in c.dumps:
        k.Y.inc(k.Y.dma_start(k.E['dmp_' + nm].ap(), k.E[nm].ap()), 'dq', 16)
        k.Y.wait('dq', k.cnt['dq'])
    for s in ('dq', 'dg', 'cc', 'pe', 'ac', 'dv', 'po'):
        if k.cnt[s] > 0:
            k.Y.wait(s, k.cnt[s])


def emit_init(k):
    c, S, E = k.c, k.S, k.E
    Y, P, V = k.Y, k.P, k.V
    for nm, dst in [('c_iota', S.iota), ('c_pmod', S.pmod), ('c_tokid', S.tokid),
                    ('c_maskdiag', S.mdiag), ('c_indic8', S.indic), ('c_kd', S.kd),
                    ('c_qrow', S.qrow), ('c_srow', S.srow)]:
        Y.inc(Y.dma_start(dst[:], E[nm].ap()), 'dq', 16)
    P.inc(P.memset(S.iden[:], 0.0), 'po')
    P.wait('po', k.cnt['po'])
    P.inc(P.affine_select(out=S.iden[:], in_=S.iden[:], compare_op=ALU.not_equal,
                          fill=1.0, base=0, pattern=[[-1, 128]],
                          channel_multiplier=1), 'po')
    P.inc(P.memset(S.iden32[:], 0.0), 'po')
    P.wait('po', k.cnt['po'])
    P.inc(P.affine_select(out=S.iden32[:], in_=S.iden32[:], compare_op=ALU.not_equal,
                          fill=1.0, base=0, pattern=[[-1, 128]],
                          channel_multiplier=1), 'po')
    P.inc(P.memset(S.constv[:], 1e-5), 'po')
    P.inc(P.memset(S.onesb[:], 1.0), 'po')
    P.inc(P.memset(S.ones32c[:], 1.0), 'po')
    P.inc(P.memset(S.ones1rb[:], 1.0), 'po')
    P.inc(P.memset(S.ones1r32[:], 1.0), 'po')
    P.inc(P.memset(S.vev[:], 0.0), 'po')
    P.wait('po', k.cnt['po'])
    for g in range(8):
        P.inc(P.memset(S.vev[:, g * 66 + 64: g * 66 + 65], 1.0), 'po')
    V.wait('dq', k.cnt['dq'])
    V.inc(V.tensor_copy(S.tki[:], S.tokid[:]), 'dv')
    P.inc(P.memset(S.ysl[:, 0, 0:((c.To // 128) * 32)], 0.0), 'po')
    P.wait('po', k.cnt['po'])
    for nm in ('ktok', 'vtok'):
        P.inc(P.dma_start(E[nm].ap()[:, 1024:1056]
                          .rearrange('(a p) v -> p a v', p=128),
                          S.ysl[:, 0, 0:((c.To // 128) * 32)]
                          .rearrange('p (a v) -> p a v', v=32)), 'dg', 16)
    vflat = S.vsup[:].rearrange('p a b -> p (a b)')
    piece = 4096
    stride = 4224
    if 2 in c.parts:
        for t in 'qkvf':
            for (src, dstn, n) in [('f%s_w1g' % t, 'w1gB_' + t, c.NLEAF * 128 * 128),
                                   ('f%s_w2g' % t, 'w2gB_' + t, 4096 * (c.D + 1)),
                                   ('f%s_b2' % t, 'b2B_' + t, c.NLEAF * c.D)]:
                npc = max(n // (128 * piece), 1)
                per = n // 128 // npc
                assert per <= stride and n == per * npc * 128
                sap = E[src].reshape([128, n // 128]).ap()
                dap = E[dstn].reshape([128, n // 128]).ap()
                for i in range(npc):
                    buf = vflat[:, (i % 2) * stride:(i % 2) * stride + per]
                    P.wait('dk', k.cnt['dk'])
                    P.inc(P.dma_start(buf, sap[:, ds(i, per)]), 'dk', 16)
                    P.wait('dk', k.cnt['dk'])
                    P.inc(P.dma_start(dap[:, ds(i, per)], buf), 'dk', 16)
    barrier(k)



def emit_prologue(k, src, kind):
    c, S, E = k.c, k.S, k.E
    Y, P, V, C, T = k.Y, k.P, k.V, k.C, k.T
    PA, PB = k.PA, k.PB
    xs = E[src].ap()

    if kind == 'aa':
        Y.inc(Y.dma_start(S.cols[:, 0:8], E['aa_gcol'].ap()), 'dq', 16)
        barrier(k, ('ve',))
        V.inc(V.tensor_scalar_add(S.cols[:, 0:8], S.cols[:, 0:8], 1.0), 'dv')
    if kind == 'ab':
        Y.inc(Y.dma_start(S.cols[:, 0:8], E['ab_bcq'].ap()), 'dq', 16)
        P.inc(P.dma_start(S.brow[0:1, 0:c.D], E['ab_brk'].ap()), 'dg', 16)
        P.inc(P.dma_start(S.brow[0:1, c.D:2 * c.D], E['ab_brv'].ap()), 'dg', 16)

    for s in range(c.ST):
        tsl = ds(s, 512)
        barrier(k)
        Y.inc(Y.dma_start(S.x32[:], xs[:, tsl].rearrange('(a p) t -> p a t', p=128)),
              'dq', 16)
        ldq = k.cnt['dq']
        C.wait('dq', ldq)
        for dt in range(c.DT):
            C.inc(C.activation(S.xw[:, dt, :], S.x32[:, dt, :], AF.Square), 'ac')
        T.wait('ac', k.cnt['ac'])
        T.wait('dq', ldq)
        for dt in range(c.DT):
            T.matmul(PA[0:1, 1024:1536], S.ones32c[:, 0:1], S.x32[:, dt, :],
                     start=(dt == 0), stop=(dt == c.DT - 1))
        for dt in range(c.DT):
            i = T.matmul(PA[0:1, 1536:2048], S.onesb[:, 0:1], S.xw[:, dt, :],
                         start=(dt == 0), stop=(dt == c.DT - 1))
        T.inc(i, 'pe')
        R = S.rowsc
        R2 = S.rows2
        m_ = R[0:1, 0:512]; m2_ = R[0:1, 512:1024]
        fs_ = R[0:1, 1024:1536]; f_ = R[0:1, 1536:2048]
        v_ = R2[0:1, 0:512]; sq_ = R2[0:1, 512:1024]
        r_ = R2[0:1, 1024:1536]; t_ = R2[0:1, 1536:2048]
        V.wait('pe', k.cnt['pe'])
        V.inc(V.tensor_scalar_mul(m_, PA[0:1, 1024:1536], 1.0 / 1024.0), 'dv')
        V.inc(V.tensor_mul(m2_, m_, m_), 'dv')
        V.inc(V.scalar_tensor_tensor(v_, PA[0:1, 1536:2048], 1.0 / 1024.0, m2_,
                                     ALU.mult, ALU.subtract), 'dv')
        C.wait('dv', k.cnt['dv'])
        C.inc(C.activation(sq_, v_, AF.Sqrt, bias=1e-5), 'ac')
        V.wait('ac', k.cnt['ac'])
        V.inc(V.reciprocal(r_, sq_), 'dv')
        if kind == 'aa':
            C.wait('dv', k.cnt['dv'])
            C.inc(C.activation(t_, v_, AF.Sqrt, scale=1024.0), 'ac')
            V.wait('ac', k.cnt['ac'])
            V.inc(V.tensor_mul(t_, t_, r_), 'dv')
            V.inc(V.tensor_scalar_max(t_, t_, 1e-12), 'dv')
            V.inc(V.reciprocal(t_, t_), 'dv')
            V.inc(V.scalar_tensor_tensor(t_, t_, SQD, r_, ALU.mult, ALU.mult), 'dv')
            V.inc(V.tensor_copy(f_, t_), 'dv')
        elif kind == 'ffn':
            C.wait('dv', k.cnt['dv'])
            C.inc(C.activation(t_, v_, AF.Sqrt, scale=1024.0), 'ac')
            V.wait('ac', k.cnt['ac'])
            V.inc(V.scalar_tensor_tensor(t_, t_, 1.0 / float(np.sqrt(1023.0)), r_,
                                         ALU.mult, ALU.mult), 'dv')
            C.wait('dv', k.cnt['dv'])
            C.inc(C.activation(t_, t_, AF.Sqrt, bias=1e-5), 'ac')
            V.wait('ac', k.cnt['ac'])
            V.inc(V.reciprocal(t_, t_), 'dv')
            V.inc(V.tensor_mul(t_, t_, r_), 'dv')
            V.inc(V.tensor_copy(f_, t_), 'dv')
        else:
            V.inc(V.tensor_copy(f_, r_), 'dv')
        T.wait('dv', k.cnt['dv'])
        T.matmul(PB[:, 0:512], S.ones1r32[:, :], S.rowsc[0:1, 0:512], start=True, stop=True)
        i = T.matmul(PB[:, 512:1024], S.ones1r32[:, :], S.rowsc[0:1, 1536:2048], start=True, stop=True)
        T.inc(i, 'pe')
        V.wait('pe', k.cnt['pe'])
        for dt in range(c.DT):
            V.inc(V.scalar_tensor_tensor(S.x32[:, dt, :], S.x32[:, dt, :], 1.0,
                                         PB[:, 0:512], ALU.mult, ALU.subtract), 'dv')
            if kind == 'aa':
                V.inc(V.scalar_tensor_tensor(S.x32[:, dt, :], S.x32[:, dt, :],
                                             S.cols[:, dt:dt + 1], PB[:, 512:1024],
                                             ALU.mult, ALU.mult), 'dv')
            else:
                V.inc(V.scalar_tensor_tensor(S.x32[:, dt, :], S.x32[:, dt, :], 1.0,
                                             PB[:, 512:1024], ALU.mult, ALU.mult), 'dv')
            V.inc(V.tensor_copy(S.xw[:, dt, :], S.x32[:, dt, :]), 'dv')
        if kind == 'ffn':
            Y.wait('dv', k.cnt['dv'])
            Y.inc(Y.dma_start(E['xn232'].ap()[:, tsl].rearrange('(a p) t -> p a t', p=128),
                              S.x32[:]), 'dq', 16)
            emit_tokmajor(k, 'xtok', s)
        else:
            emit_stripe_projections(k, s, kind)
    barrier(k)


def emit_tokmajor(k, dsttok, s):
    c, S, E = k.c, k.S, k.E
    Y, V, T = k.Y, k.V, k.T
    T.wait('dv', k.cnt['dv'])
    for tt in range(4):
        for dt in range(c.DT):
            bank = dt % 2
            T.wait('dv', k.tkm_tok[bank])
            i = T.matmul(PAq(k)[:, bank * 512: bank * 512 + 128],
                         S.xw[:, dt, ds(tt, 128)], S.iden[:], start=True, stop=True)
            T.inc(i, 'pe')
            V.wait('pe', k.cnt['pe'])
            V.wait('dq', k.tkm_st)
            i = V.tensor_copy(S.ttk[:, tt % 2, ds(dt, 128)],
                              PAq(k)[:, bank * 512: bank * 512 + 128])
            k.tkm_tok[bank] = V.inc(i, 'dv')
        if tt % 2 == 1:
            Y.wait('dv', k.cnt['dv'])
            k.tkm_st = Y.inc(
                Y.dma_start(E[dsttok].ap()[s * 512 + (tt - 1) * 128:
                                           s * 512 + (tt + 1) * 128, :]
                            .rearrange('(a p) d -> p a d', p=128), S.ttk[:]),
                'dq', 16)


def PAq(k):
    return k.PA


def emit_stripe_projections(k, s, kind):
    c, S, E = k.c, k.S, k.E
    Y, P, V, C, T = k.Y, k.P, k.V, k.C, k.T
    PA, PB = k.PA, k.PB
    tsl = ds(s, 512)
    w0f = S.w0[:].rearrange('p a b -> p (a b)')

    if kind == 'aa':
        qsrc = 'aa_wqkv'
        kvspec = [('aa_wqkv', 1024, 'ktok', False), ('aa_wqkv', 2048, 'vtok', True)]
    else:
        qsrc = 'ab_wq'
        kvspec = [('ab_wk', 0, 'ktokF', False), ('ab_wv', 0, 'vtokF', False)]

    k.pe_qw[0] = max(k.pe_qw[0], k.pe_kw[0])
    k.pe_qw[1] = max(k.pe_qw[1], k.pe_kw[1])
    for g in range(8):
        buf = g % 2
        if g >= 2:
            P.wait('pe', k.pe_qw[buf])
        else:
            P.wait('pe', k.pe_qw[buf])
        P.inc(P.dma_start(S.w0[:, buf, 0:1024].rearrange('p (a q) -> p a q', q=128),
                          E[qsrc].ap()[:, ds(g, 128)]
                          .rearrange('(a p) q -> p a q', p=128)), 'dw%d' % buf, 16)
        T.wait('dw%d' % buf, k.cnt['dw%d' % buf])
        T.wait('dv', k.cnt['dv'])
        T.wait('ac', k.qp_ev[g % 2])
        T.wait('dv', k.qp_evv[g % 2])
        for dt in range(c.DT):
            i = T.matmul(PB[:, 1024 + (g % 2) * 512: 1024 + (g % 2) * 512 + 512],
                         S.w0[:, buf, ds(dt, 128)], S.xw[:, dt, :],
                         start=(dt == 0), stop=(dt == c.DT - 1))
        k.pe_qw[buf] = T.inc(i, 'pe')
        if kind == 'ab':
            V.wait('pe', k.cnt['pe'])
            V.wait('sp%d' % (g % 2), k.qtd_st[g % 2])
            i = V.tensor_scalar(S.pT[:, 2 * (g % 2), :],
                                PB[:, 1024 + (g % 2) * 512:1024 + (g % 2) * 512 + 512],
                                S.cols[:, g:g + 1], None, ALU.add)
            k.qp_evv[g % 2] = V.inc(i, 'dv')
            Y.wait('dv', k.cnt['dv'])
        else:
            C.wait('pe', k.cnt['pe'])
            C.wait('sp%d' % (g % 2), k.qtd_st[g % 2])
            i = C.activation(S.pT[:, 2 * (g % 2), :],
                             PB[:, 1024 + (g % 2) * 512:1024 + (g % 2) * 512 + 512],
                             AF.Copy)
            k.qp_ev[g % 2] = C.inc(i, 'ac')
            Y.wait('ac', k.cnt['ac'])
        Y.wait('dv', k.cnt['dv'])
        Y.wait('ac', k.cnt['ac'])
        k.qtd_st[g % 2] = Y.inc(
            Y.dma_start(E['qTd'].ap()[ds(g, 128), tsl], S.pT[:, 2 * (g % 2), :]),
            'sp%d' % (g % 2), 16)

    k.pe_kw[0] = max(k.pe_kw[0], k.pe_qw[0])
    k.pe_kw[1] = max(k.pe_kw[1], k.pe_qw[1])
    for (wsrc, coff, dstname, aug) in kvspec:
        for kvg in range(4):
            buf = kvg % 2
            P.wait('pe', k.pe_kw[buf])
            P.inc(P.dma_start(S.w0[:, buf, :].rearrange('p (a q) -> p a q', q=256),
                              E[wsrc].ap()[:, coff + kvg * 256: coff + kvg * 256 + 256]
                              .rearrange('(a p) q -> p a q', p=128)), 'dw%d' % buf, 16)
            wv = S.w0[:, buf, :].rearrange('p (a q) -> p a q', q=256)
            for tt in range(4):
                T.wait('dw%d' % buf, k.cnt['dw%d' % buf])
                T.wait('dv', k.cnt['dv'])
                T.wait('dv', k.qp_evv[tt % 2])
                for dt in range(c.DT):
                    i = T.matmul(PB[:, (tt % 2) * 512:(tt % 2) * 512 + 256],
                                 S.xw[:, dt, ds(tt, 128)], wv[:, dt, :],
                                 start=(dt == 0),
                                 stop=(dt == c.DT - 1 and kind != 'ab'))
                if kind == 'ab':
                    brow = S.brow[0:1, 0:c.D] if dstname == 'ktokF' else S.brow[0:1, c.D:2 * c.D]
                    T.wait('dg', k.cnt['dg'])
                    i = T.matmul(PB[:, (tt % 2) * 512:(tt % 2) * 512 + 256],
                                 S.ones1rb[:, :],
                                 brow[:, kvg * 256:kvg * 256 + 256],
                                 start=False, stop=True)
                T.inc(i, 'pe')
                V.wait('pe', k.cnt['pe'])
                if aug:
                    V.wait('sv', k.vev_st)
                    V.inc(V.tensor_copy(
                        S.vev[:, 0:264].rearrange('p (g e) -> p g e', e=66)[:, :, 0:64],
                        PB[:, (tt % 2) * 512:(tt % 2) * 512 + 256]
                        .rearrange('p (g e) -> p g e', e=64)), 'dv')
                    k.qp_evv[tt % 2] = k.cnt['dv']
                    Y.wait('dv', k.cnt['dv'])
                    k.vev_st = Y.inc(Y.dma_start(
                        E[dstname].ap()[s * 512 + tt * 128: s * 512 + tt * 128 + 128,
                                        kvg * 264: kvg * 264 + 264],
                        S.vev[:, 0:264]), 'sv', 16)
                else:
                    V.wait('sp%d' % (tt % 2), k.qtd_st[tt % 2])
                    V.inc(V.tensor_copy(S.pT[:, 2 * (tt % 2), 0:256],
                                        PB[:, (tt % 2) * 512:(tt % 2) * 512 + 256]), 'dv')
                    k.qp_evv[tt % 2] = k.cnt['dv']
                    Y.wait('dv', k.cnt['dv'])
                    k.qtd_st[tt % 2] = Y.inc(Y.dma_start(
                        E[dstname].ap()[s * 512 + tt * 128: s * 512 + tt * 128 + 128,
                                        kvg * 256: kvg * 256 + 256],
                        S.pT[:, 2 * (tt % 2), 0:256]), 'sp%d' % (tt % 2), 16)
            k.pe_kw[buf] = k.cnt['pe']


def emit_attn_sublayer(k, src, dst, kind):
    c = k.c
    emit_prologue(k, src, kind)
    emit_exchange(k, kind == 'aa')
    for qh in range(c.NQH):
        emit_attention_quarter(k, qh, kind)
        if kind == 'aa':
            emit_epilogue_proj(k, qh, 'aa_wout', None, src, dst, heads=True)
        else:
            emit_epilogue_proj(k, qh, 'ab_wo', 'ab_bco', src, dst, heads=False)
    barrier(k)


def emit_exchange(k, aug):
    c, S, E = k.c, k.S, k.E
    P = k.P
    kk, vv = ('ktok', 'vtok') if aug else ('ktokF', 'vtokF')
    kf, vf = ('kfull', 'vfull') if aug else ('kfullF', 'vfullF')
    barrier(k)
    i = P.collective_compute("AllGather", ALU.bypass, replica_groups=c.groups,
                             ins=[E[kk].ap().opt()], outs=[E[kf].ap().opt()])
    P.inc(i, 'cc')
    i = P.collective_compute("AllGather", ALU.bypass, replica_groups=c.groups,
                             ins=[E[vv].ap().opt()], outs=[E[vf].ap().opt()])
    P.inc(i, 'cc')


def emit_attention_quarter(k, qh, kind):
    c, S, E = k.c, k.S, k.E
    Y, P, V, C, T = k.Y, k.P, k.V, k.C, k.T
    PA, PB = k.PA, k.PB
    QH = c.QH
    qsl = ds(qh, QH)
    scale = 0.125 if kind == 'aa' else 1.0 / 32.0
    arena = S.arena[:].rearrange('p (a q) -> p a q', q=QH)

    barrier(k)
    if kind == 'ffn':
        for dt in range(c.DT):
            Y.inc(Y.dma_start(S.qT[:, dt, :], E['qtok'].ap()[qsl, ds(dt, 128)],
                              transpose=True), 'dq', 16)
    else:
        Y.inc(Y.dma_start(S.qT[:], E['qTd'].ap()[:, qsl]
                          .rearrange('(a p) t -> p a t', p=128)), 'dq', 16)
    P.inc(P.memset(S.arena[:], 0.0), 'po')
    P.inc(P.memset(S.rows2[:], 0.0), 'po')
    P.inc(P.memset(S.rowsc[32:33, :], 0.0), 'po')
    P.inc(P.memset(S.rowsc[64:65, :], 0.0), 'po')
    barrier(k)

    kf, vf = ('kfull', 'vfull') if kind == 'aa' else ('kfullF', 'vfullF')
    Y.wait('cc', k.cnt['cc'])
    for sup in range(c.NSUP):
        ksl = slice(sup * c.KSUP, (sup + 1) * c.KSUP)
        barrier(k)
        for dt in range(c.DT):
            Y.inc(Y.dma_start(S.ktT[:, dt, :], E[kf].ap()[ksl, ds(dt, 128)],
                              transpose=True), 'dq', 16)
        if kind == 'aa':
            Y.inc(Y.dma_start(S.vsup[:], E[vf].ap()[ksl, :]
                              .rearrange('(a p) v -> p a v', p=128)), 'dq', 16)
        else:
            Y.inc(Y.dma_start(S.vsup[:, :, 0:1024], E[vf].ap()[ksl, :]
                              .rearrange('(a p) v -> p a v', p=128)), 'dq', 16)
        barrier(k)
        if kind == 'aa':
            emit_super_aa(k, scale)
        else:
            emit_super_single(k, scale, kind, qh, sup)
        barrier(k)


def aa_sum_slot(k, hh, off, width):
    S, c = k.S, k.c
    QH = c.QH
    if hh < 12:
        return S.rows2[32 * (hh // 4):32 * (hh // 4) + 1,
                       (hh % 4) * QH + off:(hh % 4) * QH + off + width]
    h2 = hh - 12
    return S.rowsc[32 * (h2 // 2 + 1):32 * (h2 // 2 + 1) + 1,
                   (h2 % 2) * 1024 + off:(h2 % 2) * 1024 + off + width]


def emit_super_aa(k, scale):
    c, S = k.c, k.S
    V, C, T = k.V, k.C, k.T
    PA, PB = k.PA, k.PB
    QH = c.QH
    arena = S.arena[:].rearrange('p (a q) -> p a q', q=QH)
    pTf = S.pT[:].rearrange('p a b -> p (a b)')
    for qb in range(c.NQBA):
        qbs = ds(qb, c.QBA)
        for pr in range(8):
            hA, hB = 2 * pr, 2 * pr + 1
            def pv(kt):
                par = kt % 2
                T.wait('ac', k.ac_exp[par])
                T.matmul(PB[0:65, 0:512], S.vsup[:, kt, 66 * hA:66 * hA + 65],
                         S.pT[:, 2 * par, :], start=(kt == 0), stop=(kt == c.NKT - 1))
                return T.matmul(PB[0:65, 512:1024],
                                S.vsup[:, kt, 66 * hB:66 * hB + 65],
                                S.pT[:, 2 * par + 1, :],
                                start=(kt == 0), stop=(kt == c.NKT - 1))
            for kt in range(c.NKT):
                par = kt % 2
                T.wait('ac', k.ac_exp[par])
                T.matmul(PA[:, par * 1024: par * 1024 + 512],
                         S.ktT[0:64, pr, ds(kt, 128)],
                         S.qT[0:64, pr, qbs], start=True, stop=True)
                iB = T.matmul(PA[:, par * 1024 + 512: par * 1024 + 1024],
                              S.ktT[64:128, pr, ds(kt, 128)],
                              S.qT[64:128, pr, qbs], start=True, stop=True)
                T.inc(iB, 'pe')
                pe_s = k.cnt['pe']
                C.wait('pe', pe_s)
                i = C.activation(pTf[:, par * 1024:(par + 1) * 1024],
                                 PA[:, par * 1024:(par + 1) * 1024], AF.Exp,
                                 scale=scale)
                k.ac_exp[par] = C.inc(i, 'ac')
                if kt >= 1:
                    pv(kt - 1)
            i = pv(c.NKT - 1)
            T.inc(i, 'pe')
            V.wait('pe', k.cnt['pe'])
            for hh, reg in ((hA, 0), (hB, 1)):
                dt = hh // 2
                rb = (hh % 2) * 64
                V.inc(V.tensor_add(arena[rb:rb + 64, dt, qbs],
                                   PB[0:64, reg * 512:reg * 512 + 512],
                                   arena[rb:rb + 64, dt, qbs]), 'dv')
                slot = aa_sum_slot(k, hh, qb * c.QBA, c.QBA)
                V.inc(V.tensor_add(slot, PB[64:65, reg * 512:reg * 512 + 512],
                                   slot), 'dv')
            T.wait('dv', k.cnt['dv'])


def emit_super_single(k, scale, kind, qh, sup):
    c, S = k.c, k.S
    V, C, T = k.V, k.C, k.T
    PA, PB = k.PA, k.PB
    QH = c.QH
    arena = S.arena[:].rearrange('p (a q) -> p a q', q=QH)
    pTf = S.xw[:].rearrange('p a b -> p (a b)')
    nqb = c.To // c.QB
    for qb in range(c.NQB):
        qbs = ds(qb, c.QB)
        exp_done = [0] * c.NKT
        msk_done = [0] * c.NKT

        def sums(kt):
            if kind == 'ab':
                T.wait('dv', msk_done[kt])
            else:
                T.wait('ac', exp_done[kt])
            return T.matmul(PA[0:1, 1024:1536], S.onesb[:, 0:1],
                            pTf[:, kt * 512:(kt + 1) * 512],
                            start=(kt == 0), stop=(kt == c.NKT - 1))
        for kt in range(c.NKT):
            par = kt % 2
            T.wait('ac', k.ac_exp[par])
            T.wait('dv', k.dv_msk[par])
            for dt in range(c.DT):
                i = T.matmul(PA[:, par * 512: par * 512 + 512],
                             S.ktT[:, dt, ds(kt, 128)], S.qT[:, dt, qbs],
                             start=(dt == 0), stop=(dt == c.DT - 1))
            T.inc(i, 'pe')
            C.wait('pe', k.cnt['pe'])
            i = C.activation(pTf[:, kt * 512:(kt + 1) * 512],
                             PA[:, par * 512: par * 512 + 512], AF.Exp, scale=scale)
            k.ac_exp[par] = exp_done[kt] = C.inc(i, 'ac')
            if kind == 'ab':
                V.wait('ac', k.ac_exp[par])
                kdcol = S.kd[:, (sup * c.NKT + kt) * nqb + qh * c.NQB + qb:
                             (sup * c.NKT + kt) * nqb + qh * c.NQB + qb + 1]
                i = V.scalar_tensor_tensor(pTf[:, kt * 512:(kt + 1) * 512],
                                           S.qrow[:, 0:c.QB], kdcol,
                                           pTf[:, kt * 512:(kt + 1) * 512],
                                           ALU.is_le, ALU.mult)
                k.dv_msk[par] = msk_done[kt] = V.inc(i, 'dv')
            if kt >= 1:
                sums(kt - 1)
        i = sums(c.NKT - 1)
        T.inc(i, 'pe')
        V.wait('pe', k.cnt['pe'])
        V.inc(V.tensor_add(S.rows2[0:1, qb * c.QB:(qb + 1) * c.QB],
                           PA[0:1, 1024:1536],
                           S.rows2[0:1, qb * c.QB:(qb + 1) * c.QB]), 'dv')
        T.wait('dv', k.cnt['dv'])
        T.wait('ac', k.cnt['ac'])
        for kt in range(c.NKT):
            for dt in range(c.DT):
                ps = PA if dt < 4 else PB
                col = (dt % 4) * 512
                i = T.matmul(ps[:, col:col + 512], S.vsup[:, kt, ds(dt, 128)],
                             pTf[:, kt * 512:(kt + 1) * 512],
                             start=(kt == 0), stop=(kt == c.NKT - 1))
        T.inc(i, 'pe')
        V.wait('pe', k.cnt['pe'])
        i = V.tensor_add(arena[:, 0:4, qbs],
                         PA[:].rearrange('p (a q) -> p a q', q=512),
                         arena[:, 0:4, qbs])
        V.inc(i, 'dv')
        i = V.tensor_add(arena[:, 4:8, qbs],
                         PB[:].rearrange('p (a q) -> p a q', q=512),
                         arena[:, 4:8, qbs])
        V.inc(i, 'dv')
        k.ac_exp = [0, 0]
        k.dv_msk = [0, 0]
        T.wait('dv', k.cnt['dv'])


def emit_epilogue_proj(k, qh, wname, bname, src, dst, heads):
    c, S, E = k.c, k.S, k.E
    Y, P, V, C, T = k.Y, k.P, k.V, k.C, k.T
    PA, PB = k.PA, k.PB
    QH = c.QH
    qsl = ds(qh, QH)
    arena = S.arena[:].rearrange('p (a q) -> p a q', q=QH)
    barrier(k)
    if heads:
        for hh in range(16):
            V.inc(V.reciprocal(S.rowsc[0:1, 0:QH], aa_sum_slot(k, hh, 0, QH)), 'dv')
            T.wait('dv', k.cnt['dv'])
            for ch in range(QH // 512):
                i = T.matmul(PB[0:64, (ch % 2) * 512:(ch % 2) * 512 + 512],
                             S.ones1r32[0:1, 0:64], S.rowsc[0:1, ds(ch, 512)],
                             start=True, stop=True)
                T.inc(i, 'pe')
                V.wait('pe', k.cnt['pe'])
                rb = (hh % 2) * 64
                V.inc(V.tensor_mul(S.qT[rb:rb + 64, hh // 2, ds(ch, 512)],
                                   arena[rb:rb + 64, hh // 2, ds(ch, 512)],
                                   PB[0:64, (ch % 2) * 512:(ch % 2) * 512 + 512]), 'dv')
            T.wait('dv', k.cnt['dv'])
    else:
        V.inc(V.reciprocal(S.rowsc[0:1, 0:QH], S.rows2[0:1, 0:QH]), 'dv')
        T.wait('dv', k.cnt['dv'])
        for ch in range(QH // 512):
            i = T.matmul(PB[:, (ch % 2) * 512:(ch % 2) * 512 + 512],
                         S.ones1r32[:, :], S.rowsc[0:1, ds(ch, 512)],
                         start=True, stop=True)
            T.inc(i, 'pe')
            V.wait('pe', k.cnt['pe'])
            for dt in range(c.DT):
                V.inc(V.tensor_mul(S.qT[:, dt, ds(ch, 512)],
                                   arena[:, dt, ds(ch, 512)],
                                   PB[:, (ch % 2) * 512:(ch % 2) * 512 + 512]), 'dv')
            T.wait('dv', k.cnt['dv'])
    if bname:
        Y.inc(Y.dma_start(S.cols[:, 8:16], E[bname].ap()), 'dq', 16)
    barrier(k)
    xs = E[src].ap()
    dd = E[dst].ap()
    for g in range(8):
        buf = g % 2
        if g >= 2:
            P.wait('pe', k.pe_qw[buf])
        P.inc(P.dma_start(S.w0[:, buf, 0:1024].rearrange('p (a q) -> p a q', q=128),
                          E[wname].ap()[:, ds(g, 128)]
                          .rearrange('(a p) q -> p a q', p=128)), 'dw%d' % buf, 16)
        Y.wait('dv', k.cnt['dv'])
        Y.wait('sb', k.bc_st)
        Y.inc(Y.dma_start(S.bc[:], xs[ds(g, 128), qsl]), 'sb', 16)
        T.wait('sb', k.cnt['sb'])
        T.wait('dw%d' % buf, k.cnt['dw%d' % buf])
        for ch in range(QH // 512):
            for dt in range(c.DT):
                i = T.matmul(PA[:, (ch % 2) * 512:(ch % 2) * 512 + 512],
                             S.w0[:, buf, ds(dt, 128)], S.qT[:, dt, ds(ch, 512)],
                             start=(dt == 0), stop=(dt == c.DT - 1))
        k.pe_qw[buf] = T.inc(i, 'pe')
        V.wait('pe', k.cnt['pe'])
        V.wait('sb', k.cnt['sb'])
        bcol = S.cols[:, 8 + g:9 + g] if bname else 0.0
        i = V.scalar_tensor_tensor(S.bc[:], PA[:, 0:1024], bcol, S.bc[:],
                                   ALU.add, ALU.add)
        V.inc(i, 'dv')
        Y.wait('dv', k.cnt['dv'])
        k.bc_st = Y.inc(Y.dma_start(dd[ds(g, 128), qsl], S.bc[:]), 'sb', 16)



def emit_ffn_sublayer(k, src, dst):
    c = k.c
    emit_prologue(k, src, 'ffn')
    specs = [('k', 'ktokF'), ('v', 'vtokF'), ('q', 'qtok')]
    drive(k, gen_rd(k, 'k', 'xn232'))
    barrier(k)
    for i, (t, dsttok) in enumerate(specs):
        emit_sort(k, t)
        emit_xgather(k, 'xtok')
        rg = gen_rd(k, specs[i + 1][0], 'xn232') if i + 1 < len(specs) else None
        drive(k, gen_tiles(k, t, dsttok, AF.Relu), rg)
        barrier(k)
        if t == 'v':
            emit_exchange(k, False)
    if c.dumps_q:
        barrier(k)
        for nm in c.dumps_q:
            if nm.startswith('S.'):
                sb = getattr(k.S, nm[2:])
                k.Y.inc(k.Y.dma_start(k.E['dmpq_' + nm[2:]].ap(), sb[:]),
                        'dq', 16)
            else:
                k.Y.inc(k.Y.dma_start(k.E['dmpq_' + nm].ap(), k.E[nm].ap()),
                        'dq', 16)
            k.Y.wait('dq', k.cnt['dq'])
        barrier(k)
    for qh in range(c.NQH):
        emit_attention_quarter(k, qh, 'ffn')
        emit_epilogue_y(k, qh)
    drive(k, gen_rd(k, 'f', 'yT32'))
    barrier(k)
    emit_sort(k, 'f')
    emit_xgather(k, 'ytokM')
    drive(k, gen_tiles(k, 'f', 'ftok', AF.Gelu))
    barrier(k)
    emit_final_add(k, src, dst)
    barrier(k)


def emit_epilogue_y(k, qh):
    c, S, E = k.c, k.S, k.E
    Y, V, C, T = k.Y, k.V, k.C, k.T
    PA, PB = k.PA, k.PB
    QH = c.QH
    arena = S.arena[:].rearrange('p (a q) -> p a q', q=QH)
    barrier(k)
    V.inc(V.reciprocal(S.rowsc[0:1, 0:QH], S.rows2[0:1, 0:QH]), 'dv')
    for hf in range(QH // 512):
        s_global = (qh * QH + hf * 512) // 512
        tsl = ds(s_global, 512)
        Y.wait('dq', k.cnt['dq'])
        Y.wait('dv', k.cnt['dv'])
        Y.inc(Y.dma_start(S.x32[:], E['xn232'].ap()[:, tsl]
                          .rearrange('(a p) t -> p a t', p=128)), 'dq', 16)
        T.wait('dv', k.cnt['dv'])
        i = T.matmul(PB[:, 0:512], S.ones1r32[:, :], S.rowsc[0:1, ds(hf, 512)],
                     start=True, stop=True)
        T.inc(i, 'pe')
        V.wait('pe', k.cnt['pe'])
        V.wait('dq', k.cnt['dq'])
        for dt in range(c.DT):
            V.inc(V.tensor_mul(arena[:, dt, ds(hf, 512)], arena[:, dt, ds(hf, 512)],
                               PB[:, 0:512]), 'dv')
            V.inc(V.tensor_add(S.x32[:, dt, :], arena[:, dt, ds(hf, 512)],
                               S.x32[:, dt, :]), 'dv')
            V.inc(V.tensor_copy(S.xw[:, dt, :], S.x32[:, dt, :]), 'dv')
        Y.wait('dv', k.cnt['dv'])
        Y.inc(Y.dma_start(E['yT32'].ap()[:, tsl].rearrange('(a p) t -> p a t', p=128),
                          S.x32[:]), 'dq', 16)
        emit_tokmajor(k, 'ytokM', s_global)
    barrier(k)


def emit_final_add(k, src, dst):
    c, S, E = k.c, k.S, k.E
    Y, V, T = k.Y, k.V, k.T
    qt32 = k.S.qT[:].rearrange('p a b -> p (a b)').bitcast(F32)
    barrier(k)
    for s in range(c.ST):
        tsl = ds(s, 512)
        barrier(k)
        Y.inc(Y.dma_start(S.x32[:], E[src].ap()[:, tsl]
                          .rearrange('(a p) t -> p a t', p=128)), 'dq', 16)
        Y.inc(Y.dma_start(qt32.rearrange('p (a t) -> p a t', t=512),
                          E['yT32'].ap()[:, tsl]
                          .rearrange('(a p) t -> p a t', p=128)), 'dq', 16)
        for dt in range(c.DT):
            Y.inc(Y.dma_start(S.xw[:, dt, :], E['ftok'].ap()[tsl, ds(dt, 128)],
                              transpose=True), 'dq', 16)
        V.wait('dq', k.cnt['dq'])
        for dt in range(c.DT):
            V.inc(V.tensor_add(S.x32[:, dt, :], S.x32[:, dt, :],
                               qt32.rearrange('p (a t) -> p a t', t=512)[:, dt, :]), 'dv')
            V.inc(V.tensor_add(S.x32[:, dt, :], S.x32[:, dt, :], S.xw[:, dt, :]), 'dv')
        Y.wait('dv', k.cnt['dv'])
        Y.inc(Y.dma_start(E[dst].ap()[:, tsl].rearrange('(a p) t -> p a t', p=128),
                          S.x32[:]), 'dq', 16)



def drive(k, *gens):
    live = [g for g in gens if g is not None]
    while live:
        for g in list(live):
            try:
                next(g)
            except StopIteration:
                live.remove(g)


def gen_rd(k, tree, src32):
    c, S, E = k.c, k.S, k.E
    Y, P, V, C, T = k.Y, k.P, k.V, k.C, k.T
    PA, PB = k.PA, k.PB
    arenaF = S.arena[:]
    lgT0 = arenaF[:, 0:c.To]
    lgT1 = arenaF[:, c.To:2 * c.To]
    x32f = S.x32[:].rearrange('p a b -> p (a b)')
    P.wait('dv', k.cnt['dv'])
    P.wait('pe', k.cnt['pe'])
    P.inc(P.dma_start(S.nwT[:, :, 0:255].rearrange('p a n -> p a n'),
                      E['f%s_nwT' % tree].ap().rearrange('(a p) n -> p a n', p=128)),
          'dg', 16)
    Y.inc(Y.dma_start(S.nb[:], E['f%s_nb' % tree].ap()), 'sr', 16)
    P.inc(P.memset(S.arena[:, 0:2 * c.To], 0.0), 'po')
    Y.wait('dv', k.cnt['dv'])
    Y.wait('pe', k.cnt['pe'])
    yield

    rt_dv = 0
    for s in range(c.ST):
        tsl = ds(s, 512)
        Y.wait('dv', rt_dv)
        Y.inc(Y.dma_start(S.x32[:], E[src32].ap()[:, tsl]
                          .rearrange('(a p) t -> p a t', p=128)), 'sr', 16)
        xr = S.xr[:]
        V.wait('sr', k.cnt['sr'])
        V.wait('po', k.cnt['po'])
        for dt in range(c.DT):
            V.inc(V.tensor_copy(xr[:, dt, :], S.x32[:, dt, :]), 'dv')
        rt_dv = k.cnt['dv']
        yield
        T.wait('dv', rt_dv)
        T.wait('dg', k.cnt['dg'])
        for half in range(2):
            ncols = 128 if half == 0 else 127
            for dt in range(c.DT):
                i = T.matmul(PA[0:ncols, 1024 + half * 512: 1536 + half * 512],
                             S.nwT[:, dt, half * 128: half * 128 + ncols],
                             xr[:, dt, :],
                             start=(dt == 0), stop=(dt == c.DT - 1))
            T.inc(i, 'pe')
            V.wait('pe', k.cnt['pe'])
            dstl = lgT0 if half == 0 else lgT1
            i = V.tensor_scalar(dstl[:, tsl] if half == 0 else dstl[0:127, tsl],
                                PA[:, 1024 + half * 512: 1536 + half * 512] if half == 0
                                else PA[0:127, 1024 + half * 512: 1536 + half * 512],
                                S.nb[:, half:half + 1] if half == 0
                                else S.nb[0:127, half:half + 1],
                                None, ALU.add)
            V.inc(i, 'dv')
        yield

    r_pref = S.rows2[0:1, 0:c.To]
    r_b = S.rowsc[0:1, 0:512]
    NCH = c.To // 512
    V.wait('dq', k.cnt['dq'])
    V.inc(V.tensor_single_scalar(r_pref, lgT0[0:1, :], 0.0, ALU.is_gt), 'dv')
    yield
    for lv in range(1, 8):
        off = (1 << lv) - 1
        segs = [(lgT0, float(off))]
        if lv == 7:
            segs = [(lgT0, 127.0), (lgT1, -1.0)]
        units = [(chix, si) for chix in range(NCH) for si in range(len(segs))]
        nu = len(units)
        dv_A = [0] * nu; pe_B = [0] * nu; dv_C = [0] * nu; pe_D = [0] * nu

        def stA(u):
            chix, si = units[u]
            sl = u % 2
            if u >= 2:
                V.wait('pe', pe_B[u - 2])
            V.inc(V.tensor_scalar_add(S.rowsc[0:1, 1024 + sl * 512:1536 + sl * 512],
                                      r_pref[:, ds(chix, 512)], segs[si][1]), 'dv')
            dv_A[u] = k.cnt['dv']

        def stB(u):
            sl = u % 2
            T.wait('dv', dv_A[u])
            i = T.matmul(PA[:, 1024 + sl * 512:1536 + sl * 512], S.ones1r32[:, :],
                         S.rowsc[0:1, 1024 + sl * 512:1536 + sl * 512],
                         start=True, stop=True)
            T.inc(i, 'pe')
            pe_B[u] = k.cnt['pe']

        def stC(u):
            chix, si = units[u]
            sl = u % 2
            V.wait('pe', pe_B[u])
            if si == 1:
                V.wait('pe', pe_D[u - 1])
            i = V.scalar_tensor_tensor(x32f[:, ds(chix, 512)],
                                       PA[:, 1024 + sl * 512:1536 + sl * 512],
                                       S.iota[:, 0:1], segs[si][0][:, ds(chix, 512)],
                                       ALU.is_equal, ALU.mult)
            V.inc(i, 'dv')
            dv_C[u] = k.cnt['dv']

        def stD(u):
            chix, si = units[u]
            T.wait('dv', dv_C[u])
            i = T.matmul(PB[0:1, 1024 + (chix % 2) * 512:1536 + (chix % 2) * 512],
                         S.ones32c[:], x32f[:, ds(chix, 512)],
                         start=(si == 0), stop=(si == len(segs) - 1))
            T.inc(i, 'pe')
            pe_D[u] = k.cnt['pe']

        def stE(u):
            chix, si = units[u]
            V.wait('pe', pe_D[u])
            V.inc(V.tensor_single_scalar(r_b, PB[0:1, 1024 + (chix % 2) * 512:
                                                   1536 + (chix % 2) * 512],
                                         0.0, ALU.is_gt), 'dv')
            V.inc(V.scalar_tensor_tensor(r_pref[:, ds(chix, 512)],
                                         r_pref[:, ds(chix, 512)], 2.0,
                                         r_b, ALU.mult, ALU.add), 'dv')

        for step in range(nu + 2):
            if step < nu:
                stA(step); stB(step)
            if 1 <= step <= nu:
                stC(step - 1); stD(step - 1)
            if 2 <= step <= nu + 1:
                u = step - 2
                if units[u][1] == len(segs) - 1:
                    stE(u)
            yield


def emit_sort(k, tree):
    c, S, E = k.c, k.S, k.E
    Y, P, V, C, T = k.Y, k.P, k.V, k.C, k.T
    PA, PB = k.PA, k.PB
    arenaF = S.arena[:]
    lgT0 = arenaF[:, 0:c.To]
    lgT1 = arenaF[:, c.To:2 * c.To]
    x32f = S.x32[:].rearrange('p a b -> p (a b)')[:, 0:c.To]
    r_leaf = S.rows2[0:1, 0:c.To]
    r_d = S.rows2[32:33, 0:c.To]
    x32row = S.x32[0:1, :, :].rearrange('p a b -> p (a b)')[:, 0:c.To]
    NCH = c.To // 512
    rsc_c = E['rsc'].reshape([8192, 1])
    rsc_r = E['rsc'].reshape([1, 8192])

    for H in range(2):
        for chix in range(NCH):
            chs = ds(chix, 512)
            V.inc(V.tensor_scalar_add(S.rowsc[0:1, 1024:1536], r_leaf[:, chs],
                                      -128.0 * H), 'dv')
            T.wait('dv', k.cnt['dv'])
            i = T.matmul(PB[:, 0:512], S.ones1r32[:, :], S.rowsc[0:1, 1024:1536],
                         start=True, stop=True)
            T.inc(i, 'pe')
            V.wait('pe', k.cnt['pe'])
            V.inc(V.tensor_scalar(lgT0[:, chs], PB[:, 0:512], S.iota[:, 0:1], None,
                                  ALU.is_equal), 'dv')
        V.inc(V.tensor_tensor_scan(x32f, lgT0, lgT0, 0.0, ALU.add, ALU.bypass), 'dv')
        V.inc(V.tensor_copy(S.cols[:, 8 + H:9 + H], x32f[:, c.To - 1:c.To]), 'dv')
        for chix in range(NCH):
            chs = ds(chix, 512)
            V.inc(V.tensor_mul(lgT1[:, chs], lgT0[:, chs], x32f[:, chs]), 'dv')
            T.wait('dv', k.cnt['dv'])
            i = T.matmul(PA[0:1, 1024:1536], S.ones32c[:],
                         lgT1[:, chs], start=True, stop=True)
            T.inc(i, 'pe')
            V.wait('pe', k.cnt['pe'])
            if H == 0:
                V.inc(V.tensor_copy(r_d[:, chs], PA[0:1, 1024:1536]), 'dv')
            else:
                V.inc(V.tensor_add(r_d[:, chs], PA[0:1, 1024:1536], r_d[:, chs]), 'dv')
            T.wait('dv', k.cnt['dv'])
        ncol = S.cols[:, 8 + H:9 + H]
        ti = S.cols[:, 14:15].bitcast(I32)
        V.inc(V.tensor_scalar_add(S.cols[:, 10 + H:11 + H], ncol, 15.0), 'dv')
        V.inc(V.tensor_copy(ti, S.cols[:, 10 + H:11 + H]), 'dv')
        V.inc(V.tensor_single_scalar(ti, ti, -16, ALU.bitwise_and), 'dv')
        V.inc(V.tensor_copy(S.cols[:, 10 + H:11 + H], ti), 'dv')
        Y.wait('dv', k.cnt['dv'])
        Y.inc(Y.dma_start(rsc_c.ap()[H * 128:(H + 1) * 128, :],
                          S.cols[:, 10 + H:11 + H]), 'dq', 16)
    Y.wait('dq', k.cnt['dq'])
    Y.inc(Y.dma_start(S.rowsc[0:1, 0:256], rsc_r.ap()[:, 0:256]), 'dq', 16)
    V.wait('dq', k.cnt['dq'])
    V.inc(V.tensor_tensor_scan(S.rowsc[0:1, 256:512], S.rowsc[0:1, 0:256],
                               S.rowsc[0:1, 0:256], 0.0, ALU.add, ALU.bypass), 'dv')
    V.inc(V.tensor_sub(S.rowsc[0:1, 256:512], S.rowsc[0:1, 256:512],
                       S.rowsc[0:1, 0:256]), 'dv')
    Y.wait('dv', k.cnt['dv'])
    Y.inc(Y.dma_start(rsc_r.ap()[:, 256:512], S.rowsc[0:1, 256:512]), 'dq', 16)
    Y.wait('dq', k.cnt['dq'])
    for H in range(2):
        Y.inc(Y.dma_start(S.cols[:, 12 + H:13 + H],
                          rsc_c.ap()[256 + 128 * H:256 + 128 * H + 128, :]), 'dq', 16)
    V.wait('dq', k.cnt['dq'])
    for H in range(2):
        offcol = S.cols[:, 12 + H:13 + H]
        for chix in range(NCH):
            chs = ds(chix, 512)
            V.inc(V.tensor_scalar_add(S.rowsc[0:1, 1024:1536], r_leaf[:, chs],
                                      -128.0 * H), 'dv')
            T.wait('dv', k.cnt['dv'])
            i = T.matmul(PB[:, 0:512], S.ones1r32[:, :], S.rowsc[0:1, 1024:1536],
                         start=True, stop=True)
            T.inc(i, 'pe')
            V.wait('pe', k.cnt['pe'])
            V.inc(V.tensor_scalar(lgT0[:, chs], PB[:, 0:512], S.iota[:, 0:1], None,
                                  ALU.is_equal), 'dv')
            V.inc(V.tensor_scalar(lgT0[:, chs], lgT0[:, chs], offcol, None, ALU.mult), 'dv')
            T.wait('dv', k.cnt['dv'])
            i = T.matmul(PA[0:1, 1024:1536], S.ones32c[:],
                         lgT0[:, chs], start=True, stop=True)
            T.inc(i, 'pe')
            V.wait('pe', k.cnt['pe'])
            V.inc(V.tensor_add(r_d[:, chs], PA[0:1, 1024:1536], r_d[:, chs]), 'dv')
            T.wait('dv', k.cnt['dv'])
    V.inc(V.tensor_scalar_add(r_leaf, r_d, -1.0), 'dv')
    di = lgT0[0:1, 0:c.To].bitcast(I32)
    V.inc(V.tensor_copy(di, r_leaf), 'dv')
    V.inc(V.tensor_single_scalar(di, di, 127, ALU.bitwise_and), 'dv')
    V.inc(V.tensor_copy(x32row, di), 'dv')
    V.inc(V.tensor_sub(r_leaf, r_leaf, x32row), 'dv')
    V.inc(V.tensor_scalar(r_leaf, r_leaf, 1.0 / 128.0, None, ALU.mult), 'dv')
    V.inc(V.tensor_scalar(x32row, x32row, float(c.NTILE), None, ALU.mult), 'dv')
    V.inc(V.tensor_add(r_leaf, r_leaf, x32row), 'dv')
    nvt = c.To // 128
    Y.wait('dv', k.cnt['dv'])
    Y.inc(Y.dma_start(rsc_r.ap()[:, 0:c.To], r_leaf), 'dq', 16)
    Y.wait('dq', k.cnt['dq'])
    Y.inc(Y.dma_start(S.bc[0:nvt, 0:128], E['rsc'].reshape([64, 128]).ap()[0:nvt, :]), 'dq', 16)
    T.wait('dq', k.cnt['dq'])
    i = T.matmul(PB[:, 0:nvt], S.bc[0:nvt, 0:128],
                 S.iden32[0:nvt, 0:nvt], start=True, stop=True)
    T.inc(i, 'pe')
    V.wait('pe', k.cnt['pe'])
    V.inc(V.tensor_copy(S.dstc[:, 0:nvt], PB[:, 0:nvt]), 'dv')
    P.wait('dv', k.cnt['dv'])
    P.inc(P.dma_start(E['slotTab'].ap(), E['c_oob'].ap()), 'dg', 16)
    P.wait('dg', k.cnt['dg'])
    for i_t in range(nvt):
        P.wait('dg', k.cnt['dg'])
        P.inc(P.indirect_dma_start(
            rsc_i32(k), IndirectOffsetOnAxis(ap=S.dstc[:, i_t:i_t + 1], axis=0),
            S.tki[:, i_t:i_t + 1], None), 'dg', 16)
    P.wait('dg', k.cnt['dg'])
    P.inc(P.dma_start(S.srt[:], E['slotTab'].reshape([128, c.NTILE]).ap()), 'dg', 16)
    for H in range(2):
        offcol = S.cols[:, 12 + H:13 + H]
        T.wait('dv', k.cnt['dv'])
        i = T.matmul(PB[:, 0:c.NCHUNK], S.ones1r32[:, :], S.srow[0:1, 0:c.NCHUNK],
                     start=True, stop=True)
        T.inc(i, 'pe')
        V.wait('pe', k.cnt['pe'])
        V.inc(V.tensor_scalar(lgT0[:, 0:c.NCHUNK], PB[:, 0:c.NCHUNK], offcol, None,
                              ALU.is_ge), 'dv')
        T.wait('dv', k.cnt['dv'])
        i = T.matmul(PA[0:1, 1024:1024 + c.NCHUNK], S.ones32c[:],
                     lgT0[:, 0:c.NCHUNK],
                     start=(H == 0), stop=(H == 1))
        T.inc(i, 'pe')
    V.wait('pe', k.cnt['pe'])
    V.inc(V.tensor_scalar_add(S.rowsc[0:1, 1024:1024 + c.NCHUNK], PA[0:1, 1024:1024 + c.NCHUNK],
                              -1.0), 'dv')
    Y.wait('dv', k.cnt['dv'])
    Y.inc(Y.dma_start(rsc_r.ap()[:, 0:c.NCHUNK], S.rowsc[0:1, 1024:1024 + c.NCHUNK]), 'dq', 16)
    Y.wait('dq', k.cnt['dq'])
    Y.inc(Y.dma_start(S.bc[0:c.NTILE, 0:8], E['rsc'].reshape([1024, 8]).ap()[0:c.NTILE, :]), 'dq', 16)
    V.wait('dq', k.cnt['dq'])
    V.inc(V.tensor_copy(S.ysl[0:c.NTILE, 0, 0:8], S.bc[0:c.NTILE, 0:8]), 'dv')
    T.wait('dv', k.cnt['dv'])
    i = T.matmul(PB[0:8, 0:c.NTILE], S.ysl[0:c.NTILE, 0, 0:8],
                 S.iden[0:c.NTILE, 0:c.NTILE], start=True, stop=True)
    T.inc(i, 'pe')
    V.wait('pe', k.cnt['pe'])
    V.inc(V.tensor_copy(S.clrf[:], PB[0:8, 0:c.NTILE]), 'dv')
    P.wait('dv', k.cnt['dv'])
    P.inc(P.memset(S.clri[:], 255), 'po')
    V.wait('po', k.cnt['po'])
    V.inc(V.tensor_copy(S.clri[0:8, :], S.clrf[:]), 'dv')
    V.inc(V.tensor_copy(S.ysl[0:8, 0, 0:c.NTILE], S.clrf[:]), 'dv')
    T.wait('dv', k.cnt['dv'])
    i = T.matmul(PB[:, 1024:1024 + c.NTILE], S.indic[:], S.ysl[0:8, 0, 0:c.NTILE],
                 start=True, stop=True)
    T.inc(i, 'pe')
    V.wait('pe', k.cnt['pe'])
    V.inc(V.tensor_scalar(S.bc[:, 0:c.NTILE], PB[:, 1024:1024 + c.NTILE], 16.0,
                          S.pmod[:, 0:1], ALU.mult, ALU.add), 'dv')
    V.inc(V.tensor_copy(S.idxw2[:], S.bc[:, 0:c.NTILE]), 'dv')
    cs = min(512, c.NCHUNK)
    for chix in range(c.NCHUNK // cs):
        chs = ds(chix, cs)
        T.wait('dv', k.cnt['dv'])
        i = T.matmul(PB[:, 0:cs], S.ones1r32[:, :], S.rowsc[0:1, 1024:1024 + c.NCHUNK][:, chs],
                     start=True, stop=True)
        T.inc(i, 'pe')
        V.wait('pe', k.cnt['pe'])
        V.inc(V.tensor_scalar(S.bc[:, 0:cs], PB[:, 0:cs], 128.0, S.iota[:, 0:1],
                              ALU.mult, ALU.add), 'dv')
        V.inc(V.tensor_copy(S.idxw1[:, chs], S.bc[:, 0:cs]), 'dv')
    barrier(k)


def rsc_i32(k):
    return k.E['slotTab'].reshape([k.c.SLOTS, 1]).ap()


def emit_xgather(k, srctok):
    c, S, E = k.c, k.S, k.E
    Y, P = k.Y, k.P
    for g in range(c.NTILE // 4):
        hb = g % 2
        if g >= 2:
            P.wait('gx%d' % hb, k.wtok_fff[hb])
        for j in range(4):
            i_t = 4 * g + j
            P.inc(P.indirect_dma_start(
                S.qT[:, 4 * hb + j, :], None, E[srctok].ap(),
                IndirectOffsetOnAxis(ap=S.srt[:, i_t:i_t + 1], axis=0),
                bounds_check=k.breg, oob_is_err=False), 'gg%d' % hb, 16)
        Y.wait('gg%d' % hb, k.cnt['gg%d' % hb])
        k.wtok_fff[hb] = Y.inc(
            Y.dma_start(E['xsorted'].ap()[ds(g, 512), :]
                        .rearrange('(j p) d -> p j d', p=128),
                        S.qT[:, 4 * hb:4 * hb + 4, :]),
            'gx%d' % hb, 16)
    barrier(k)


def gen_tiles(k, tree, dsttok, act):
    c, S, E = k.c, k.S, k.E
    Y, P, V, C, T = k.Y, k.P, k.V, k.C, k.T
    PA, PB = k.PA, k.PB
    dstap = E[dsttok].ap()[:, 0:1024]
    tok_pe = [0, 0, 0, 0]
    tok_ac = [0, 0]
    tok_msk = [0, 0]
    tok_dvy = [0, 0]
    y_pe = [0, 0]
    dq_blk = [0] * c.NBLK
    for dt in range(c.DT):
        Y.inc(Y.dma_start(S.qT[:, dt, 0:1024],
                          E['xsorted'].ap()[ds(0, 1024), ds(dt, 128)],
                          transpose=True), 'dq', 16)
    dq_blk[0] = k.cnt['dq']

    def gath(it):
        buf = it % 4
        P.wait('pe', tok_pe[buf])
        gsem = 'gt%d' % buf
        for cc in range(8):
            P.inc(P.indirect_dma_start(
                S.ktT[:, buf, 128 * cc:128 * cc + 128], None,
                E['w1gB_' + tree].ap(),
                IndirectOffsetOnAxis(ap=S.idxw1[:, it * 8 + cc:it * 8 + cc + 1], axis=0)), gsem, 16)
        P.inc(P.indirect_dma_start(
            S.vsup[:, buf, 0:1025], None, E['w2gB_' + tree].ap(),
            IndirectOffsetOnAxis(ap=S.idxw2[:, it:it + 1], axis=0)), gsem, 16)
        P.inc(P.indirect_dma_start(
            S.vsup[0:8, 4 + buf, 0:1024], None, E['b2B_' + tree].ap(),
            IndirectOffsetOnAxis(ap=S.clri[0:8, it:it + 1], axis=0)), gsem, 16)

    for it in range(min(3, c.NTILE)):
        gath(it)
    for it in range(c.NTILE):
        blk, ti = it // 8, it % 8
        buf = it % 4
        pp = it % 2
        gsem = 'gt%d' % buf
        if it + 3 < c.NTILE:
            gath(it + 3)
        V.wait(gsem, k.cnt[gsem])
        i = V.tensor_copy(
            S.ktT[:, 4 + buf, :].rearrange('p (a b j) -> p a b j', a=8, j=16),
            S.ktT[:, buf, :].rearrange('p (a b j) -> p b a j', a=8, j=16))
        V.inc(i, 'dv')
        i = V.tensor_copy(S.b1sb[:, buf, :], S.vsup[:, buf, 1024:1025])
        k.w1cp[buf] = V.inc(i, 'dv')
        if ti == 0:
            T.wait('dq', dq_blk[blk])
        T.wait(gsem, k.cnt[gsem])
        T.wait('dv', k.w1cp[buf])
        T.wait('ac', tok_ac[pp])
        for dt in range(c.DT):
            i = T.matmul(PA[:, pp * 512: pp * 512 + 128],
                         S.ktT[:, 4 + buf, ds(dt, 128)], S.qT[:, dt, ds(ti, 128)],
                         start=(dt == 0), stop=(dt == c.DT - 1))
        hpe = T.inc(i, 'pe')
        if ti == 7 and blk + 1 < c.NBLK:
            Y.wait('pe', hpe)
            for dt in range(c.DT):
                Y.inc(Y.dma_start(S.qT[:, dt, 0:1024],
                                  E['xsorted'].ap()[ds(blk + 1, 1024), ds(dt, 128)],
                                  transpose=True), 'dq', 16)
            dq_blk[blk + 1] = k.cnt['dq']
        C.wait('pe', k.cnt['pe'])
        C.wait(gsem, k.cnt[gsem])
        C.wait('dv', k.w1cp[buf])
        C.wait('dv', tok_msk[pp])
        i = C.activation(S.hA[:, pp, :], PA[:, pp * 512: pp * 512 + 128],
                         act, bias=S.b1sb[:, buf, :])
        tok_ac[pp] = C.inc(i, 'ac')
        V.wait('ac', k.cnt['ac'])
        V.wait('pe', y_pe[pp])
        i = V.tensor_mul(S.hT[:, pp, :], S.hA[:, pp, :], S.mdiag[:])
        tok_msk[pp] = V.inc(i, 'dv')
        T.wait('dv', k.cnt['dv'])
        T.wait('dv', tok_dvy[0])
        T.matmul(PB[:, 0:512], S.hT[:, pp, :], S.vsup[:, buf, 0:512],
                 start=True, stop=False)
        T.matmul(PB[:, 512:1024], S.hT[:, pp, :],
                 S.vsup[:, buf, 512:1024], start=True, stop=False)
        T.matmul(PB[:, 0:512], S.indic[:], S.vsup[0:8, 4 + buf, 0:512],
                 start=False, stop=True)
        i = T.matmul(PB[:, 512:1024], S.indic[:],
                     S.vsup[0:8, 4 + buf, 512:1024], start=False, stop=True)
        tok_pe[buf] = y_pe[pp] = T.inc(i, 'pe')
        V.wait('pe', k.cnt['pe'])
        V.wait('gy%d' % pp, k.ysl_tok[pp])
        i = V.tensor_copy(S.ysl[:, pp, :], PB[:, 0:1024])
        tok_dvy[0] = V.inc(i, 'dv')
        P.wait('dv', k.cnt['dv'])
        k.ysl_tok[pp] = P.inc(P.indirect_dma_start(
            dstap, IndirectOffsetOnAxis(ap=S.srt[:, it:it + 1], axis=0),
            S.ysl[:, pp, :], None,
            bounds_check=k.breg, oob_is_err=False), 'gy%d' % pp, 16)
        if c.dbg_serial:
            barrier(k)
        yield



def kernel(**inputs):
    cfg = Cfg()
    k = build(cfg)
    emit_program(k, script)
    k.es.close()
    in_maps = [prep_core_inputs(inputs, core, cfg) for core in range(8)]
    res = run_bass_kernel_spmd(k.nc, in_maps, core_ids=list(range(8)))
    out = np.empty((4, 8192, 1024), np.float32)
    for core in range(8):
        b, h = core // 2, core % 2
        out[b, h * cfg.To:(h + 1) * cfg.To, :] = res.results[core]['outT'].T
    return out
